# version 04
# baseline (speedup 1.0000x reference)
import numpy as np
import concourse.bass as bass
import concourse.mybir as mybir
from concourse.bass_utils import run_bass_kernel_spmd

F32 = mybir.dt.float32
BF16 = mybir.dt.bfloat16
AF = mybir.ActivationFunctionType
ALU = mybir.AluOpType
AX = mybir.AxisListType

NCORES = 8
P = 128

import ml_dtypes
NP_BF16 = ml_dtypes.bfloat16

ENGINES = ("sync", "vector", "scalar", "tensor", "gpsimd")


class Op:
    __slots__ = ("eng", "emit", "deps", "inc", "cum", "sem")

    def __init__(self, eng, emit, deps, inc, sem):
        self.eng, self.emit, self.deps, self.inc = eng, emit, deps, inc
        self.sem = sem
        self.cum = None


class Prog:

    def __init__(self, nc):
        self.nc = nc
        self.ops = {e: [] for e in ENGINES}
        self.cum = {e: 0 for e in ENGINES}
        self.last_writer = {}
        self.readers = {}
        self.fence_deps = []

    def op(self, eng, emit, r=(), w=(), deps=(), dma=False, no_inc=False):
        dep = set(d for d in deps if d is not None)
        dep.update(self.fence_deps)
        sem = "gpsimd_c" if (eng == "gpsimd" and not dma) else eng
        for buf in r:
            lw = self.last_writer.get(id(buf))
            if lw is not None:
                dep.add(lw)
        for buf in w:
            lw = self.last_writer.get(id(buf))
            if lw is not None:
                dep.add(lw)
            for rd in self.readers.get(id(buf), ()):
                dep.add(rd)
        o = Op(eng, emit, list(dep), 0 if no_inc else (16 if dma else 1), sem)
        self.cum[sem] = self.cum.get(sem, 0) + o.inc
        o.cum = self.cum[sem]
        self.ops[eng].append(o)
        for buf in r:
            self.readers.setdefault(id(buf), []).append(o)
        for buf in w:
            self.last_writer[id(buf)] = o
            self.readers[id(buf)] = []
        return o

    def dma(self, eng, out, in_, r=(), w=(), deps=(), **kw):
        return self.op(eng, lambda e: e.dma_start(out=out, in_=in_, **kw),
                       r=r, w=w, deps=deps, dma=True)

    def _batch_ends(self):
        be = {}
        for eng in ENGINES:
            ops = self.ops[eng]
            i = 0
            while i < len(ops):
                j = i
                while j + 1 < len(ops) and not ops[j + 1].deps:
                    j += 1
                last_by_sem = {}
                for k2 in range(i, j + 1):
                    last_by_sem[ops[k2].sem] = ops[k2].cum
                for k2 in range(i, j + 1):
                    be[ops[k2]] = last_by_sem[ops[k2].sem]
                i = j + 1
        return be

    def finalize(self, block, sems):
        batch_end = self._batch_ends()
        for eng in ENGINES:
            ops = self.ops[eng]
            if not ops:
                continue

            has_dma = any(o.inc == 16 for o in ops)

            def section(e, _ops=ops, _eng=eng, _has_dma=has_dma):
                seen = {}
                prev_dma_cum = 0
                for o in _ops:
                    if o.deps and _has_dma and prev_dma_cum > seen.get(_eng, 0):
                        e.wait_ge(sems[_eng], prev_dma_cum)
                        seen[_eng] = prev_dma_cum
                    need = {}
                    for d in o.deps:
                        if d.sem == "tensor" and _eng == "tensor":
                            continue
                        v = batch_end[d] if d.inc == 16 else d.cum
                        if v > seen.get(d.sem, 0) and v > need.get(d.sem, 0):
                            need[d.sem] = v
                    for sn in sorted(need):
                        e.wait_ge(sems[sn], need[sn])
                        seen[sn] = need[sn]
                    inst = o.emit(e)
                    if o.inc:
                        inst.then_inc(sems[o.sem], o.inc)
                    if o.inc == 16:
                        prev_dma_cum = o.cum

            getattr(block, eng)(section)


class Builder:
    def __init__(self):
        self.nc = bass.Bass("TRN2", target_bir_lowering=False, debug=False)
        self.p = Prog(self.nc)
        self._ctx = []
        self._name_i = 0

    def name(self, pfx):
        self._name_i += 1
        return f"{pfx}{self._name_i}"

    def get_eye(self):
        if not hasattr(self, "_eyeT"):
            self._eyeT = self.sbuf([P, P], BF16, name="eyeTg")
            self.p.dma("sync", self._eyeT[:], self.eye_hbm[:, :], w=[self._eyeT])
        return self._eyeT

    def get_ptr(self):
        if not hasattr(self, "_ptr"):
            self._ptr = self.psum([P, P], BF16, name="ptrg")
        return self._ptr

    def get_pmisc(self):
        if not hasattr(self, "_pmisc"):
            self._pmisc = self.psum([P, 512], name="pmiscg")
        return self._pmisc

    def get_onesr(self):
        if not hasattr(self, "_onesr"):
            self._onesr = self.sbuf([1, P], name="onesrg")
            self.p.op("vector", lambda e: e.memset(self._onesr[:], 1.0), w=[self._onesr])
        return self._onesr

    def cconst(self, value):
        if not hasattr(self, "_cpool"):
            self._cpool = self.sbuf([P, 16], F32, name="cpool")
            self._cvals = {}
        v = float(value)
        if v not in self._cvals:
            k = len(self._cvals)
            assert k < 16
            self._cvals[v] = k
            col = self._cpool[:, k:k + 1]
            self.p.op("vector", lambda e, col=col, v=v: e.memset(col, v), w=[self._cpool])
        k = self._cvals[v]
        return self._cpool[:, k:k + 1]

    def sbuf(self, shape, dtype=F32, name=None):
        cm = self.nc.sbuf_tensor(name or self.name("sb"), shape, dtype)
        t = cm.__enter__()
        self._ctx.append(cm)
        self._ctx_t = getattr(self, "_ctx_t", {})
        self._ctx_t[id(cm)] = t
        return t

    def free(self, t):
        for i, cm in enumerate(self._ctx):
            if getattr(self, "_ctx_t", {}).get(id(cm)) is t:
                cm.__exit__(None, None, None)
                del self._ctx[i]
                break
        lw = self.p.last_writer.get(id(t))
        rd = self.p.readers.get(id(t), [])
        out = ([lw] if lw is not None else []) + list(rd)
        self.p.fence_deps = list(set(self.p.fence_deps) | set(out))
        return out

    def psum(self, shape, dtype=F32, name=None):
        cm = self.nc.psum_tensor(name or self.name("ps"), shape, dtype)
        t = cm.__enter__()
        self._ctx.append(cm)
        self._ctx_t = getattr(self, "_ctx_t", {})
        self._ctx_t[id(cm)] = t
        return t

    def din(self, name, shape, dtype=F32):
        return self.nc.dram_tensor(name, shape, dtype, kind="ExternalInput").ap()

    def dout(self, name, shape, dtype=F32):
        return self.nc.dram_tensor(name, shape, dtype, kind="ExternalOutput").ap()

    def dscratch(self, name, shape, dtype=F32):
        return self.nc.dram_tensor(name, shape, dtype).ap()

    def build(self):
        sems = {}
        for e in list(ENGINES) + ["gpsimd_c"]:
            cm = self.nc.semaphore(f"sem_{e}")
            sems[e] = cm.__enter__()
            self._ctx.append(cm)
        blk_cm = self.nc.Block()
        block = blk_cm.__enter__()
        self.p.finalize(block, sems)
        blk_cm.__exit__(None, None, None)
        for cm in reversed(self._ctx):
            cm.__exit__(None, None, None)
        return self.nc



def emit_plain_ln(b, xt, out_t, sq, st, D, eps=1e-5, extra_r=()):
    p = b.p
    p.op("vector", lambda e: e.tensor_reduce(out=st[:, 0:1], in_=xt[:], axis=AX.X, op=ALU.add),
         r=[xt], w=[st])
    p.op("vector", lambda e: e.tensor_scalar_mul(out=st[:, 1:2], in0=st[:, 0:1], scalar1=1.0 / D),
         r=[st], w=[st])
    p.op("vector", lambda e: e.tensor_scalar(out=out_t[:], in0=xt[:], scalar1=st[:, 1:2],
                                             scalar2=None, op0=ALU.subtract),
         r=[xt, st], w=[out_t])
    p.op("scalar", lambda e: e.activation(out=sq[:, 0:D], in_=out_t[:], func=AF.Square,
                                          accum_out=st[:, 2:3]),
         r=[out_t], w=[sq, st])
    epsc = b.cconst(float(eps))
    p.op("scalar", lambda e: e.activation(out=st[:, 4:5], in_=st[:, 2:3], func=AF.Ln,
                                          bias=epsc, scale=1.0 / D),
         r=[st, b._cpool], w=[st])
    p.op("scalar", lambda e: e.activation(out=st[:, 5:6], in_=st[:, 4:5], func=AF.Exp, scale=-0.5),
         r=[st], w=[st])
    return p.op("vector", lambda e: e.tensor_scalar(out=out_t[:], in0=out_t[:], scalar1=st[:, 5:6],
                                                    scalar2=None, op0=ALU.mult),
                r=[out_t, st], w=[out_t])


def emit_bcast_row(b, dst, src_row, cols, ones_row, pmisc, r_extra=()):
    p = b.p
    for c0 in range(0, cols, 512):
        c1 = min(c0 + 512, cols)
        p.op("tensor", lambda e, c0=c0, c1=c1: e.matmul(
            pmisc[:, 0:c1 - c0], ones_row[0:1, 0:P], src_row[0:1, c0:c1], start=True, stop=True),
            r=[ones_row] + list(r_extra), w=[pmisc])
        p.op("scalar", lambda e, c0=c0, c1=c1, w_=list(r_extra): e.copy(out=dst[:, c0:c1], in_=pmisc[:, 0:c1 - c0]),
             r=[pmisc], w=[dst] + list(r_extra))



_prog_cache = {}


def build_l1ffn(R):
    key = ("l1ffn", R)
    if key in _prog_cache:
        return _prog_cache[key]
    D, H = 1024, 4096
    nT = R // P
    b = Builder()
    p = b.p
    x_in = b.din("x", [R, D])
    wfcT = b.din("wfcT", [D, H])
    wprT = b.din("wprT", [H, D])
    lnw = b.din("lnw", [1, H])
    lnb = b.din("lnb", [1, H])
    eye = b.din("eye", [P, P], BF16)
    out = b.dout("out", [R, D])
    hn_scr = b.dscratch("hn_scr", [R, H], BF16)

    eyeT = b.sbuf([P, P], BF16)
    onesr = b.sbuf([1, P], F32)
    p.dma("sync", eyeT[:], eye[:, :], w=[eyeT])
    p.op("vector", lambda e: e.memset(onesr[:], 1.0), w=[onesr])
    onesc = b.sbuf([P, 1], F32)
    p.op("vector", lambda e: e.memset(onesc[:], 1.0), w=[onesc])

    qw = b.sbuf([P, 32768], BF16, name="qw")
    wtmp = [b.sbuf([P, H], F32) for _ in range(2)]
    gtb = b.sbuf([P, H], F32)
    pmisc = b.psum([P, 512], F32)

    def quantize(whbm, nrow_tiles, W, dst_slices, tag):
        absr = b.sbuf([P, nrow_tiles + 2], F32, name=f"absr{tag}")
        scl = b.sbuf([P, 2], F32, name=f"scl{tag}")
        thrb = b.sbuf([P, 2], F32, name=f"thrb{tag}")
        for j in range(nrow_tiles):
            t = wtmp[j % 2]
            p.dma("sync", t[:, 0:W], whbm[j * P:(j + 1) * P, :], w=[t])
            p.op("vector", lambda e, j=j, t=t: e.tensor_reduce(
                out=absr[:, j:j + 1], in_=t[:, 0:W], axis=AX.X, op=ALU.add,
                apply_absolute_value=True), r=[t], w=[absr])
        p.op("vector", lambda e: e.tensor_reduce(out=absr[:, nrow_tiles:nrow_tiles + 1],
                                                 in_=absr[:, 0:nrow_tiles], axis=AX.X, op=ALU.add),
             r=[absr], w=[absr])
        p.op("tensor", lambda e: e.matmul(pmisc[0:1, 0:1], absr[:, nrow_tiles:nrow_tiles + 1],
                                          onesc[:, 0:1], start=True, stop=True),
             r=[absr, onesc], w=[pmisc])
        N = nrow_tiles * P * W
        halfeps = b.cconst(0.5e-6)
        p.op("scalar", lambda e: e.activation(out=scl[0:1, 0:1], in_=pmisc[0:1, 0:1],
                                              func=AF.Identity, scale=0.5 / N, bias=halfeps[0:1, 0:1]),
             r=[pmisc, b._cpool], w=[scl])
        p.op("scalar", lambda e: e.activation(out=scl[0:1, 1:2], in_=scl[0:1, 0:1],
                                              func=AF.Identity, scale=-1.0), r=[scl], w=[scl])
        emit_bcast_row(b, thrb, scl, 2, onesr, pmisc, r_extra=[scl])
        for j in range(nrow_tiles):
            t = wtmp[j % 2]
            p.dma("sync", t[:, 0:W], whbm[j * P:(j + 1) * P, :], w=[t], r=[thrb])
            p.op("vector", lambda e, t=t: e.tensor_scalar(
                out=gtb[:, 0:W], in0=t[:, 0:W], scalar1=thrb[:, 0:1], scalar2=None, op0=ALU.is_gt),
                r=[t, thrb], w=[gtb])
            p.op("vector", lambda e, t=t: e.tensor_scalar(
                out=t[:, 0:W], in0=t[:, 0:W], scalar1=thrb[:, 1:2], scalar2=None, op0=ALU.is_lt),
                r=[t, thrb], w=[t])
            p.op("vector", lambda e, j=j, t=t, d=dst_slices[j]: e.tensor_sub(
                out=d, in0=gtb[:, 0:W], in1=t[:, 0:W]), r=[gtb, t], w=[qw])

    q1_slices = [qw[:, j * H:(j + 1) * H] for j in range(8)]
    quantize(wfcT, 8, H, q1_slices, "fc")

    wbc = b.sbuf([P, H], F32)
    bbc = b.sbuf([P, H], F32)
    p.dma("sync", wbc[0:1, :], lnw[:, :], w=[wbc])
    p.dma("sync", bbc[0:1, :], lnb[:, :], w=[bbc])
    emit_bcast_row(b, wbc, wbc[0:1, 0:H], H, onesr, pmisc, r_extra=[wbc])
    emit_bcast_row(b, bbc, bbc[0:1, 0:H], H, onesr, pmisc, r_extra=[bbc])

    xt = b.sbuf([P, D], F32)
    st = b.sbuf([P, 8], F32)
    sq = gtb
    xn32 = b.sbuf([P, D], F32)
    xnb = b.sbuf([P, D], BF16)
    ptr = b.psum([P, P], BF16)
    xf = b.sbuf([P, 8 * P], BF16)
    ph = b.psum([P, 2048], F32)
    hg = b.sbuf([P, H], F32)
    hnb = b.sbuf([P, H], BF16)
    hfm = b.sbuf([P, 32 * P], BF16)

    for i in range(nT):
        p.dma("sync", xt[:], x_in[i * P:(i + 1) * P, :], w=[xt])
        emit_plain_ln(b, xt, xn32, sq, st, D)
        p.op("vector", lambda e: e.tensor_copy(out=xnb[:], in_=xn32[:]), r=[xn32], w=[xnb])
        for c in range(8):
            p.op("tensor", lambda e, c=c: e.transpose(ptr[:], xnb[:, c * P:(c + 1) * P], eyeT[:]),
                 r=[xnb, eyeT], w=[ptr])
            p.op("scalar", lambda e, c=c: e.copy(out=xf[:, c * P:(c + 1) * P], in_=ptr[:]),
                 r=[ptr], w=[xf])
        for half in range(2):
            for nb_ in range(4):
                for k in range(8):
                    p.op("tensor", lambda e, k=k, nb_=nb_, half=half: e.matmul(
                        ph[:, nb_ * 512:(nb_ + 1) * 512],
                        xf[:, k * P:(k + 1) * P],
                        qw[:, k * H + half * 2048 + nb_ * 512:k * H + half * 2048 + (nb_ + 1) * 512],
                        start=(k == 0), stop=(k == 7)),
                        r=[xf, qw], w=[ph])
            p.op("scalar", lambda e, half=half: e.activation(
                out=hg[:, half * 2048:(half + 1) * 2048], in_=ph[:, :], func=AF.Gelu),
                r=[ph], w=[hg])
        p.op("vector", lambda e: e.tensor_reduce(out=st[:, 0:1], in_=hg[:], axis=AX.X, op=ALU.add),
             r=[hg], w=[st])
        p.op("vector", lambda e: e.tensor_scalar(out=hg[:], in0=hg[:], scalar1=st[:, 0:1],
                                                 scalar2=None, op0=ALU.subtract), r=[hg, st], w=[hg],
             deps=[p.op("vector", lambda e: e.tensor_scalar_mul(out=st[:, 0:1], in0=st[:, 0:1],
                                                                scalar1=1.0 / H), r=[st], w=[st])])
        p.op("scalar", lambda e: e.activation(out=sq[:], in_=hg[:], func=AF.Square,
                                              accum_out=st[:, 2:3]), r=[hg], w=[sq, st])
        lnepsc = b.cconst(1e-5)
        p.op("scalar", lambda e, lnepsc=lnepsc: e.activation(out=st[:, 4:5], in_=st[:, 2:3], func=AF.Ln,
                                              bias=lnepsc, scale=1.0 / H), r=[st, b._cpool], w=[st])
        p.op("scalar", lambda e: e.activation(out=st[:, 5:6], in_=st[:, 4:5], func=AF.Exp,
                                              scale=-0.5), r=[st], w=[st])
        p.op("vector", lambda e: e.tensor_scalar(out=hg[:], in0=hg[:], scalar1=st[:, 5:6],
                                                 scalar2=None, op0=ALU.mult), r=[hg, st], w=[hg])
        p.op("vector", lambda e: e.tensor_mul(out=hg[:], in0=hg[:], in1=wbc[:]), r=[hg, wbc], w=[hg])
        p.op("vector", lambda e: e.tensor_add(out=hnb[:], in0=hg[:], in1=bbc[:]), r=[hg, bbc], w=[hnb])
        for c in range(32):
            p.op("tensor", lambda e, c=c: e.transpose(ptr[:], hnb[:, c * P:(c + 1) * P], eyeT[:]),
                 r=[hnb, eyeT], w=[ptr])
            p.op("scalar", lambda e, c=c: e.copy(out=hfm[:, c * P:(c + 1) * P], in_=ptr[:]),
                 r=[ptr], w=[hfm])
        p.dma("gpsimd", hn_scr[i * P:(i + 1) * P, :], hfm[:], r=[hfm], w=[hn_scr])

    q2_slices = [qw[:, j * D:(j + 1) * D] for j in range(32)]
    quantize(wprT, 32, D, q2_slices, "pr")

    hstrip = b.sbuf([P, 32 * P], BF16)
    pof = b.psum([P, 512], F32)
    outt = b.sbuf([P, D], F32)
    xt2 = b.sbuf([P, D], F32)
    for i in range(nT):
        p.dma("sync", hstrip[:], hn_scr[i * P:(i + 1) * P, :], r=[hn_scr], w=[hstrip])
        p.dma("gpsimd", xt2[:], x_in[i * P:(i + 1) * P, :], w=[xt2])
        for half in range(2):
            for k in range(32):
                p.op("tensor", lambda e, k=k, half=half: e.matmul(
                    pof[:, :], hstrip[:, k * P:(k + 1) * P],
                    qw[:, k * D + half * 512:k * D + (half + 1) * 512],
                    start=(k == 0), stop=(k == 31)),
                    r=[hstrip, qw], w=[pof])
            p.op("vector", lambda e, half=half: e.tensor_add(
                out=outt[:, half * 512:(half + 1) * 512], in0=pof[:, :],
                in1=xt2[:, half * 512:(half + 1) * 512]), r=[pof, xt2], w=[outt])
        p.dma("sync", out[i * P:(i + 1) * P, :], outt[:], r=[outt], w=[out])

    p.op("gpsimd", lambda e: e.engine_nop(), no_inc=True, r=[out])
    nc = b.build()
    _prog_cache[key] = nc
    return nc


def run_l1ffn(x, W_fc, ln_w, ln_b, W_proj):
    B, T, D = x.shape
    rows = x.reshape(B * T, D)
    R = (B * T) // NCORES
    nc = build_l1ffn(R)
    wfcT = np.ascontiguousarray(W_fc.T)
    wprT = np.ascontiguousarray(W_proj.T)
    eye = np.eye(P, dtype=np.float32).astype(NP_BF16)
    in_maps = []
    for c in range(NCORES):
        in_maps.append({
            "x": np.ascontiguousarray(rows[c * R:(c + 1) * R]),
            "wfcT": wfcT, "wprT": wprT,
            "lnw": ln_w.reshape(1, -1), "lnb": ln_b.reshape(1, -1),
            "eye": eye,
        })
    res = run_bass_kernel_spmd(nc, in_maps, core_ids=list(range(NCORES)))
    outr = np.concatenate([res.results[c]["out"] for c in range(NCORES)], 0)
    return outr.reshape(B, T, D)


def kernel(**inputs):
    g = inputs
    x = np.ascontiguousarray(g["L0_attn_a_hidden_states"], dtype=np.float32)
    x = run_l0_stack(x, g["L0_attn_a_norm_w"], g["L0_attn_a_norm_b"], g["L0_attn_a_in_proj_w"],
                     g["L0_attn_a_conv_w"], g["L0_attn_a_conv_b"], g["L0_attn_a_x_proj_w"],
                     g["L0_attn_a_dt_proj_w"], g["L0_attn_a_dt_proj_b"], g["L0_attn_a_A_log"],
                     g["L0_attn_a_D_param"], g["L0_attn_a_out_proj_w"], g["L0_attn_a_fnorm_w"],
                     g["L0_attn_a_fnorm_b"])
    ffn_args = (g["L0_ffn_gamma1"], g["L0_ffn_Wq"], g["L0_ffn_bq"], g["L0_ffn_Wdown"],
                g["L0_ffn_bdown"], g["L0_ffn_Wup"], g["L0_ffn_bup"], g["L0_ffn_Wo"],
                g["L0_ffn_bo"], g["L0_ffn_gamma2"], g["L0_ffn_Wg"], g["L0_ffn_bg"], g["L0_ffn_We"])
    x = run_l0ffn(x, *ffn_args)
    x = run_attnb_full(x, g["L0_attn_b_Wq"], g["L0_attn_b_Wk"], g["L0_attn_b_Wv"])
    x = run_l0ffn(x, *ffn_args)
    x = run_l1_stack(x, g["L1_attn_a_lin1_w"], g["L1_attn_a_lin1_b"], g["L1_attn_a_norm_w"],
                     g["L1_attn_a_in_w"], g["L1_attn_a_conv_w"], g["L1_attn_a_conv_b"],
                     g["L1_attn_a_xproj_w"], g["L1_attn_a_dt_w"], g["L1_attn_a_dt_b"],
                     g["L1_attn_a_A_log"], g["L1_attn_a_Dp"], g["L1_attn_a_out_w"],
                     g["L1_attn_a_lin2_w"], g["L1_attn_a_lin2_b"])
    l1f_args = (g["L1_ffn_W_fc"], g["L1_ffn_ln_w"], g["L1_ffn_ln_b"], g["L1_ffn_W_proj"])
    x = run_l1ffn(x, *l1f_args)
    x = run_l1ab(x, g["L1_attn_b_W_qkv"], g["L1_attn_b_W_out"],
                 float(np.asarray(g["L1_attn_b_a"])), float(np.asarray(g["L1_attn_b_b"])),
                 float(np.asarray(g["L1_attn_b_dt"])))
    x = run_l1ffn(x, *l1f_args)
    return np.ascontiguousarray(x, dtype=np.float32)


L0_RQ, L0_RH = 2048, 2176
L0_DI = 2048


def emit_affine_ln_tile(b, rt, ln32, sq, st, D, nwbc, nbbc, mask_buf=None):
    p = b.p
    emit_plain_ln(b, rt, ln32, sq, st, D)
    p.op("vector", lambda e: e.tensor_mul(out=ln32[:], in0=ln32[:], in1=nwbc[:]),
         r=[ln32, nwbc], w=[ln32])
    p.op("vector", lambda e: e.tensor_add(out=ln32[:], in0=ln32[:], in1=nbbc[:]),
         r=[ln32, nbbc], w=[ln32])
    if mask_buf is not None:
        p.op("vector", lambda e: e.tensor_scalar(out=ln32[:], in0=ln32[:], scalar1=mask_buf[:, 0:1],
                                                 scalar2=None, op0=ALU.mult),
             r=[ln32, mask_buf], w=[ln32])


def emit_l0_pre_body(b, provider, nT, inw_hbm, convw_hbm, convb_hbm, xpwT_hbm,
                     dtpwT_hbm, dtpb_hbm, nw_hbm, nb_hbm, hmask_hbm,
                     xc_out, zs_out, dt_out, bc_out, D=1024, DI=2048, DTR=64, first_tile_mask=True):
    p = b.p
    TC = nT * P
    OWN = TC - P
    onesr = b.get_onesr()
    pmisc = b.get_pmisc()
    nwbc = b.sbuf([P, D], name="nwbc")
    nbbc = b.sbuf([P, D], name="nbbc")
    p.dma("sync", nwbc[0:1, 0:D], nw_hbm[0:D, 0:1].rearrange("a b -> b a"), w=[nwbc])
    p.dma("sync", nbbc[0:1, 0:D], nb_hbm[0:D, 0:1].rearrange("a b -> b a"), w=[nbbc])
    emit_bcast_row(b, nwbc, nwbc[0:1, 0:D], D, onesr, pmisc, r_extra=[nwbc])
    emit_bcast_row(b, nbbc, nbbc[0:1, 0:D], D, onesr, pmisc, r_extra=[nbbc])
    hmask = b.sbuf([P, 1], name="hmask_sb")
    p.dma("sync", hmask[:], hmask_hbm[:, :], w=[hmask])

    eyeT = b.get_eye()

    hs_fm = b.sbuf([P, 8 * TC], BF16, name="hs_fm")
    ln32 = b.sbuf([P, D], name="ln32")
    sq = b.sbuf([P, D], name="sq0")
    st = b.sbuf([P, 8], name="st0")
    lnb16 = b.sbuf([P, D], BF16, name="lnb16")
    ptr = b.get_ptr()
    for i in range(nT):
        rt = provider(i)
        emit_affine_ln_tile(b, rt, ln32, sq, st, D, nwbc, nbbc,
                            mask_buf=(hmask if (i == 0 and first_tile_mask) else None))
        p.op("vector", lambda e: e.tensor_copy(out=lnb16[:], in_=ln32[:]), r=[ln32], w=[lnb16])
        for c in range(8):
            p.op("tensor", lambda e, c=c: e.transpose(ptr[:], lnb16[:, c * P:(c + 1) * P], eyeT[:]),
                 r=[lnb16, eyeT], w=[ptr])
            p.op("scalar", lambda e, c=c, i=i: e.copy(out=hs_fm[:, c * TC + i * P:c * TC + (i + 1) * P],
                                                      in_=ptr[:]), r=[ptr], w=[hs_fm])

    NSL = [(j * 512, min((j + 1) * 512, TC)) for j in range((TC + 511) // 512)]
    inw_t = [b.sbuf([P, 8 * P], BF16, name=f"inw{j}") for j in range(2)]
    pxdbl = b.psum([96, 2048], name="pxdbl")
    pxzb = [b.psum([P, 512], name=f"pxz{j}") for j in range(2)]
    xzx = b.sbuf([P, TC], name="xzx")
    acc = b.sbuf([P, OWN], name="acc")
    xc_m = b.sbuf([P, OWN], BF16, name="xc_m")
    zs_m = b.sbuf([P, OWN], BF16, name="zs_m")
    cw = b.sbuf([P, 4], name="cw")
    cb = b.sbuf([P, 1], name="cb")
    xpwT_t = b.sbuf([P, 96], BF16, name="xpwT_t")
    for m in range(32):
        wbuf = inw_t[m % 2]
        for k in range(8):
            p.dma("sync", wbuf[:, k * P:(k + 1) * P],
                  inw_hbm[k * P:(k + 1) * P, m * P:(m + 1) * P], w=[wbuf])
        if m < 16:
            p.dma("gpsimd", cw[:, :], convw_hbm[m * P:(m + 1) * P, :], w=[cw])
            p.dma("gpsimd", cb[:, :], convb_hbm[m * P:(m + 1) * P, :], w=[cb])
            p.dma("gpsimd", xpwT_t[:, :], xpwT_hbm[m * P:(m + 1) * P, :], w=[xpwT_t])
        for ni, (n0, n1) in enumerate(NSL):
            pxz = pxzb[ni % 2]
            for k in range(8):
                p.op("tensor", lambda e, k=k, n0=n0, n1=n1, wbuf=wbuf, pxz=pxz: e.matmul(
                    pxz[:, 0:n1 - n0], wbuf[:, k * P:(k + 1) * P],
                    hs_fm[:, k * TC + n0:k * TC + n1],
                    start=(k == 0), stop=(k == 7)), r=[wbuf, hs_fm], w=[pxz])
            if m < 16:
                p.op("scalar", lambda e, n0=n0, n1=n1, pxz=pxz: e.copy(out=xzx[:, n0:n1], in_=pxz[:, 0:n1 - n0]),
                     r=[pxz], w=[xzx])
            else:
                o0, o1 = max(n0, P), n1
                if o1 > o0:
                    p.op("scalar", lambda e, n0=n0, o0=o0, o1=o1, pxz=pxz: e.activation(
                        out=zs_m[:, o0 - P:o1 - P], in_=pxz[:, o0 - n0:o1 - n0], func=AF.Silu),
                        r=[pxz], w=[zs_m])
        if m < 16:
            p.op("vector", lambda e: e.tensor_scalar(out=acc[:], in0=xzx[:, P - 3:P - 3 + OWN],
                                                     scalar1=cw[:, 0:1], scalar2=None, op0=ALU.mult),
                 r=[xzx, cw], w=[acc])
            for k in range(1, 4):
                p.op("vector", lambda e, k=k: e.scalar_tensor_tensor(
                    out=acc[:], in0=xzx[:, P - 3 + k:P - 3 + k + OWN], scalar=cw[:, k:k + 1],
                    in1=acc[:], op0=ALU.mult, op1=ALU.add), r=[xzx, cw, acc], w=[acc])
            p.op("scalar", lambda e: e.activation(out=xc_m[:], in_=acc[:], func=AF.Silu,
                                                  bias=cb[:, 0:1]), r=[acc, cb], w=[xc_m])
            p.dma("sync", xc_out[m * P:(m + 1) * P, :], xc_m[:], r=[xc_m], w=[xc_out])
            for j in range(4):
                p.op("tensor", lambda e, j=j, m=m: e.matmul(
                    pxdbl[:, j * 512:(j + 1) * 512], xpwT_t[:, :], xc_m[:, j * 512:(j + 1) * 512],
                    start=(m == 0), stop=(m == 15)), r=[xpwT_t, xc_m], w=[pxdbl])
        else:
            p.dma("sync", zs_out[(m - 16) * P:(m - 15) * P, :], zs_m[:], r=[zs_m], w=[zs_out])
    b.free(pxzb[1])
    b.free(pxzb[0])
    xdbl = b.sbuf([96, 2048], name="xdbl")
    for j in range(4):
        p.op("scalar", lambda e, j=j: e.copy(out=xdbl[:, j * 512:(j + 1) * 512],
                                             in_=pxdbl[:, j * 512:(j + 1) * 512]),
             r=[pxdbl], w=[xdbl])
    p.dma("sync", bc_out[:, :], xdbl[DTR:96, :], r=[xdbl], w=[bc_out])
    dtin_b = b.sbuf([64, 2048], BF16, name="dtin_b")
    p.op("vector", lambda e: e.tensor_copy(out=dtin_b[:, :], in_=xdbl[0:DTR, :]), r=[xdbl], w=[dtin_b])
    dtpwT_t = b.sbuf([64, DI], BF16, name="dtpwT_t")
    p.dma("sync", dtpwT_t[:, :], dtpwT_hbm[:, :], w=[dtpwT_t])
    b.free(pxdbl)
    dtb = b.sbuf([P, 1], name="dtb")
    pdt = b.psum([P, 512], name="pdt")
    v32 = b.sbuf([P, 2048], name="v32")
    a1 = b.sbuf([P, 2048], name="a1_sp")
    for m2 in range(DI // P):
        p.dma("gpsimd", dtb[:, :], dtpb_hbm[m2 * P:(m2 + 1) * P, :], w=[dtb])
        for j in range(4):
            p.op("tensor", lambda e, m2=m2, j=j: e.matmul(
                pdt[:, :], dtpwT_t[:, m2 * P:(m2 + 1) * P], dtin_b[:, j * 512:(j + 1) * 512],
                start=True, stop=True), r=[dtpwT_t, dtin_b], w=[pdt])
            p.op("scalar", lambda e, j=j: e.activation(out=v32[:, j * 512:(j + 1) * 512],
                                                       in_=pdt[:, :], func=AF.Identity,
                                                       bias=dtb[:, 0:1]), r=[pdt, dtb], w=[v32])
        p.op("scalar", lambda e: e.activation(out=a1[:], in_=v32[:], func=AF.Abs), r=[v32], w=[a1])
        p.op("scalar", lambda e: e.activation(out=a1[:], in_=a1[:], func=AF.Exp, scale=-1.0),
             r=[a1], w=[a1])
        onec = b.cconst(1.0)
        p.op("scalar", lambda e, onec=onec: e.activation(out=a1[:], in_=a1[:], func=AF.Ln, bias=onec),
             r=[a1, b._cpool], w=[a1])
        p.op("vector", lambda e: e.tensor_relu(out=v32[:], in_=v32[:]), r=[v32], w=[v32])
        p.op("vector", lambda e: e.tensor_add(out=v32[:], in0=v32[:], in1=a1[:]), r=[v32, a1], w=[v32])
        p.dma("sync", dt_out[m2 * P:(m2 + 1) * P, :], v32[:], r=[v32], w=[dt_out])


def build_l0first():
    key = ("l0first",)
    if key in _prog_cache:
        return _prog_cache[key]
    b = Builder()
    p = b.p
    D = 1024
    x_in = b.din("x", [L0_RH, D])
    b.eye_hbm = b.din("eye", [P, P], BF16)
    inw = b.din("inwT", [D, 2 * L0_DI], BF16)
    convw = b.din("convw", [L0_DI, 4])
    convb = b.din("convb", [L0_DI, 1])
    xpwT = b.din("xpwT", [L0_DI, 96], BF16)
    dtpwT = b.din("dtpwT", [64, L0_DI], BF16)
    dtpb = b.din("dtpb", [L0_DI, 1])
    nw = b.din("nw", [D, 1])
    nb = b.din("nb", [D, 1])
    hmaskh = b.din("hmask", [P, 1])
    resid0 = b.dout("resid0", [L0_RQ, D])
    xc_out = b.dout("xc", [L0_DI, L0_RQ], BF16)
    zs_out = b.dout("zs", [L0_DI, L0_RQ], BF16)
    dt_out = b.dout("dt", [L0_DI, L0_RQ])
    bc_out = b.dout("bc", [32, L0_RQ])

    xt = b.sbuf([P, D], name="xt")
    rt = b.sbuf([P, D], name="rt")
    sq1 = b.sbuf([P, D], name="sq1")
    st1 = b.sbuf([P, 8], name="st1")

    def provider(i):
        p.dma("sync", xt[:], x_in[i * P:(i + 1) * P, :], w=[xt])
        emit_plain_ln(b, xt, rt, sq1, st1, D)
        if i > 0:
            p.dma("gpsimd", resid0[(i - 1) * P:i * P, :], rt[:], r=[rt], w=[resid0])
        return rt

    emit_l0_pre_body(b, provider, 17, inw, convw, convb, xpwT, dtpwT, dtpb,
                     nw, nb, hmaskh, xc_out, zs_out, dt_out, bc_out)
    p.op("gpsimd", lambda e: e.engine_nop(), no_inc=True, r=[resid0, xc_out, zs_out, dt_out, bc_out])
    nc = b.build()
    _prog_cache[key] = nc
    return nc


def build_l0scan(T=8192, NCH=512, DS=16):
    key = ("l0scan", T, NCH)
    if key in _prog_cache:
        return _prog_cache[key]
    b = Builder()
    p = b.p
    nD = NCH // P
    CH = 2048
    nC = T // CH
    dt_in = b.din("dt", [NCH, T])
    xc_in = b.din("xc", [NCH, T], BF16)
    zs_in = b.din("zs", [NCH, T], BF16)
    Brow = b.din("Brow", [DS, T])
    Crow = b.din("Crow", [DS, T])
    alog = b.din("alog", [NCH, DS])
    dp = b.din("dp", [NCH, 1])
    yg_out = b.dout("yg", [NCH, T], BF16)

    onesr = b.sbuf([1, P], name="onesr")
    p.op("vector", lambda e: e.memset(onesr[:], 1.0), w=[onesr])
    pmisc = b.psum([P, 512], name="pmisc")
    b1row = b.sbuf([1, 2048], name="b1row")
    c1row = b.sbuf([1, 2048], name="c1row")
    A_sb = [b.sbuf([P, DS], name=f"A{d}") for d in range(nD)]
    dp_sb = [b.sbuf([P, 1], name=f"dp{d}") for d in range(nD)]
    for d in range(nD):
        p.dma("sync", A_sb[d][:, :], alog[d * P:(d + 1) * P, :], w=[A_sb[d]])
        p.op("scalar", lambda e, d=d: e.activation(out=A_sb[d][:, :], in_=A_sb[d][:, :], func=AF.Exp),
             r=[A_sb[d]], w=[A_sb[d]])
        p.op("vector", lambda e, d=d: e.tensor_scalar_mul(out=A_sb[d][:, :], in0=A_sb[d][:, :],
                                                          scalar1=-1.0), r=[A_sb[d]], w=[A_sb[d]])
        p.dma("sync", dp_sb[d][:, :], dp[d * P:(d + 1) * P, :], w=[dp_sb[d]])

    Bbc = b.sbuf([P, CH], BF16, name="Bbc")
    Cbc = b.sbuf([P, CH], BF16, name="Cbc")
    dt_t = [b.sbuf([P, CH], name=f"dtt{d}") for d in range(nD)]
    xc_t = [b.sbuf([P, CH], BF16, name=f"xct{d}") for d in range(nD)]
    dtx = [b.sbuf([P, CH], BF16, name=f"dtx{d}") for d in range(nD)]
    yacc = [[b.sbuf([P, CH], BF16, name=f"yacc{d}_{j}") for j in range(2)] for d in range(nD)]
    dA = [b.sbuf([P, CH], BF16, name=f"dA{j}") for j in range(2)]
    dBu = [b.sbuf([P, CH], BF16, name=f"dBu{j}") for j in range(2)]
    hS = [b.sbuf([P, CH], BF16, name=f"hS{j}") for j in range(2)]
    prod = [b.sbuf([P, CH], BF16, name=f"prod{j}") for j in range(2)]
    Bbc2 = b.sbuf([P, CH], BF16, name="Bbc2")
    Cbc2 = b.sbuf([P, CH], BF16, name="Cbc2")
    carry = b.sbuf([P, nD * DS], BF16, name="carry")
    zs_t = b.sbuf([P, CH], BF16, name="zs_t")
    ygt = b.sbuf([P, CH], BF16, name="ygt")

    for c in range(nC):
        t0, t1 = c * CH, (c + 1) * CH
        for d in range(nD):
            p.dma("sync", dt_t[d][:, :], dt_in[d * P:(d + 1) * P, t0:t1], w=[dt_t[d]])
            p.dma("gpsimd", xc_t[d][:, :], xc_in[d * P:(d + 1) * P, t0:t1], w=[xc_t[d]])
            p.op("gpsimd", lambda e, d=d: e.tensor_mul(out=dtx[d][:, :], in0=dt_t[d][:, :],
                                                       in1=xc_t[d][:, :]), r=[dt_t[d], xc_t[d]], w=[dtx[d]])
            p.op("vector", lambda e, d=d: e.memset(yacc[d][0][:, :], 0.0), w=[yacc[d][0]])
            p.op("vector", lambda e, d=d: e.memset(yacc[d][1][:, :], 0.0), w=[yacc[d][1]])
        for n in range(DS):
            pb = n % 2
            Bb = Bbc if pb == 0 else Bbc2
            Cb = Cbc if pb == 0 else Cbc2
            p.dma("sync", b1row[0:1, :], Brow[n:n + 1, t0:t1], w=[b1row])
            p.dma("sync", c1row[0:1, :], Crow[n:n + 1, t0:t1], w=[c1row])
            for rsrc, dst in ((b1row, Bb), (c1row, Cb)):
                for j in range(CH // 512):
                    p.op("tensor", lambda e, j=j, rsrc=rsrc: e.matmul(
                        pmisc[:, :], onesr[0:1, 0:P], rsrc[0:1, j * 512:(j + 1) * 512],
                        start=True, stop=True), r=[onesr, rsrc], w=[pmisc])
                    p.op("scalar", lambda e, j=j, dst=dst: e.copy(out=dst[:, j * 512:(j + 1) * 512],
                                                                  in_=pmisc[:, :]), r=[pmisc], w=[dst])
            for d in range(nD):
                pp = (n * nD + d) % 2
                p.op("scalar", lambda e, d=d, n=n, pp=pp: e.activation(
                    out=dA[pp][:], in_=dt_t[d][:, :], func=AF.Exp, scale=A_sb[d][:, n:n + 1]),
                    r=[dt_t[d], A_sb[d]], w=[dA[pp]])
                p.op("gpsimd", lambda e, d=d, pp=pp, Bb=Bb: e.tensor_mul(
                    out=dBu[pp][:], in0=dtx[d][:, :], in1=Bb[:]), r=[dtx[d], Bb], w=[dBu[pp]])
                if c == 0:
                    p.op("vector", lambda e, pp=pp: e.tensor_tensor_scan(
                        out=hS[pp][:], data0=dA[pp][:], data1=dBu[pp][:], initial=0.0,
                        op0=ALU.mult, op1=ALU.add), r=[dA[pp], dBu[pp]], w=[hS[pp]])
                else:
                    p.op("vector", lambda e, d=d, n=n, pp=pp: e.tensor_tensor_scan(
                        out=hS[pp][:], data0=dA[pp][:], data1=dBu[pp][:],
                        initial=carry[:, d * DS + n:d * DS + n + 1],
                        op0=ALU.mult, op1=ALU.add), r=[dA[pp], dBu[pp], carry], w=[hS[pp]])
                p.op("vector", lambda e, d=d, n=n, pp=pp: e.tensor_copy(
                    out=carry[:, d * DS + n:d * DS + n + 1], in_=hS[pp][:, CH - 1:CH]),
                    r=[hS[pp]], w=[carry])
                p.op("vector", lambda e, pp=pp, Cb=Cb: e.tensor_mul(
                    out=prod[pp][:], in0=hS[pp][:], in1=Cb[:]), r=[hS[pp], Cb], w=[prod[pp]])
                yp = n % 2
                p.op("vector", lambda e, d=d, pp=pp, yp=yp: e.tensor_add(
                    out=yacc[d][yp][:, :], in0=yacc[d][1 - yp][:, :], in1=prod[pp][:]),
                    r=[prod[pp], yacc[d][1 - yp]], w=[yacc[d][yp]])
        for d in range(nD):
            p.dma("sync", zs_t[:, :], zs_in[d * P:(d + 1) * P, t0:t1], w=[zs_t])
            yfin = yacc[d][(DS - 1) % 2]
            p.op("vector", lambda e, d=d, yfin=yfin: e.scalar_tensor_tensor(
                out=yfin[:, :], in0=xc_t[d][:, :], scalar=dp_sb[d][:, 0:1], in1=yfin[:, :],
                op0=ALU.mult, op1=ALU.add), r=[xc_t[d], dp_sb[d], yfin], w=[yfin])
            p.op("vector", lambda e, d=d, yfin=yfin: e.tensor_mul(out=ygt[:], in0=yfin[:, :], in1=zs_t[:]),
                 r=[yfin, zs_t], w=[ygt])
            p.dma("sync", yg_out[d * P:(d + 1) * P, t0:t1], ygt[:], r=[ygt], w=[yg_out])
    p.op("gpsimd", lambda e: e.engine_nop(), no_inc=True, r=[yg_out])
    nc = b.build()
    _prog_cache[key] = nc
    return nc


def emit_l0_post(b, resid_in, yg_fm, owT_hbm, nT, D=1024, DI=2048):
    p = b.p
    TC = nT * P
    eyeT2 = b.get_eye()
    out_fm = b.sbuf([P, 8 * TC], BF16, name="out_fm")
    ygk = [b.sbuf([P, TC], BF16, name=f"ygk{k}") for k in range(DI // P)]
    owk = [b.sbuf([P, D], BF16, name=f"owk{k}") for k in range(DI // P)]
    for k in range(DI // P):
        p.dma("sync", ygk[k][:, :], yg_fm[k * P:(k + 1) * P, :], w=[ygk[k]])
        p.dma("gpsimd", owk[k][:, :], owT_hbm[k * P:(k + 1) * P, :], w=[owk[k]])
    pof = b.psum([P, 512], name="pof")
    NSL = [(j * 512, min((j + 1) * 512, TC)) for j in range((TC + 511) // 512)]
    for m in range(8):
        for (n0, n1) in NSL:
            for k in range(DI // P):
                p.op("tensor", lambda e, m=m, k=k, n0=n0, n1=n1: e.matmul(
                    pof[:, 0:n1 - n0], owk[k][:, m * P:(m + 1) * P], ygk[k][:, n0:n1],
                    start=(k == 0), stop=(k == DI // P - 1)), r=[owk[k], ygk[k]], w=[pof])
            p.op("scalar", lambda e, m=m, n0=n0, n1=n1: e.copy(
                out=out_fm[:, m * TC + n0:m * TC + n1], in_=pof[:, 0:n1 - n0]), r=[pof], w=[out_fm])
    deps = []
    deps += b.free(pof)
    for k in reversed(range(DI // P)):
        deps += b.free(owk[k])
    for k in reversed(range(DI // P)):
        deps += b.free(ygk[k])

    rti = b.sbuf([P, D], name="rti")
    rto = b.sbuf([P, D], name="rto")
    ptr2 = b.get_ptr()
    first = [True]

    def provider(i):
        dd = deps if first[0] else []
        first[0] = False
        p.dma("sync", rti[:], resid_in[i * P:(i + 1) * P, :], w=[rti], deps=dd)
        for m in range(8):
            p.op("tensor", lambda e, m=m, i=i: e.transpose(
                ptr2[:], out_fm[:, m * TC + i * P:m * TC + (i + 1) * P], eyeT2[:]),
                r=[out_fm, eyeT2], w=[ptr2])
            p.op("vector", lambda e, m=m: e.tensor_add(
                out=rto[:, m * P:(m + 1) * P], in0=rti[:, m * P:(m + 1) * P], in1=ptr2[:]),
                r=[rti, ptr2], w=[rto])
        return rto

    return provider


def build_l0mid():
    key = ("l0mid",)
    if key in _prog_cache:
        return _prog_cache[key]
    b = Builder()
    p = b.p
    D = 1024
    resid_in = b.din("resid", [L0_RH, D])
    b.eye_hbm = b.din("eye", [P, P], BF16)
    yg_fm = b.din("ygfm", [L0_DI, L0_RH], BF16)
    owT = b.din("owT", [L0_DI, D], BF16)
    inw = b.din("inwT", [D, 2 * L0_DI], BF16)
    convw = b.din("convw", [L0_DI, 4])
    convb = b.din("convb", [L0_DI, 1])
    xpwT = b.din("xpwT", [L0_DI, 96], BF16)
    dtpwT = b.din("dtpwT", [64, L0_DI], BF16)
    dtpb = b.din("dtpb", [L0_DI, 1])
    nw = b.din("nw", [D, 1])
    nb = b.din("nb", [D, 1])
    hmaskh = b.din("hmask", [P, 1])
    resid_out = b.dout("resid_o", [L0_RQ, D])
    xc_out = b.dout("xc", [L0_DI, L0_RQ], BF16)
    zs_out = b.dout("zs", [L0_DI, L0_RQ], BF16)
    dt_out = b.dout("dt", [L0_DI, L0_RQ])
    bc_out = b.dout("bc", [32, L0_RQ])

    provider0 = emit_l0_post(b, resid_in, yg_fm, owT, 17)

    def provider(i):
        rto = provider0(i)
        if i > 0:
            p.dma("gpsimd", resid_out[(i - 1) * P:i * P, :], rto[:], r=[rto], w=[resid_out])
        return rto

    emit_l0_pre_body(b, provider, 17, inw, convw, convb, xpwT, dtpwT, dtpb,
                     nw, nb, hmaskh, xc_out, zs_out, dt_out, bc_out)
    p.op("gpsimd", lambda e: e.engine_nop(), no_inc=True, r=[resid_out, xc_out, zs_out, dt_out, bc_out])
    nc = b.build()
    _prog_cache[key] = nc
    return nc


def build_l0last():
    key = ("l0last",)
    if key in _prog_cache:
        return _prog_cache[key]
    b = Builder()
    p = b.p
    D = 1024
    resid_in = b.din("resid", [L0_RQ, D])
    b.eye_hbm = b.din("eye", [P, P], BF16)
    yg_fm = b.din("ygfm", [L0_DI, L0_RQ], BF16)
    owT = b.din("owT", [L0_DI, D], BF16)
    fw = b.din("fw", [D, 1])
    fb = b.din("fb", [D, 1])
    xouter = b.din("xouter", [L0_RQ, D])
    out = b.dout("out", [L0_RQ, D])

    provider0 = emit_l0_post(b, resid_in, yg_fm, owT, 16)

    onesr = b.get_onesr()
    pmisc = b.get_pmisc()
    fwbc = b.sbuf([P, D], name="fwbc")
    fbbc = b.sbuf([P, D], name="fbbc")
    p.dma("sync", fwbc[0:1, 0:D], fw[0:D, 0:1].rearrange("a b -> b a"), w=[fwbc])
    p.dma("sync", fbbc[0:1, 0:D], fb[0:D, 0:1].rearrange("a b -> b a"), w=[fbbc])
    emit_bcast_row(b, fwbc, fwbc[0:1, 0:D], D, onesr, pmisc, r_extra=[fwbc])
    emit_bcast_row(b, fbbc, fbbc[0:1, 0:D], D, onesr, pmisc, r_extra=[fbbc])
    ln32 = b.sbuf([P, D], name="ln32L")
    sq = b.sbuf([P, D], name="sqL")
    st = b.sbuf([P, 8], name="stL")
    xot = b.sbuf([P, D], name="xot")
    for i in range(16):
        rto = provider0(i)
        emit_affine_ln_tile(b, rto, ln32, sq, st, D, fwbc, fbbc)
        p.dma("gpsimd", xot[:], xouter[i * P:(i + 1) * P, :], w=[xot])
        p.op("vector", lambda e: e.tensor_add(out=ln32[:], in0=ln32[:], in1=xot[:]),
             r=[ln32, xot], w=[ln32])
        p.dma("sync", out[i * P:(i + 1) * P, :], ln32[:], r=[ln32], w=[out])
    p.op("gpsimd", lambda e: e.engine_nop(), no_inc=True, r=[out])
    nc = b.build()
    _prog_cache[key] = nc
    return nc


def _bf(a):
    return np.ascontiguousarray(a).astype(NP_BF16)


def _shard_rows_halo(arr_b, beta, q, RQ=2048, HB=P):
    T = arr_b.shape[1]
    lo = q * RQ - HB
    if lo >= 0:
        return np.ascontiguousarray(arr_b[beta, lo:(q + 1) * RQ])
    blk = np.zeros((HB + RQ, arr_b.shape[2]), arr_b.dtype)
    blk[HB:] = arr_b[beta, 0:(q + 1) * RQ]
    return blk


def _shard_cols_halo(arr_2d, q, RQ=2048, HB=P):
    lo = q * RQ - HB
    if lo >= 0:
        return np.ascontiguousarray(arr_2d[:, lo:(q + 1) * RQ])
    blk = np.zeros((arr_2d.shape[0], HB + RQ), arr_2d.dtype)
    blk[:, HB:] = arr_2d[:, 0:(q + 1) * RQ]
    return blk


def run_l0_stack(x, norm_w, norm_b, in_proj_w, conv_w, conv_b, x_proj_w,
                 dt_proj_w, dt_proj_b, A_log, D_param, out_proj_w, fnorm_w, fnorm_b):
    B, T, D = x.shape
    RQ = T // 4
    eye = np.eye(P, dtype=np.float32).astype(NP_BF16)

    def pre_weights(i):
        return {
            "inwT": _bf(in_proj_w[i].T), "convw": np.ascontiguousarray(conv_w[i]),
            "convb": conv_b[i].reshape(-1, 1), "xpwT": _bf(x_proj_w[i].T),
            "dtpwT": _bf(dt_proj_w[i].T), "dtpb": dt_proj_b[i].reshape(-1, 1),
            "nw": norm_w[i].reshape(-1, 1), "nb": norm_b[i].reshape(-1, 1),
        }

    ones_m = np.ones((P, 1), np.float32)
    zeros_m = np.zeros((P, 1), np.float32)

    def launch_pre(prog, extra_per_core, wts):
        in_maps = []
        for c in range(NCORES):
            beta, q = c // 4, c % 4
            im = {"eye": eye, "hmask": (zeros_m if q == 0 else ones_m)}
            im.update(wts)
            im.update(extra_per_core(c, beta, q))
            in_maps.append(im)
        return run_bass_kernel_spmd(prog, in_maps, core_ids=list(range(NCORES)))

    res = launch_pre(build_l0first(), lambda c, beta, q: {"x": _shard_rows_halo(x, beta, q, RQ)},
                     pre_weights(0))
    R = np.stack([np.concatenate([res.results[b_ * 4 + q]["resid0"] for q in range(4)], 0)
                  for b_ in range(B)], 0)

    def gather_chan(res, key):
        return [np.concatenate([np.asarray(res.results[b_ * 4 + q][key]) for q in range(4)], 1)
                for b_ in range(B)]

    for i in range(4):
        XC = gather_chan(res, "xc")
        ZS = gather_chan(res, "zs")
        DT = gather_chan(res, "dt")
        BCm = gather_chan(res, "bc")
        scan_maps = []
        for c in range(NCORES):
            beta, g = c // 4, c % 4
            scan_maps.append({
                "dt": np.ascontiguousarray(DT[beta][g * 512:(g + 1) * 512]),
                "xc": np.ascontiguousarray(XC[beta][g * 512:(g + 1) * 512]),
                "zs": np.ascontiguousarray(ZS[beta][g * 512:(g + 1) * 512]),
                "Brow": np.ascontiguousarray(BCm[beta][0:16]),
                "Crow": np.ascontiguousarray(BCm[beta][16:32]),
                "alog": np.ascontiguousarray(A_log[i][g * 512:(g + 1) * 512]),
                "dp": D_param[i][g * 512:(g + 1) * 512].reshape(-1, 1),
            })
        sres = run_bass_kernel_spmd(build_l0scan(T, 512), scan_maps, core_ids=list(range(NCORES)))
        YG = [np.concatenate([np.asarray(sres.results[b_ * 4 + g]["yg"]) for g in range(4)], 0)
              for b_ in range(B)]
        if i < 3:
            res = launch_pre(
                build_l0mid(),
                lambda c, beta, q: {
                    "resid": _shard_rows_halo(R, beta, q, RQ),
                    "ygfm": _shard_cols_halo(YG[beta], q, RQ).astype(NP_BF16, copy=False),
                    "owT": _bf(out_proj_w[i].T),
                }, pre_weights(i + 1))
            R = np.stack([np.concatenate([res.results[b_ * 4 + q]["resid_o"] for q in range(4)], 0)
                          for b_ in range(B)], 0)
        else:
            in_maps = []
            for c in range(NCORES):
                beta, q = c // 4, c % 4
                in_maps.append({
                    "eye": eye,
                    "resid": np.ascontiguousarray(R[beta, q * RQ:(q + 1) * RQ]),
                    "ygfm": np.ascontiguousarray(YG[beta][:, q * RQ:(q + 1) * RQ]).astype(NP_BF16, copy=False),
                    "owT": _bf(out_proj_w[3].T),
                    "fw": fnorm_w.reshape(-1, 1), "fb": fnorm_b.reshape(-1, 1),
                    "xouter": np.ascontiguousarray(x[beta, q * RQ:(q + 1) * RQ]),
                })
            lres = run_bass_kernel_spmd(build_l0last(), in_maps, core_ids=list(range(NCORES)))
            out = np.stack([np.concatenate([lres.results[b_ * 4 + q]["out"] for q in range(4)], 0)
                            for b_ in range(B)], 0)
            return out


L1_D, L1_DI, L1_DTR = 512, 1024, 32


def emit_rms_tile(b, rt, ln32, sq, st, D, nwbc, eps=1e-5, mask_buf=None):
    p = b.p
    epsc = b.cconst(float(eps))
    p.op("scalar", lambda e: e.activation(out=sq[:, 0:D], in_=rt[:], func=AF.Square,
                                          accum_out=st[:, 2:3]), r=[rt], w=[sq, st])
    p.op("scalar", lambda e: e.activation(out=st[:, 4:5], in_=st[:, 2:3], func=AF.Ln,
                                          bias=epsc, scale=1.0 / D), r=[st, b._cpool], w=[st])
    p.op("scalar", lambda e: e.activation(out=st[:, 5:6], in_=st[:, 4:5], func=AF.Exp, scale=-0.5),
         r=[st], w=[st])
    p.op("vector", lambda e: e.tensor_scalar(out=ln32[:], in0=rt[:], scalar1=st[:, 5:6],
                                             scalar2=None, op0=ALU.mult), r=[rt, st], w=[ln32])
    p.op("vector", lambda e: e.tensor_mul(out=ln32[:], in0=ln32[:], in1=nwbc[:]),
         r=[ln32, nwbc], w=[ln32])
    if mask_buf is not None:
        p.op("vector", lambda e: e.tensor_scalar(out=ln32[:], in0=ln32[:], scalar1=mask_buf[:, 0:1],
                                                 scalar2=None, op0=ALU.mult), r=[ln32, mask_buf], w=[ln32])


def emit_l1_pre_body(b, provider, nT, inw_hbm, convw_hbm, convb_hbm, xpw_hbm,
                     dtw_hbm, dtb_hbm, nw_hbm, hmask_hbm,
                     xc_out, zs_out, dt_out, bc_out):
    p = b.p
    D, DI, DTR = L1_D, L1_DI, L1_DTR
    TC = nT * P
    OWN = TC - P
    onesr = b.get_onesr()
    pmisc = b.get_pmisc()
    nwbc = b.sbuf([P, D], name="nwbc1")
    p.dma("sync", nwbc[0:1, 0:D], nw_hbm[0:D, 0:1].rearrange("a b -> b a"), w=[nwbc])
    emit_bcast_row(b, nwbc, nwbc[0:1, 0:D], D, onesr, pmisc, r_extra=[nwbc])
    hmask = b.sbuf([P, 1], name="hmask1")
    p.dma("sync", hmask[:], hmask_hbm[:, :], w=[hmask])
    eyeT = b.get_eye()

    hs_fm = b.sbuf([P, 4 * TC], BF16, name="hs_fm1")
    ln32 = b.sbuf([P, D], name="ln321")
    sq = b.sbuf([P, D], name="sq01")
    st = b.sbuf([P, 8], name="st01")
    lnb16 = b.sbuf([P, D], BF16, name="lnb161")
    ptr = b.get_ptr()
    for i in range(nT):
        rt = provider(i)
        emit_rms_tile(b, rt, ln32, sq, st, D, nwbc, mask_buf=(hmask if i == 0 else None))
        p.op("vector", lambda e: e.tensor_copy(out=lnb16[:], in_=ln32[:]), r=[ln32], w=[lnb16])
        for c in range(4):
            p.op("tensor", lambda e, c=c: e.transpose(ptr[:], lnb16[:, c * P:(c + 1) * P], eyeT[:]),
                 r=[lnb16, eyeT], w=[ptr])
            p.op("scalar", lambda e, c=c, i=i: e.copy(out=hs_fm[:, c * TC + i * P:c * TC + (i + 1) * P],
                                                      in_=ptr[:]), r=[ptr], w=[hs_fm])

    NSL = [(j * 512, min((j + 1) * 512, TC)) for j in range((TC + 511) // 512)]
    inw_t = [b.sbuf([P, 4 * P], BF16, name=f"inw1_{j}") for j in range(2)]
    pxdbl = b.psum([64, 2048], name="pxdbl1")
    pxzb = [b.psum([P, 512], name=f"pxz1{j}") for j in range(2)]
    xzx = b.sbuf([P, TC], name="xzx1")
    acc = b.sbuf([P, OWN], name="acc1")
    xc_m = b.sbuf([P, OWN], BF16, name="xc_m1")
    zs_m = b.sbuf([P, OWN], BF16, name="zs_m1")
    cw = b.sbuf([P, 4], name="cw1")
    cb = b.sbuf([P, 1], name="cb1")
    xpw_t = b.sbuf([P, 64], BF16, name="xpw_t1")
    for m in range(16):
        wbuf = inw_t[m % 2]
        for k in range(4):
            p.dma("sync", wbuf[:, k * P:(k + 1) * P],
                  inw_hbm[k * P:(k + 1) * P, m * P:(m + 1) * P], w=[wbuf])
        if m < 8:
            p.dma("gpsimd", cw[:, :], convw_hbm[m * P:(m + 1) * P, :], w=[cw])
            p.dma("gpsimd", cb[:, :], convb_hbm[m * P:(m + 1) * P, :], w=[cb])
            p.dma("gpsimd", xpw_t[:, :], xpw_hbm[m * P:(m + 1) * P, :], w=[xpw_t])
        for ni, (n0, n1) in enumerate(NSL):
            pxz = pxzb[ni % 2]
            for k in range(4):
                p.op("tensor", lambda e, k=k, n0=n0, n1=n1, wbuf=wbuf, pxz=pxz: e.matmul(
                    pxz[:, 0:n1 - n0], wbuf[:, k * P:(k + 1) * P],
                    hs_fm[:, k * TC + n0:k * TC + n1],
                    start=(k == 0), stop=(k == 3)), r=[wbuf, hs_fm], w=[pxz])
            if m < 8:
                p.op("scalar", lambda e, n0=n0, n1=n1, pxz=pxz: e.copy(out=xzx[:, n0:n1], in_=pxz[:, 0:n1 - n0]),
                     r=[pxz], w=[xzx])
            else:
                o0, o1 = max(n0, P), n1
                if o1 > o0:
                    p.op("scalar", lambda e, n0=n0, o0=o0, o1=o1, pxz=pxz: e.activation(
                        out=zs_m[:, o0 - P:o1 - P], in_=pxz[:, o0 - n0:o1 - n0], func=AF.Silu),
                        r=[pxz], w=[zs_m])
        if m < 8:
            p.op("vector", lambda e: e.tensor_scalar(out=acc[:], in0=xzx[:, P - 3:P - 3 + OWN],
                                                     scalar1=cw[:, 0:1], scalar2=None, op0=ALU.mult),
                 r=[xzx, cw], w=[acc])
            for k in range(1, 4):
                p.op("vector", lambda e, k=k: e.scalar_tensor_tensor(
                    out=acc[:], in0=xzx[:, P - 3 + k:P - 3 + k + OWN], scalar=cw[:, k:k + 1],
                    in1=acc[:], op0=ALU.mult, op1=ALU.add), r=[xzx, cw, acc], w=[acc])
            p.op("scalar", lambda e: e.activation(out=xc_m[:], in_=acc[:], func=AF.Silu,
                                                  bias=cb[:, 0:1]), r=[acc, cb], w=[xc_m])
            p.dma("sync", xc_out[m * P:(m + 1) * P, :], xc_m[:], r=[xc_m], w=[xc_out])
            for j in range(4):
                p.op("tensor", lambda e, j=j, m=m: e.matmul(
                    pxdbl[:, j * 512:(j + 1) * 512], xpw_t[:, :], xc_m[:, j * 512:(j + 1) * 512],
                    start=(m == 0), stop=(m == 7)), r=[xpw_t, xc_m], w=[pxdbl])
        else:
            p.dma("sync", zs_out[(m - 8) * P:(m - 7) * P, :], zs_m[:], r=[zs_m], w=[zs_out])
    b.free(pxzb[1])
    b.free(pxzb[0])
    xdbl = b.sbuf([64, 2048], name="xdbl1")
    for j in range(4):
        p.op("scalar", lambda e, j=j: e.copy(out=xdbl[:, j * 512:(j + 1) * 512],
                                             in_=pxdbl[:, j * 512:(j + 1) * 512]),
             r=[pxdbl], w=[xdbl])
    p.dma("sync", bc_out[:, :], xdbl[DTR:64, :], r=[xdbl], w=[bc_out])
    dtin_b = b.sbuf([DTR, 2048], BF16, name="dtin_b1")
    p.op("vector", lambda e: e.tensor_copy(out=dtin_b[:, :], in_=xdbl[0:DTR, :]), r=[xdbl], w=[dtin_b])
    dtw_t = b.sbuf([DTR, DI], BF16, name="dtw_t1")
    p.dma("sync", dtw_t[:, :], dtw_hbm[:, :], w=[dtw_t])
    b.free(pxdbl)
    dtb = b.sbuf([P, 1], name="dtb1")
    pdt = b.psum([P, 512], name="pdt1")
    v32 = b.sbuf([P, 2048], name="v321")
    a1 = b.sbuf([P, 2048], name="a1_sp1")
    for m2 in range(DI // P):
        p.dma("gpsimd", dtb[:, :], dtb_hbm[m2 * P:(m2 + 1) * P, :], w=[dtb])
        for j in range(4):
            p.op("tensor", lambda e, m2=m2, j=j: e.matmul(
                pdt[:, :], dtw_t[:, m2 * P:(m2 + 1) * P], dtin_b[:, j * 512:(j + 1) * 512],
                start=True, stop=True), r=[dtw_t, dtin_b], w=[pdt])
            p.op("scalar", lambda e, j=j: e.activation(out=v32[:, j * 512:(j + 1) * 512],
                                                       in_=pdt[:, :], func=AF.Identity,
                                                       bias=dtb[:, 0:1]), r=[pdt, dtb], w=[v32])
        p.op("scalar", lambda e: e.activation(out=a1[:], in_=v32[:], func=AF.Abs), r=[v32], w=[a1])
        p.op("scalar", lambda e: e.activation(out=a1[:], in_=a1[:], func=AF.Exp, scale=-1.0),
             r=[a1], w=[a1])
        onec = b.cconst(1.0)
        p.op("scalar", lambda e, onec=onec: e.activation(out=a1[:], in_=a1[:], func=AF.Ln, bias=onec),
             r=[a1, b._cpool], w=[a1])
        p.op("vector", lambda e: e.tensor_relu(out=v32[:], in_=v32[:]), r=[v32], w=[v32])
        p.op("vector", lambda e: e.tensor_add(out=v32[:], in0=v32[:], in1=a1[:]), r=[v32, a1], w=[v32])
        p.dma("sync", dt_out[m2 * P:(m2 + 1) * P, :], v32[:], r=[v32], w=[dt_out])


def build_l1first():
    key = ("l1first",)
    if key in _prog_cache:
        return _prog_cache[key]
    b = Builder()
    p = b.p
    D, DM = 1024, L1_D
    x_in = b.din("x", [L0_RH, D])
    b.eye_hbm = b.din("eye", [P, P], BF16)
    lin1T = b.din("lin1T", [D, DM], BF16)
    lin1b = b.din("lin1b", [DM, 1])
    inw = b.din("inw", [DM, 2 * L1_DI], BF16)
    convw = b.din("convw", [L1_DI, 4])
    convb = b.din("convb", [L1_DI, 1])
    xpw = b.din("xpw", [L1_DI, 64], BF16)
    dtw = b.din("dtw", [L1_DTR, L1_DI], BF16)
    dtbh = b.din("dtb", [L1_DI, 1])
    nw = b.din("nw", [DM, 1])
    hmaskh = b.din("hmask", [P, 1])
    resid0 = b.dout("resid0", [L0_RQ, DM])
    xc_out = b.dout("xc", [L1_DI, L0_RQ], BF16)
    zs_out = b.dout("zs", [L1_DI, L0_RQ], BF16)
    dt_out = b.dout("dt", [L1_DI, L0_RQ])
    bc_out = b.dout("bc", [32, L0_RQ])

    onesr = b.get_onesr()
    pmisc = b.get_pmisc()
    l1bbc = b.sbuf([P, DM], name="l1bbc")
    p.dma("sync", l1bbc[0:1, 0:DM], lin1b[0:DM, 0:1].rearrange("a b -> b a"), w=[l1bbc])
    emit_bcast_row(b, l1bbc, l1bbc[0:1, 0:DM], DM, onesr, pmisc, r_extra=[l1bbc])
    l1w = b.sbuf([P, 8 * DM], BF16, name="l1w")
    for k in range(8):
        p.dma("sync", l1w[:, k * DM:(k + 1) * DM], lin1T[k * P:(k + 1) * P, :], w=[l1w])
    eyeT = b.get_eye()
    ptr = b.get_ptr()
    xt = b.sbuf([P, D], name="xt1")
    rt = b.sbuf([P, D], name="rt1")
    sq1 = b.sbuf([P, D], name="sq11")
    st1 = b.sbuf([P, 8], name="st11")
    xf = b.sbuf([P, 8 * P], BF16, name="xf1")
    ph = b.get_pmisc()
    ht = b.sbuf([P, DM], name="ht1")

    def provider(i):
        p.dma("sync", xt[:], x_in[i * P:(i + 1) * P, :], w=[xt])
        emit_plain_ln(b, xt, rt, sq1, st1, D)
        p.op("vector", lambda e: e.tensor_copy(out=sq1[:, 0:D], in_=rt[:]), r=[rt], w=[sq1])
        p.op("vector", lambda e: e.tensor_copy(out=xf[:, 0:P], in_=rt[:, 0:P]), r=[rt], w=[xf]) if False else None
        rb16 = sq1
        for c in range(8):
            p.op("vector", lambda e, c=c: e.tensor_copy(out=xf[:, c * P:(c + 1) * P],
                                                        in_=rt[:, c * P:(c + 1) * P]), r=[rt], w=[xf]) if False else None
        p.op("vector", lambda e: e.tensor_copy(out=rtb[:], in_=rt[:]), r=[rt], w=[rtb])
        for c in range(8):
            p.op("tensor", lambda e, c=c: e.transpose(ptr[:], rtb[:, c * P:(c + 1) * P], eyeT[:]),
                 r=[rtb, eyeT], w=[ptr])
            p.op("scalar", lambda e, c=c: e.copy(out=xf[:, c * P:(c + 1) * P], in_=ptr[:]),
                 r=[ptr], w=[xf])
        for k in range(8):
            p.op("tensor", lambda e, k=k: e.matmul(ph[:, :], xf[:, k * P:(k + 1) * P],
                                                   l1w[:, k * DM:(k + 1) * DM],
                                                   start=(k == 0), stop=(k == 7)), r=[xf, l1w], w=[ph])
        p.op("vector", lambda e: e.tensor_add(out=ht[:], in0=ph[:, :], in1=l1bbc[:]),
             r=[ph, l1bbc], w=[ht])
        if i > 0:
            p.dma("gpsimd", resid0[(i - 1) * P:i * P, :], ht[:], r=[ht], w=[resid0])
        return ht

    rtb = b.sbuf([P, D], BF16, name="rtb1")
    emit_l1_pre_body(b, provider, 17, inw, convw, convb, xpw, dtw, dtbh,
                     nw, hmaskh, xc_out, zs_out, dt_out, bc_out)
    p.op("gpsimd", lambda e: e.engine_nop(), no_inc=True, r=[resid0, xc_out, zs_out, dt_out, bc_out])
    nc = b.build()
    _prog_cache[key] = nc
    return nc


def emit_l1_post(b, resid_in, yg_fm, ow_hbm, nT):
    p = b.p
    D, DI = L1_D, L1_DI
    TC = nT * P
    eyeT2 = b.get_eye()
    out_fm = b.sbuf([P, 4 * TC], BF16, name="out_fm1")
    ygk = [b.sbuf([P, TC], BF16, name=f"yg1k{k}") for k in range(DI // P)]
    owk = [b.sbuf([P, D], BF16, name=f"ow1k{k}") for k in range(DI // P)]
    for k in range(DI // P):
        p.dma("sync", ygk[k][:, :], yg_fm[k * P:(k + 1) * P, :], w=[ygk[k]])
        p.dma("gpsimd", owk[k][:, :], ow_hbm[k * P:(k + 1) * P, :], w=[owk[k]])
    pof = b.psum([P, 512], name="pof1")
    NSL = [(j * 512, min((j + 1) * 512, TC)) for j in range((TC + 511) // 512)]
    for m in range(4):
        for (n0, n1) in NSL:
            for k in range(DI // P):
                p.op("tensor", lambda e, m=m, k=k, n0=n0, n1=n1: e.matmul(
                    pof[:, 0:n1 - n0], owk[k][:, m * P:(m + 1) * P], ygk[k][:, n0:n1],
                    start=(k == 0), stop=(k == DI // P - 1)), r=[owk[k], ygk[k]], w=[pof])
            p.op("scalar", lambda e, m=m, n0=n0, n1=n1: e.copy(
                out=out_fm[:, m * TC + n0:m * TC + n1], in_=pof[:, 0:n1 - n0]), r=[pof], w=[out_fm])
    deps = []
    deps += b.free(pof)
    for k in reversed(range(DI // P)):
        deps += b.free(owk[k])
    for k in reversed(range(DI // P)):
        deps += b.free(ygk[k])

    rti = b.sbuf([P, D], name="rti1")
    rto = b.sbuf([P, D], name="rto1")
    ptr2 = b.get_ptr()
    first = [True]

    def provider(i):
        dd = deps if first[0] else []
        first[0] = False
        p.dma("sync", rti[:], resid_in[i * P:(i + 1) * P, :], w=[rti], deps=dd)
        for m in range(4):
            p.op("tensor", lambda e, m=m, i=i: e.transpose(
                ptr2[:], out_fm[:, m * TC + i * P:m * TC + (i + 1) * P], eyeT2[:]),
                r=[out_fm, eyeT2], w=[ptr2])
            p.op("vector", lambda e, m=m: e.tensor_add(
                out=rto[:, m * P:(m + 1) * P], in0=rti[:, m * P:(m + 1) * P], in1=ptr2[:]),
                r=[rti, ptr2], w=[rto])
        return rto

    return provider


def build_l1mid():
    key = ("l1mid",)
    if key in _prog_cache:
        return _prog_cache[key]
    b = Builder()
    p = b.p
    DM = L1_D
    resid_in = b.din("resid", [L0_RH, DM])
    b.eye_hbm = b.din("eye", [P, P], BF16)
    yg_fm = b.din("ygfm", [L1_DI, L0_RH], BF16)
    ow = b.din("ow", [L1_DI, DM], BF16)
    inw = b.din("inw", [DM, 2 * L1_DI], BF16)
    convw = b.din("convw", [L1_DI, 4])
    convb = b.din("convb", [L1_DI, 1])
    xpw = b.din("xpw", [L1_DI, 64], BF16)
    dtw = b.din("dtw", [L1_DTR, L1_DI], BF16)
    dtbh = b.din("dtb", [L1_DI, 1])
    nw = b.din("nw", [DM, 1])
    hmaskh = b.din("hmask", [P, 1])
    resid_out = b.dout("resid_o", [L0_RQ, DM])
    xc_out = b.dout("xc", [L1_DI, L0_RQ], BF16)
    zs_out = b.dout("zs", [L1_DI, L0_RQ], BF16)
    dt_out = b.dout("dt", [L1_DI, L0_RQ])
    bc_out = b.dout("bc", [32, L0_RQ])

    provider0 = emit_l1_post(b, resid_in, yg_fm, ow, 17)

    def provider(i):
        rto = provider0(i)
        if i > 0:
            p.dma("gpsimd", resid_out[(i - 1) * P:i * P, :], rto[:], r=[rto], w=[resid_out])
        return rto

    emit_l1_pre_body(b, provider, 17, inw, convw, convb, xpw, dtw, dtbh,
                     nw, hmaskh, xc_out, zs_out, dt_out, bc_out)
    p.op("gpsimd", lambda e: e.engine_nop(), no_inc=True, r=[resid_out, xc_out, zs_out, dt_out, bc_out])
    nc = b.build()
    _prog_cache[key] = nc
    return nc


def build_l1last():
    key = ("l1last",)
    if key in _prog_cache:
        return _prog_cache[key]
    b = Builder()
    p = b.p
    DM, D = L1_D, 1024
    resid_in = b.din("resid", [L0_RQ, DM])
    b.eye_hbm = b.din("eye", [P, P], BF16)
    yg_fm = b.din("ygfm", [L1_DI, L0_RQ], BF16)
    ow = b.din("ow", [L1_DI, DM], BF16)
    lin2 = b.din("lin2", [DM, D], BF16)
    lin2b = b.din("lin2b", [D, 1])
    xouter = b.din("xouter", [L0_RQ, D])
    out = b.dout("out", [L0_RQ, D])

    provider0 = emit_l1_post(b, resid_in, yg_fm, ow, 16)

    onesr = b.get_onesr()
    pmisc = b.get_pmisc()
    l2bbc = b.sbuf([P, D], name="l2bbc")
    p.dma("sync", l2bbc[0:1, 0:D], lin2b[0:D, 0:1].rearrange("a b -> b a"), w=[l2bbc])
    emit_bcast_row(b, l2bbc, l2bbc[0:1, 0:D], D, onesr, pmisc, r_extra=[l2bbc])
    l2w = b.sbuf([P, 4 * D], BF16, name="l2w")
    for k in range(4):
        p.dma("sync", l2w[:, k * D:(k + 1) * D], lin2[k * P:(k + 1) * P, :], w=[l2w])
    eyeT = b.get_eye()
    ptr = b.get_ptr()
    hb16 = b.sbuf([P, DM], BF16, name="hb16")
    hfm = b.sbuf([P, 4 * P], BF16, name="hfm1")
    pl2 = b.psum([P, 512], name="pl2")
    ot32 = b.sbuf([P, D], name="ot32")
    mx = b.sbuf([P, 32], name="mx1")
    ex = b.sbuf([P, D], name="ex1")
    sm = b.sbuf([P, 32], name="sm1")
    xot = b.sbuf([P, D], name="xot1")
    import concourse.bass as bass_mod
    for i in range(16):
        rto = provider0(i)
        p.op("vector", lambda e: e.tensor_copy(out=hb16[:], in_=rto[:]), r=[rto], w=[hb16])
        for c in range(4):
            p.op("tensor", lambda e, c=c: e.transpose(ptr[:], hb16[:, c * P:(c + 1) * P], eyeT[:]),
                 r=[hb16, eyeT], w=[ptr])
            p.op("scalar", lambda e, c=c: e.copy(out=hfm[:, c * P:(c + 1) * P], in_=ptr[:]),
                 r=[ptr], w=[hfm])
        for j in range(2):
            for k in range(4):
                p.op("tensor", lambda e, j=j, k=k: e.matmul(
                    pl2[:, :], hfm[:, k * P:(k + 1) * P], l2w[:, k * D + j * 512:k * D + (j + 1) * 512],
                    start=(k == 0), stop=(k == 3)), r=[hfm, l2w], w=[pl2])
            p.op("vector", lambda e, j=j: e.tensor_add(out=ot32[:, j * 512:(j + 1) * 512],
                                                       in0=pl2[:, :], in1=l2bbc[:, j * 512:(j + 1) * 512]),
                 r=[pl2, l2bbc], w=[ot32])
        p.op("vector", lambda e: e.tensor_reduce(
            out=mx[:], in_=ot32.ap().rearrange("p (a c) -> p a c", c=32), axis=AX.X, op=ALU.max),
            r=[ot32], w=[mx])
        def sub_emit(e):
            in1 = bass_mod.AP(mx.ap().tensor, 0, [[32, P], [1, 32], [0, 32]])
            return e.tensor_tensor(out=ot32.ap().rearrange("p (a c) -> p a c", c=32),
                                   in0=ot32.ap().rearrange("p (a c) -> p a c", c=32),
                                   in1=in1, op=ALU.subtract)
        p.op("vector", sub_emit, r=[ot32, mx], w=[ot32])
        p.op("scalar", lambda e: e.activation(out=ex[:], in_=ot32[:], func=AF.Exp), r=[ot32], w=[ex])
        p.op("vector", lambda e: e.tensor_reduce(
            out=sm[:], in_=ex.ap().rearrange("p (a c) -> p a c", c=32), axis=AX.X, op=ALU.add),
            r=[ex], w=[sm])
        p.op("vector", lambda e: e.reciprocal(out=sm[:], in_=sm[:]), r=[sm], w=[sm])
        def mul_emit(e):
            in1 = bass_mod.AP(sm.ap().tensor, 0, [[32, P], [1, 32], [0, 32]])
            return e.tensor_tensor(out=ex.ap().rearrange("p (a c) -> p a c", c=32),
                                   in0=ex.ap().rearrange("p (a c) -> p a c", c=32),
                                   in1=in1, op=ALU.mult)
        p.op("vector", mul_emit, r=[ex, sm], w=[ex])
        p.dma("gpsimd", xot[:], xouter[i * P:(i + 1) * P, :], w=[xot])
        p.op("vector", lambda e: e.tensor_add(out=ex[:], in0=ex[:], in1=xot[:]), r=[ex, xot], w=[ex])
        p.dma("sync", out[i * P:(i + 1) * P, :], ex[:], r=[ex], w=[out])
    p.op("gpsimd", lambda e: e.engine_nop(), no_inc=True, r=[out])
    nc = b.build()
    _prog_cache[key] = nc
    return nc


def run_l1_stack(x, lin1_w, lin1_b, norm_w, in_w, conv_w, conv_b, xproj_w,
                 dt_w, dt_b, A_log, Dp, out_w, lin2_w, lin2_b):
    B, T, D = x.shape
    RQ = T // 4
    eye = np.eye(P, dtype=np.float32).astype(NP_BF16)
    ones_m = np.ones((P, 1), np.float32)
    zeros_m = np.zeros((P, 1), np.float32)

    def pre_weights(i):
        return {"inw": _bf(in_w[i]), "convw": np.ascontiguousarray(conv_w[i]),
                "convb": conv_b[i].reshape(-1, 1), "xpw": _bf(xproj_w[i]),
                "dtw": _bf(dt_w[i]), "dtb": dt_b[i].reshape(-1, 1),
                "nw": norm_w[i].reshape(-1, 1)}

    in_maps = []
    for c in range(NCORES):
        beta, q = c // 4, c % 4
        im = {"eye": eye, "hmask": (zeros_m if q == 0 else ones_m),
              "x": _shard_rows_halo(x, beta, q, RQ),
              "lin1T": _bf(lin1_w), "lin1b": lin1_b.reshape(-1, 1)}
        im.update(pre_weights(0))
        in_maps.append(im)
    res = run_bass_kernel_spmd(build_l1first(), in_maps, core_ids=list(range(NCORES)))
    R = np.stack([np.concatenate([res.results[b_ * 4 + q]["resid0"] for q in range(4)], 0)
                  for b_ in range(B)], 0)

    def gather_chan(res, key):
        return [np.concatenate([np.asarray(res.results[b_ * 4 + q][key]) for q in range(4)], 1)
                for b_ in range(B)]

    for i in range(4):
        XC = gather_chan(res, "xc")
        ZS = gather_chan(res, "zs")
        DT = gather_chan(res, "dt")
        BCm = gather_chan(res, "bc")
        NCH = L1_DI // 4
        scan_maps = []
        for c in range(NCORES):
            beta, g = c // 4, c % 4
            scan_maps.append({
                "dt": np.ascontiguousarray(DT[beta][g * NCH:(g + 1) * NCH]),
                "xc": np.ascontiguousarray(XC[beta][g * NCH:(g + 1) * NCH]),
                "zs": np.ascontiguousarray(ZS[beta][g * NCH:(g + 1) * NCH]),
                "Brow": np.ascontiguousarray(BCm[beta][0:16]),
                "Crow": np.ascontiguousarray(BCm[beta][16:32]),
                "alog": np.ascontiguousarray(A_log[i][g * NCH:(g + 1) * NCH]),
                "dp": Dp[i][g * NCH:(g + 1) * NCH].reshape(-1, 1),
            })
        sres = run_bass_kernel_spmd(build_l0scan(T, NCH), scan_maps, core_ids=list(range(NCORES)))
        YG = [np.concatenate([np.asarray(sres.results[b_ * 4 + g]["yg"]) for g in range(4)], 0)
              for b_ in range(B)]
        if i < 3:
            in_maps = []
            for c in range(NCORES):
                beta, q = c // 4, c % 4
                im = {"eye": eye, "hmask": (zeros_m if q == 0 else ones_m),
                      "resid": _shard_rows_halo(R, beta, q, RQ),
                      "ygfm": _shard_cols_halo(YG[beta], q, RQ).astype(NP_BF16, copy=False),
                      "ow": _bf(out_w[i])}
                im.update(pre_weights(i + 1))
                in_maps.append(im)
            res = run_bass_kernel_spmd(build_l1mid(), in_maps, core_ids=list(range(NCORES)))
            R = np.stack([np.concatenate([res.results[b_ * 4 + q]["resid_o"] for q in range(4)], 0)
                          for b_ in range(B)], 0)
        else:
            in_maps = []
            for c in range(NCORES):
                beta, q = c // 4, c % 4
                in_maps.append({
                    "eye": eye,
                    "resid": np.ascontiguousarray(R[beta, q * RQ:(q + 1) * RQ]),
                    "ygfm": np.ascontiguousarray(YG[beta][:, q * RQ:(q + 1) * RQ]).astype(NP_BF16, copy=False),
                    "ow": _bf(out_w[3]),
                    "lin2": _bf(lin2_w), "lin2b": lin2_b.reshape(-1, 1),
                    "xouter": np.ascontiguousarray(x[beta, q * RQ:(q + 1) * RQ]),
                })
            lres = run_bass_kernel_spmd(build_l1last(), in_maps, core_ids=list(range(NCORES)))
            return np.stack([np.concatenate([lres.results[b_ * 4 + q]["out"] for q in range(4)], 0)
                             for b_ in range(B)], 0)


def build_l0ffn(R=2048):
    key = ("l0ffn", R)
    if key in _prog_cache:
        return _prog_cache[key]
    D, H_, HD, E = 1024, 16, 64, 8
    nT = R // P
    b = Builder()
    p = b.p
    x_in = b.din("x", [R, D])
    b.eye_hbm = b.din("eye", [P, P], BF16)
    wq = b.din("wq", [D, D], BF16)
    bq = b.din("bq", [D, 1])
    wdn = b.din("wdn", [D, 64], BF16)
    bdn = b.din("bdn", [64, 1])
    wup = b.din("wup", [64, 2 * D], BF16)
    bup = b.din("bup", [2 * D, 1])
    wo = b.din("wo", [D, D], BF16)
    bo = b.din("bo", [D, 1])
    g1 = b.din("g1", [D, 1])
    g2 = b.din("g2", [D, 1])
    wg = b.din("wg", [D, E], BF16)
    bg = b.din("bg", [E, 1])
    we = b.din("we", [E * D, D], BF16)
    bdm = b.din("bdm", [P, P])
    out = b.dout("out", [R, D])
    x2_scr = b.dscratch("x2_scr", [R, D])
    q_scr = b.dscratch("q_scr", [16 * P, 64], BF16)
    kv_scr = b.dscratch("kv_scr", [16 * P, P], BF16)
    o_scr = b.dscratch("o_scr", [16 * P, 64], BF16)

    onesr = b.get_onesr()
    pmisc = b.get_pmisc()
    eyeT = b.get_eye()
    ptr = b.get_ptr()

    def bcast(src, cols, name):
        t = b.sbuf([P, cols], name=name)
        p.dma("sync", t[0:1, 0:cols], src[0:cols, 0:1].rearrange("a b -> b a"), w=[t])
        emit_bcast_row(b, t, t[0:1, 0:cols], cols, onesr, pmisc, r_extra=[t])
        return t

    bq_bc = bcast(bq, D, "bq_bc")
    bdn_bc = bcast(bdn, 64, "bdn_bc")
    bup_bc = bcast(bup, 2 * D, "bup_bc")
    bo_bc = bcast(bo, D, "bo_bc")
    g1_bc = bcast(g1, D, "g1_bc")
    g2_bc = bcast(g2, D, "g2_bc")
    bg_bc = bcast(bg, E, "bg_bc")
    bdm_sb = b.sbuf([P, P], name="bdm_sb")
    p.dma("sync", bdm_sb[:], bdm[:, :], w=[bdm_sb])

    wo_sb = b.sbuf([P, 8 * D], BF16, name="wo_sb")
    for k in range(8):
        p.dma("sync", wo_sb[:, k * D:(k + 1) * D], wo[k * P:(k + 1) * P, :], w=[wo_sb])
    wg_sb = b.sbuf([P, 8 * E], BF16, name="wg_sb")
    for k in range(8):
        p.dma("sync", wg_sb[:, k * E:(k + 1) * E], wg[k * P:(k + 1) * P, :], w=[wg_sb])

    h2fm = b.sbuf([P, 8 * R], BF16, name="h2fm")
    gate = b.sbuf([P, 16 * E], name="gate")
    comb = [b.sbuf([P, D], name=f"comb{i}") for i in range(nT // 2)]

    xt = b.sbuf([P, D], name="xtf")
    xn = b.sbuf([P, D], name="xnf")
    sq = b.sbuf([P, D], name="sqf")
    st = b.sbuf([P, 8], name="stf")
    hrm = b.sbuf([P, D], name="hrm")
    hb = b.sbuf([P, D], BF16, name="hbf")
    hfm = b.sbuf([P, 8 * P], BF16, name="hfmf")
    pq = b.psum([P, 512], name="pq")
    qb16 = b.sbuf([P, D], BF16, name="qb16")
    rsb = b.sbuf([P, 64], BF16, name="rsb")
    rfm = b.sbuf([64, P], BF16, name="rfm")
    kvb16 = b.sbuf([P, 2 * D], BF16, name="kvb16")
    kn = b.sbuf([P, 16], name="kn")
    qn = b.sbuf([P, 16], name="qn")
    bnd = b.sbuf([P, 16], name="bnd")
    bnd_row = b.sbuf([1, 2048], name="bnd_row")
    bnd_scr = b.dscratch("bnd_scr", [P, 16])
    qth = b.sbuf([P, 64], BF16, name="qth")
    kvtg = b.sbuf([P, P], BF16, name="kvtg")
    qth2 = b.sbuf([P, 64], BF16, name="qth2")
    kvtg2 = b.sbuf([P, P], BF16, name="kvtg2")
    lhs_a = b.sbuf([65, P], BF16, name="lhs_a")
    rhs_a = b.sbuf([65, P], BF16, name="rhs_a")
    lhs_b = b.sbuf([65, P], BF16, name="lhs_b")
    rhs_b = b.sbuf([65, P], BF16, name="rhs_b")
    patt = b.psum([P, 1024], name="patt")
    pq2 = b.psum([P, 512], name="pq2")
    ptr2f = b.psum([P, P], BF16, name="ptr2f")
    probs = b.sbuf([P, 2048], BF16, name="probs")
    rs = b.sbuf([P, 16], name="rs")
    probsT = b.sbuf([P, P], BF16, name="probsT")
    probsT2 = b.sbuf([P, P], BF16, name="probsT2x")
    pout = b.psum([P, 64], name="pout")
    oth = b.sbuf([P, 64], BF16, name="oth")
    ott = b.sbuf([P, D], BF16, name="ott")
    otfm = b.sbuf([P, 8 * P], BF16, name="otfm")
    x2t = b.sbuf([P, D], name="x2t")
    h2t = b.sbuf([P, D], name="h2t")
    h2b = b.sbuf([P, D], BF16, name="h2b")
    gt32 = b.sbuf([P, E], name="gt32")

    p.op("vector", lambda e: e.memset(rhs_a[64:65, :], 1.0), w=[rhs_a])
    p.op("vector", lambda e: e.memset(rhs_b[64:65, :], 1.0), w=[rhs_b])
    b.cconst(1e-5)
    b.cconst(1e-6)

    wq_sb = b.sbuf([P, 8 * D], BF16, name="wq_sb")
    for k in range(8):
        p.dma("sync", wq_sb[:, k * D:(k + 1) * D], wq[k * P:(k + 1) * P, :], w=[wq_sb])
    wdn_sb = b.sbuf([P, 8 * 64], BF16, name="wdn_sb")
    for k in range(8):
        p.dma("sync", wdn_sb[:, k * 64:(k + 1) * 64], wdn[k * P:(k + 1) * P, :], w=[wdn_sb])
    wup_sb = b.sbuf([64, 2 * D], BF16, name="wup_sb")
    p.dma("sync", wup_sb[:, :], wup[:, :], w=[wup_sb])

    for i in range(nT):
        p.dma("sync", xt[:], x_in[i * P:(i + 1) * P, :], w=[xt])
        emit_plain_ln(b, xt, xn, sq, st, D)
        emit_rms_tile(b, xn, hrm, sq, st, D, g1_bc, eps=1e-6)
        p.op("vector", lambda e: e.tensor_copy(out=hb[:], in_=hrm[:]), r=[hrm], w=[hb])
        for c in range(8):
            p.op("tensor", lambda e, c=c: e.transpose(ptr[:], hb[:, c * P:(c + 1) * P], eyeT[:]),
                 r=[hb, eyeT], w=[ptr])
            p.op("scalar", lambda e, c=c: e.copy(out=hfm[:, c * P:(c + 1) * P], in_=ptr[:]),
                 r=[ptr], w=[hfm])
        for j in range(2):
            for k in range(8):
                p.op("tensor", lambda e, j=j, k=k: e.matmul(
                    pq[:, :], hfm[:, k * P:(k + 1) * P], wq_sb[:, k * D + j * 512:k * D + (j + 1) * 512],
                    start=(k == 0), stop=(k == 7)), r=[hfm, wq_sb], w=[pq])
            p.op("vector", lambda e, j=j: e.tensor_add(out=qb16[:, j * 512:(j + 1) * 512],
                                                       in0=pq[:, :], in1=bq_bc[:, j * 512:(j + 1) * 512]),
                 r=[pq, bq_bc], w=[qb16])
        for k in range(8):
            p.op("tensor", lambda e, k=k: e.matmul(
                pq[:, 0:64], hfm[:, k * P:(k + 1) * P], wdn_sb[:, k * 64:(k + 1) * 64],
                start=(k == 0), stop=(k == 7)), r=[hfm, wdn_sb], w=[pq])
        p.op("vector", lambda e: e.tensor_add(out=sq[:, 0:64], in0=pq[:, 0:64], in1=bdn_bc[:, 0:64]),
             r=[pq, bdn_bc], w=[sq])
        p.op("vector", lambda e: e.tensor_copy(out=rsb[:], in_=sq[:, 0:64]), r=[sq], w=[rsb])
        p.op("tensor", lambda e: e.transpose(ptr[0:64, 0:P], rsb.ap(), eyeT[:]), r=[rsb, eyeT], w=[ptr])
        p.op("scalar", lambda e: e.copy(out=rfm[:, :], in_=ptr[0:64, 0:P]), r=[ptr], w=[rfm])
        for j in range(4):
            p.op("tensor", lambda e, j=j: e.matmul(
                pq[:, :], rfm[:, :], wup_sb[:, j * 512:(j + 1) * 512],
                start=True, stop=True), r=[rfm, wup_sb], w=[pq])
            p.op("vector", lambda e, j=j: e.tensor_add(out=kvb16[:, j * 512:(j + 1) * 512],
                                                       in0=pq[:, :], in1=bup_bc[:, j * 512:(j + 1) * 512]),
                 r=[pq, bup_bc], w=[kvb16])
        for hh in range(2):
            p.op("scalar", lambda e, hh=hh: e.activation(out=sq[:, 0:D], in_=kvb16[:, hh * D:(hh + 1) * D],
                                                         func=AF.Square), r=[kvb16], w=[sq])
            p.op("vector", lambda e, hh=hh: e.tensor_reduce(
                out=kn[:, hh * 8:(hh + 1) * 8], in_=bass.AP(sq.ap().tensor, 0, [[D, P], [P, 8], [1, 64]]),
                axis=AX.X, op=ALU.add), r=[sq], w=[kn])
        p.op("vector", lambda e: e.tensor_reduce(out=st[:, 6:7], in_=kn[:], axis=AX.X, op=ALU.max),
             r=[kn], w=[st])
        p.op("scalar", lambda e: e.activation(out=sq[:, 0:D], in_=qb16[:], func=AF.Square),
             r=[qb16], w=[sq])
        p.op("vector", lambda e: e.tensor_reduce(
            out=qn[:], in_=bass.AP(sq.ap().tensor, 0, [[D, P], [64, 16], [1, 64]]),
            axis=AX.X, op=ALU.add), r=[sq], w=[qn])
        p.op("vector", lambda e: e.tensor_scalar(out=bnd[:], in0=qn[:], scalar1=st[:, 6:7],
                                                 scalar2=None, op0=ALU.mult), r=[qn, st], w=[bnd])
        p.op("scalar", lambda e: e.activation(out=bnd[:], in_=bnd[:], func=AF.Sqrt, scale=1.0 / 64.0),
             r=[bnd], w=[bnd])
        p.dma("gpsimd", q_scr.rearrange("(a b) d -> a (b d)", b=16), qb16[:], r=[qb16], w=[q_scr])
        p.dma("gpsimd", kv_scr.rearrange("(a b) d -> a (b d)", b=16), kvb16[:], r=[kvb16], w=[kv_scr])
        p.dma("sync", bnd_scr[:, :], bnd[:], r=[bnd], w=[bnd_scr])
        p.dma("sync", bnd_row[0:1, :], bnd_scr.rearrange("(o a) b -> o (a b)", o=1),
              r=[bnd_scr], w=[bnd_row])
        lhsb = [lhs_a, lhs_b]
        rhsb = [rhs_a, rhs_b]
        ptrs = [ptr, ptr2f]
        qthb = [qth, qth2]
        kvtgb = [kvtg, kvtg2]
        for wave in range(2):
            for mci in range(8):
                mc = wave * 8 + mci
                pb_ = mci % 2
                p.dma("gpsimd", qthb[pb_][:, :], q_scr[mc * P:(mc + 1) * P, :], r=[q_scr], w=[qthb[pb_]])
                p.dma("gpsimd", kvtgb[pb_][:, :], kv_scr[mc * P:(mc + 1) * P, :], r=[kv_scr], w=[kvtgb[pb_]])
                p.op("tensor", lambda e, pb_=pb_: e.transpose(ptrs[pb_][0:64, 0:P], qthb[pb_].ap(), eyeT[:]),
                     r=[qthb[pb_], eyeT], w=[ptrs[pb_]])
                p.op("scalar", lambda e, pb_=pb_: e.activation(out=lhsb[pb_][0:64, :], in_=ptrs[pb_][0:64, 0:P],
                                                      func=AF.Identity, scale=0.125), r=[ptrs[pb_]], w=[lhsb[pb_]])
                p.op("scalar", lambda e, mc=mc, pb_=pb_: e.activation(out=lhsb[pb_][64:65, :],
                                                             in_=bnd_row[0:1, mc * P:(mc + 1) * P],
                                                             func=AF.Identity, scale=-1.0),
                     r=[bnd_row], w=[lhsb[pb_]])
                p.op("tensor", lambda e, pb_=pb_: e.transpose(ptrs[pb_][0:64, 0:P], kvtgb[pb_][:, 0:64], eyeT[:]),
                     r=[kvtgb[pb_], eyeT], w=[ptrs[pb_]])
                p.op("scalar", lambda e, pb_=pb_: e.copy(out=rhsb[pb_][0:64, :], in_=ptrs[pb_][0:64, 0:P]),
                     r=[ptrs[pb_]], w=[rhsb[pb_]])
                p.op("tensor", lambda e, mci=mci, pb_=pb_: e.matmul(patt[:, mci * P:(mci + 1) * P],
                                                         lhsb[pb_][:, :], rhsb[pb_][:, :],
                                                         start=True, stop=True),
                     r=[lhsb[pb_], rhsb[pb_]], w=[patt])
            p.op("scalar", lambda e, wave=wave: e.activation(out=probs[:, wave * 1024:(wave + 1) * 1024],
                                                             in_=patt[:, :], func=AF.Exp),
                 r=[patt], w=[probs])
        def mask_emit(e):
            in1 = bass.AP(bdm_sb.ap().tensor, 0, [[P, P], [0, 16], [1, P]])
            return e.tensor_tensor(out=probs.ap().rearrange("p (a c) -> p a c", c=P),
                                   in0=probs.ap().rearrange("p (a c) -> p a c", c=P),
                                   in1=in1, op=ALU.mult)
        p.op("vector", mask_emit, r=[probs, bdm_sb], w=[probs])
        p.op("vector", lambda e: e.tensor_reduce(
            out=rs[:], in_=probs.ap().rearrange("p (a c) -> p a c", c=P), axis=AX.X, op=ALU.add),
            r=[probs], w=[rs])
        p.op("vector", lambda e: e.reciprocal(out=rs[:], in_=rs[:]), r=[rs], w=[rs])
        def norm_emit(e):
            in1 = bass.AP(rs.ap().tensor, 0, [[16, P], [1, 16], [0, P]])
            return e.tensor_tensor(out=probs.ap().rearrange("p (a c) -> p a c", c=P),
                                   in0=probs.ap().rearrange("p (a c) -> p a c", c=P),
                                   in1=in1, op=ALU.mult)
        p.op("vector", norm_emit, r=[probs, rs], w=[probs])
        for mc in range(16):
            pb_ = mc % 2
            ptx = [ptr, ptr2f][pb_]
            kvx = [kvtg, kvtg2][pb_]
            pTx = [probsT, probsT2][pb_]
            p.op("tensor", lambda e, mc=mc, ptx=ptx: e.transpose(ptx[:], probs[:, mc * P:(mc + 1) * P], eyeT[:]),
                 r=[probs, eyeT], w=[ptx])
            p.op("scalar", lambda e, ptx=ptx, pTx=pTx: e.copy(out=pTx[:], in_=ptx[:]), r=[ptx], w=[pTx])
            p.dma("gpsimd", kvx[:, :], kv_scr[mc * P:(mc + 1) * P, :], r=[kv_scr], w=[kvx])
            p.op("tensor", lambda e, mc=mc, kvx=kvx, pTx=pTx: e.matmul(pout[:, :], pTx[:, :], kvx[:, 64:P],
                                                     start=True, stop=True),
                 r=[pTx, kvx], w=[pout])
            p.op("scalar", lambda e: e.copy(out=oth[:], in_=pout[:, :]), r=[pout], w=[oth])
            p.dma("sync", o_scr[mc * P:(mc + 1) * P, :], oth[:, :], r=[oth], w=[o_scr])
        p.dma("sync", ott[:, :], o_scr.rearrange("(a b) d -> a (b d)", b=16), r=[o_scr], w=[ott])
        for c in range(8):
            p.op("tensor", lambda e, c=c: e.transpose(ptr[:], ott[:, c * P:(c + 1) * P], eyeT[:]),
                 r=[ott, eyeT], w=[ptr])
            p.op("scalar", lambda e, c=c: e.copy(out=otfm[:, c * P:(c + 1) * P], in_=ptr[:]),
                 r=[ptr], w=[otfm])
        for j in range(2):
            for k in range(8):
                p.op("tensor", lambda e, j=j, k=k: e.matmul(
                    pq[:, :], otfm[:, k * P:(k + 1) * P], wo_sb[:, k * D + j * 512:k * D + (j + 1) * 512],
                    start=(k == 0), stop=(k == 7)), r=[otfm, wo_sb], w=[pq])
            p.op("vector", lambda e, j=j: e.tensor_add(out=x2t[:, j * 512:(j + 1) * 512],
                                                       in0=pq[:, :], in1=bo_bc[:, j * 512:(j + 1) * 512]),
                 r=[pq, bo_bc], w=[x2t])
        p.op("vector", lambda e: e.tensor_add(out=x2t[:], in0=x2t[:], in1=xn[:]), r=[x2t, xn], w=[x2t])
        p.dma("sync", x2_scr[i * P:(i + 1) * P, :], x2t[:], r=[x2t], w=[x2_scr])
        emit_rms_tile(b, x2t, h2t, sq, st, D, g2_bc, eps=1e-6)
        p.op("vector", lambda e: e.tensor_copy(out=h2b[:], in_=h2t[:]), r=[h2t], w=[h2b])
        for c in range(8):
            p.op("tensor", lambda e, c=c: e.transpose(ptr[:], h2b[:, c * P:(c + 1) * P], eyeT[:]),
                 r=[h2b, eyeT], w=[ptr])
            p.op("scalar", lambda e, c=c, i=i: e.copy(out=h2fm[:, c * R + i * P:c * R + (i + 1) * P],
                                                      in_=ptr[:]), r=[ptr], w=[h2fm])
        for k in range(8):
            p.op("tensor", lambda e, k=k, i=i: e.matmul(
                pout[:, 0:E], h2fm[:, k * R + i * P:k * R + (i + 1) * P], wg_sb[:, k * E:(k + 1) * E],
                start=(k == 0), stop=(k == 7)), r=[h2fm, wg_sb], w=[pout])
        p.op("vector", lambda e: e.tensor_add(out=gt32[:], in0=pout[:, 0:E], in1=bg_bc[:, 0:E]),
             r=[pout, bg_bc], w=[gt32])
        p.op("vector", lambda e: e.tensor_reduce(out=st[:, 6:7], in_=gt32[:], axis=AX.X, op=ALU.max),
             r=[gt32], w=[st])
        p.op("vector", lambda e: e.tensor_scalar(out=gt32[:], in0=gt32[:], scalar1=st[:, 6:7],
                                                 scalar2=None, op0=ALU.subtract), r=[gt32, st], w=[gt32])
        p.op("scalar", lambda e: e.activation(out=gt32[:], in_=gt32[:], func=AF.Exp,
                                              accum_out=st[:, 7:8]), r=[gt32], w=[gt32, st])
        p.op("vector", lambda e: e.reciprocal(out=st[:, 7:8], in_=st[:, 7:8]), r=[st], w=[st])
        p.op("vector", lambda e, i=i: e.tensor_scalar(out=gate[:, i * E:(i + 1) * E], in0=gt32[:],
                                                      scalar1=st[:, 7:8], scalar2=None, op0=ALU.mult),
             r=[gt32, st], w=[gate])

    b.free(wup_sb)
    b.free(wdn_sb)
    b.free(wq_sb)
    we_sb = [b.sbuf([P, 8 * D], BF16, name=f"we_sb{j}") for j in range(2)]
    xo = xt
    x2r = xn
    for half in range(2):
        hi0 = half * (nT // 2)
        for ci in range(nT // 2):
            p.op("vector", lambda e, ci=ci: e.memset(comb[ci][:, :], 0.0), w=[comb[ci]])
        for ei in range(E):
            wb = we_sb[ei % 2]
            for k in range(8):
                p.dma("sync", wb[:, k * D:(k + 1) * D], we[ei * D + k * P:ei * D + (k + 1) * P, :], w=[wb])
            for ci in range(nT // 2):
                i = hi0 + ci
                for j in range(2):
                    pqx = pq if j == 0 else pq2
                    for k in range(8):
                        p.op("tensor", lambda e, i=i, j=j, k=k, wb=wb, pqx=pqx: e.matmul(
                            pqx[:, :], h2fm[:, k * R + i * P:k * R + (i + 1) * P],
                            wb[:, k * D + j * 512:k * D + (j + 1) * 512],
                            start=(k == 0), stop=(k == 7)), r=[h2fm, wb], w=[pqx])
                    p.op("vector", lambda e, i=i, j=j, ei=ei, ci=ci, pqx=pqx: e.scalar_tensor_tensor(
                        out=comb[ci][:, j * 512:(j + 1) * 512], in0=pqx[:, :],
                        scalar=gate[:, i * E + ei:i * E + ei + 1],
                        in1=comb[ci][:, j * 512:(j + 1) * 512], op0=ALU.mult, op1=ALU.add),
                        r=[pqx, gate, comb[ci]], w=[comb[ci]])
        for ci in range(nT // 2):
            i = hi0 + ci
            p.dma("sync", xo[:], x_in[i * P:(i + 1) * P, :], w=[xo])
            p.dma("gpsimd", x2r[:], x2_scr[i * P:(i + 1) * P, :], r=[x2_scr], w=[x2r])
            p.op("vector", lambda e, ci=ci: e.tensor_add(out=x2r[:], in0=x2r[:], in1=comb[ci][:, :]),
                 r=[x2r, comb[ci]], w=[x2r])
            p.op("vector", lambda e: e.tensor_add(out=x2r[:], in0=x2r[:], in1=xo[:]), r=[x2r, xo], w=[x2r])
            p.dma("sync", out[i * P:(i + 1) * P, :], x2r[:], r=[x2r], w=[out])
    p.op("gpsimd", lambda e: e.engine_nop(), no_inc=True, r=[out])
    nc = b.build()
    _prog_cache[key] = nc
    return nc


def run_l0ffn(x, gamma1, Wq, bq_, Wdown, bdown, Wup, bup_, Wo, bo_, gamma2, Wg, bg_, We):
    B, T, D = x.shape
    rows = x.reshape(B * T, D)
    R = (B * T) // NCORES
    nc = build_l0ffn(R)
    eye = np.eye(P, dtype=np.float32).astype(NP_BF16)
    bdm = np.zeros((P, P), np.float32)
    for t in range(8):
        bdm[t * 16:(t + 1) * 16, t * 16:(t + 1) * 16] = 1.0
    wts = {"eye": eye, "wq": _bf(Wq), "bq": bq_.reshape(-1, 1), "wdn": _bf(Wdown),
           "bdn": bdown.reshape(-1, 1), "wup": _bf(Wup), "bup": bup_.reshape(-1, 1),
           "wo": _bf(Wo), "bo": bo_.reshape(-1, 1), "g1": gamma1.reshape(-1, 1),
           "g2": gamma2.reshape(-1, 1), "wg": _bf(Wg), "bg": bg_.reshape(-1, 1),
           "we": _bf(We.reshape(8 * 1024, 1024)), "bdm": bdm}
    in_maps = []
    for c in range(NCORES):
        im = dict(wts)
        im["x"] = np.ascontiguousarray(rows[c * R:(c + 1) * R])
        in_maps.append(im)
    res = run_bass_kernel_spmd(nc, in_maps, core_ids=list(range(NCORES)))
    outr = np.concatenate([res.results[c]["out"] for c in range(NCORES)], 0)
    return outr.reshape(B, T, D)


def build_attnb_proj(R=2048):
    key = ("attnb_proj", R)
    if key in _prog_cache:
        return _prog_cache[key]
    D = 1024
    nT = R // P
    b = Builder()
    p = b.p
    x_in = b.din("x", [R, D])
    b.eye_hbm = b.din("eye", [P, P], BF16)
    wqT = b.din("wqT", [D, D], BF16)
    wkT = b.din("wkT", [D, D], BF16)
    wvT = b.din("wvT", [D, D], BF16)
    qs = b.din("qscale", [P, 1])
    q_out = b.dout("q", [R, D], BF16)
    k_out = b.dout("k", [R, D], BF16)
    v_out = b.dout("v", [R, D], BF16)

    eyeT = b.get_eye()
    ptr = b.get_ptr()
    qsc = b.sbuf([P, 1], name="qsc")
    p.dma("sync", qsc[:], qs[:, :], w=[qsc])
    w_sb = {}
    for nm, src in (("q", wqT), ("k", wkT), ("v", wvT)):
        t = b.sbuf([P, 8 * D], BF16, name=f"w{nm}sb")
        for kk in range(8):
            p.dma("sync", t[:, kk * D:(kk + 1) * D], src[kk * P:(kk + 1) * P, :], w=[t])
        w_sb[nm] = t
    xt = b.sbuf([P, D], name="xta")
    xn = b.sbuf([P, D], name="xna")
    sq = b.sbuf([P, D], name="sqa")
    st = b.sbuf([P, 8], name="sta")
    xnb = b.sbuf([P, D], BF16, name="xnba")
    xf = b.sbuf([P, 8 * P], BF16, name="xfa")
    pm = b.psum([P, 512], name="pma")
    ob = b.sbuf([P, D], BF16, name="oba")
    for i in range(nT):
        p.dma("sync", xt[:], x_in[i * P:(i + 1) * P, :], w=[xt])
        emit_plain_ln(b, xt, xn, sq, st, D)
        p.op("vector", lambda e: e.tensor_copy(out=xnb[:], in_=xn[:]), r=[xn], w=[xnb])
        for c in range(8):
            p.op("tensor", lambda e, c=c: e.transpose(ptr[:], xnb[:, c * P:(c + 1) * P], eyeT[:]),
                 r=[xnb, eyeT], w=[ptr])
            p.op("scalar", lambda e, c=c: e.copy(out=xf[:, c * P:(c + 1) * P], in_=ptr[:]),
                 r=[ptr], w=[xf])
        for nm, outd in (("q", q_out), ("k", k_out), ("v", v_out)):
            for j in range(2):
                for kk in range(8):
                    p.op("tensor", lambda e, nm=nm, j=j, kk=kk: e.matmul(
                        pm[:, :], xf[:, kk * P:(kk + 1) * P],
                        w_sb[nm][:, kk * D + j * 512:kk * D + (j + 1) * 512],
                        start=(kk == 0), stop=(kk == 7)), r=[xf, w_sb[nm]], w=[pm])
                if nm == "q":
                    p.op("vector", lambda e, j=j: e.tensor_scalar(
                        out=ob[:, j * 512:(j + 1) * 512], in0=pm[:, :], scalar1=qsc[:, 0:1],
                        scalar2=None, op0=ALU.mult), r=[pm, qsc], w=[ob])
                else:
                    p.op("scalar", lambda e, j=j: e.copy(out=ob[:, j * 512:(j + 1) * 512],
                                                         in_=pm[:, :]), r=[pm], w=[ob])
            p.dma("sync", outd[i * P:(i + 1) * P, :], ob[:], r=[ob], w=[outd])
    p.op("gpsimd", lambda e: e.engine_nop(), no_inc=True, r=[q_out, k_out, v_out])
    nc = b.build()
    _prog_cache[key] = nc
    return nc


def build_attnb_attn(T=8192, SLOT=512, NSLOT=2):
    key = ("attnb_attn", T, SLOT, NSLOT)
    if key in _prog_cache:
        return _prog_cache[key]
    D = 1024
    GS = 1024
    nG = T // GS
    b = Builder()
    p = b.p
    qf = [b.din(f"qf{s}", [D, 2 * SLOT], BF16) for s in range(NSLOT)]
    kT0 = b.din("kT0", [D, T], BF16)
    kT1 = b.din("kT1", [D, T], BF16)
    v0 = b.din("v0", [T, D], BF16)
    v1 = b.din("v1", [T, D], BF16)
    rowp = b.din("rowp", [1, NSLOT * SLOT])
    pidx = b.din("pidx", [P, 1])
    xr = b.din("xr", [2 * NSLOT * SLOT, D])
    out = b.dout("out", [2 * NSLOT * SLOT, D])

    onesr = b.get_onesr()
    onesc = b.sbuf([P, 1], BF16, name="onescb")
    p.op("vector", lambda e: e.memset(onesc[:], 1.0), w=[onesc])
    pmisc = b.get_pmisc()
    rp_sb = b.sbuf([1, NSLOT * SLOT], name="rp_sb")
    p.dma("sync", rp_sb[:, :], rowp[:, :], w=[rp_sb])
    rp_bc = [b.sbuf([P, SLOT], name=f"rpbc{s}") for s in range(NSLOT)]
    for s in range(NSLOT):
        emit_bcast_row(b, rp_bc[s], rp_sb[0:1, s * SLOT:(s + 1) * SLOT], SLOT, onesr, pmisc,
                       r_extra=[rp_sb])
    kg = [b.sbuf([P, 8 * GS], BF16, name=f"kg{bb}") for bb in range(2)]
    vg = [b.sbuf([P, 8 * D], BF16, name=f"vg{bb}") for bb in range(2)]
    qsb = [b.sbuf([P, 8 * 2 * SLOT], BF16, name=f"qsb{s}") for s in range(NSLOT)]
    for s in range(NSLOT):
        for kk in range(8):
            p.dma("sync", qsb[s][:, kk * 2 * SLOT:(kk + 1) * 2 * SLOT],
                  qf[s][kk * P:(kk + 1) * P, :], w=[qsb[s]])
    acc = [[b.sbuf([P, (SLOT // P) * D], name=f"acc{s}_{bb}") for bb in range(2)]
           for s in range(NSLOT)]
    for s in range(NSLOT):
        for bb in range(2):
            p.op("vector", lambda e, s=s, bb=bb: e.memset(acc[s][bb][:, :], 0.0), w=[acc[s][bb]])
    pSv = b.psum([1, 2 * D], name="pSv")
    pD = b.psum([P, SLOT], name="pD")
    pV = b.psum([P, D], name="pV")
    scol = b.sbuf([P, 1], name="scol")
    siota = b.sbuf([P, 1], name="siota")
    p.dma("sync", siota[:], pidx[:, :], w=[siota])
    a0 = b.sbuf([P, SLOT], BF16, name="a0b")
    a1 = b.sbuf([P, SLOT], BF16, name="a1b")
    msk = b.sbuf([P, SLOT], name="mskb")
    ae = [b.sbuf([P, 8 * SLOT], BF16, name=f"ae{bb}") for bb in range(2)]
    Svbc = [b.sbuf([P, D], name=f"Svbc{bb}") for bb in range(2)]
    xrt = b.sbuf([P, D], name="xrtb")
    outt = b.sbuf([P, D], name="outtb")

    for g in range(nG):
        for bb, (kT, vv) in enumerate(((kT0, v0), (kT1, v1))):
            for kk in range(8):
                p.dma("sync", kg[bb][:, kk * GS:(kk + 1) * GS],
                      kT[kk * P:(kk + 1) * P, g * GS:(g + 1) * GS], w=[kg[bb]])
            for j in range(8):
                p.dma("gpsimd", vg[bb][:, j * D:(j + 1) * D],
                      vv[g * GS + j * P:g * GS + (j + 1) * P, :], w=[vg[bb]])
            for j in range(8):
                for n in range(2):
                    p.op("tensor", lambda e, bb=bb, j=j, n=n, g=g: e.matmul(
                        pSv[0:1, bb * D + n * 512:bb * D + (n + 1) * 512], onesc[:, 0:1],
                        vg[bb][:, j * D + n * 512:j * D + (n + 1) * 512],
                        start=(g == 0 and j == 0), stop=(g == nG - 1 and j == 7)),
                        r=[onesc, vg[bb]], w=[pSv])
        for s in range(NSLOT):
            for sc in range(8):
                for bb in range(2):
                    for kk in range(8):
                        p.op("tensor", lambda e, s=s, sc=sc, bb=bb, kk=kk: e.matmul(
                            pD[:, :], kg[bb][:, kk * GS + sc * P:kk * GS + (sc + 1) * P],
                            qsb[s][:, kk * 2 * SLOT + bb * SLOT:kk * 2 * SLOT + (bb + 1) * SLOT],
                            start=(bb == 0 and kk == 0), stop=(bb == 1 and kk == 7)),
                            r=[kg[bb], qsb[s]], w=[pD])
                p.op("scalar", lambda e: e.activation(out=a0[:], in_=pD[:, :], func=AF.Sigmoid),
                     r=[pD], w=[a0])
                p.op("scalar", lambda e: e.activation(out=a1[:], in_=pD[:, :], func=AF.Sigmoid,
                                                      scale=-1.0), r=[pD], w=[a1])
                p.op("vector", lambda e, g=g, sc=sc: e.tensor_scalar(
                    out=scol[:], in0=siota[:], scalar1=float(g * GS + sc * P), scalar2=None,
                    op0=ALU.add), r=[siota], w=[scol])
                p.op("vector", lambda e, s=s: e.tensor_scalar(
                    out=msk[:], in0=rp_bc[s][:, :], scalar1=scol[:, 0:1], scalar2=None,
                    op0=ALU.is_ge), r=[rp_bc[s], scol], w=[msk])
                for bb, at in ((0, a0), (1, a1)):
                    p.op("vector", lambda e, at=at, bb=bb, sc=sc: e.scalar_tensor_tensor(
                        out=ae[bb][:, sc * SLOT:(sc + 1) * SLOT], in0=at[:], scalar=1e-9,
                        in1=msk[:], op0=ALU.subtract, op1=ALU.mult),
                        r=[at, msk], w=[ae[bb]])
            for bb in range(2):
                for mt in range(SLOT // P):
                    for n in range(2):
                        for sc in range(8):
                            p.op("tensor", lambda e, bb=bb, mt=mt, n=n, sc=sc, s=s: e.matmul(
                                pV[:, n * 512:(n + 1) * 512],
                                ae[bb][:, sc * SLOT + mt * P:sc * SLOT + (mt + 1) * P],
                                vg[bb][:, sc * D + n * 512:sc * D + (n + 1) * 512],
                                start=(sc == 0), stop=(sc == 7)), r=[ae[bb], vg[bb]], w=[pV])
                    p.op("vector", lambda e, s=s, bb=bb, mt=mt: e.tensor_add(
                        out=acc[s][bb][:, mt * D:(mt + 1) * D], in0=acc[s][bb][:, mt * D:(mt + 1) * D],
                        in1=pV[:, :]), r=[pV, acc[s][bb]], w=[acc[s][bb]])
    for bb in range(2):
        p.op("scalar", lambda e, bb=bb: e.copy(out=Svbc[0][0:1, 0:D] if bb == 0 else Svbc[1][0:1, 0:D],
                                               in_=pSv[0:1, bb * D:(bb + 1) * D]), r=[pSv], w=[Svbc[bb]])
        emit_bcast_row(b, Svbc[bb], Svbc[bb][0:1, 0:D], D, onesr, pmisc, r_extra=[Svbc[bb]])
    for bb in range(2):
        for s in range(NSLOT):
            for mt in range(SLOT // P):
                r0 = bb * NSLOT * SLOT + s * SLOT + mt * P
                p.dma("sync", xrt[:], xr[r0:r0 + P, :], w=[xrt])
                p.op("vector", lambda e, s=s, bb=bb, mt=mt: e.scalar_tensor_tensor(
                    out=outt[:], in0=Svbc[bb][:, :], scalar=1e-9, in1=acc[s][bb][:, mt * D:(mt + 1) * D],
                    op0=ALU.mult, op1=ALU.add), r=[Svbc[bb], acc[s][bb]], w=[outt])
                p.op("vector", lambda e: e.tensor_add(out=outt[:], in0=outt[:], in1=xrt[:]),
                     r=[outt, xrt], w=[outt])
                p.dma("sync", out[r0:r0 + P, :], outt[:], r=[outt], w=[out])
    p.op("gpsimd", lambda e: e.engine_nop(), no_inc=True, r=[out])
    nc = b.build()
    _prog_cache[key] = nc
    return nc


def run_attnb(x):
    raise NotImplementedError


def run_attnb_full(x, Wq, Wk, Wv):
    B, T, D = x.shape
    rows = x.reshape(B * T, D)
    R = (B * T) // NCORES
    eye = np.eye(P, dtype=np.float32).astype(NP_BF16)
    nc1 = build_attnb_proj(R)
    in_maps = []
    for c in range(NCORES):
        sgn = 1.0 if c < 4 else -1.0
        in_maps.append({"x": np.ascontiguousarray(rows[c * R:(c + 1) * R]), "eye": eye,
                        "wqT": _bf(Wq.T), "wkT": _bf(Wk.T), "wvT": _bf(Wv.T),
                        "qscale": np.full((P, 1), sgn / 32.0, np.float32)})
    res = run_bass_kernel_spmd(nc1, in_maps, core_ids=list(range(NCORES)))
    q = np.concatenate([np.asarray(res.results[c]["q"]) for c in range(NCORES)], 0).reshape(B, T, D)
    k = np.concatenate([np.asarray(res.results[c]["k"]) for c in range(NCORES)], 0).reshape(B, T, D)
    v = np.concatenate([np.asarray(res.results[c]["v"]) for c in range(NCORES)], 0).reshape(B, T, D)
    kT0 = np.ascontiguousarray(k[0].reshape(D, T))
    kT1 = np.ascontiguousarray(k[1].reshape(D, T))
    nc2 = build_attnb_attn(T, 512, 2)
    in_maps = []
    slot_of = [(c, 15 - c) for c in range(NCORES)]
    for c in range(NCORES):
        s0, s1 = slot_of[c]
        qf = []
        for s in (s0, s1):
            q0 = np.ascontiguousarray(q[0, s * 512:(s + 1) * 512].T)
            q1 = np.ascontiguousarray(q[1, s * 512:(s + 1) * 512].T)
            qf.append(np.concatenate([q0, q1], 1))
        rowp = np.concatenate([np.arange(s * 512, (s + 1) * 512, dtype=np.float32)
                               for s in (s0, s1)]).reshape(1, -1)
        xrows = np.concatenate([x[bb, s * 512:(s + 1) * 512] for bb in range(2) for s in (s0, s1)], 0)
        in_maps.append({"qf0": qf[0].astype(NP_BF16, copy=False), "qf1": qf[1].astype(NP_BF16, copy=False),
                        "kT0": kT0, "kT1": kT1, "v0": np.ascontiguousarray(v[0]),
                        "v1": np.ascontiguousarray(v[1]), "rowp": rowp,
                        "pidx": np.arange(P, dtype=np.float32).reshape(P, 1),
                        "xr": np.ascontiguousarray(xrows)})
    res2 = run_bass_kernel_spmd(nc2, in_maps, core_ids=list(range(NCORES)))
    xout = np.empty((B, T, D), np.float32)
    for c in range(NCORES):
        s0, s1 = slot_of[c]
        o = np.asarray(res2.results[c]["out"])
        for bi, bb in enumerate(range(2)):
            for si, s in enumerate((s0, s1)):
                xout[bb, s * 512:(s + 1) * 512] = o[(bi * 2 + si) * 512:(bi * 2 + si + 1) * 512]
    return xout


def build_l1ab_proj(R=2048):
    key = ("l1ab_proj", R)
    if key in _prog_cache:
        return _prog_cache[key]
    D, DO = 1024, 3072
    nT = R // P
    b = Builder()
    p = b.p
    x_in = b.din("x", [R, D])
    b.eye_hbm = b.din("eye", [P, P], BF16)
    wT = b.din("wT", [D, DO], BF16)
    qkv_out = b.dout("qkv", [R, DO], BF16)
    eyeT = b.get_eye()
    ptr = b.get_ptr()
    w_sb = b.sbuf([P, 8 * DO], BF16, name="wqkvsb")
    for kk in range(8):
        p.dma("sync", w_sb[:, kk * DO:(kk + 1) * DO], wT[kk * P:(kk + 1) * P, :], w=[w_sb])
    xt = b.sbuf([P, D], name="xtc")
    xn = b.sbuf([P, D], name="xnc")
    sq = b.sbuf([P, D], name="sqc")
    st = b.sbuf([P, 8], name="stc")
    xnb = b.sbuf([P, D], BF16, name="xnbc")
    xf = b.sbuf([P, 8 * P], BF16, name="xfc")
    pm = b.psum([P, 512], name="pmc")
    ob = b.sbuf([P, DO], BF16, name="obc")
    for i in range(nT):
        p.dma("sync", xt[:], x_in[i * P:(i + 1) * P, :], w=[xt])
        emit_plain_ln(b, xt, xn, sq, st, D)
        p.op("vector", lambda e: e.tensor_copy(out=xnb[:], in_=xn[:]), r=[xn], w=[xnb])
        for c in range(8):
            p.op("tensor", lambda e, c=c: e.transpose(ptr[:], xnb[:, c * P:(c + 1) * P], eyeT[:]),
                 r=[xnb, eyeT], w=[ptr])
            p.op("scalar", lambda e, c=c: e.copy(out=xf[:, c * P:(c + 1) * P], in_=ptr[:]),
                 r=[ptr], w=[xf])
        for j in range(6):
            for kk in range(8):
                p.op("tensor", lambda e, j=j, kk=kk: e.matmul(
                    pm[:, :], xf[:, kk * P:(kk + 1) * P],
                    w_sb[:, kk * DO + j * 512:kk * DO + (j + 1) * 512],
                    start=(kk == 0), stop=(kk == 7)), r=[xf, w_sb], w=[pm])
            p.op("scalar", lambda e, j=j: e.copy(out=ob[:, j * 512:(j + 1) * 512], in_=pm[:, :]),
                 r=[pm], w=[ob])
        p.dma("sync", qkv_out[i * P:(i + 1) * P, :], ob[:], r=[ob], w=[qkv_out])
    p.op("gpsimd", lambda e: e.engine_nop(), no_inc=True, r=[qkv_out])
    nc = b.build()
    _prog_cache[key] = nc
    return nc


def build_l1ab_attn(T=8192, CH=512, NH=4):
    key = ("l1ab_attn", T, CH, NH)
    if key in _prog_cache:
        return _prog_cache[key]
    Dh = 64
    nC = T // CH
    b = Builder()
    p = b.p
    qfm = b.din("qfm", [NH * Dh, T], BF16)
    kfm = b.din("kfm", [NH * Dh, T], BF16)
    vtk = b.din("vtk", [T, NH * Dh], BF16)
    a_in = b.din("a_in", [1, 1])
    b_in = b.din("b_in", [1, 1])
    dt_in = b.din("dt_in", [1, 1])
    out = b.dout("out", [T, NH * Dh], BF16)
    fs_scr = b.dscratch("fs_scr", [NH * nC, CH])

    onesr = b.get_onesr()
    pmisc = b.get_pmisc()
    ones1 = b.sbuf([1, P], BF16, name="ones1b")
    p.op("vector", lambda e: e.memset(ones1[:], 1.0), w=[ones1])
    onescb = b.sbuf([P, 1], BF16, name="onescb2")
    p.op("vector", lambda e: e.memset(onescb[:], 1.0), w=[onescb])

    q_sb = b.sbuf([P, 2 * T], BF16, name="q_sb2")
    k_sb = b.sbuf([P, 2 * T], BF16, name="k_sb2")
    v_sb = b.sbuf([P, (T // P) * NH * Dh], BF16, name="v_sb2")
    for j in range(2):
        p.dma("sync", q_sb[:, j * T:(j + 1) * T], qfm[j * P:(j + 1) * P, :], w=[q_sb])
        p.dma("sync", k_sb[:, j * T:(j + 1) * T], kfm[j * P:(j + 1) * P, :], w=[k_sb])
    for j in range(T // P):
        p.dma("sync", v_sb[:, j * NH * Dh:(j + 1) * NH * Dh], vtk[j * P:(j + 1) * P, :], w=[v_sb])
    ab_sb = b.sbuf([1, 8], name="ab_sb")
    p.dma("sync", ab_sb[0:1, 0:1], a_in[:, :], w=[ab_sb])
    p.dma("sync", ab_sb[0:1, 1:2], b_in[:, :], w=[ab_sb])
    p.dma("sync", ab_sb[0:1, 2:3], dt_in[:, :], w=[ab_sb])
    p.op("scalar", lambda e: e.activation(out=ab_sb[0:1, 3:4], in_=ab_sb[0:1, 2:3],
                                          func=AF.Identity, scale=1.0 / 12.5), r=[ab_sb], w=[ab_sb])
    p.op("vector", lambda e: e.tensor_tensor(out=ab_sb[0:1, 4:5], in0=ab_sb[0:1, 3:4],
                                             in1=ab_sb[0:1, 1:2], op=ALU.mult), r=[ab_sb], w=[ab_sb])
    onec2 = b.cconst(1.0)
    p.op("scalar", lambda e, onec2=onec2: e.activation(out=ab_sb[0:1, 5:6], in_=ab_sb[0:1, 4:5],
                                                       func=AF.Identity, bias=onec2[0:1, 0:1]),
         r=[ab_sb, b._cpool], w=[ab_sb])
    p.op("vector", lambda e: e.reciprocal(out=ab_sb[0:1, 6:7], in_=ab_sb[0:1, 5:6]), r=[ab_sb], w=[ab_sb])
    ab64 = b.sbuf([64, 8], name="ab64")
    p.op("tensor", lambda e: e.matmul(pmisc[0:64, 0:7], onesr[0:1, 0:64], ab_sb[0:1, 0:7],
                                      start=True, stop=True), r=[onesr, ab_sb], w=[pmisc])
    p.op("scalar", lambda e: e.copy(out=ab64[:, 0:7], in_=pmisc[0:64, 0:7]), r=[pmisc], w=[ab64])

    sqk = b.sbuf([P, T], BF16, name="sqk2")
    kn_row = b.sbuf([1, T], name="kn_row2")
    pkn = b.psum([1, 512], name="pkn2")
    sqk2 = b.sbuf([P, T], BF16, name="sqk3")
    p.op("scalar", lambda e: e.activation(out=sqk[:], in_=k_sb[:, 0:T], func=AF.Square),
         r=[k_sb], w=[sqk])
    p.op("scalar", lambda e: e.activation(out=sqk2[:], in_=k_sb[:, T:2 * T], func=AF.Square),
         r=[k_sb], w=[sqk2])
    for c in range(T // 512):
        p.op("tensor", lambda e, c=c: e.matmul(pkn[0:1, :], onescb[:, 0:1],
                                               sqk[:, c * 512:(c + 1) * 512], start=True, stop=False),
             r=[onescb, sqk], w=[pkn])
        p.op("tensor", lambda e, c=c: e.matmul(pkn[0:1, :], onescb[:, 0:1],
                                               sqk2[:, c * 512:(c + 1) * 512], start=False, stop=True),
             r=[onescb, sqk2], w=[pkn])
        p.op("scalar", lambda e, c=c: e.copy(out=kn_row[0:1, c * 512:(c + 1) * 512], in_=pkn[0:1, :]),
             r=[pkn], w=[kn_row])
    p.op("vector", lambda e: e.tensor_reduce(out=ab_sb[0:1, 7:8], in_=kn_row[0:1, :], axis=AX.X,
                                             op=ALU.max), r=[kn_row], w=[ab_sb])
    b.free(sqk2)
    b.free(kn_row)
    b.free(sqk)

    lrows = b.sbuf([NH * nC, CH], name="lrows2")
    frows = b.sbuf([NH * nC, CH], name="frows2")
    tmpr = b.sbuf([1, CH], name="tmpr2")
    sqq = b.sbuf([64, CH], BF16, name="sqq2")
    bndr = b.sbuf([1, CH], name="bndr2")
    bndb = b.sbuf([1, CH], BF16, name="bndb2")
    probsT = b.sbuf([P, (T // P) * CH], BF16, name="probsT2")
    pl = b.psum([1, CH], name="pl2")
    psc2b = [b.psum([P, CH], name=f"psc2{j}") for j in range(2)]
    pout = b.psum([P, 4 * Dh], name="pout2")
    colf = b.sbuf([P, 1], name="colf2")
    osb = b.sbuf([P, NH * Dh], BF16, name="osb2")
    o_scr = b.dscratch("o_scr2", [T, NH * Dh], BF16)

    for h in range(NH):
        jt = h // 2
        hb = (h % 2) * 64
        for ci in range(nC):
            ns = (ci + 1) * (CH // P)
            p.op("scalar", lambda e, jt=jt, hb=hb, ci=ci: e.activation(
                out=sqq[:, :], in_=q_sb[0 + hb:64 + hb, jt * T + ci * CH:jt * T + (ci + 1) * CH]
                if False else q_sb[hb:hb + 64, jt * T + ci * CH:jt * T + (ci + 1) * CH],
                func=AF.Square), r=[q_sb], w=[sqq])
            p.op("tensor", lambda e: e.matmul(pmisc[0:1, 0:CH], onescb[0:64, 0:1], sqq[:, :],
                                              start=True, stop=True), r=[onescb, sqq], w=[pmisc])
            p.op("vector", lambda e: e.tensor_scalar(out=bndr[0:1, :], in0=pmisc[0:1, 0:CH],
                                                     scalar1=ab_sb[0:1, 7:8], scalar2=None,
                                                     op0=ALU.mult), r=[pmisc, ab_sb], w=[bndr])
            p.op("scalar", lambda e: e.activation(out=bndr[0:1, :], in_=bndr[0:1, :], func=AF.Sqrt),
                 r=[bndr], w=[bndr])
            p.op("scalar", lambda e: e.activation(out=bndb[0:1, :], in_=bndr[0:1, :], func=AF.Identity,
                                                  scale=-1.0), r=[bndr], w=[bndb])
            for sc in range(ns):
                psc = psc2b[sc % 2]
                p.op("tensor", lambda e, jt=jt, hb=hb, ci=ci, sc=sc, psc=psc: e.matmul(
                    psc[:, :], k_sb[hb:hb + 64, jt * T + sc * P:jt * T + (sc + 1) * P],
                    q_sb[hb:hb + 64, jt * T + ci * CH:jt * T + (ci + 1) * CH],
                    start=True, stop=False), r=[k_sb, q_sb], w=[psc])
                p.op("tensor", lambda e, psc=psc: e.matmul(psc[:, :], ones1[0:1, 0:P], bndb[0:1, :],
                                                  start=False, stop=True), r=[ones1, bndb], w=[psc])
                p.op("scalar", lambda e, sc=sc, psc=psc: e.activation(out=probsT[:, sc * CH:(sc + 1) * CH],
                                                             in_=psc[:, :], func=AF.Exp, scale=0.125),
                     r=[psc], w=[probsT])
            for sc in range(ns):
                p.op("tensor", lambda e, sc=sc, ns=ns: e.matmul(pl[0:1, :], onescb[:, 0:1],
                                                                probsT[:, sc * CH:(sc + 1) * CH],
                                                                start=(sc == 0), stop=(sc == ns - 1)),
                     r=[onescb, probsT], w=[pl])
            p.op("scalar", lambda e: e.copy(out=tmpr[0:1, :], in_=pl[0:1, :]), r=[pl], w=[tmpr])
            p.dma("gpsimd", lrows[h * nC + ci:h * nC + ci + 1, :], tmpr[0:1, :], r=[tmpr], w=[lrows])
            for mt in range(CH // P):
                for sc in range(ns):
                    p.op("tensor", lambda e, mt=mt, sc=sc, ns=ns: e.matmul(
                        pout[:, 0:NH * Dh], probsT[:, sc * CH + mt * P:sc * CH + (mt + 1) * P],
                        v_sb[:, sc * NH * Dh:(sc + 1) * NH * Dh],
                        start=(sc == 0), stop=(sc == ns - 1)), r=[probsT, v_sb], w=[pout])
                p.op("scalar", lambda e, mt=mt, h=h: e.copy(
                    out=osb[:, 0:Dh], in_=pout[:, h * Dh:(h + 1) * Dh]), r=[pout], w=[osb])
                p.dma("sync", o_scr[ci * CH + mt * P:ci * CH + (mt + 1) * P, h * Dh:(h + 1) * Dh],
                      osb[:, 0:Dh], r=[osb], w=[o_scr])

    en = b.sbuf([NH * nC, CH], name="en2")
    t1 = b.sbuf([NH * nC, CH], name="t1_2")
    t2 = b.sbuf([NH * nC, CH], name="t2_2")
    t3 = b.sbuf([NH * nC, CH], name="t3_2")
    vv = b.sbuf([NH * nC, CH], name="vv2")
    ww = b.sbuf([NH * nC, CH], name="ww2")
    NR = NH * nC
    p.op("vector", lambda e: e.reciprocal(out=en[:, :], in_=lrows[:, :]), r=[lrows], w=[en])
    p.op("vector", lambda e: e.tensor_mul(out=en[:, :], in0=en[:, :], in1=lrows[:, :]),
         r=[en, lrows], w=[en])
    p.op("scalar", lambda e: e.activation(out=t1[:, :], in_=en[:, :], func=AF.Abs), r=[en], w=[t1])
    p.op("vector", lambda e: e.tensor_scalar_max(out=t2[:, :], in0=t1[:, :], scalar1=1e-6),
         r=[t1], w=[t2])
    p.op("vector", lambda e: e.reciprocal(out=t3[:, :], in_=t2[:, :]), r=[t2], w=[t3])
    p.op("vector", lambda e: e.tensor_mul(out=t2[:, :], in0=en[:, :], in1=t3[:, :]),
         r=[en, t3], w=[t2])
    fivec = b.cconst(-5.0)
    p.op("scalar", lambda e, fivec=fivec: e.activation(out=t1[:, :], in_=t1[:, :], func=AF.Sigmoid,
                                                       scale=10.0, bias=fivec[0:64, 0:1]), r=[t1, b._cpool], w=[t1])
    p.op("vector", lambda e: e.tensor_scalar(out=t1[:, :], in0=t1[:, :], scalar1=0.9, scalar2=0.1,
                                             op0=ALU.mult, op1=ALU.add), r=[t1], w=[t1])
    p.op("vector", lambda e: e.tensor_mul(out=t1[:, :], in0=t2[:, :], in1=t1[:, :]),
         r=[t1, t2], w=[t1])
    p.op("vector", lambda e: e.memset(vv[:, :], 0.0), w=[vv])
    p.op("vector", lambda e: e.memset(ww[:, :], 0.0), w=[ww])
    for step in range(2):
        p.op("scalar", lambda e: e.activation(out=t2[:, :], in_=vv[:, :], func=AF.Square),
             r=[vv], w=[t2])
        p.op("vector", lambda e: e.tensor_mul(out=t2[:, :], in0=t2[:, :], in1=vv[:, :]),
             r=[t2, vv], w=[t2])
        p.op("vector", lambda e: e.tensor_scalar(out=t2[:, :], in0=t2[:, :], scalar1=-1.0 / 3.0,
                                                 scalar2=None, op0=ALU.mult), r=[t2], w=[t2])
        p.op("vector", lambda e: e.tensor_add(out=t2[:, :], in0=t2[:, :], in1=vv[:, :]),
             r=[t2, vv], w=[t2])
        p.op("vector", lambda e: e.tensor_sub(out=t2[:, :], in0=t2[:, :], in1=ww[:, :]),
             r=[t2, ww], w=[t2])
        p.op("vector", lambda e: e.tensor_add(out=t2[:, :], in0=t2[:, :], in1=t1[:, :]),
             r=[t2, t1], w=[t2])
        p.op("vector", lambda e: e.scalar_tensor_tensor(out=t2[:, :], in0=t2[:, :],
                                                        scalar=ab64[:, 2:3], in1=vv[:, :],
                                                        op0=ALU.mult, op1=ALU.add),
             r=[t2, ab64, vv], w=[t2])
        p.op("vector", lambda e: e.tensor_scalar(out=frows[:, :],
                                                 in0=t2[:, :], scalar1=ab64[:, 0:1],
                                                 scalar2=ab64[:, 3:4], op0=ALU.add, op1=ALU.mult),
             r=[t2, ab64], w=[frows])
        p.op("vector", lambda e: e.tensor_add(out=ww[:, :], in0=ww[:, :], in1=frows[:, :]),
             r=[ww, frows], w=[ww])
        p.op("vector", lambda e: e.tensor_scalar(out=ww[:, :], in0=ww[:, :],
                                                 scalar1=ab64[:, 6:7], scalar2=None,
                                                 op0=ALU.mult), r=[ww, ab64], w=[ww])
        p.op("vector", lambda e: e.tensor_scalar_min(out=vv[:, :], in0=t2[:, :], scalar1=3.0),
             r=[t2], w=[vv])
        p.op("vector", lambda e: e.tensor_scalar_max(out=vv[:, :], in0=vv[:, :], scalar1=-3.0),
             r=[vv], w=[vv])
        p.op("vector", lambda e: e.tensor_scalar_min(out=ww[:, :], in0=ww[:, :], scalar1=3.0),
             r=[ww], w=[ww])
        p.op("vector", lambda e: e.tensor_scalar_max(out=ww[:, :], in0=ww[:, :], scalar1=-3.0),
             r=[ww], w=[ww])
    p.op("scalar", lambda e: e.activation(out=t1[:, :], in_=en[:, :], func=AF.Abs), r=[en], w=[t1])
    p.op("vector", lambda e: e.tensor_scalar_max(out=t1[:, :], in0=t1[:, :], scalar1=1e-6),
         r=[t1], w=[t1])
    p.op("vector", lambda e: e.tensor_mul(out=t2[:, :], in0=vv[:, :], in1=t1[:, :]),
         r=[vv, t1], w=[t2])
    p.op("scalar", lambda e: e.activation(out=t2[:, :], in_=t2[:, :], func=AF.Sigmoid),
         r=[t2], w=[t2])
    p.op("vector", lambda e: e.tensor_scalar(out=t2[:, :], in0=t2[:, :], scalar1=0.5, scalar2=0.5,
                                             op0=ALU.mult, op1=ALU.add), r=[t2], w=[t2])
    eps8 = b.cconst(1e-8)
    p.op("scalar", lambda e, eps8=eps8: e.activation(out=frows[:, :], in_=t2[:, :], func=AF.Identity,
                                                     bias=eps8[0:64, 0:1]), r=[t2, b._cpool], w=[frows])
    p.op("vector", lambda e: e.reciprocal(out=frows[:, :], in_=frows[:, :]), r=[frows], w=[frows])
    p.op("vector", lambda e: e.tensor_mul(out=frows[:, :], in0=t2[:, :], in1=frows[:, :]),
         r=[t2, frows], w=[frows])
    p.op("vector", lambda e: e.reciprocal(out=t3[:, :], in_=lrows[:, :]), r=[lrows], w=[t3])
    p.op("vector", lambda e: e.tensor_mul(out=frows[:, :], in0=frows[:, :], in1=t3[:, :]),
         r=[frows, t3], w=[frows])
    p.dma("sync", fs_scr[:, :], frows[:, :], r=[frows], w=[fs_scr])
    ot = b.sbuf([P, NH * Dh], BF16, name="ot2")
    fcol = b.sbuf([P, NH], name="fcol2")
    for ci in range(nC):
        for mt in range(CH // P):
            p.dma("sync", ot[:, :], o_scr[ci * CH + mt * P:ci * CH + (mt + 1) * P, :],
                  r=[o_scr], w=[ot])
            for h in range(NH):
                p.dma("gpsimd", fcol[:, h:h + 1],
                      bass.AP(fs_scr.tensor, (h * nC + ci) * CH + mt * P, [[1, P], [1, 1]]),
                      r=[fs_scr], w=[fcol])
            for h in range(NH):
                p.op("vector", lambda e, h=h: e.tensor_scalar(
                    out=ot[:, h * Dh:(h + 1) * Dh], in0=ot[:, h * Dh:(h + 1) * Dh],
                    scalar1=fcol[:, h:h + 1], scalar2=None, op0=ALU.mult), r=[ot, fcol], w=[ot])
            p.dma("sync", out[ci * CH + mt * P:ci * CH + (mt + 1) * P, :], ot[:, :],
                  r=[ot], w=[out])
    p.op("gpsimd", lambda e: e.engine_nop(), no_inc=True, r=[out])
    nc = b.build()
    _prog_cache[key] = nc
    return nc


def build_l1ab_out(R=2048):
    key = ("l1ab_out", R)
    if key in _prog_cache:
        return _prog_cache[key]
    D = 1024
    nT = R // P
    b = Builder()
    p = b.p
    x_in = b.din("x", [R, D])
    oin = b.din("oin", [R, D], BF16)
    b.eye_hbm = b.din("eye", [P, P], BF16)
    wT = b.din("wT", [D, D], BF16)
    out = b.dout("out", [R, D])
    eyeT = b.get_eye()
    ptr = b.get_ptr()
    w_sb = b.sbuf([P, 8 * D], BF16, name="woutsb")
    for kk in range(8):
        p.dma("sync", w_sb[:, kk * D:(kk + 1) * D], wT[kk * P:(kk + 1) * P, :], w=[w_sb])
    ot = b.sbuf([P, D], BF16, name="otd")
    of = b.sbuf([P, 8 * P], BF16, name="ofd")
    xt = b.sbuf([P, D], name="xtd")
    pm = b.psum([P, 512], name="pmd")
    rt = b.sbuf([P, D], name="rtd")
    for i in range(nT):
        p.dma("sync", ot[:], oin[i * P:(i + 1) * P, :], w=[ot])
        p.dma("gpsimd", xt[:], x_in[i * P:(i + 1) * P, :], w=[xt])
        for c in range(8):
            p.op("tensor", lambda e, c=c: e.transpose(ptr[:], ot[:, c * P:(c + 1) * P], eyeT[:]),
                 r=[ot, eyeT], w=[ptr])
            p.op("scalar", lambda e, c=c: e.copy(out=of[:, c * P:(c + 1) * P], in_=ptr[:]),
                 r=[ptr], w=[of])
        for j in range(2):
            for kk in range(8):
                p.op("tensor", lambda e, j=j, kk=kk: e.matmul(
                    pm[:, :], of[:, kk * P:(kk + 1) * P],
                    w_sb[:, kk * D + j * 512:kk * D + (j + 1) * 512],
                    start=(kk == 0), stop=(kk == 7)), r=[of, w_sb], w=[pm])
            p.op("vector", lambda e, j=j: e.tensor_add(out=rt[:, j * 512:(j + 1) * 512],
                                                       in0=pm[:, :], in1=xt[:, j * 512:(j + 1) * 512]),
                 r=[pm, xt], w=[rt])
        p.dma("sync", out[i * P:(i + 1) * P, :], rt[:], r=[rt], w=[out])
    p.op("gpsimd", lambda e: e.engine_nop(), no_inc=True, r=[out])
    nc = b.build()
    _prog_cache[key] = nc
    return nc


def run_l1ab(x, W_qkv, W_out, a_s, b_s, dt_s):
    B, T, D = x.shape
    rows = x.reshape(B * T, D)
    R = (B * T) // NCORES
    eye = np.eye(P, dtype=np.float32).astype(NP_BF16)
    nc1 = build_l1ab_proj(R)
    in_maps = [{"x": np.ascontiguousarray(rows[c * R:(c + 1) * R]), "eye": eye,
                "wT": _bf(W_qkv.T)} for c in range(NCORES)]
    res = run_bass_kernel_spmd(nc1, in_maps, core_ids=list(range(NCORES)))
    qkv = np.concatenate([np.asarray(res.results[c]["qkv"]) for c in range(NCORES)], 0)
    qkv = qkv.reshape(B, T, 3, 16, 64)
    nc2 = build_l1ab_attn(T, 512, 4)
    in_maps = []
    for c in range(NCORES):
        bb, hg = c // 4, c % 4
        hs = slice(hg * 4, hg * 4 + 4)
        q = np.ascontiguousarray(qkv[bb, :, 0, hs].transpose(1, 2, 0).reshape(256, T))
        k = np.ascontiguousarray(qkv[bb, :, 1, hs].transpose(1, 2, 0).reshape(256, T))
        v = np.ascontiguousarray(qkv[bb, :, 2, hs].transpose(1, 0, 2).reshape(T, 256))
        in_maps.append({"qfm": q, "kfm": k, "vtk": v,
                        "a_in": np.full((1, 1), a_s, np.float32),
                        "b_in": np.full((1, 1), b_s, np.float32),
                        "dt_in": np.full((1, 1), dt_s, np.float32)})
    res2 = run_bass_kernel_spmd(nc2, in_maps, core_ids=list(range(NCORES)))
    out_all = np.zeros((B, T, 16, 64), NP_BF16)
    for c in range(NCORES):
        bb, hg = c // 4, c % 4
        o = np.asarray(res2.results[c]["out"]).reshape(T, 4, 64)
        out_all[bb, :, hg * 4:hg * 4 + 4] = o
    oin = out_all.reshape(B * T, D)
    nc3 = build_l1ab_out(R)
    in_maps = [{"x": np.ascontiguousarray(rows[c * R:(c + 1) * R]),
                "oin": np.ascontiguousarray(oin[c * R:(c + 1) * R]), "eye": eye,
                "wT": _bf(W_out.T)} for c in range(NCORES)]
    res3 = run_bass_kernel_spmd(nc3, in_maps, core_ids=list(range(NCORES)))
    outr = np.concatenate([res3.results[c]["out"] for c in range(NCORES)], 0)
    return outr.reshape(B, T, D)


# version 05
# speedup vs baseline: 1.0562x; 1.0562x over previous
import numpy as np
import concourse.bass as bass
import concourse.mybir as mybir
from concourse.bass_utils import run_bass_kernel_spmd

F32 = mybir.dt.float32
BF16 = mybir.dt.bfloat16
AF = mybir.ActivationFunctionType
ALU = mybir.AluOpType
AX = mybir.AxisListType

NCORES = 8
P = 128

import ml_dtypes
NP_BF16 = ml_dtypes.bfloat16

ENGINES = ("sync", "vector", "scalar", "tensor", "gpsimd")


class Op:
    __slots__ = ("eng", "emit", "deps", "inc", "cum", "sem")

    def __init__(self, eng, emit, deps, inc, sem):
        self.eng, self.emit, self.deps, self.inc = eng, emit, deps, inc
        self.sem = sem
        self.cum = None


class Prog:

    def __init__(self, nc):
        self.nc = nc
        self.ops = {e: [] for e in ENGINES}
        self.cum = {e: 0 for e in ENGINES}
        self.last_writer = {}
        self.readers = {}
        self.fence_deps = []

    def op(self, eng, emit, r=(), w=(), deps=(), dma=False, no_inc=False):
        dep = set(d for d in deps if d is not None)
        dep.update(self.fence_deps)
        sem = "gpsimd_c" if (eng == "gpsimd" and not dma) else eng
        for buf in r:
            lw = self.last_writer.get(id(buf))
            if lw is not None:
                dep.add(lw)
        for buf in w:
            lw = self.last_writer.get(id(buf))
            if lw is not None:
                dep.add(lw)
            for rd in self.readers.get(id(buf), ()):
                dep.add(rd)
        o = Op(eng, emit, list(dep), 0 if no_inc else (16 if dma else 1), sem)
        self.cum[sem] = self.cum.get(sem, 0) + o.inc
        o.cum = self.cum[sem]
        self.ops[eng].append(o)
        for buf in r:
            self.readers.setdefault(id(buf), []).append(o)
        for buf in w:
            self.last_writer[id(buf)] = o
            self.readers[id(buf)] = []
        return o

    def dma(self, eng, out, in_, r=(), w=(), deps=(), **kw):
        return self.op(eng, lambda e: e.dma_start(out=out, in_=in_, **kw),
                       r=r, w=w, deps=deps, dma=True)

    def _batch_ends(self):
        be = {}
        for eng in ENGINES:
            ops = self.ops[eng]
            i = 0
            while i < len(ops):
                j = i
                while j + 1 < len(ops) and not ops[j + 1].deps:
                    j += 1
                last_by_sem = {}
                for k2 in range(i, j + 1):
                    last_by_sem[ops[k2].sem] = ops[k2].cum
                for k2 in range(i, j + 1):
                    be[ops[k2]] = last_by_sem[ops[k2].sem]
                i = j + 1
        return be

    def finalize(self, block, sems):
        batch_end = self._batch_ends()
        for eng in ENGINES:
            ops = self.ops[eng]
            if not ops:
                continue

            has_dma = any(o.inc == 16 for o in ops)

            def section(e, _ops=ops, _eng=eng, _has_dma=has_dma):
                seen = {}
                prev_dma_cum = 0
                for o in _ops:
                    if o.deps and _has_dma and prev_dma_cum > seen.get(_eng, 0):
                        e.wait_ge(sems[_eng], prev_dma_cum)
                        seen[_eng] = prev_dma_cum
                    need = {}
                    for d in o.deps:
                        if d.sem == "tensor" and _eng == "tensor":
                            continue
                        v = batch_end[d] if d.inc == 16 else d.cum
                        if v > seen.get(d.sem, 0) and v > need.get(d.sem, 0):
                            need[d.sem] = v
                    for sn in sorted(need):
                        e.wait_ge(sems[sn], need[sn])
                        seen[sn] = need[sn]
                    inst = o.emit(e)
                    if o.inc:
                        inst.then_inc(sems[o.sem], o.inc)
                    if o.inc == 16:
                        prev_dma_cum = o.cum

            getattr(block, eng)(section)


class Builder:
    def __init__(self):
        self.nc = bass.Bass("TRN2", target_bir_lowering=False, debug=False)
        self.p = Prog(self.nc)
        self._ctx = []
        self._name_i = 0

    def name(self, pfx):
        self._name_i += 1
        return f"{pfx}{self._name_i}"

    def get_eye(self):
        if not hasattr(self, "_eyeT"):
            self._eyeT = self.sbuf([P, P], BF16, name="eyeTg")
            self.p.dma("sync", self._eyeT[:], self.eye_hbm[:, :], w=[self._eyeT])
        return self._eyeT

    def get_ptr(self):
        if not hasattr(self, "_ptr"):
            self._ptr = self.psum([P, P], BF16, name="ptrg")
        return self._ptr

    def get_pmisc(self):
        if not hasattr(self, "_pmisc"):
            self._pmisc = self.psum([P, 512], name="pmiscg")
        return self._pmisc

    def get_onesr(self):
        if not hasattr(self, "_onesr"):
            self._onesr = self.sbuf([1, P], name="onesrg")
            self.p.op("vector", lambda e: e.memset(self._onesr[:], 1.0), w=[self._onesr])
        return self._onesr

    def cconst(self, value):
        if not hasattr(self, "_cpool"):
            self._cpool = self.sbuf([P, 16], F32, name="cpool")
            self._cvals = {}
        v = float(value)
        if v not in self._cvals:
            k = len(self._cvals)
            assert k < 16
            self._cvals[v] = k
            col = self._cpool[:, k:k + 1]
            self.p.op("vector", lambda e, col=col, v=v: e.memset(col, v), w=[self._cpool])
        k = self._cvals[v]
        return self._cpool[:, k:k + 1]

    def sbuf(self, shape, dtype=F32, name=None):
        cm = self.nc.sbuf_tensor(name or self.name("sb"), shape, dtype)
        t = cm.__enter__()
        self._ctx.append(cm)
        self._ctx_t = getattr(self, "_ctx_t", {})
        self._ctx_t[id(cm)] = t
        return t

    def free(self, t):
        for i, cm in enumerate(self._ctx):
            if getattr(self, "_ctx_t", {}).get(id(cm)) is t:
                cm.__exit__(None, None, None)
                del self._ctx[i]
                break
        lw = self.p.last_writer.get(id(t))
        rd = self.p.readers.get(id(t), [])
        out = ([lw] if lw is not None else []) + list(rd)
        self.p.fence_deps = list(set(self.p.fence_deps) | set(out))
        return out

    def psum(self, shape, dtype=F32, name=None):
        cm = self.nc.psum_tensor(name or self.name("ps"), shape, dtype)
        t = cm.__enter__()
        self._ctx.append(cm)
        self._ctx_t = getattr(self, "_ctx_t", {})
        self._ctx_t[id(cm)] = t
        return t

    def din(self, name, shape, dtype=F32):
        return self.nc.dram_tensor(name, shape, dtype, kind="ExternalInput").ap()

    def dout(self, name, shape, dtype=F32):
        return self.nc.dram_tensor(name, shape, dtype, kind="ExternalOutput").ap()

    def dscratch(self, name, shape, dtype=F32):
        return self.nc.dram_tensor(name, shape, dtype).ap()

    def build(self):
        sems = {}
        for e in list(ENGINES) + ["gpsimd_c"]:
            cm = self.nc.semaphore(f"sem_{e}")
            sems[e] = cm.__enter__()
            self._ctx.append(cm)
        blk_cm = self.nc.Block()
        block = blk_cm.__enter__()
        self.p.finalize(block, sems)
        blk_cm.__exit__(None, None, None)
        for cm in reversed(self._ctx):
            cm.__exit__(None, None, None)
        return self.nc



def emit_plain_ln(b, xt, out_t, sq, st, D, eps=1e-5, extra_r=()):
    p = b.p
    p.op("vector", lambda e: e.tensor_reduce(out=st[:, 0:1], in_=xt[:], axis=AX.X, op=ALU.add),
         r=[xt], w=[st])
    p.op("vector", lambda e: e.tensor_scalar_mul(out=st[:, 1:2], in0=st[:, 0:1], scalar1=1.0 / D),
         r=[st], w=[st])
    p.op("vector", lambda e: e.tensor_scalar(out=out_t[:], in0=xt[:], scalar1=st[:, 1:2],
                                             scalar2=None, op0=ALU.subtract),
         r=[xt, st], w=[out_t])
    p.op("scalar", lambda e: e.activation(out=sq[:, 0:D], in_=out_t[:], func=AF.Square,
                                          accum_out=st[:, 2:3]),
         r=[out_t], w=[sq, st])
    epsc = b.cconst(float(eps))
    p.op("scalar", lambda e: e.activation(out=st[:, 4:5], in_=st[:, 2:3], func=AF.Ln,
                                          bias=epsc, scale=1.0 / D),
         r=[st, b._cpool], w=[st])
    p.op("scalar", lambda e: e.activation(out=st[:, 5:6], in_=st[:, 4:5], func=AF.Exp, scale=-0.5),
         r=[st], w=[st])
    return p.op("vector", lambda e: e.tensor_scalar(out=out_t[:], in0=out_t[:], scalar1=st[:, 5:6],
                                                    scalar2=None, op0=ALU.mult),
                r=[out_t, st], w=[out_t])


def emit_bcast_row(b, dst, src_row, cols, ones_row, pmisc, r_extra=()):
    p = b.p
    for c0 in range(0, cols, 512):
        c1 = min(c0 + 512, cols)
        p.op("tensor", lambda e, c0=c0, c1=c1: e.matmul(
            pmisc[:, 0:c1 - c0], ones_row[0:1, 0:P], src_row[0:1, c0:c1], start=True, stop=True),
            r=[ones_row] + list(r_extra), w=[pmisc])
        p.op("scalar", lambda e, c0=c0, c1=c1, w_=list(r_extra): e.copy(out=dst[:, c0:c1], in_=pmisc[:, 0:c1 - c0]),
             r=[pmisc], w=[dst] + list(r_extra))



_prog_cache = {}


def build_l1ffn(R):
    key = ("l1ffn", R)
    if key in _prog_cache:
        return _prog_cache[key]
    D, H = 1024, 4096
    nT = R // P
    b = Builder()
    p = b.p
    x_in = b.din("x", [R, D])
    wfcT = b.din("wfcT", [D, H])
    wprT = b.din("wprT", [H, D])
    lnw = b.din("lnw", [1, H])
    lnb = b.din("lnb", [1, H])
    eye = b.din("eye", [P, P], BF16)
    out = b.dout("out", [R, D])
    hn_scr = b.dscratch("hn_scr", [R, H], BF16)

    eyeT = b.sbuf([P, P], BF16)
    onesr = b.sbuf([1, P], F32)
    p.dma("sync", eyeT[:], eye[:, :], w=[eyeT])
    p.op("vector", lambda e: e.memset(onesr[:], 1.0), w=[onesr])
    onesc = b.sbuf([P, 1], F32)
    p.op("vector", lambda e: e.memset(onesc[:], 1.0), w=[onesc])

    qw = b.sbuf([P, 32768], BF16, name="qw")
    wtmp = [b.sbuf([P, H], F32) for _ in range(2)]
    gtb = b.sbuf([P, H], F32)
    pmisc = b.psum([P, 512], F32)

    def quantize(whbm, nrow_tiles, W, dst_slices, tag):
        absr = b.sbuf([P, nrow_tiles + 2], F32, name=f"absr{tag}")
        scl = b.sbuf([P, 2], F32, name=f"scl{tag}")
        thrb = b.sbuf([P, 2], F32, name=f"thrb{tag}")
        for j in range(nrow_tiles):
            t = wtmp[j % 2]
            p.dma("sync", t[:, 0:W], whbm[j * P:(j + 1) * P, :], w=[t])
            p.op("vector", lambda e, j=j, t=t: e.tensor_reduce(
                out=absr[:, j:j + 1], in_=t[:, 0:W], axis=AX.X, op=ALU.add,
                apply_absolute_value=True), r=[t], w=[absr])
        p.op("vector", lambda e: e.tensor_reduce(out=absr[:, nrow_tiles:nrow_tiles + 1],
                                                 in_=absr[:, 0:nrow_tiles], axis=AX.X, op=ALU.add),
             r=[absr], w=[absr])
        p.op("tensor", lambda e: e.matmul(pmisc[0:1, 0:1], absr[:, nrow_tiles:nrow_tiles + 1],
                                          onesc[:, 0:1], start=True, stop=True),
             r=[absr, onesc], w=[pmisc])
        N = nrow_tiles * P * W
        halfeps = b.cconst(0.5e-6)
        p.op("scalar", lambda e: e.activation(out=scl[0:1, 0:1], in_=pmisc[0:1, 0:1],
                                              func=AF.Identity, scale=0.5 / N, bias=halfeps[0:1, 0:1]),
             r=[pmisc, b._cpool], w=[scl])
        p.op("scalar", lambda e: e.activation(out=scl[0:1, 1:2], in_=scl[0:1, 0:1],
                                              func=AF.Identity, scale=-1.0), r=[scl], w=[scl])
        emit_bcast_row(b, thrb, scl, 2, onesr, pmisc, r_extra=[scl])
        for j in range(nrow_tiles):
            t = wtmp[j % 2]
            p.dma("sync", t[:, 0:W], whbm[j * P:(j + 1) * P, :], w=[t], r=[thrb])
            p.op("vector", lambda e, t=t: e.tensor_scalar(
                out=gtb[:, 0:W], in0=t[:, 0:W], scalar1=thrb[:, 0:1], scalar2=None, op0=ALU.is_gt),
                r=[t, thrb], w=[gtb])
            p.op("vector", lambda e, t=t: e.tensor_scalar(
                out=t[:, 0:W], in0=t[:, 0:W], scalar1=thrb[:, 1:2], scalar2=None, op0=ALU.is_lt),
                r=[t, thrb], w=[t])
            p.op("vector", lambda e, j=j, t=t, d=dst_slices[j]: e.tensor_sub(
                out=d, in0=gtb[:, 0:W], in1=t[:, 0:W]), r=[gtb, t], w=[qw])

    q1_slices = [qw[:, j * H:(j + 1) * H] for j in range(8)]
    quantize(wfcT, 8, H, q1_slices, "fc")

    wbc = b.sbuf([P, H], F32)
    bbc = b.sbuf([P, H], F32)
    p.dma("sync", wbc[0:1, :], lnw[:, :], w=[wbc])
    p.dma("sync", bbc[0:1, :], lnb[:, :], w=[bbc])
    emit_bcast_row(b, wbc, wbc[0:1, 0:H], H, onesr, pmisc, r_extra=[wbc])
    emit_bcast_row(b, bbc, bbc[0:1, 0:H], H, onesr, pmisc, r_extra=[bbc])

    xt = b.sbuf([P, D], F32)
    st = b.sbuf([P, 8], F32)
    sq = gtb
    xn32 = b.sbuf([P, D], F32)
    xnb = b.sbuf([P, D], BF16)
    ptr = b.psum([P, P], BF16)
    xf = b.sbuf([P, 8 * P], BF16)
    ph = b.psum([P, 2048], F32)
    hg = b.sbuf([P, H], F32)
    hnb = b.sbuf([P, H], BF16)
    hfm = b.sbuf([P, 32 * P], BF16)

    for i in range(nT):
        p.dma("sync", xt[:], x_in[i * P:(i + 1) * P, :], w=[xt])
        emit_plain_ln(b, xt, xn32, sq, st, D)
        p.op("vector", lambda e: e.tensor_copy(out=xnb[:], in_=xn32[:]), r=[xn32], w=[xnb])
        for c in range(8):
            p.op("tensor", lambda e, c=c: e.transpose(ptr[:], xnb[:, c * P:(c + 1) * P], eyeT[:]),
                 r=[xnb, eyeT], w=[ptr])
            p.op("scalar", lambda e, c=c: e.copy(out=xf[:, c * P:(c + 1) * P], in_=ptr[:]),
                 r=[ptr], w=[xf])
        for half in range(2):
            for nb_ in range(4):
                for k in range(8):
                    p.op("tensor", lambda e, k=k, nb_=nb_, half=half: e.matmul(
                        ph[:, nb_ * 512:(nb_ + 1) * 512],
                        xf[:, k * P:(k + 1) * P],
                        qw[:, k * H + half * 2048 + nb_ * 512:k * H + half * 2048 + (nb_ + 1) * 512],
                        start=(k == 0), stop=(k == 7)),
                        r=[xf, qw], w=[ph])
            p.op("scalar", lambda e, half=half: e.activation(
                out=hg[:, half * 2048:(half + 1) * 2048], in_=ph[:, :], func=AF.Gelu),
                r=[ph], w=[hg])
        p.op("vector", lambda e: e.tensor_reduce(out=st[:, 0:1], in_=hg[:], axis=AX.X, op=ALU.add),
             r=[hg], w=[st])
        p.op("vector", lambda e: e.tensor_scalar(out=hg[:], in0=hg[:], scalar1=st[:, 0:1],
                                                 scalar2=None, op0=ALU.subtract), r=[hg, st], w=[hg],
             deps=[p.op("vector", lambda e: e.tensor_scalar_mul(out=st[:, 0:1], in0=st[:, 0:1],
                                                                scalar1=1.0 / H), r=[st], w=[st])])
        p.op("scalar", lambda e: e.activation(out=sq[:], in_=hg[:], func=AF.Square,
                                              accum_out=st[:, 2:3]), r=[hg], w=[sq, st])
        lnepsc = b.cconst(1e-5)
        p.op("scalar", lambda e, lnepsc=lnepsc: e.activation(out=st[:, 4:5], in_=st[:, 2:3], func=AF.Ln,
                                              bias=lnepsc, scale=1.0 / H), r=[st, b._cpool], w=[st])
        p.op("scalar", lambda e: e.activation(out=st[:, 5:6], in_=st[:, 4:5], func=AF.Exp,
                                              scale=-0.5), r=[st], w=[st])
        p.op("vector", lambda e: e.tensor_scalar(out=hg[:], in0=hg[:], scalar1=st[:, 5:6],
                                                 scalar2=None, op0=ALU.mult), r=[hg, st], w=[hg])
        p.op("vector", lambda e: e.tensor_mul(out=hg[:], in0=hg[:], in1=wbc[:]), r=[hg, wbc], w=[hg])
        p.op("vector", lambda e: e.tensor_add(out=hnb[:], in0=hg[:], in1=bbc[:]), r=[hg, bbc], w=[hnb])
        for c in range(32):
            p.op("tensor", lambda e, c=c: e.transpose(ptr[:], hnb[:, c * P:(c + 1) * P], eyeT[:]),
                 r=[hnb, eyeT], w=[ptr])
            p.op("scalar", lambda e, c=c: e.copy(out=hfm[:, c * P:(c + 1) * P], in_=ptr[:]),
                 r=[ptr], w=[hfm])
        p.dma("gpsimd", hn_scr[i * P:(i + 1) * P, :], hfm[:], r=[hfm], w=[hn_scr])

    b.free(hfm)
    b.free(hnb)
    q2_slices = [qw[:, j * D:(j + 1) * D] for j in range(32)]
    quantize(wprT, 32, D, q2_slices, "pr")

    hstrip = [b.sbuf([P, 32 * P], BF16) for _ in range(2)]
    pofb = [b.psum([P, 512], F32) for _ in range(2)]
    outt = b.sbuf([P, D], F32)
    xt2 = b.sbuf([P, D], F32)
    for i in range(nT):
        hs_ = hstrip[i % 2]
        ot_ = outt
        xx_ = xt2
        p.dma("sync", hs_[:], hn_scr[i * P:(i + 1) * P, :], r=[hn_scr], w=[hs_])
        p.dma("gpsimd", xx_[:], x_in[i * P:(i + 1) * P, :], w=[xx_])
        for half in range(2):
            pof = pofb[half]
            for k in range(32):
                p.op("tensor", lambda e, k=k, half=half, pof=pof, hs_=hs_: e.matmul(
                    pof[:, :], hs_[:, k * P:(k + 1) * P],
                    qw[:, k * D + half * 512:k * D + (half + 1) * 512],
                    start=(k == 0), stop=(k == 31)),
                    r=[hs_, qw], w=[pof])
            p.op("vector", lambda e, half=half, pof=pof, ot_=ot_, xx_=xx_: e.tensor_add(
                out=ot_[:, half * 512:(half + 1) * 512], in0=pof[:, :],
                in1=xx_[:, half * 512:(half + 1) * 512]), r=[pof, xx_], w=[ot_])
        p.dma("sync", out[i * P:(i + 1) * P, :], ot_[:], r=[ot_], w=[out])

    p.op("gpsimd", lambda e: e.engine_nop(), no_inc=True, r=[out])
    nc = b.build()
    _prog_cache[key] = nc
    return nc


def run_l1ffn(x, W_fc, ln_w, ln_b, W_proj):
    B, T, D = x.shape
    rows = x.reshape(B * T, D)
    R = (B * T) // NCORES
    nc = build_l1ffn(R)
    wfcT = np.ascontiguousarray(W_fc.T)
    wprT = np.ascontiguousarray(W_proj.T)
    eye = np.eye(P, dtype=np.float32).astype(NP_BF16)
    in_maps = []
    for c in range(NCORES):
        in_maps.append({
            "x": np.ascontiguousarray(rows[c * R:(c + 1) * R]),
            "wfcT": wfcT, "wprT": wprT,
            "lnw": ln_w.reshape(1, -1), "lnb": ln_b.reshape(1, -1),
            "eye": eye,
        })
    res = run_bass_kernel_spmd(nc, in_maps, core_ids=list(range(NCORES)))
    outr = np.concatenate([res.results[c]["out"] for c in range(NCORES)], 0)
    return outr.reshape(B, T, D)


def kernel(**inputs):
    g = inputs
    x = np.ascontiguousarray(g["L0_attn_a_hidden_states"], dtype=np.float32)
    x = run_l0_stack(x, g["L0_attn_a_norm_w"], g["L0_attn_a_norm_b"], g["L0_attn_a_in_proj_w"],
                     g["L0_attn_a_conv_w"], g["L0_attn_a_conv_b"], g["L0_attn_a_x_proj_w"],
                     g["L0_attn_a_dt_proj_w"], g["L0_attn_a_dt_proj_b"], g["L0_attn_a_A_log"],
                     g["L0_attn_a_D_param"], g["L0_attn_a_out_proj_w"], g["L0_attn_a_fnorm_w"],
                     g["L0_attn_a_fnorm_b"])
    ffn_args = (g["L0_ffn_gamma1"], g["L0_ffn_Wq"], g["L0_ffn_bq"], g["L0_ffn_Wdown"],
                g["L0_ffn_bdown"], g["L0_ffn_Wup"], g["L0_ffn_bup"], g["L0_ffn_Wo"],
                g["L0_ffn_bo"], g["L0_ffn_gamma2"], g["L0_ffn_Wg"], g["L0_ffn_bg"], g["L0_ffn_We"])
    x = run_l0ffn(x, *ffn_args)
    x = run_attnb_full(x, g["L0_attn_b_Wq"], g["L0_attn_b_Wk"], g["L0_attn_b_Wv"])
    x = run_l0ffn(x, *ffn_args)
    x = run_l1_stack(x, g["L1_attn_a_lin1_w"], g["L1_attn_a_lin1_b"], g["L1_attn_a_norm_w"],
                     g["L1_attn_a_in_w"], g["L1_attn_a_conv_w"], g["L1_attn_a_conv_b"],
                     g["L1_attn_a_xproj_w"], g["L1_attn_a_dt_w"], g["L1_attn_a_dt_b"],
                     g["L1_attn_a_A_log"], g["L1_attn_a_Dp"], g["L1_attn_a_out_w"],
                     g["L1_attn_a_lin2_w"], g["L1_attn_a_lin2_b"])
    l1f_args = (g["L1_ffn_W_fc"], g["L1_ffn_ln_w"], g["L1_ffn_ln_b"], g["L1_ffn_W_proj"])
    x = run_l1ffn(x, *l1f_args)
    x = run_l1ab(x, g["L1_attn_b_W_qkv"], g["L1_attn_b_W_out"],
                 float(np.asarray(g["L1_attn_b_a"])), float(np.asarray(g["L1_attn_b_b"])),
                 float(np.asarray(g["L1_attn_b_dt"])))
    x = run_l1ffn(x, *l1f_args)
    return np.ascontiguousarray(x, dtype=np.float32)


L0_RQ, L0_RH = 2048, 2176
L0_DI = 2048


def emit_affine_ln_tile(b, rt, ln32, sq, st, D, nwbc, nbbc, mask_buf=None):
    p = b.p
    emit_plain_ln(b, rt, ln32, sq, st, D)
    p.op("vector", lambda e: e.tensor_mul(out=ln32[:], in0=ln32[:], in1=nwbc[:]),
         r=[ln32, nwbc], w=[ln32])
    p.op("vector", lambda e: e.tensor_add(out=ln32[:], in0=ln32[:], in1=nbbc[:]),
         r=[ln32, nbbc], w=[ln32])
    if mask_buf is not None:
        p.op("vector", lambda e: e.tensor_scalar(out=ln32[:], in0=ln32[:], scalar1=mask_buf[:, 0:1],
                                                 scalar2=None, op0=ALU.mult),
             r=[ln32, mask_buf], w=[ln32])


def emit_l0_pre_body(b, provider, nT, inw_hbm, convw_hbm, convb_hbm, xpwT_hbm,
                     dtpwT_hbm, dtpb_hbm, nw_hbm, nb_hbm, hmask_hbm,
                     xc_out, zs_out, dt_out, bc_out, D=1024, DI=2048, DTR=64, first_tile_mask=True):
    p = b.p
    TC = nT * P
    OWN = TC - P
    onesr = b.get_onesr()
    pmisc = b.get_pmisc()
    nwbc = b.sbuf([P, D], name="nwbc")
    nbbc = b.sbuf([P, D], name="nbbc")
    p.dma("sync", nwbc[0:1, 0:D], nw_hbm[0:D, 0:1].rearrange("a b -> b a"), w=[nwbc])
    p.dma("sync", nbbc[0:1, 0:D], nb_hbm[0:D, 0:1].rearrange("a b -> b a"), w=[nbbc])
    emit_bcast_row(b, nwbc, nwbc[0:1, 0:D], D, onesr, pmisc, r_extra=[nwbc])
    emit_bcast_row(b, nbbc, nbbc[0:1, 0:D], D, onesr, pmisc, r_extra=[nbbc])
    hmask = b.sbuf([P, 1], name="hmask_sb")
    p.dma("sync", hmask[:], hmask_hbm[:, :], w=[hmask])

    eyeT = b.get_eye()

    hs_fm = b.sbuf([P, 8 * TC], BF16, name="hs_fm")
    ln32 = b.sbuf([P, D], name="ln32")
    sq = b.sbuf([P, D], name="sq0")
    st = b.sbuf([P, 8], name="st0")
    lnb16 = b.sbuf([P, D], BF16, name="lnb16")
    ptr = b.get_ptr()
    for i in range(nT):
        rt = provider(i)
        emit_affine_ln_tile(b, rt, ln32, sq, st, D, nwbc, nbbc,
                            mask_buf=(hmask if (i == 0 and first_tile_mask) else None))
        p.op("vector", lambda e: e.tensor_copy(out=lnb16[:], in_=ln32[:]), r=[ln32], w=[lnb16])
        for c in range(8):
            p.op("tensor", lambda e, c=c: e.transpose(ptr[:], lnb16[:, c * P:(c + 1) * P], eyeT[:]),
                 r=[lnb16, eyeT], w=[ptr])
            p.op("scalar", lambda e, c=c, i=i: e.copy(out=hs_fm[:, c * TC + i * P:c * TC + (i + 1) * P],
                                                      in_=ptr[:]), r=[ptr], w=[hs_fm])

    NSL = [(j * 512, min((j + 1) * 512, TC)) for j in range((TC + 511) // 512)]
    inw_t = [b.sbuf([P, 8 * P], BF16, name=f"inw{j}") for j in range(2)]
    pxdbl = b.psum([96, 2048], name="pxdbl")
    pxzb = [b.psum([P, 512], name=f"pxz{j}") for j in range(2)]
    xzx = b.sbuf([P, TC], name="xzx")
    acc = b.sbuf([P, OWN], name="acc")
    xc_m = b.sbuf([P, OWN], BF16, name="xc_m")
    zs_m = b.sbuf([P, OWN], BF16, name="zs_m")
    cw = b.sbuf([P, 4], name="cw")
    cb = b.sbuf([P, 1], name="cb")
    xpwT_t = b.sbuf([P, 96], BF16, name="xpwT_t")
    for m in range(32):
        wbuf = inw_t[m % 2]
        for k in range(8):
            p.dma("sync", wbuf[:, k * P:(k + 1) * P],
                  inw_hbm[k * P:(k + 1) * P, m * P:(m + 1) * P], w=[wbuf])
        if m < 16:
            p.dma("gpsimd", cw[:, :], convw_hbm[m * P:(m + 1) * P, :], w=[cw])
            p.dma("gpsimd", cb[:, :], convb_hbm[m * P:(m + 1) * P, :], w=[cb])
            p.dma("gpsimd", xpwT_t[:, :], xpwT_hbm[m * P:(m + 1) * P, :], w=[xpwT_t])
        for ni, (n0, n1) in enumerate(NSL):
            pxz = pxzb[ni % 2]
            for k in range(8):
                p.op("tensor", lambda e, k=k, n0=n0, n1=n1, wbuf=wbuf, pxz=pxz: e.matmul(
                    pxz[:, 0:n1 - n0], wbuf[:, k * P:(k + 1) * P],
                    hs_fm[:, k * TC + n0:k * TC + n1],
                    start=(k == 0), stop=(k == 7)), r=[wbuf, hs_fm], w=[pxz])
            if m < 16:
                p.op("scalar", lambda e, n0=n0, n1=n1, pxz=pxz: e.copy(out=xzx[:, n0:n1], in_=pxz[:, 0:n1 - n0]),
                     r=[pxz], w=[xzx])
            else:
                o0, o1 = max(n0, P), n1
                if o1 > o0:
                    p.op("scalar", lambda e, n0=n0, o0=o0, o1=o1, pxz=pxz: e.activation(
                        out=zs_m[:, o0 - P:o1 - P], in_=pxz[:, o0 - n0:o1 - n0], func=AF.Silu),
                        r=[pxz], w=[zs_m])
        if m < 16:
            p.op("vector", lambda e: e.tensor_scalar(out=acc[:], in0=xzx[:, P - 3:P - 3 + OWN],
                                                     scalar1=cw[:, 0:1], scalar2=None, op0=ALU.mult),
                 r=[xzx, cw], w=[acc])
            for k in range(1, 4):
                p.op("vector", lambda e, k=k: e.scalar_tensor_tensor(
                    out=acc[:], in0=xzx[:, P - 3 + k:P - 3 + k + OWN], scalar=cw[:, k:k + 1],
                    in1=acc[:], op0=ALU.mult, op1=ALU.add), r=[xzx, cw, acc], w=[acc])
            p.op("scalar", lambda e: e.activation(out=xc_m[:], in_=acc[:], func=AF.Silu,
                                                  bias=cb[:, 0:1]), r=[acc, cb], w=[xc_m])
            p.dma("sync", xc_out[m * P:(m + 1) * P, :], xc_m[:], r=[xc_m], w=[xc_out])
            for j in range(4):
                p.op("tensor", lambda e, j=j, m=m: e.matmul(
                    pxdbl[:, j * 512:(j + 1) * 512], xpwT_t[:, :], xc_m[:, j * 512:(j + 1) * 512],
                    start=(m == 0), stop=(m == 15)), r=[xpwT_t, xc_m], w=[pxdbl])
        else:
            p.dma("sync", zs_out[(m - 16) * P:(m - 15) * P, :], zs_m[:], r=[zs_m], w=[zs_out])
    b.free(pxzb[1])
    b.free(pxzb[0])
    xdbl = b.sbuf([96, 2048], name="xdbl")
    for j in range(4):
        p.op("scalar", lambda e, j=j: e.copy(out=xdbl[:, j * 512:(j + 1) * 512],
                                             in_=pxdbl[:, j * 512:(j + 1) * 512]),
             r=[pxdbl], w=[xdbl])
    p.dma("sync", bc_out[:, :], xdbl[DTR:96, :], r=[xdbl], w=[bc_out])
    dtin_b = b.sbuf([64, 2048], BF16, name="dtin_b")
    p.op("vector", lambda e: e.tensor_copy(out=dtin_b[:, :], in_=xdbl[0:DTR, :]), r=[xdbl], w=[dtin_b])
    dtpwT_t = b.sbuf([64, DI], BF16, name="dtpwT_t")
    p.dma("sync", dtpwT_t[:, :], dtpwT_hbm[:, :], w=[dtpwT_t])
    b.free(pxdbl)
    dtb = b.sbuf([P, 1], name="dtb")
    pdt = b.psum([P, 512], name="pdt")
    v32 = b.sbuf([P, 2048], name="v32")
    a1 = b.sbuf([P, 2048], name="a1_sp")
    for m2 in range(DI // P):
        p.dma("gpsimd", dtb[:, :], dtpb_hbm[m2 * P:(m2 + 1) * P, :], w=[dtb])
        for j in range(4):
            p.op("tensor", lambda e, m2=m2, j=j: e.matmul(
                pdt[:, :], dtpwT_t[:, m2 * P:(m2 + 1) * P], dtin_b[:, j * 512:(j + 1) * 512],
                start=True, stop=True), r=[dtpwT_t, dtin_b], w=[pdt])
            p.op("scalar", lambda e, j=j: e.activation(out=v32[:, j * 512:(j + 1) * 512],
                                                       in_=pdt[:, :], func=AF.Identity,
                                                       bias=dtb[:, 0:1]), r=[pdt, dtb], w=[v32])
        p.op("scalar", lambda e: e.activation(out=a1[:], in_=v32[:], func=AF.Abs), r=[v32], w=[a1])
        p.op("scalar", lambda e: e.activation(out=a1[:], in_=a1[:], func=AF.Exp, scale=-1.0),
             r=[a1], w=[a1])
        onec = b.cconst(1.0)
        p.op("scalar", lambda e, onec=onec: e.activation(out=a1[:], in_=a1[:], func=AF.Ln, bias=onec),
             r=[a1, b._cpool], w=[a1])
        p.op("vector", lambda e: e.tensor_relu(out=v32[:], in_=v32[:]), r=[v32], w=[v32])
        p.op("vector", lambda e: e.tensor_add(out=v32[:], in0=v32[:], in1=a1[:]), r=[v32, a1], w=[v32])
        p.dma("sync", dt_out[m2 * P:(m2 + 1) * P, :], v32[:], r=[v32], w=[dt_out])


def build_l0first():
    key = ("l0first",)
    if key in _prog_cache:
        return _prog_cache[key]
    b = Builder()
    p = b.p
    D = 1024
    x_in = b.din("x", [L0_RH, D])
    b.eye_hbm = b.din("eye", [P, P], BF16)
    inw = b.din("inwT", [D, 2 * L0_DI], BF16)
    convw = b.din("convw", [L0_DI, 4])
    convb = b.din("convb", [L0_DI, 1])
    xpwT = b.din("xpwT", [L0_DI, 96], BF16)
    dtpwT = b.din("dtpwT", [64, L0_DI], BF16)
    dtpb = b.din("dtpb", [L0_DI, 1])
    nw = b.din("nw", [D, 1])
    nb = b.din("nb", [D, 1])
    hmaskh = b.din("hmask", [P, 1])
    resid0 = b.dout("resid0", [L0_RQ, D])
    xc_out = b.dout("xc", [L0_DI, L0_RQ], BF16)
    zs_out = b.dout("zs", [L0_DI, L0_RQ], BF16)
    dt_out = b.dout("dt", [L0_DI, L0_RQ])
    bc_out = b.dout("bc", [32, L0_RQ])

    xt = b.sbuf([P, D], name="xt")
    rt = b.sbuf([P, D], name="rt")
    sq1 = b.sbuf([P, D], name="sq1")
    st1 = b.sbuf([P, 8], name="st1")

    def provider(i):
        p.dma("sync", xt[:], x_in[i * P:(i + 1) * P, :], w=[xt])
        emit_plain_ln(b, xt, rt, sq1, st1, D)
        if i > 0:
            p.dma("gpsimd", resid0[(i - 1) * P:i * P, :], rt[:], r=[rt], w=[resid0])
        return rt

    emit_l0_pre_body(b, provider, 17, inw, convw, convb, xpwT, dtpwT, dtpb,
                     nw, nb, hmaskh, xc_out, zs_out, dt_out, bc_out)
    p.op("gpsimd", lambda e: e.engine_nop(), no_inc=True, r=[resid0, xc_out, zs_out, dt_out, bc_out])
    nc = b.build()
    _prog_cache[key] = nc
    return nc


def build_l0scan(T=8192, NCH=512, DS=16):
    key = ("l0scan", T, NCH)
    if key in _prog_cache:
        return _prog_cache[key]
    b = Builder()
    p = b.p
    nD = NCH // P
    CH = 2048
    nC = T // CH
    dt_in = b.din("dt", [NCH, T])
    xc_in = b.din("xc", [NCH, T], BF16)
    zs_in = b.din("zs", [NCH, T], BF16)
    Brow = b.din("Brow", [DS, T])
    Crow = b.din("Crow", [DS, T])
    alog = b.din("alog", [NCH, DS])
    dp = b.din("dp", [NCH, 1])
    yg_out = b.dout("yg", [NCH, T], BF16)

    onesr = b.sbuf([1, P], name="onesr")
    p.op("vector", lambda e: e.memset(onesr[:], 1.0), w=[onesr])
    pmisc = b.psum([P, 512], name="pmisc")
    b1row = b.sbuf([1, 2048], name="b1row")
    c1row = b.sbuf([1, 2048], name="c1row")
    A_sb = [b.sbuf([P, DS], name=f"A{d}") for d in range(nD)]
    dp_sb = [b.sbuf([P, 1], name=f"dp{d}") for d in range(nD)]
    for d in range(nD):
        p.dma("sync", A_sb[d][:, :], alog[d * P:(d + 1) * P, :], w=[A_sb[d]])
        p.op("scalar", lambda e, d=d: e.activation(out=A_sb[d][:, :], in_=A_sb[d][:, :], func=AF.Exp),
             r=[A_sb[d]], w=[A_sb[d]])
        p.op("vector", lambda e, d=d: e.tensor_scalar_mul(out=A_sb[d][:, :], in0=A_sb[d][:, :],
                                                          scalar1=-1.0), r=[A_sb[d]], w=[A_sb[d]])
        p.dma("sync", dp_sb[d][:, :], dp[d * P:(d + 1) * P, :], w=[dp_sb[d]])

    Bbc = b.sbuf([P, CH], BF16, name="Bbc")
    Cbc = b.sbuf([P, CH], BF16, name="Cbc")
    dt_t = [b.sbuf([P, CH], name=f"dtt{d}") for d in range(nD)]
    xc_t = [b.sbuf([P, CH], BF16, name=f"xct{d}") for d in range(nD)]
    dtx = [b.sbuf([P, CH], BF16, name=f"dtx{d}") for d in range(nD)]
    yacc = [[b.sbuf([P, CH], BF16, name=f"yacc{d}_{j}") for j in range(2)] for d in range(nD)]
    dA = [b.sbuf([P, CH], BF16, name=f"dA{j}") for j in range(2)]
    dBu = [b.sbuf([P, CH], BF16, name=f"dBu{j}") for j in range(2)]
    hS = [b.sbuf([P, CH], BF16, name=f"hS{j}") for j in range(2)]
    prod = [b.sbuf([P, CH], BF16, name=f"prod{j}") for j in range(2)]
    Bbc2 = b.sbuf([P, CH], BF16, name="Bbc2")
    Cbc2 = b.sbuf([P, CH], BF16, name="Cbc2")
    carry = b.sbuf([P, nD * DS], BF16, name="carry")
    zs_t = b.sbuf([P, CH], BF16, name="zs_t")
    ygt = b.sbuf([P, CH], BF16, name="ygt")

    for c in range(nC):
        t0, t1 = c * CH, (c + 1) * CH
        for d in range(nD):
            p.dma("sync", dt_t[d][:, :], dt_in[d * P:(d + 1) * P, t0:t1], w=[dt_t[d]])
            p.dma("gpsimd", xc_t[d][:, :], xc_in[d * P:(d + 1) * P, t0:t1], w=[xc_t[d]])
            p.op("gpsimd", lambda e, d=d: e.tensor_mul(out=dtx[d][:, :], in0=dt_t[d][:, :],
                                                       in1=xc_t[d][:, :]), r=[dt_t[d], xc_t[d]], w=[dtx[d]])
            p.op("vector", lambda e, d=d: e.memset(yacc[d][0][:, :], 0.0), w=[yacc[d][0]])
            p.op("vector", lambda e, d=d: e.memset(yacc[d][1][:, :], 0.0), w=[yacc[d][1]])
        for n in range(DS):
            pb = n % 2
            Bb = Bbc if pb == 0 else Bbc2
            Cb = Cbc if pb == 0 else Cbc2
            p.dma("sync", b1row[0:1, :], Brow[n:n + 1, t0:t1], w=[b1row])
            p.dma("sync", c1row[0:1, :], Crow[n:n + 1, t0:t1], w=[c1row])
            for rsrc, dst in ((b1row, Bb), (c1row, Cb)):
                for j in range(CH // 512):
                    p.op("tensor", lambda e, j=j, rsrc=rsrc: e.matmul(
                        pmisc[:, :], onesr[0:1, 0:P], rsrc[0:1, j * 512:(j + 1) * 512],
                        start=True, stop=True), r=[onesr, rsrc], w=[pmisc])
                    p.op("scalar", lambda e, j=j, dst=dst: e.copy(out=dst[:, j * 512:(j + 1) * 512],
                                                                  in_=pmisc[:, :]), r=[pmisc], w=[dst])
            for d in range(nD):
                pp = (n * nD + d) % 2
                p.op("scalar", lambda e, d=d, n=n, pp=pp: e.activation(
                    out=dA[pp][:], in_=dt_t[d][:, :], func=AF.Exp, scale=A_sb[d][:, n:n + 1]),
                    r=[dt_t[d], A_sb[d]], w=[dA[pp]])
                p.op("gpsimd", lambda e, d=d, pp=pp, Bb=Bb: e.tensor_mul(
                    out=dBu[pp][:], in0=dtx[d][:, :], in1=Bb[:]), r=[dtx[d], Bb], w=[dBu[pp]])
                if c == 0:
                    p.op("vector", lambda e, pp=pp: e.tensor_tensor_scan(
                        out=hS[pp][:], data0=dA[pp][:], data1=dBu[pp][:], initial=0.0,
                        op0=ALU.mult, op1=ALU.add), r=[dA[pp], dBu[pp]], w=[hS[pp]])
                else:
                    p.op("vector", lambda e, d=d, n=n, pp=pp: e.tensor_tensor_scan(
                        out=hS[pp][:], data0=dA[pp][:], data1=dBu[pp][:],
                        initial=carry[:, d * DS + n:d * DS + n + 1],
                        op0=ALU.mult, op1=ALU.add), r=[dA[pp], dBu[pp], carry], w=[hS[pp]])
                p.op("scalar", lambda e, d=d, n=n, pp=pp: e.copy(
                    out=carry[:, d * DS + n:d * DS + n + 1], in_=hS[pp][:, CH - 1:CH]),
                    r=[hS[pp]], w=[carry])
                p.op("vector", lambda e, pp=pp, Cb=Cb: e.tensor_mul(
                    out=prod[pp][:], in0=hS[pp][:], in1=Cb[:]), r=[hS[pp], Cb], w=[prod[pp]])
                yp = n % 2
                p.op("vector", lambda e, d=d, pp=pp, yp=yp: e.tensor_add(
                    out=yacc[d][yp][:, :], in0=prod[pp][:], in1=yacc[d][1 - yp][:, :]),
                    r=[prod[pp], yacc[d][1 - yp]], w=[yacc[d][yp]])
        for d in range(nD):
            p.dma("sync", zs_t[:, :], zs_in[d * P:(d + 1) * P, t0:t1], w=[zs_t])
            yfin = yacc[d][(DS - 1) % 2]
            p.op("vector", lambda e, d=d, yfin=yfin: e.scalar_tensor_tensor(
                out=yfin[:, :], in0=xc_t[d][:, :], scalar=dp_sb[d][:, 0:1], in1=yfin[:, :],
                op0=ALU.mult, op1=ALU.add), r=[xc_t[d], dp_sb[d], yfin], w=[yfin])
            p.op("vector", lambda e, d=d, yfin=yfin: e.tensor_mul(out=ygt[:], in0=yfin[:, :], in1=zs_t[:]),
                 r=[yfin, zs_t], w=[ygt])
            p.dma("sync", yg_out[d * P:(d + 1) * P, t0:t1], ygt[:], r=[ygt], w=[yg_out])
    p.op("gpsimd", lambda e: e.engine_nop(), no_inc=True, r=[yg_out])
    nc = b.build()
    _prog_cache[key] = nc
    return nc


def emit_l0_post(b, resid_in, yg_fm, owT_hbm, nT, D=1024, DI=2048):
    p = b.p
    TC = nT * P
    eyeT2 = b.get_eye()
    out_fm = b.sbuf([P, 8 * TC], BF16, name="out_fm")
    ygk = [b.sbuf([P, TC], BF16, name=f"ygk{k}") for k in range(DI // P)]
    owk = [b.sbuf([P, D], BF16, name=f"owk{k}") for k in range(DI // P)]
    for k in range(DI // P):
        p.dma("sync", ygk[k][:, :], yg_fm[k * P:(k + 1) * P, :], w=[ygk[k]])
        p.dma("gpsimd", owk[k][:, :], owT_hbm[k * P:(k + 1) * P, :], w=[owk[k]])
    pofb = [b.psum([P, 512], name=f"pof{j}") for j in range(2)]
    NSL = [(j * 512, min((j + 1) * 512, TC)) for j in range((TC + 511) // 512)]
    for m in range(8):
        for ni, (n0, n1) in enumerate(NSL):
            pof = pofb[ni % 2]
            for k in range(DI // P):
                p.op("tensor", lambda e, m=m, k=k, n0=n0, n1=n1, pof=pof: e.matmul(
                    pof[:, 0:n1 - n0], owk[k][:, m * P:(m + 1) * P], ygk[k][:, n0:n1],
                    start=(k == 0), stop=(k == DI // P - 1)), r=[owk[k], ygk[k]], w=[pof])
            p.op("scalar", lambda e, m=m, n0=n0, n1=n1, pof=pof: e.copy(
                out=out_fm[:, m * TC + n0:m * TC + n1], in_=pof[:, 0:n1 - n0]), r=[pof], w=[out_fm])
    deps = []
    deps += b.free(pofb[1])
    deps += b.free(pofb[0])
    for k in reversed(range(DI // P)):
        deps += b.free(owk[k])
    for k in reversed(range(DI // P)):
        deps += b.free(ygk[k])

    rti = b.sbuf([P, D], name="rti")
    rto = b.sbuf([P, D], name="rto")
    ptr2 = b.get_ptr()
    first = [True]

    def provider(i):
        dd = deps if first[0] else []
        first[0] = False
        p.dma("sync", rti[:], resid_in[i * P:(i + 1) * P, :], w=[rti], deps=dd)
        for m in range(8):
            p.op("tensor", lambda e, m=m, i=i: e.transpose(
                ptr2[:], out_fm[:, m * TC + i * P:m * TC + (i + 1) * P], eyeT2[:]),
                r=[out_fm, eyeT2], w=[ptr2])
            p.op("vector", lambda e, m=m: e.tensor_add(
                out=rto[:, m * P:(m + 1) * P], in0=rti[:, m * P:(m + 1) * P], in1=ptr2[:]),
                r=[rti, ptr2], w=[rto])
        return rto

    return provider


def build_l0mid():
    key = ("l0mid",)
    if key in _prog_cache:
        return _prog_cache[key]
    b = Builder()
    p = b.p
    D = 1024
    resid_in = b.din("resid", [L0_RH, D])
    b.eye_hbm = b.din("eye", [P, P], BF16)
    yg_fm = b.din("ygfm", [L0_DI, L0_RH], BF16)
    owT = b.din("owT", [L0_DI, D], BF16)
    inw = b.din("inwT", [D, 2 * L0_DI], BF16)
    convw = b.din("convw", [L0_DI, 4])
    convb = b.din("convb", [L0_DI, 1])
    xpwT = b.din("xpwT", [L0_DI, 96], BF16)
    dtpwT = b.din("dtpwT", [64, L0_DI], BF16)
    dtpb = b.din("dtpb", [L0_DI, 1])
    nw = b.din("nw", [D, 1])
    nb = b.din("nb", [D, 1])
    hmaskh = b.din("hmask", [P, 1])
    resid_out = b.dout("resid_o", [L0_RQ, D])
    xc_out = b.dout("xc", [L0_DI, L0_RQ], BF16)
    zs_out = b.dout("zs", [L0_DI, L0_RQ], BF16)
    dt_out = b.dout("dt", [L0_DI, L0_RQ])
    bc_out = b.dout("bc", [32, L0_RQ])

    provider0 = emit_l0_post(b, resid_in, yg_fm, owT, 17)

    def provider(i):
        rto = provider0(i)
        if i > 0:
            p.dma("gpsimd", resid_out[(i - 1) * P:i * P, :], rto[:], r=[rto], w=[resid_out])
        return rto

    emit_l0_pre_body(b, provider, 17, inw, convw, convb, xpwT, dtpwT, dtpb,
                     nw, nb, hmaskh, xc_out, zs_out, dt_out, bc_out)
    p.op("gpsimd", lambda e: e.engine_nop(), no_inc=True, r=[resid_out, xc_out, zs_out, dt_out, bc_out])
    nc = b.build()
    _prog_cache[key] = nc
    return nc


def build_l0last():
    key = ("l0last",)
    if key in _prog_cache:
        return _prog_cache[key]
    b = Builder()
    p = b.p
    D = 1024
    resid_in = b.din("resid", [L0_RQ, D])
    b.eye_hbm = b.din("eye", [P, P], BF16)
    yg_fm = b.din("ygfm", [L0_DI, L0_RQ], BF16)
    owT = b.din("owT", [L0_DI, D], BF16)
    fw = b.din("fw", [D, 1])
    fb = b.din("fb", [D, 1])
    xouter = b.din("xouter", [L0_RQ, D])
    out = b.dout("out", [L0_RQ, D])

    provider0 = emit_l0_post(b, resid_in, yg_fm, owT, 16)

    onesr = b.get_onesr()
    pmisc = b.get_pmisc()
    fwbc = b.sbuf([P, D], name="fwbc")
    fbbc = b.sbuf([P, D], name="fbbc")
    p.dma("sync", fwbc[0:1, 0:D], fw[0:D, 0:1].rearrange("a b -> b a"), w=[fwbc])
    p.dma("sync", fbbc[0:1, 0:D], fb[0:D, 0:1].rearrange("a b -> b a"), w=[fbbc])
    emit_bcast_row(b, fwbc, fwbc[0:1, 0:D], D, onesr, pmisc, r_extra=[fwbc])
    emit_bcast_row(b, fbbc, fbbc[0:1, 0:D], D, onesr, pmisc, r_extra=[fbbc])
    ln32 = b.sbuf([P, D], name="ln32L")
    sq = b.sbuf([P, D], name="sqL")
    st = b.sbuf([P, 8], name="stL")
    xot = b.sbuf([P, D], name="xot")
    for i in range(16):
        rto = provider0(i)
        emit_affine_ln_tile(b, rto, ln32, sq, st, D, fwbc, fbbc)
        p.dma("gpsimd", xot[:], xouter[i * P:(i + 1) * P, :], w=[xot])
        p.op("vector", lambda e: e.tensor_add(out=ln32[:], in0=ln32[:], in1=xot[:]),
             r=[ln32, xot], w=[ln32])
        p.dma("sync", out[i * P:(i + 1) * P, :], ln32[:], r=[ln32], w=[out])
    p.op("gpsimd", lambda e: e.engine_nop(), no_inc=True, r=[out])
    nc = b.build()
    _prog_cache[key] = nc
    return nc


def _bf(a):
    return np.ascontiguousarray(a).astype(NP_BF16)


def _shard_rows_halo(arr_b, beta, q, RQ=2048, HB=P):
    T = arr_b.shape[1]
    lo = q * RQ - HB
    if lo >= 0:
        return np.ascontiguousarray(arr_b[beta, lo:(q + 1) * RQ])
    blk = np.zeros((HB + RQ, arr_b.shape[2]), arr_b.dtype)
    blk[HB:] = arr_b[beta, 0:(q + 1) * RQ]
    return blk


def _shard_cols_halo(arr_2d, q, RQ=2048, HB=P):
    lo = q * RQ - HB
    if lo >= 0:
        return np.ascontiguousarray(arr_2d[:, lo:(q + 1) * RQ])
    blk = np.zeros((arr_2d.shape[0], HB + RQ), arr_2d.dtype)
    blk[:, HB:] = arr_2d[:, 0:(q + 1) * RQ]
    return blk


def run_l0_stack(x, norm_w, norm_b, in_proj_w, conv_w, conv_b, x_proj_w,
                 dt_proj_w, dt_proj_b, A_log, D_param, out_proj_w, fnorm_w, fnorm_b):
    B, T, D = x.shape
    RQ = T // 4
    eye = np.eye(P, dtype=np.float32).astype(NP_BF16)

    def pre_weights(i):
        return {
            "inwT": _bf(in_proj_w[i].T), "convw": np.ascontiguousarray(conv_w[i]),
            "convb": conv_b[i].reshape(-1, 1), "xpwT": _bf(x_proj_w[i].T),
            "dtpwT": _bf(dt_proj_w[i].T), "dtpb": dt_proj_b[i].reshape(-1, 1),
            "nw": norm_w[i].reshape(-1, 1), "nb": norm_b[i].reshape(-1, 1),
        }

    ones_m = np.ones((P, 1), np.float32)
    zeros_m = np.zeros((P, 1), np.float32)

    def launch_pre(prog, extra_per_core, wts):
        in_maps = []
        for c in range(NCORES):
            beta, q = c // 4, c % 4
            im = {"eye": eye, "hmask": (zeros_m if q == 0 else ones_m)}
            im.update(wts)
            im.update(extra_per_core(c, beta, q))
            in_maps.append(im)
        return run_bass_kernel_spmd(prog, in_maps, core_ids=list(range(NCORES)))

    res = launch_pre(build_l0first(), lambda c, beta, q: {"x": _shard_rows_halo(x, beta, q, RQ)},
                     pre_weights(0))
    R = np.stack([np.concatenate([res.results[b_ * 4 + q]["resid0"] for q in range(4)], 0)
                  for b_ in range(B)], 0)

    def gather_chan(res, key):
        return [np.concatenate([np.asarray(res.results[b_ * 4 + q][key]) for q in range(4)], 1)
                for b_ in range(B)]

    for i in range(4):
        XC = gather_chan(res, "xc")
        ZS = gather_chan(res, "zs")
        DT = gather_chan(res, "dt")
        BCm = gather_chan(res, "bc")
        scan_maps = []
        for c in range(NCORES):
            beta, g = c // 4, c % 4
            scan_maps.append({
                "dt": np.ascontiguousarray(DT[beta][g * 512:(g + 1) * 512]),
                "xc": np.ascontiguousarray(XC[beta][g * 512:(g + 1) * 512]),
                "zs": np.ascontiguousarray(ZS[beta][g * 512:(g + 1) * 512]),
                "Brow": np.ascontiguousarray(BCm[beta][0:16]),
                "Crow": np.ascontiguousarray(BCm[beta][16:32]),
                "alog": np.ascontiguousarray(A_log[i][g * 512:(g + 1) * 512]),
                "dp": D_param[i][g * 512:(g + 1) * 512].reshape(-1, 1),
            })
        sres = run_bass_kernel_spmd(build_l0scan(T, 512), scan_maps, core_ids=list(range(NCORES)))
        YG = [np.concatenate([np.asarray(sres.results[b_ * 4 + g]["yg"]) for g in range(4)], 0)
              for b_ in range(B)]
        if i < 3:
            res = launch_pre(
                build_l0mid(),
                lambda c, beta, q: {
                    "resid": _shard_rows_halo(R, beta, q, RQ),
                    "ygfm": _shard_cols_halo(YG[beta], q, RQ).astype(NP_BF16, copy=False),
                    "owT": _bf(out_proj_w[i].T),
                }, pre_weights(i + 1))
            R = np.stack([np.concatenate([res.results[b_ * 4 + q]["resid_o"] for q in range(4)], 0)
                          for b_ in range(B)], 0)
        else:
            in_maps = []
            for c in range(NCORES):
                beta, q = c // 4, c % 4
                in_maps.append({
                    "eye": eye,
                    "resid": np.ascontiguousarray(R[beta, q * RQ:(q + 1) * RQ]),
                    "ygfm": np.ascontiguousarray(YG[beta][:, q * RQ:(q + 1) * RQ]).astype(NP_BF16, copy=False),
                    "owT": _bf(out_proj_w[3].T),
                    "fw": fnorm_w.reshape(-1, 1), "fb": fnorm_b.reshape(-1, 1),
                    "xouter": np.ascontiguousarray(x[beta, q * RQ:(q + 1) * RQ]),
                })
            lres = run_bass_kernel_spmd(build_l0last(), in_maps, core_ids=list(range(NCORES)))
            out = np.stack([np.concatenate([lres.results[b_ * 4 + q]["out"] for q in range(4)], 0)
                            for b_ in range(B)], 0)
            return out


L1_D, L1_DI, L1_DTR = 512, 1024, 32


def emit_rms_tile(b, rt, ln32, sq, st, D, nwbc, eps=1e-5, mask_buf=None):
    p = b.p
    epsc = b.cconst(float(eps))
    p.op("scalar", lambda e: e.activation(out=sq[:, 0:D], in_=rt[:], func=AF.Square,
                                          accum_out=st[:, 2:3]), r=[rt], w=[sq, st])
    p.op("scalar", lambda e: e.activation(out=st[:, 4:5], in_=st[:, 2:3], func=AF.Ln,
                                          bias=epsc, scale=1.0 / D), r=[st, b._cpool], w=[st])
    p.op("scalar", lambda e: e.activation(out=st[:, 5:6], in_=st[:, 4:5], func=AF.Exp, scale=-0.5),
         r=[st], w=[st])
    p.op("vector", lambda e: e.tensor_scalar(out=ln32[:], in0=rt[:], scalar1=st[:, 5:6],
                                             scalar2=None, op0=ALU.mult), r=[rt, st], w=[ln32])
    p.op("vector", lambda e: e.tensor_mul(out=ln32[:], in0=ln32[:], in1=nwbc[:]),
         r=[ln32, nwbc], w=[ln32])
    if mask_buf is not None:
        p.op("vector", lambda e: e.tensor_scalar(out=ln32[:], in0=ln32[:], scalar1=mask_buf[:, 0:1],
                                                 scalar2=None, op0=ALU.mult), r=[ln32, mask_buf], w=[ln32])


def emit_l1_pre_body(b, provider, nT, inw_hbm, convw_hbm, convb_hbm, xpw_hbm,
                     dtw_hbm, dtb_hbm, nw_hbm, hmask_hbm,
                     xc_out, zs_out, dt_out, bc_out):
    p = b.p
    D, DI, DTR = L1_D, L1_DI, L1_DTR
    TC = nT * P
    OWN = TC - P
    onesr = b.get_onesr()
    pmisc = b.get_pmisc()
    nwbc = b.sbuf([P, D], name="nwbc1")
    p.dma("sync", nwbc[0:1, 0:D], nw_hbm[0:D, 0:1].rearrange("a b -> b a"), w=[nwbc])
    emit_bcast_row(b, nwbc, nwbc[0:1, 0:D], D, onesr, pmisc, r_extra=[nwbc])
    hmask = b.sbuf([P, 1], name="hmask1")
    p.dma("sync", hmask[:], hmask_hbm[:, :], w=[hmask])
    eyeT = b.get_eye()

    hs_fm = b.sbuf([P, 4 * TC], BF16, name="hs_fm1")
    ln32 = b.sbuf([P, D], name="ln321")
    sq = b.sbuf([P, D], name="sq01")
    st = b.sbuf([P, 8], name="st01")
    lnb16 = b.sbuf([P, D], BF16, name="lnb161")
    ptr = b.get_ptr()
    for i in range(nT):
        rt = provider(i)
        emit_rms_tile(b, rt, ln32, sq, st, D, nwbc, mask_buf=(hmask if i == 0 else None))
        p.op("vector", lambda e: e.tensor_copy(out=lnb16[:], in_=ln32[:]), r=[ln32], w=[lnb16])
        for c in range(4):
            p.op("tensor", lambda e, c=c: e.transpose(ptr[:], lnb16[:, c * P:(c + 1) * P], eyeT[:]),
                 r=[lnb16, eyeT], w=[ptr])
            p.op("scalar", lambda e, c=c, i=i: e.copy(out=hs_fm[:, c * TC + i * P:c * TC + (i + 1) * P],
                                                      in_=ptr[:]), r=[ptr], w=[hs_fm])

    NSL = [(j * 512, min((j + 1) * 512, TC)) for j in range((TC + 511) // 512)]
    inw_t = [b.sbuf([P, 4 * P], BF16, name=f"inw1_{j}") for j in range(2)]
    pxdbl = b.psum([64, 2048], name="pxdbl1")
    pxzb = [b.psum([P, 512], name=f"pxz1{j}") for j in range(2)]
    xzx = b.sbuf([P, TC], name="xzx1")
    acc = b.sbuf([P, OWN], name="acc1")
    xc_m = b.sbuf([P, OWN], BF16, name="xc_m1")
    zs_m = b.sbuf([P, OWN], BF16, name="zs_m1")
    cw = b.sbuf([P, 4], name="cw1")
    cb = b.sbuf([P, 1], name="cb1")
    xpw_t = b.sbuf([P, 64], BF16, name="xpw_t1")
    for m in range(16):
        wbuf = inw_t[m % 2]
        for k in range(4):
            p.dma("sync", wbuf[:, k * P:(k + 1) * P],
                  inw_hbm[k * P:(k + 1) * P, m * P:(m + 1) * P], w=[wbuf])
        if m < 8:
            p.dma("gpsimd", cw[:, :], convw_hbm[m * P:(m + 1) * P, :], w=[cw])
            p.dma("gpsimd", cb[:, :], convb_hbm[m * P:(m + 1) * P, :], w=[cb])
            p.dma("gpsimd", xpw_t[:, :], xpw_hbm[m * P:(m + 1) * P, :], w=[xpw_t])
        for ni, (n0, n1) in enumerate(NSL):
            pxz = pxzb[ni % 2]
            for k in range(4):
                p.op("tensor", lambda e, k=k, n0=n0, n1=n1, wbuf=wbuf, pxz=pxz: e.matmul(
                    pxz[:, 0:n1 - n0], wbuf[:, k * P:(k + 1) * P],
                    hs_fm[:, k * TC + n0:k * TC + n1],
                    start=(k == 0), stop=(k == 3)), r=[wbuf, hs_fm], w=[pxz])
            if m < 8:
                p.op("scalar", lambda e, n0=n0, n1=n1, pxz=pxz: e.copy(out=xzx[:, n0:n1], in_=pxz[:, 0:n1 - n0]),
                     r=[pxz], w=[xzx])
            else:
                o0, o1 = max(n0, P), n1
                if o1 > o0:
                    p.op("scalar", lambda e, n0=n0, o0=o0, o1=o1, pxz=pxz: e.activation(
                        out=zs_m[:, o0 - P:o1 - P], in_=pxz[:, o0 - n0:o1 - n0], func=AF.Silu),
                        r=[pxz], w=[zs_m])
        if m < 8:
            p.op("vector", lambda e: e.tensor_scalar(out=acc[:], in0=xzx[:, P - 3:P - 3 + OWN],
                                                     scalar1=cw[:, 0:1], scalar2=None, op0=ALU.mult),
                 r=[xzx, cw], w=[acc])
            for k in range(1, 4):
                p.op("vector", lambda e, k=k: e.scalar_tensor_tensor(
                    out=acc[:], in0=xzx[:, P - 3 + k:P - 3 + k + OWN], scalar=cw[:, k:k + 1],
                    in1=acc[:], op0=ALU.mult, op1=ALU.add), r=[xzx, cw, acc], w=[acc])
            p.op("scalar", lambda e: e.activation(out=xc_m[:], in_=acc[:], func=AF.Silu,
                                                  bias=cb[:, 0:1]), r=[acc, cb], w=[xc_m])
            p.dma("sync", xc_out[m * P:(m + 1) * P, :], xc_m[:], r=[xc_m], w=[xc_out])
            for j in range(4):
                p.op("tensor", lambda e, j=j, m=m: e.matmul(
                    pxdbl[:, j * 512:(j + 1) * 512], xpw_t[:, :], xc_m[:, j * 512:(j + 1) * 512],
                    start=(m == 0), stop=(m == 7)), r=[xpw_t, xc_m], w=[pxdbl])
        else:
            p.dma("sync", zs_out[(m - 8) * P:(m - 7) * P, :], zs_m[:], r=[zs_m], w=[zs_out])
    b.free(pxzb[1])
    b.free(pxzb[0])
    xdbl = b.sbuf([64, 2048], name="xdbl1")
    for j in range(4):
        p.op("scalar", lambda e, j=j: e.copy(out=xdbl[:, j * 512:(j + 1) * 512],
                                             in_=pxdbl[:, j * 512:(j + 1) * 512]),
             r=[pxdbl], w=[xdbl])
    p.dma("sync", bc_out[:, :], xdbl[DTR:64, :], r=[xdbl], w=[bc_out])
    dtin_b = b.sbuf([DTR, 2048], BF16, name="dtin_b1")
    p.op("vector", lambda e: e.tensor_copy(out=dtin_b[:, :], in_=xdbl[0:DTR, :]), r=[xdbl], w=[dtin_b])
    dtw_t = b.sbuf([DTR, DI], BF16, name="dtw_t1")
    p.dma("sync", dtw_t[:, :], dtw_hbm[:, :], w=[dtw_t])
    b.free(pxdbl)
    dtb = b.sbuf([P, 1], name="dtb1")
    pdt = b.psum([P, 512], name="pdt1")
    v32 = b.sbuf([P, 2048], name="v321")
    a1 = b.sbuf([P, 2048], name="a1_sp1")
    for m2 in range(DI // P):
        p.dma("gpsimd", dtb[:, :], dtb_hbm[m2 * P:(m2 + 1) * P, :], w=[dtb])
        for j in range(4):
            p.op("tensor", lambda e, m2=m2, j=j: e.matmul(
                pdt[:, :], dtw_t[:, m2 * P:(m2 + 1) * P], dtin_b[:, j * 512:(j + 1) * 512],
                start=True, stop=True), r=[dtw_t, dtin_b], w=[pdt])
            p.op("scalar", lambda e, j=j: e.activation(out=v32[:, j * 512:(j + 1) * 512],
                                                       in_=pdt[:, :], func=AF.Identity,
                                                       bias=dtb[:, 0:1]), r=[pdt, dtb], w=[v32])
        p.op("scalar", lambda e: e.activation(out=a1[:], in_=v32[:], func=AF.Abs), r=[v32], w=[a1])
        p.op("scalar", lambda e: e.activation(out=a1[:], in_=a1[:], func=AF.Exp, scale=-1.0),
             r=[a1], w=[a1])
        onec = b.cconst(1.0)
        p.op("scalar", lambda e, onec=onec: e.activation(out=a1[:], in_=a1[:], func=AF.Ln, bias=onec),
             r=[a1, b._cpool], w=[a1])
        p.op("vector", lambda e: e.tensor_relu(out=v32[:], in_=v32[:]), r=[v32], w=[v32])
        p.op("vector", lambda e: e.tensor_add(out=v32[:], in0=v32[:], in1=a1[:]), r=[v32, a1], w=[v32])
        p.dma("sync", dt_out[m2 * P:(m2 + 1) * P, :], v32[:], r=[v32], w=[dt_out])


def build_l1first():
    key = ("l1first",)
    if key in _prog_cache:
        return _prog_cache[key]
    b = Builder()
    p = b.p
    D, DM = 1024, L1_D
    x_in = b.din("x", [L0_RH, D])
    b.eye_hbm = b.din("eye", [P, P], BF16)
    lin1T = b.din("lin1T", [D, DM], BF16)
    lin1b = b.din("lin1b", [DM, 1])
    inw = b.din("inw", [DM, 2 * L1_DI], BF16)
    convw = b.din("convw", [L1_DI, 4])
    convb = b.din("convb", [L1_DI, 1])
    xpw = b.din("xpw", [L1_DI, 64], BF16)
    dtw = b.din("dtw", [L1_DTR, L1_DI], BF16)
    dtbh = b.din("dtb", [L1_DI, 1])
    nw = b.din("nw", [DM, 1])
    hmaskh = b.din("hmask", [P, 1])
    resid0 = b.dout("resid0", [L0_RQ, DM])
    xc_out = b.dout("xc", [L1_DI, L0_RQ], BF16)
    zs_out = b.dout("zs", [L1_DI, L0_RQ], BF16)
    dt_out = b.dout("dt", [L1_DI, L0_RQ])
    bc_out = b.dout("bc", [32, L0_RQ])

    onesr = b.get_onesr()
    pmisc = b.get_pmisc()
    l1bbc = b.sbuf([P, DM], name="l1bbc")
    p.dma("sync", l1bbc[0:1, 0:DM], lin1b[0:DM, 0:1].rearrange("a b -> b a"), w=[l1bbc])
    emit_bcast_row(b, l1bbc, l1bbc[0:1, 0:DM], DM, onesr, pmisc, r_extra=[l1bbc])
    l1w = b.sbuf([P, 8 * DM], BF16, name="l1w")
    for k in range(8):
        p.dma("sync", l1w[:, k * DM:(k + 1) * DM], lin1T[k * P:(k + 1) * P, :], w=[l1w])
    eyeT = b.get_eye()
    ptr = b.get_ptr()
    xt = b.sbuf([P, D], name="xt1")
    rt = b.sbuf([P, D], name="rt1")
    sq1 = b.sbuf([P, D], name="sq11")
    st1 = b.sbuf([P, 8], name="st11")
    xf = b.sbuf([P, 8 * P], BF16, name="xf1")
    ph = b.get_pmisc()
    ht = b.sbuf([P, DM], name="ht1")

    def provider(i):
        p.dma("sync", xt[:], x_in[i * P:(i + 1) * P, :], w=[xt])
        emit_plain_ln(b, xt, rt, sq1, st1, D)
        p.op("vector", lambda e: e.tensor_copy(out=sq1[:, 0:D], in_=rt[:]), r=[rt], w=[sq1])
        p.op("vector", lambda e: e.tensor_copy(out=xf[:, 0:P], in_=rt[:, 0:P]), r=[rt], w=[xf]) if False else None
        rb16 = sq1
        for c in range(8):
            p.op("vector", lambda e, c=c: e.tensor_copy(out=xf[:, c * P:(c + 1) * P],
                                                        in_=rt[:, c * P:(c + 1) * P]), r=[rt], w=[xf]) if False else None
        p.op("vector", lambda e: e.tensor_copy(out=rtb[:], in_=rt[:]), r=[rt], w=[rtb])
        for c in range(8):
            p.op("tensor", lambda e, c=c: e.transpose(ptr[:], rtb[:, c * P:(c + 1) * P], eyeT[:]),
                 r=[rtb, eyeT], w=[ptr])
            p.op("scalar", lambda e, c=c: e.copy(out=xf[:, c * P:(c + 1) * P], in_=ptr[:]),
                 r=[ptr], w=[xf])
        for k in range(8):
            p.op("tensor", lambda e, k=k: e.matmul(ph[:, :], xf[:, k * P:(k + 1) * P],
                                                   l1w[:, k * DM:(k + 1) * DM],
                                                   start=(k == 0), stop=(k == 7)), r=[xf, l1w], w=[ph])
        p.op("vector", lambda e: e.tensor_add(out=ht[:], in0=ph[:, :], in1=l1bbc[:]),
             r=[ph, l1bbc], w=[ht])
        if i > 0:
            p.dma("gpsimd", resid0[(i - 1) * P:i * P, :], ht[:], r=[ht], w=[resid0])
        return ht

    rtb = b.sbuf([P, D], BF16, name="rtb1")
    emit_l1_pre_body(b, provider, 17, inw, convw, convb, xpw, dtw, dtbh,
                     nw, hmaskh, xc_out, zs_out, dt_out, bc_out)
    p.op("gpsimd", lambda e: e.engine_nop(), no_inc=True, r=[resid0, xc_out, zs_out, dt_out, bc_out])
    nc = b.build()
    _prog_cache[key] = nc
    return nc


def emit_l1_post(b, resid_in, yg_fm, ow_hbm, nT):
    p = b.p
    D, DI = L1_D, L1_DI
    TC = nT * P
    eyeT2 = b.get_eye()
    out_fm = b.sbuf([P, 4 * TC], BF16, name="out_fm1")
    ygk = [b.sbuf([P, TC], BF16, name=f"yg1k{k}") for k in range(DI // P)]
    owk = [b.sbuf([P, D], BF16, name=f"ow1k{k}") for k in range(DI // P)]
    for k in range(DI // P):
        p.dma("sync", ygk[k][:, :], yg_fm[k * P:(k + 1) * P, :], w=[ygk[k]])
        p.dma("gpsimd", owk[k][:, :], ow_hbm[k * P:(k + 1) * P, :], w=[owk[k]])
    pofb = [b.psum([P, 512], name=f"pof1{j}") for j in range(2)]
    NSL = [(j * 512, min((j + 1) * 512, TC)) for j in range((TC + 511) // 512)]
    for m in range(4):
        for ni, (n0, n1) in enumerate(NSL):
            pof = pofb[ni % 2]
            for k in range(DI // P):
                p.op("tensor", lambda e, m=m, k=k, n0=n0, n1=n1, pof=pof: e.matmul(
                    pof[:, 0:n1 - n0], owk[k][:, m * P:(m + 1) * P], ygk[k][:, n0:n1],
                    start=(k == 0), stop=(k == DI // P - 1)), r=[owk[k], ygk[k]], w=[pof])
            p.op("scalar", lambda e, m=m, n0=n0, n1=n1, pof=pof: e.copy(
                out=out_fm[:, m * TC + n0:m * TC + n1], in_=pof[:, 0:n1 - n0]), r=[pof], w=[out_fm])
    deps = []
    deps += b.free(pofb[1])
    deps += b.free(pofb[0])
    for k in reversed(range(DI // P)):
        deps += b.free(owk[k])
    for k in reversed(range(DI // P)):
        deps += b.free(ygk[k])

    rti = b.sbuf([P, D], name="rti1")
    rto = b.sbuf([P, D], name="rto1")
    ptr2 = b.get_ptr()
    first = [True]

    def provider(i):
        dd = deps if first[0] else []
        first[0] = False
        p.dma("sync", rti[:], resid_in[i * P:(i + 1) * P, :], w=[rti], deps=dd)
        for m in range(4):
            p.op("tensor", lambda e, m=m, i=i: e.transpose(
                ptr2[:], out_fm[:, m * TC + i * P:m * TC + (i + 1) * P], eyeT2[:]),
                r=[out_fm, eyeT2], w=[ptr2])
            p.op("vector", lambda e, m=m: e.tensor_add(
                out=rto[:, m * P:(m + 1) * P], in0=rti[:, m * P:(m + 1) * P], in1=ptr2[:]),
                r=[rti, ptr2], w=[rto])
        return rto

    return provider


def build_l1mid():
    key = ("l1mid",)
    if key in _prog_cache:
        return _prog_cache[key]
    b = Builder()
    p = b.p
    DM = L1_D
    resid_in = b.din("resid", [L0_RH, DM])
    b.eye_hbm = b.din("eye", [P, P], BF16)
    yg_fm = b.din("ygfm", [L1_DI, L0_RH], BF16)
    ow = b.din("ow", [L1_DI, DM], BF16)
    inw = b.din("inw", [DM, 2 * L1_DI], BF16)
    convw = b.din("convw", [L1_DI, 4])
    convb = b.din("convb", [L1_DI, 1])
    xpw = b.din("xpw", [L1_DI, 64], BF16)
    dtw = b.din("dtw", [L1_DTR, L1_DI], BF16)
    dtbh = b.din("dtb", [L1_DI, 1])
    nw = b.din("nw", [DM, 1])
    hmaskh = b.din("hmask", [P, 1])
    resid_out = b.dout("resid_o", [L0_RQ, DM])
    xc_out = b.dout("xc", [L1_DI, L0_RQ], BF16)
    zs_out = b.dout("zs", [L1_DI, L0_RQ], BF16)
    dt_out = b.dout("dt", [L1_DI, L0_RQ])
    bc_out = b.dout("bc", [32, L0_RQ])

    provider0 = emit_l1_post(b, resid_in, yg_fm, ow, 17)

    def provider(i):
        rto = provider0(i)
        if i > 0:
            p.dma("gpsimd", resid_out[(i - 1) * P:i * P, :], rto[:], r=[rto], w=[resid_out])
        return rto

    emit_l1_pre_body(b, provider, 17, inw, convw, convb, xpw, dtw, dtbh,
                     nw, hmaskh, xc_out, zs_out, dt_out, bc_out)
    p.op("gpsimd", lambda e: e.engine_nop(), no_inc=True, r=[resid_out, xc_out, zs_out, dt_out, bc_out])
    nc = b.build()
    _prog_cache[key] = nc
    return nc


def build_l1last():
    key = ("l1last",)
    if key in _prog_cache:
        return _prog_cache[key]
    b = Builder()
    p = b.p
    DM, D = L1_D, 1024
    resid_in = b.din("resid", [L0_RQ, DM])
    b.eye_hbm = b.din("eye", [P, P], BF16)
    yg_fm = b.din("ygfm", [L1_DI, L0_RQ], BF16)
    ow = b.din("ow", [L1_DI, DM], BF16)
    lin2 = b.din("lin2", [DM, D], BF16)
    lin2b = b.din("lin2b", [D, 1])
    xouter = b.din("xouter", [L0_RQ, D])
    out = b.dout("out", [L0_RQ, D])

    provider0 = emit_l1_post(b, resid_in, yg_fm, ow, 16)

    onesr = b.get_onesr()
    pmisc = b.get_pmisc()
    l2bbc = b.sbuf([P, D], name="l2bbc")
    p.dma("sync", l2bbc[0:1, 0:D], lin2b[0:D, 0:1].rearrange("a b -> b a"), w=[l2bbc])
    emit_bcast_row(b, l2bbc, l2bbc[0:1, 0:D], D, onesr, pmisc, r_extra=[l2bbc])
    l2w = b.sbuf([P, 4 * D], BF16, name="l2w")
    for k in range(4):
        p.dma("sync", l2w[:, k * D:(k + 1) * D], lin2[k * P:(k + 1) * P, :], w=[l2w])
    eyeT = b.get_eye()
    ptr = b.get_ptr()
    hb16 = b.sbuf([P, DM], BF16, name="hb16")
    hfm = b.sbuf([P, 4 * P], BF16, name="hfm1")
    pl2 = b.psum([P, 512], name="pl2")
    ot32 = b.sbuf([P, D], name="ot32")
    mx = b.sbuf([P, 32], name="mx1")
    ex = b.sbuf([P, D], name="ex1")
    sm = b.sbuf([P, 32], name="sm1")
    xot = b.sbuf([P, D], name="xot1")
    import concourse.bass as bass_mod
    for i in range(16):
        rto = provider0(i)
        p.op("vector", lambda e: e.tensor_copy(out=hb16[:], in_=rto[:]), r=[rto], w=[hb16])
        for c in range(4):
            p.op("tensor", lambda e, c=c: e.transpose(ptr[:], hb16[:, c * P:(c + 1) * P], eyeT[:]),
                 r=[hb16, eyeT], w=[ptr])
            p.op("scalar", lambda e, c=c: e.copy(out=hfm[:, c * P:(c + 1) * P], in_=ptr[:]),
                 r=[ptr], w=[hfm])
        for j in range(2):
            for k in range(4):
                p.op("tensor", lambda e, j=j, k=k: e.matmul(
                    pl2[:, :], hfm[:, k * P:(k + 1) * P], l2w[:, k * D + j * 512:k * D + (j + 1) * 512],
                    start=(k == 0), stop=(k == 3)), r=[hfm, l2w], w=[pl2])
            p.op("vector", lambda e, j=j: e.tensor_add(out=ot32[:, j * 512:(j + 1) * 512],
                                                       in0=pl2[:, :], in1=l2bbc[:, j * 512:(j + 1) * 512]),
                 r=[pl2, l2bbc], w=[ot32])
        p.op("vector", lambda e: e.tensor_reduce(
            out=mx[:], in_=ot32.ap().rearrange("p (a c) -> p a c", c=32), axis=AX.X, op=ALU.max),
            r=[ot32], w=[mx])
        def sub_emit(e):
            in1 = bass_mod.AP(mx.ap().tensor, 0, [[32, P], [1, 32], [0, 32]])
            return e.tensor_tensor(out=ot32.ap().rearrange("p (a c) -> p a c", c=32),
                                   in0=ot32.ap().rearrange("p (a c) -> p a c", c=32),
                                   in1=in1, op=ALU.subtract)
        p.op("vector", sub_emit, r=[ot32, mx], w=[ot32])
        p.op("scalar", lambda e: e.activation(out=ex[:], in_=ot32[:], func=AF.Exp), r=[ot32], w=[ex])
        p.op("vector", lambda e: e.tensor_reduce(
            out=sm[:], in_=ex.ap().rearrange("p (a c) -> p a c", c=32), axis=AX.X, op=ALU.add),
            r=[ex], w=[sm])
        p.op("vector", lambda e: e.reciprocal(out=sm[:], in_=sm[:]), r=[sm], w=[sm])
        def mul_emit(e):
            in1 = bass_mod.AP(sm.ap().tensor, 0, [[32, P], [1, 32], [0, 32]])
            return e.tensor_tensor(out=ex.ap().rearrange("p (a c) -> p a c", c=32),
                                   in0=ex.ap().rearrange("p (a c) -> p a c", c=32),
                                   in1=in1, op=ALU.mult)
        p.op("vector", mul_emit, r=[ex, sm], w=[ex])
        p.dma("gpsimd", xot[:], xouter[i * P:(i + 1) * P, :], w=[xot])
        p.op("vector", lambda e: e.tensor_add(out=ex[:], in0=ex[:], in1=xot[:]), r=[ex, xot], w=[ex])
        p.dma("sync", out[i * P:(i + 1) * P, :], ex[:], r=[ex], w=[out])
    p.op("gpsimd", lambda e: e.engine_nop(), no_inc=True, r=[out])
    nc = b.build()
    _prog_cache[key] = nc
    return nc


def run_l1_stack(x, lin1_w, lin1_b, norm_w, in_w, conv_w, conv_b, xproj_w,
                 dt_w, dt_b, A_log, Dp, out_w, lin2_w, lin2_b):
    B, T, D = x.shape
    RQ = T // 4
    eye = np.eye(P, dtype=np.float32).astype(NP_BF16)
    ones_m = np.ones((P, 1), np.float32)
    zeros_m = np.zeros((P, 1), np.float32)

    def pre_weights(i):
        return {"inw": _bf(in_w[i]), "convw": np.ascontiguousarray(conv_w[i]),
                "convb": conv_b[i].reshape(-1, 1), "xpw": _bf(xproj_w[i]),
                "dtw": _bf(dt_w[i]), "dtb": dt_b[i].reshape(-1, 1),
                "nw": norm_w[i].reshape(-1, 1)}

    in_maps = []
    for c in range(NCORES):
        beta, q = c // 4, c % 4
        im = {"eye": eye, "hmask": (zeros_m if q == 0 else ones_m),
              "x": _shard_rows_halo(x, beta, q, RQ),
              "lin1T": _bf(lin1_w), "lin1b": lin1_b.reshape(-1, 1)}
        im.update(pre_weights(0))
        in_maps.append(im)
    res = run_bass_kernel_spmd(build_l1first(), in_maps, core_ids=list(range(NCORES)))
    R = np.stack([np.concatenate([res.results[b_ * 4 + q]["resid0"] for q in range(4)], 0)
                  for b_ in range(B)], 0)

    def gather_chan(res, key):
        return [np.concatenate([np.asarray(res.results[b_ * 4 + q][key]) for q in range(4)], 1)
                for b_ in range(B)]

    for i in range(4):
        XC = gather_chan(res, "xc")
        ZS = gather_chan(res, "zs")
        DT = gather_chan(res, "dt")
        BCm = gather_chan(res, "bc")
        NCH = L1_DI // 4
        scan_maps = []
        for c in range(NCORES):
            beta, g = c // 4, c % 4
            scan_maps.append({
                "dt": np.ascontiguousarray(DT[beta][g * NCH:(g + 1) * NCH]),
                "xc": np.ascontiguousarray(XC[beta][g * NCH:(g + 1) * NCH]),
                "zs": np.ascontiguousarray(ZS[beta][g * NCH:(g + 1) * NCH]),
                "Brow": np.ascontiguousarray(BCm[beta][0:16]),
                "Crow": np.ascontiguousarray(BCm[beta][16:32]),
                "alog": np.ascontiguousarray(A_log[i][g * NCH:(g + 1) * NCH]),
                "dp": Dp[i][g * NCH:(g + 1) * NCH].reshape(-1, 1),
            })
        sres = run_bass_kernel_spmd(build_l0scan(T, NCH), scan_maps, core_ids=list(range(NCORES)))
        YG = [np.concatenate([np.asarray(sres.results[b_ * 4 + g]["yg"]) for g in range(4)], 0)
              for b_ in range(B)]
        if i < 3:
            in_maps = []
            for c in range(NCORES):
                beta, q = c // 4, c % 4
                im = {"eye": eye, "hmask": (zeros_m if q == 0 else ones_m),
                      "resid": _shard_rows_halo(R, beta, q, RQ),
                      "ygfm": _shard_cols_halo(YG[beta], q, RQ).astype(NP_BF16, copy=False),
                      "ow": _bf(out_w[i])}
                im.update(pre_weights(i + 1))
                in_maps.append(im)
            res = run_bass_kernel_spmd(build_l1mid(), in_maps, core_ids=list(range(NCORES)))
            R = np.stack([np.concatenate([res.results[b_ * 4 + q]["resid_o"] for q in range(4)], 0)
                          for b_ in range(B)], 0)
        else:
            in_maps = []
            for c in range(NCORES):
                beta, q = c // 4, c % 4
                in_maps.append({
                    "eye": eye,
                    "resid": np.ascontiguousarray(R[beta, q * RQ:(q + 1) * RQ]),
                    "ygfm": np.ascontiguousarray(YG[beta][:, q * RQ:(q + 1) * RQ]).astype(NP_BF16, copy=False),
                    "ow": _bf(out_w[3]),
                    "lin2": _bf(lin2_w), "lin2b": lin2_b.reshape(-1, 1),
                    "xouter": np.ascontiguousarray(x[beta, q * RQ:(q + 1) * RQ]),
                })
            lres = run_bass_kernel_spmd(build_l1last(), in_maps, core_ids=list(range(NCORES)))
            return np.stack([np.concatenate([lres.results[b_ * 4 + q]["out"] for q in range(4)], 0)
                             for b_ in range(B)], 0)


def build_l0ffn(R=2048):
    key = ("l0ffn", R)
    if key in _prog_cache:
        return _prog_cache[key]
    D, H_, HD, E = 1024, 16, 64, 8
    nT = R // P
    b = Builder()
    p = b.p
    x_in = b.din("x", [R, D])
    b.eye_hbm = b.din("eye", [P, P], BF16)
    wq = b.din("wq", [D, D], BF16)
    bq = b.din("bq", [D, 1])
    wdn = b.din("wdn", [D, 64], BF16)
    bdn = b.din("bdn", [64, 1])
    wup = b.din("wup", [64, 2 * D], BF16)
    bup = b.din("bup", [2 * D, 1])
    wo = b.din("wo", [D, D], BF16)
    bo = b.din("bo", [D, 1])
    g1 = b.din("g1", [D, 1])
    g2 = b.din("g2", [D, 1])
    wg = b.din("wg", [D, E], BF16)
    bg = b.din("bg", [E, 1])
    we = b.din("we", [E * D, D], BF16)
    bdm = b.din("bdm", [P, P])
    out = b.dout("out", [R, D])
    x2_scr = b.dscratch("x2_scr", [R, D])
    q_scr = b.dscratch("q_scr", [16 * P, 64], BF16)
    kv_scr = b.dscratch("kv_scr", [16 * P, P], BF16)
    o_scr = b.dscratch("o_scr", [16 * P, 64], BF16)

    onesr = b.get_onesr()
    pmisc = b.get_pmisc()
    eyeT = b.get_eye()
    ptr = b.get_ptr()

    def bcast(src, cols, name):
        t = b.sbuf([P, cols], name=name)
        p.dma("sync", t[0:1, 0:cols], src[0:cols, 0:1].rearrange("a b -> b a"), w=[t])
        emit_bcast_row(b, t, t[0:1, 0:cols], cols, onesr, pmisc, r_extra=[t])
        return t

    bq_bc = bcast(bq, D, "bq_bc")
    bdn_bc = bcast(bdn, 64, "bdn_bc")
    bup_bc = bcast(bup, 2 * D, "bup_bc")
    bo_bc = bcast(bo, D, "bo_bc")
    g1_bc = bcast(g1, D, "g1_bc")
    g2_bc = bcast(g2, D, "g2_bc")
    bg_bc = bcast(bg, E, "bg_bc")
    bdm_sb = b.sbuf([P, P], name="bdm_sb")
    p.dma("sync", bdm_sb[:], bdm[:, :], w=[bdm_sb])

    wo_sb = b.sbuf([P, 8 * D], BF16, name="wo_sb")
    for k in range(8):
        p.dma("sync", wo_sb[:, k * D:(k + 1) * D], wo[k * P:(k + 1) * P, :], w=[wo_sb])
    wg_sb = b.sbuf([P, 8 * E], BF16, name="wg_sb")
    for k in range(8):
        p.dma("sync", wg_sb[:, k * E:(k + 1) * E], wg[k * P:(k + 1) * P, :], w=[wg_sb])

    h2fm = b.sbuf([P, 8 * R], BF16, name="h2fm")
    gate = b.sbuf([P, 16 * E], name="gate")
    comb = [b.sbuf([P, D], name=f"comb{i}") for i in range(nT // 2)]

    xt = b.sbuf([P, D], name="xtf")
    xn = b.sbuf([P, D], name="xnf")
    sq = b.sbuf([P, D], name="sqf")
    st = b.sbuf([P, 8], name="stf")
    hrm = b.sbuf([P, D], name="hrm")
    hb = b.sbuf([P, D], BF16, name="hbf")
    hfm = b.sbuf([P, 8 * P], BF16, name="hfmf")
    pq = b.psum([P, 512], name="pq")
    qb16 = b.sbuf([P, D], BF16, name="qb16")
    rsb = b.sbuf([P, 64], BF16, name="rsb")
    rfm = b.sbuf([64, P], BF16, name="rfm")
    kvb16 = b.sbuf([P, 2 * D], BF16, name="kvb16")
    kn = b.sbuf([P, 16], name="kn")
    qn = b.sbuf([P, 16], name="qn")
    bnd = b.sbuf([P, 16], name="bnd")
    bnd_row = b.sbuf([1, 2048], name="bnd_row")
    bnd_scr = b.dscratch("bnd_scr", [P, 16])
    qth = b.sbuf([P, 64], BF16, name="qth")
    kvtg = b.sbuf([P, P], BF16, name="kvtg")
    qth2 = b.sbuf([P, 64], BF16, name="qth2")
    kvtg2 = b.sbuf([P, P], BF16, name="kvtg2")
    lhs_a = b.sbuf([65, P], BF16, name="lhs_a")
    rhs_a = b.sbuf([65, P], BF16, name="rhs_a")
    lhs_b = b.sbuf([65, P], BF16, name="lhs_b")
    rhs_b = b.sbuf([65, P], BF16, name="rhs_b")
    patt = b.psum([P, 1024], name="patt")
    pq2 = b.psum([P, 512], name="pq2")
    ptr2f = b.psum([P, P], BF16, name="ptr2f")
    probs = b.sbuf([P, 2048], BF16, name="probs")
    rs = b.sbuf([P, 16], name="rs")
    probsT = b.sbuf([P, P], BF16, name="probsT")
    probsT2 = b.sbuf([P, P], BF16, name="probsT2x")
    pout = b.psum([P, 64], name="pout")
    oth = b.sbuf([P, 64], BF16, name="oth")
    ott = b.sbuf([P, D], BF16, name="ott")
    otfm = b.sbuf([P, 8 * P], BF16, name="otfm")
    x2t = b.sbuf([P, D], name="x2t")
    h2t = b.sbuf([P, D], name="h2t")
    h2b = b.sbuf([P, D], BF16, name="h2b")
    gt32 = b.sbuf([P, E], name="gt32")

    p.op("vector", lambda e: e.memset(rhs_a[64:65, :], 1.0), w=[rhs_a])
    p.op("vector", lambda e: e.memset(rhs_b[64:65, :], 1.0), w=[rhs_b])
    b.cconst(1e-5)
    b.cconst(1e-6)

    wq_sb = b.sbuf([P, 8 * D], BF16, name="wq_sb")
    for k in range(8):
        p.dma("sync", wq_sb[:, k * D:(k + 1) * D], wq[k * P:(k + 1) * P, :], w=[wq_sb])
    wdn_sb = b.sbuf([P, 8 * 64], BF16, name="wdn_sb")
    for k in range(8):
        p.dma("sync", wdn_sb[:, k * 64:(k + 1) * 64], wdn[k * P:(k + 1) * P, :], w=[wdn_sb])
    wup_sb = b.sbuf([64, 2 * D], BF16, name="wup_sb")
    p.dma("sync", wup_sb[:, :], wup[:, :], w=[wup_sb])

    for i in range(nT):
        p.dma("sync", xt[:], x_in[i * P:(i + 1) * P, :], w=[xt])
        emit_plain_ln(b, xt, xn, sq, st, D)
        emit_rms_tile(b, xn, hrm, sq, st, D, g1_bc, eps=1e-6)
        p.op("vector", lambda e: e.tensor_copy(out=hb[:], in_=hrm[:]), r=[hrm], w=[hb])
        for c in range(8):
            p.op("tensor", lambda e, c=c: e.transpose(ptr[:], hb[:, c * P:(c + 1) * P], eyeT[:]),
                 r=[hb, eyeT], w=[ptr])
            p.op("scalar", lambda e, c=c: e.copy(out=hfm[:, c * P:(c + 1) * P], in_=ptr[:]),
                 r=[ptr], w=[hfm])
        for j in range(2):
            for k in range(8):
                p.op("tensor", lambda e, j=j, k=k: e.matmul(
                    pq[:, :], hfm[:, k * P:(k + 1) * P], wq_sb[:, k * D + j * 512:k * D + (j + 1) * 512],
                    start=(k == 0), stop=(k == 7)), r=[hfm, wq_sb], w=[pq])
            p.op("vector", lambda e, j=j: e.tensor_add(out=qb16[:, j * 512:(j + 1) * 512],
                                                       in0=pq[:, :], in1=bq_bc[:, j * 512:(j + 1) * 512]),
                 r=[pq, bq_bc], w=[qb16])
        for k in range(8):
            p.op("tensor", lambda e, k=k: e.matmul(
                pq[:, 0:64], hfm[:, k * P:(k + 1) * P], wdn_sb[:, k * 64:(k + 1) * 64],
                start=(k == 0), stop=(k == 7)), r=[hfm, wdn_sb], w=[pq])
        p.op("vector", lambda e: e.tensor_add(out=sq[:, 0:64], in0=pq[:, 0:64], in1=bdn_bc[:, 0:64]),
             r=[pq, bdn_bc], w=[sq])
        p.op("vector", lambda e: e.tensor_copy(out=rsb[:], in_=sq[:, 0:64]), r=[sq], w=[rsb])
        p.op("tensor", lambda e: e.transpose(ptr[0:64, 0:P], rsb.ap(), eyeT[:]), r=[rsb, eyeT], w=[ptr])
        p.op("scalar", lambda e: e.copy(out=rfm[:, :], in_=ptr[0:64, 0:P]), r=[ptr], w=[rfm])
        for j in range(4):
            p.op("tensor", lambda e, j=j: e.matmul(
                pq[:, :], rfm[:, :], wup_sb[:, j * 512:(j + 1) * 512],
                start=True, stop=True), r=[rfm, wup_sb], w=[pq])
            p.op("vector", lambda e, j=j: e.tensor_add(out=kvb16[:, j * 512:(j + 1) * 512],
                                                       in0=pq[:, :], in1=bup_bc[:, j * 512:(j + 1) * 512]),
                 r=[pq, bup_bc], w=[kvb16])
        for hh in range(2):
            p.op("scalar", lambda e, hh=hh: e.activation(out=sq[:, 0:D], in_=kvb16[:, hh * D:(hh + 1) * D],
                                                         func=AF.Square), r=[kvb16], w=[sq])
            p.op("vector", lambda e, hh=hh: e.tensor_reduce(
                out=kn[:, hh * 8:(hh + 1) * 8], in_=bass.AP(sq.ap().tensor, 0, [[D, P], [P, 8], [1, 64]]),
                axis=AX.X, op=ALU.add), r=[sq], w=[kn])
        p.op("vector", lambda e: e.tensor_reduce(out=st[:, 6:7], in_=kn[:], axis=AX.X, op=ALU.max),
             r=[kn], w=[st])
        p.op("scalar", lambda e: e.activation(out=sq[:, 0:D], in_=qb16[:], func=AF.Square),
             r=[qb16], w=[sq])
        p.op("vector", lambda e: e.tensor_reduce(
            out=qn[:], in_=bass.AP(sq.ap().tensor, 0, [[D, P], [64, 16], [1, 64]]),
            axis=AX.X, op=ALU.add), r=[sq], w=[qn])
        p.op("vector", lambda e: e.tensor_scalar(out=bnd[:], in0=qn[:], scalar1=st[:, 6:7],
                                                 scalar2=None, op0=ALU.mult), r=[qn, st], w=[bnd])
        p.op("scalar", lambda e: e.activation(out=bnd[:], in_=bnd[:], func=AF.Sqrt, scale=1.0 / 64.0),
             r=[bnd], w=[bnd])
        p.dma("gpsimd", q_scr.rearrange("(a b) d -> a (b d)", b=16), qb16[:], r=[qb16], w=[q_scr])
        p.dma("gpsimd", kv_scr.rearrange("(a b) d -> a (b d)", b=16), kvb16[:], r=[kvb16], w=[kv_scr])
        p.dma("sync", bnd_scr[:, :], bnd[:], r=[bnd], w=[bnd_scr])
        p.dma("sync", bnd_row[0:1, :], bnd_scr.rearrange("(o a) b -> o (a b)", o=1),
              r=[bnd_scr], w=[bnd_row])
        lhsb = [lhs_a, lhs_b]
        rhsb = [rhs_a, rhs_b]
        ptrs = [ptr, ptr2f]
        qthb = [qth, qth2]
        kvtgb = [kvtg, kvtg2]
        for wave in range(2):
            for mci in range(8):
                mc = wave * 8 + mci
                pb_ = mci % 2
                p.dma("gpsimd", qthb[pb_][:, :], q_scr[mc * P:(mc + 1) * P, :], r=[q_scr], w=[qthb[pb_]])
                p.dma("gpsimd", kvtgb[pb_][:, :], kv_scr[mc * P:(mc + 1) * P, :], r=[kv_scr], w=[kvtgb[pb_]])
                p.op("tensor", lambda e, pb_=pb_: e.transpose(ptrs[pb_][0:64, 0:P], qthb[pb_].ap(), eyeT[:]),
                     r=[qthb[pb_], eyeT], w=[ptrs[pb_]])
                p.op("scalar", lambda e, pb_=pb_: e.activation(out=lhsb[pb_][0:64, :], in_=ptrs[pb_][0:64, 0:P],
                                                      func=AF.Identity, scale=0.125), r=[ptrs[pb_]], w=[lhsb[pb_]])
                p.op("scalar", lambda e, mc=mc, pb_=pb_: e.activation(out=lhsb[pb_][64:65, :],
                                                             in_=bnd_row[0:1, mc * P:(mc + 1) * P],
                                                             func=AF.Identity, scale=-1.0),
                     r=[bnd_row], w=[lhsb[pb_]])
                p.op("tensor", lambda e, pb_=pb_: e.transpose(ptrs[pb_][0:64, 0:P], kvtgb[pb_][:, 0:64], eyeT[:]),
                     r=[kvtgb[pb_], eyeT], w=[ptrs[pb_]])
                p.op("scalar", lambda e, pb_=pb_: e.copy(out=rhsb[pb_][0:64, :], in_=ptrs[pb_][0:64, 0:P]),
                     r=[ptrs[pb_]], w=[rhsb[pb_]])
                p.op("tensor", lambda e, mci=mci, pb_=pb_: e.matmul(patt[:, mci * P:(mci + 1) * P],
                                                         lhsb[pb_][:, :], rhsb[pb_][:, :],
                                                         start=True, stop=True),
                     r=[lhsb[pb_], rhsb[pb_]], w=[patt])
            p.op("scalar", lambda e, wave=wave: e.activation(out=probs[:, wave * 1024:(wave + 1) * 1024],
                                                             in_=patt[:, :], func=AF.Exp),
                 r=[patt], w=[probs])
        def mask_emit(e):
            in1 = bass.AP(bdm_sb.ap().tensor, 0, [[P, P], [0, 16], [1, P]])
            return e.tensor_tensor(out=probs.ap().rearrange("p (a c) -> p a c", c=P),
                                   in0=probs.ap().rearrange("p (a c) -> p a c", c=P),
                                   in1=in1, op=ALU.mult)
        p.op("vector", mask_emit, r=[probs, bdm_sb], w=[probs])
        p.op("vector", lambda e: e.tensor_reduce(
            out=rs[:], in_=probs.ap().rearrange("p (a c) -> p a c", c=P), axis=AX.X, op=ALU.add),
            r=[probs], w=[rs])
        p.op("vector", lambda e: e.reciprocal(out=rs[:], in_=rs[:]), r=[rs], w=[rs])
        def norm_emit(e):
            in1 = bass.AP(rs.ap().tensor, 0, [[16, P], [1, 16], [0, P]])
            return e.tensor_tensor(out=probs.ap().rearrange("p (a c) -> p a c", c=P),
                                   in0=probs.ap().rearrange("p (a c) -> p a c", c=P),
                                   in1=in1, op=ALU.mult)
        p.op("vector", norm_emit, r=[probs, rs], w=[probs])
        for mc in range(16):
            pb_ = mc % 2
            ptx = [ptr, ptr2f][pb_]
            kvx = [kvtg, kvtg2][pb_]
            pTx = [probsT, probsT2][pb_]
            p.op("tensor", lambda e, mc=mc, ptx=ptx: e.transpose(ptx[:], probs[:, mc * P:(mc + 1) * P], eyeT[:]),
                 r=[probs, eyeT], w=[ptx])
            p.op("scalar", lambda e, ptx=ptx, pTx=pTx: e.copy(out=pTx[:], in_=ptx[:]), r=[ptx], w=[pTx])
            p.dma("gpsimd", kvx[:, :], kv_scr[mc * P:(mc + 1) * P, :], r=[kv_scr], w=[kvx])
            p.op("tensor", lambda e, mc=mc, kvx=kvx, pTx=pTx: e.matmul(pout[:, :], pTx[:, :], kvx[:, 64:P],
                                                     start=True, stop=True),
                 r=[pTx, kvx], w=[pout])
            p.op("scalar", lambda e: e.copy(out=oth[:], in_=pout[:, :]), r=[pout], w=[oth])
            p.dma("sync", o_scr[mc * P:(mc + 1) * P, :], oth[:, :], r=[oth], w=[o_scr])
        p.dma("sync", ott[:, :], o_scr.rearrange("(a b) d -> a (b d)", b=16), r=[o_scr], w=[ott])
        for c in range(8):
            p.op("tensor", lambda e, c=c: e.transpose(ptr[:], ott[:, c * P:(c + 1) * P], eyeT[:]),
                 r=[ott, eyeT], w=[ptr])
            p.op("scalar", lambda e, c=c: e.copy(out=otfm[:, c * P:(c + 1) * P], in_=ptr[:]),
                 r=[ptr], w=[otfm])
        for j in range(2):
            for k in range(8):
                p.op("tensor", lambda e, j=j, k=k: e.matmul(
                    pq[:, :], otfm[:, k * P:(k + 1) * P], wo_sb[:, k * D + j * 512:k * D + (j + 1) * 512],
                    start=(k == 0), stop=(k == 7)), r=[otfm, wo_sb], w=[pq])
            p.op("vector", lambda e, j=j: e.tensor_add(out=x2t[:, j * 512:(j + 1) * 512],
                                                       in0=pq[:, :], in1=bo_bc[:, j * 512:(j + 1) * 512]),
                 r=[pq, bo_bc], w=[x2t])
        p.op("vector", lambda e: e.tensor_add(out=x2t[:], in0=x2t[:], in1=xn[:]), r=[x2t, xn], w=[x2t])
        p.dma("sync", x2_scr[i * P:(i + 1) * P, :], x2t[:], r=[x2t], w=[x2_scr])
        emit_rms_tile(b, x2t, h2t, sq, st, D, g2_bc, eps=1e-6)
        p.op("vector", lambda e: e.tensor_copy(out=h2b[:], in_=h2t[:]), r=[h2t], w=[h2b])
        for c in range(8):
            p.op("tensor", lambda e, c=c: e.transpose(ptr[:], h2b[:, c * P:(c + 1) * P], eyeT[:]),
                 r=[h2b, eyeT], w=[ptr])
            p.op("scalar", lambda e, c=c, i=i: e.copy(out=h2fm[:, c * R + i * P:c * R + (i + 1) * P],
                                                      in_=ptr[:]), r=[ptr], w=[h2fm])
        for k in range(8):
            p.op("tensor", lambda e, k=k, i=i: e.matmul(
                pout[:, 0:E], h2fm[:, k * R + i * P:k * R + (i + 1) * P], wg_sb[:, k * E:(k + 1) * E],
                start=(k == 0), stop=(k == 7)), r=[h2fm, wg_sb], w=[pout])
        p.op("vector", lambda e: e.tensor_add(out=gt32[:], in0=pout[:, 0:E], in1=bg_bc[:, 0:E]),
             r=[pout, bg_bc], w=[gt32])
        p.op("vector", lambda e: e.tensor_reduce(out=st[:, 6:7], in_=gt32[:], axis=AX.X, op=ALU.max),
             r=[gt32], w=[st])
        p.op("vector", lambda e: e.tensor_scalar(out=gt32[:], in0=gt32[:], scalar1=st[:, 6:7],
                                                 scalar2=None, op0=ALU.subtract), r=[gt32, st], w=[gt32])
        p.op("scalar", lambda e: e.activation(out=gt32[:], in_=gt32[:], func=AF.Exp,
                                              accum_out=st[:, 7:8]), r=[gt32], w=[gt32, st])
        p.op("vector", lambda e: e.reciprocal(out=st[:, 7:8], in_=st[:, 7:8]), r=[st], w=[st])
        p.op("vector", lambda e, i=i: e.tensor_scalar(out=gate[:, i * E:(i + 1) * E], in0=gt32[:],
                                                      scalar1=st[:, 7:8], scalar2=None, op0=ALU.mult),
             r=[gt32, st], w=[gate])

    b.free(wup_sb)
    b.free(wdn_sb)
    b.free(wq_sb)
    we_sb = [b.sbuf([P, 8 * D], BF16, name=f"we_sb{j}") for j in range(2)]
    xo = xt
    x2r = xn
    for half in range(2):
        hi0 = half * (nT // 2)
        for ci in range(nT // 2):
            p.op("vector", lambda e, ci=ci: e.memset(comb[ci][:, :], 0.0), w=[comb[ci]])
        for ei in range(E):
            wb = we_sb[ei % 2]
            for k in range(8):
                p.dma("sync", wb[:, k * D:(k + 1) * D], we[ei * D + k * P:ei * D + (k + 1) * P, :], w=[wb])
            for ci in range(nT // 2):
                i = hi0 + ci
                for j in range(2):
                    pqx = pq if j == 0 else pq2
                    for k in range(8):
                        p.op("tensor", lambda e, i=i, j=j, k=k, wb=wb, pqx=pqx: e.matmul(
                            pqx[:, :], h2fm[:, k * R + i * P:k * R + (i + 1) * P],
                            wb[:, k * D + j * 512:k * D + (j + 1) * 512],
                            start=(k == 0), stop=(k == 7)), r=[h2fm, wb], w=[pqx])
                    p.op("vector", lambda e, i=i, j=j, ei=ei, ci=ci, pqx=pqx: e.scalar_tensor_tensor(
                        out=comb[ci][:, j * 512:(j + 1) * 512], in0=pqx[:, :],
                        scalar=gate[:, i * E + ei:i * E + ei + 1],
                        in1=comb[ci][:, j * 512:(j + 1) * 512], op0=ALU.mult, op1=ALU.add),
                        r=[pqx, gate, comb[ci]], w=[comb[ci]])
        for ci in range(nT // 2):
            i = hi0 + ci
            p.dma("sync", xo[:], x_in[i * P:(i + 1) * P, :], w=[xo])
            p.dma("gpsimd", x2r[:], x2_scr[i * P:(i + 1) * P, :], r=[x2_scr], w=[x2r])
            p.op("vector", lambda e, ci=ci: e.tensor_add(out=x2r[:], in0=x2r[:], in1=comb[ci][:, :]),
                 r=[x2r, comb[ci]], w=[x2r])
            p.op("vector", lambda e: e.tensor_add(out=x2r[:], in0=x2r[:], in1=xo[:]), r=[x2r, xo], w=[x2r])
            p.dma("sync", out[i * P:(i + 1) * P, :], x2r[:], r=[x2r], w=[out])
    p.op("gpsimd", lambda e: e.engine_nop(), no_inc=True, r=[out])
    nc = b.build()
    _prog_cache[key] = nc
    return nc


def run_l0ffn(x, gamma1, Wq, bq_, Wdown, bdown, Wup, bup_, Wo, bo_, gamma2, Wg, bg_, We):
    B, T, D = x.shape
    rows = x.reshape(B * T, D)
    R = (B * T) // NCORES
    nc = build_l0ffn(R)
    eye = np.eye(P, dtype=np.float32).astype(NP_BF16)
    bdm = np.zeros((P, P), np.float32)
    for t in range(8):
        bdm[t * 16:(t + 1) * 16, t * 16:(t + 1) * 16] = 1.0
    wts = {"eye": eye, "wq": _bf(Wq), "bq": bq_.reshape(-1, 1), "wdn": _bf(Wdown),
           "bdn": bdown.reshape(-1, 1), "wup": _bf(Wup), "bup": bup_.reshape(-1, 1),
           "wo": _bf(Wo), "bo": bo_.reshape(-1, 1), "g1": gamma1.reshape(-1, 1),
           "g2": gamma2.reshape(-1, 1), "wg": _bf(Wg), "bg": bg_.reshape(-1, 1),
           "we": _bf(We.reshape(8 * 1024, 1024)), "bdm": bdm}
    in_maps = []
    for c in range(NCORES):
        im = dict(wts)
        im["x"] = np.ascontiguousarray(rows[c * R:(c + 1) * R])
        in_maps.append(im)
    res = run_bass_kernel_spmd(nc, in_maps, core_ids=list(range(NCORES)))
    outr = np.concatenate([res.results[c]["out"] for c in range(NCORES)], 0)
    return outr.reshape(B, T, D)


def build_attnb_proj(R=2048):
    key = ("attnb_proj", R)
    if key in _prog_cache:
        return _prog_cache[key]
    D = 1024
    nT = R // P
    b = Builder()
    p = b.p
    x_in = b.din("x", [R, D])
    b.eye_hbm = b.din("eye", [P, P], BF16)
    wqT = b.din("wqT", [D, D], BF16)
    wkT = b.din("wkT", [D, D], BF16)
    wvT = b.din("wvT", [D, D], BF16)
    qs = b.din("qscale", [P, 1])
    q_out = b.dout("q", [R, D], BF16)
    k_out = b.dout("k", [R, D], BF16)
    v_out = b.dout("v", [R, D], BF16)

    eyeT = b.get_eye()
    ptr = b.get_ptr()
    qsc = b.sbuf([P, 1], name="qsc")
    p.dma("sync", qsc[:], qs[:, :], w=[qsc])
    w_sb = {}
    for nm, src in (("q", wqT), ("k", wkT), ("v", wvT)):
        t = b.sbuf([P, 8 * D], BF16, name=f"w{nm}sb")
        for kk in range(8):
            p.dma("sync", t[:, kk * D:(kk + 1) * D], src[kk * P:(kk + 1) * P, :], w=[t])
        w_sb[nm] = t
    xt = b.sbuf([P, D], name="xta")
    xn = b.sbuf([P, D], name="xna")
    sq = b.sbuf([P, D], name="sqa")
    st = b.sbuf([P, 8], name="sta")
    xnb = b.sbuf([P, D], BF16, name="xnba")
    xf = b.sbuf([P, 8 * P], BF16, name="xfa")
    pm = b.psum([P, 512], name="pma")
    ob = b.sbuf([P, D], BF16, name="oba")
    for i in range(nT):
        p.dma("sync", xt[:], x_in[i * P:(i + 1) * P, :], w=[xt])
        emit_plain_ln(b, xt, xn, sq, st, D)
        p.op("vector", lambda e: e.tensor_copy(out=xnb[:], in_=xn[:]), r=[xn], w=[xnb])
        for c in range(8):
            p.op("tensor", lambda e, c=c: e.transpose(ptr[:], xnb[:, c * P:(c + 1) * P], eyeT[:]),
                 r=[xnb, eyeT], w=[ptr])
            p.op("scalar", lambda e, c=c: e.copy(out=xf[:, c * P:(c + 1) * P], in_=ptr[:]),
                 r=[ptr], w=[xf])
        for nm, outd in (("q", q_out), ("k", k_out), ("v", v_out)):
            for j in range(2):
                for kk in range(8):
                    p.op("tensor", lambda e, nm=nm, j=j, kk=kk: e.matmul(
                        pm[:, :], xf[:, kk * P:(kk + 1) * P],
                        w_sb[nm][:, kk * D + j * 512:kk * D + (j + 1) * 512],
                        start=(kk == 0), stop=(kk == 7)), r=[xf, w_sb[nm]], w=[pm])
                if nm == "q":
                    p.op("vector", lambda e, j=j: e.tensor_scalar(
                        out=ob[:, j * 512:(j + 1) * 512], in0=pm[:, :], scalar1=qsc[:, 0:1],
                        scalar2=None, op0=ALU.mult), r=[pm, qsc], w=[ob])
                else:
                    p.op("scalar", lambda e, j=j: e.copy(out=ob[:, j * 512:(j + 1) * 512],
                                                         in_=pm[:, :]), r=[pm], w=[ob])
            p.dma("sync", outd[i * P:(i + 1) * P, :], ob[:], r=[ob], w=[outd])
    p.op("gpsimd", lambda e: e.engine_nop(), no_inc=True, r=[q_out, k_out, v_out])
    nc = b.build()
    _prog_cache[key] = nc
    return nc


def build_attnb_attn(T=8192, SLOT=512, NSLOT=2):
    key = ("attnb_attn", T, SLOT, NSLOT)
    if key in _prog_cache:
        return _prog_cache[key]
    D = 1024
    GS = 1024
    nG = T // GS
    b = Builder()
    p = b.p
    qf = [b.din(f"qf{s}", [D, 2 * SLOT], BF16) for s in range(NSLOT)]
    kT0 = b.din("kT0", [D, T], BF16)
    kT1 = b.din("kT1", [D, T], BF16)
    v0 = b.din("v0", [T, D], BF16)
    v1 = b.din("v1", [T, D], BF16)
    rowp = b.din("rowp", [1, NSLOT * SLOT])
    pidx = b.din("pidx", [P, 1])
    xr = b.din("xr", [2 * NSLOT * SLOT, D])
    out = b.dout("out", [2 * NSLOT * SLOT, D])

    onesr = b.get_onesr()
    onesc = b.sbuf([P, 1], BF16, name="onescb")
    p.op("vector", lambda e: e.memset(onesc[:], 1.0), w=[onesc])
    pmisc = b.get_pmisc()
    rp_sb = b.sbuf([1, NSLOT * SLOT], name="rp_sb")
    p.dma("sync", rp_sb[:, :], rowp[:, :], w=[rp_sb])
    rp_bc = [b.sbuf([P, SLOT], name=f"rpbc{s}") for s in range(NSLOT)]
    for s in range(NSLOT):
        emit_bcast_row(b, rp_bc[s], rp_sb[0:1, s * SLOT:(s + 1) * SLOT], SLOT, onesr, pmisc,
                       r_extra=[rp_sb])
    kg = [b.sbuf([P, 8 * GS], BF16, name=f"kg{bb}") for bb in range(2)]
    vg = [b.sbuf([P, 8 * D], BF16, name=f"vg{bb}") for bb in range(2)]
    qsb = [b.sbuf([P, 8 * 2 * SLOT], BF16, name=f"qsb{s}") for s in range(NSLOT)]
    for s in range(NSLOT):
        for kk in range(8):
            p.dma("sync", qsb[s][:, kk * 2 * SLOT:(kk + 1) * 2 * SLOT],
                  qf[s][kk * P:(kk + 1) * P, :], w=[qsb[s]])
    acc = [[b.sbuf([P, (SLOT // P) * D], name=f"acc{s}_{bb}") for bb in range(2)]
           for s in range(NSLOT)]
    for s in range(NSLOT):
        for bb in range(2):
            p.op("vector", lambda e, s=s, bb=bb: e.memset(acc[s][bb][:, :], 0.0), w=[acc[s][bb]])
    pSv = b.psum([1, 2 * D], name="pSv")
    pD = b.psum([P, SLOT], name="pD")
    pV = b.psum([P, D], name="pV")
    scol = b.sbuf([P, 1], name="scol")
    siota = b.sbuf([P, 1], name="siota")
    p.dma("sync", siota[:], pidx[:, :], w=[siota])
    a0 = b.sbuf([P, SLOT], BF16, name="a0b")
    a1 = b.sbuf([P, SLOT], BF16, name="a1b")
    msk = b.sbuf([P, SLOT], name="mskb")
    ae = [b.sbuf([P, 8 * SLOT], BF16, name=f"ae{bb}") for bb in range(2)]
    Svbc = [b.sbuf([P, D], name=f"Svbc{bb}") for bb in range(2)]
    xrt = b.sbuf([P, D], name="xrtb")
    outt = b.sbuf([P, D], name="outtb")

    for g in range(nG):
        for bb, (kT, vv) in enumerate(((kT0, v0), (kT1, v1))):
            for kk in range(8):
                p.dma("sync", kg[bb][:, kk * GS:(kk + 1) * GS],
                      kT[kk * P:(kk + 1) * P, g * GS:(g + 1) * GS], w=[kg[bb]])
            for j in range(8):
                p.dma("gpsimd", vg[bb][:, j * D:(j + 1) * D],
                      vv[g * GS + j * P:g * GS + (j + 1) * P, :], w=[vg[bb]])
            for j in range(8):
                for n in range(2):
                    p.op("tensor", lambda e, bb=bb, j=j, n=n, g=g: e.matmul(
                        pSv[0:1, bb * D + n * 512:bb * D + (n + 1) * 512], onesc[:, 0:1],
                        vg[bb][:, j * D + n * 512:j * D + (n + 1) * 512],
                        start=(g == 0 and j == 0), stop=(g == nG - 1 and j == 7)),
                        r=[onesc, vg[bb]], w=[pSv])
        for s in range(NSLOT):
            for sc in range(8):
                for bb in range(2):
                    for kk in range(8):
                        p.op("tensor", lambda e, s=s, sc=sc, bb=bb, kk=kk: e.matmul(
                            pD[:, :], kg[bb][:, kk * GS + sc * P:kk * GS + (sc + 1) * P],
                            qsb[s][:, kk * 2 * SLOT + bb * SLOT:kk * 2 * SLOT + (bb + 1) * SLOT],
                            start=(bb == 0 and kk == 0), stop=(bb == 1 and kk == 7)),
                            r=[kg[bb], qsb[s]], w=[pD])
                p.op("scalar", lambda e: e.activation(out=a0[:], in_=pD[:, :], func=AF.Sigmoid),
                     r=[pD], w=[a0])
                p.op("scalar", lambda e: e.activation(out=a1[:], in_=pD[:, :], func=AF.Sigmoid,
                                                      scale=-1.0), r=[pD], w=[a1])
                p.op("vector", lambda e, g=g, sc=sc: e.tensor_scalar(
                    out=scol[:], in0=siota[:], scalar1=float(g * GS + sc * P), scalar2=None,
                    op0=ALU.add), r=[siota], w=[scol])
                p.op("vector", lambda e, s=s: e.tensor_scalar(
                    out=msk[:], in0=rp_bc[s][:, :], scalar1=scol[:, 0:1], scalar2=None,
                    op0=ALU.is_ge), r=[rp_bc[s], scol], w=[msk])
                for bb, at in ((0, a0), (1, a1)):
                    p.op("vector", lambda e, at=at, bb=bb, sc=sc: e.scalar_tensor_tensor(
                        out=ae[bb][:, sc * SLOT:(sc + 1) * SLOT], in0=at[:], scalar=1e-9,
                        in1=msk[:], op0=ALU.subtract, op1=ALU.mult),
                        r=[at, msk], w=[ae[bb]])
            for bb in range(2):
                for mt in range(SLOT // P):
                    for n in range(2):
                        for sc in range(8):
                            p.op("tensor", lambda e, bb=bb, mt=mt, n=n, sc=sc, s=s: e.matmul(
                                pV[:, n * 512:(n + 1) * 512],
                                ae[bb][:, sc * SLOT + mt * P:sc * SLOT + (mt + 1) * P],
                                vg[bb][:, sc * D + n * 512:sc * D + (n + 1) * 512],
                                start=(sc == 0), stop=(sc == 7)), r=[ae[bb], vg[bb]], w=[pV])
                    p.op("vector", lambda e, s=s, bb=bb, mt=mt: e.tensor_add(
                        out=acc[s][bb][:, mt * D:(mt + 1) * D], in0=acc[s][bb][:, mt * D:(mt + 1) * D],
                        in1=pV[:, :]), r=[pV, acc[s][bb]], w=[acc[s][bb]])
    for bb in range(2):
        p.op("scalar", lambda e, bb=bb: e.copy(out=Svbc[0][0:1, 0:D] if bb == 0 else Svbc[1][0:1, 0:D],
                                               in_=pSv[0:1, bb * D:(bb + 1) * D]), r=[pSv], w=[Svbc[bb]])
        emit_bcast_row(b, Svbc[bb], Svbc[bb][0:1, 0:D], D, onesr, pmisc, r_extra=[Svbc[bb]])
    for bb in range(2):
        for s in range(NSLOT):
            for mt in range(SLOT // P):
                r0 = bb * NSLOT * SLOT + s * SLOT + mt * P
                p.dma("sync", xrt[:], xr[r0:r0 + P, :], w=[xrt])
                p.op("vector", lambda e, s=s, bb=bb, mt=mt: e.scalar_tensor_tensor(
                    out=outt[:], in0=Svbc[bb][:, :], scalar=1e-9, in1=acc[s][bb][:, mt * D:(mt + 1) * D],
                    op0=ALU.mult, op1=ALU.add), r=[Svbc[bb], acc[s][bb]], w=[outt])
                p.op("vector", lambda e: e.tensor_add(out=outt[:], in0=outt[:], in1=xrt[:]),
                     r=[outt, xrt], w=[outt])
                p.dma("sync", out[r0:r0 + P, :], outt[:], r=[outt], w=[out])
    p.op("gpsimd", lambda e: e.engine_nop(), no_inc=True, r=[out])
    nc = b.build()
    _prog_cache[key] = nc
    return nc


def run_attnb(x):
    raise NotImplementedError


def run_attnb_full(x, Wq, Wk, Wv):
    B, T, D = x.shape
    rows = x.reshape(B * T, D)
    R = (B * T) // NCORES
    eye = np.eye(P, dtype=np.float32).astype(NP_BF16)
    nc1 = build_attnb_proj(R)
    in_maps = []
    for c in range(NCORES):
        sgn = 1.0 if c < 4 else -1.0
        in_maps.append({"x": np.ascontiguousarray(rows[c * R:(c + 1) * R]), "eye": eye,
                        "wqT": _bf(Wq.T), "wkT": _bf(Wk.T), "wvT": _bf(Wv.T),
                        "qscale": np.full((P, 1), sgn / 32.0, np.float32)})
    res = run_bass_kernel_spmd(nc1, in_maps, core_ids=list(range(NCORES)))
    q = np.concatenate([np.asarray(res.results[c]["q"]) for c in range(NCORES)], 0).reshape(B, T, D)
    k = np.concatenate([np.asarray(res.results[c]["k"]) for c in range(NCORES)], 0).reshape(B, T, D)
    v = np.concatenate([np.asarray(res.results[c]["v"]) for c in range(NCORES)], 0).reshape(B, T, D)
    kT0 = np.ascontiguousarray(k[0].reshape(D, T))
    kT1 = np.ascontiguousarray(k[1].reshape(D, T))
    nc2 = build_attnb_attn(T, 512, 2)
    in_maps = []
    slot_of = [(c, 15 - c) for c in range(NCORES)]
    for c in range(NCORES):
        s0, s1 = slot_of[c]
        qf = []
        for s in (s0, s1):
            q0 = np.ascontiguousarray(q[0, s * 512:(s + 1) * 512].T)
            q1 = np.ascontiguousarray(q[1, s * 512:(s + 1) * 512].T)
            qf.append(np.concatenate([q0, q1], 1))
        rowp = np.concatenate([np.arange(s * 512, (s + 1) * 512, dtype=np.float32)
                               for s in (s0, s1)]).reshape(1, -1)
        xrows = np.concatenate([x[bb, s * 512:(s + 1) * 512] for bb in range(2) for s in (s0, s1)], 0)
        in_maps.append({"qf0": qf[0].astype(NP_BF16, copy=False), "qf1": qf[1].astype(NP_BF16, copy=False),
                        "kT0": kT0, "kT1": kT1, "v0": np.ascontiguousarray(v[0]),
                        "v1": np.ascontiguousarray(v[1]), "rowp": rowp,
                        "pidx": np.arange(P, dtype=np.float32).reshape(P, 1),
                        "xr": np.ascontiguousarray(xrows)})
    res2 = run_bass_kernel_spmd(nc2, in_maps, core_ids=list(range(NCORES)))
    xout = np.empty((B, T, D), np.float32)
    for c in range(NCORES):
        s0, s1 = slot_of[c]
        o = np.asarray(res2.results[c]["out"])
        for bi, bb in enumerate(range(2)):
            for si, s in enumerate((s0, s1)):
                xout[bb, s * 512:(s + 1) * 512] = o[(bi * 2 + si) * 512:(bi * 2 + si + 1) * 512]
    return xout


def build_l1ab_proj(R=2048):
    key = ("l1ab_proj", R)
    if key in _prog_cache:
        return _prog_cache[key]
    D, DO = 1024, 3072
    nT = R // P
    b = Builder()
    p = b.p
    x_in = b.din("x", [R, D])
    b.eye_hbm = b.din("eye", [P, P], BF16)
    wT = b.din("wT", [D, DO], BF16)
    qkv_out = b.dout("qkv", [R, DO], BF16)
    eyeT = b.get_eye()
    ptr = b.get_ptr()
    w_sb = b.sbuf([P, 8 * DO], BF16, name="wqkvsb")
    for kk in range(8):
        p.dma("sync", w_sb[:, kk * DO:(kk + 1) * DO], wT[kk * P:(kk + 1) * P, :], w=[w_sb])
    xt = b.sbuf([P, D], name="xtc")
    xn = b.sbuf([P, D], name="xnc")
    sq = b.sbuf([P, D], name="sqc")
    st = b.sbuf([P, 8], name="stc")
    xnb = b.sbuf([P, D], BF16, name="xnbc")
    xf = b.sbuf([P, 8 * P], BF16, name="xfc")
    pm = b.psum([P, 512], name="pmc")
    ob = b.sbuf([P, DO], BF16, name="obc")
    for i in range(nT):
        p.dma("sync", xt[:], x_in[i * P:(i + 1) * P, :], w=[xt])
        emit_plain_ln(b, xt, xn, sq, st, D)
        p.op("vector", lambda e: e.tensor_copy(out=xnb[:], in_=xn[:]), r=[xn], w=[xnb])
        for c in range(8):
            p.op("tensor", lambda e, c=c: e.transpose(ptr[:], xnb[:, c * P:(c + 1) * P], eyeT[:]),
                 r=[xnb, eyeT], w=[ptr])
            p.op("scalar", lambda e, c=c: e.copy(out=xf[:, c * P:(c + 1) * P], in_=ptr[:]),
                 r=[ptr], w=[xf])
        for j in range(6):
            for kk in range(8):
                p.op("tensor", lambda e, j=j, kk=kk: e.matmul(
                    pm[:, :], xf[:, kk * P:(kk + 1) * P],
                    w_sb[:, kk * DO + j * 512:kk * DO + (j + 1) * 512],
                    start=(kk == 0), stop=(kk == 7)), r=[xf, w_sb], w=[pm])
            p.op("scalar", lambda e, j=j: e.copy(out=ob[:, j * 512:(j + 1) * 512], in_=pm[:, :]),
                 r=[pm], w=[ob])
        p.dma("sync", qkv_out[i * P:(i + 1) * P, :], ob[:], r=[ob], w=[qkv_out])
    p.op("gpsimd", lambda e: e.engine_nop(), no_inc=True, r=[qkv_out])
    nc = b.build()
    _prog_cache[key] = nc
    return nc


def build_l1ab_attn(T=8192, CH=512, NH=4):
    key = ("l1ab_attn", T, CH, NH)
    if key in _prog_cache:
        return _prog_cache[key]
    Dh = 64
    nC = T // CH
    b = Builder()
    p = b.p
    qfm = b.din("qfm", [NH * Dh, T], BF16)
    kfm = b.din("kfm", [NH * Dh, T], BF16)
    vtk = b.din("vtk", [T, NH * Dh], BF16)
    a_in = b.din("a_in", [1, 1])
    b_in = b.din("b_in", [1, 1])
    dt_in = b.din("dt_in", [1, 1])
    out = b.dout("out", [T, NH * Dh], BF16)
    fs_scr = b.dscratch("fs_scr", [NH * nC, CH])

    onesr = b.get_onesr()
    pmisc = b.get_pmisc()
    ones1 = b.sbuf([1, P], BF16, name="ones1b")
    p.op("vector", lambda e: e.memset(ones1[:], 1.0), w=[ones1])
    onescb = b.sbuf([P, 1], BF16, name="onescb2")
    p.op("vector", lambda e: e.memset(onescb[:], 1.0), w=[onescb])

    q_sb = b.sbuf([P, 2 * T], BF16, name="q_sb2")
    k_sb = b.sbuf([P, 2 * T], BF16, name="k_sb2")
    v_sb = b.sbuf([P, (T // P) * NH * Dh], BF16, name="v_sb2")
    for j in range(2):
        p.dma("sync", q_sb[:, j * T:(j + 1) * T], qfm[j * P:(j + 1) * P, :], w=[q_sb])
        p.dma("sync", k_sb[:, j * T:(j + 1) * T], kfm[j * P:(j + 1) * P, :], w=[k_sb])
    for j in range(T // P):
        p.dma("sync", v_sb[:, j * NH * Dh:(j + 1) * NH * Dh], vtk[j * P:(j + 1) * P, :], w=[v_sb])
    ab_sb = b.sbuf([1, 8], name="ab_sb")
    p.dma("sync", ab_sb[0:1, 0:1], a_in[:, :], w=[ab_sb])
    p.dma("sync", ab_sb[0:1, 1:2], b_in[:, :], w=[ab_sb])
    p.dma("sync", ab_sb[0:1, 2:3], dt_in[:, :], w=[ab_sb])
    p.op("scalar", lambda e: e.activation(out=ab_sb[0:1, 3:4], in_=ab_sb[0:1, 2:3],
                                          func=AF.Identity, scale=1.0 / 12.5), r=[ab_sb], w=[ab_sb])
    p.op("vector", lambda e: e.tensor_tensor(out=ab_sb[0:1, 4:5], in0=ab_sb[0:1, 3:4],
                                             in1=ab_sb[0:1, 1:2], op=ALU.mult), r=[ab_sb], w=[ab_sb])
    onec2 = b.cconst(1.0)
    p.op("scalar", lambda e, onec2=onec2: e.activation(out=ab_sb[0:1, 5:6], in_=ab_sb[0:1, 4:5],
                                                       func=AF.Identity, bias=onec2[0:1, 0:1]),
         r=[ab_sb, b._cpool], w=[ab_sb])
    p.op("vector", lambda e: e.reciprocal(out=ab_sb[0:1, 6:7], in_=ab_sb[0:1, 5:6]), r=[ab_sb], w=[ab_sb])
    ab64 = b.sbuf([64, 8], name="ab64")
    p.op("tensor", lambda e: e.matmul(pmisc[0:64, 0:7], onesr[0:1, 0:64], ab_sb[0:1, 0:7],
                                      start=True, stop=True), r=[onesr, ab_sb], w=[pmisc])
    p.op("scalar", lambda e: e.copy(out=ab64[:, 0:7], in_=pmisc[0:64, 0:7]), r=[pmisc], w=[ab64])

    sqk = b.sbuf([P, T], BF16, name="sqk2")
    kn_row = b.sbuf([1, T], name="kn_row2")
    pkn = b.psum([1, 512], name="pkn2")
    sqk2 = b.sbuf([P, T], BF16, name="sqk3")
    p.op("scalar", lambda e: e.activation(out=sqk[:], in_=k_sb[:, 0:T], func=AF.Square),
         r=[k_sb], w=[sqk])
    p.op("scalar", lambda e: e.activation(out=sqk2[:], in_=k_sb[:, T:2 * T], func=AF.Square),
         r=[k_sb], w=[sqk2])
    for c in range(T // 512):
        p.op("tensor", lambda e, c=c: e.matmul(pkn[0:1, :], onescb[:, 0:1],
                                               sqk[:, c * 512:(c + 1) * 512], start=True, stop=False),
             r=[onescb, sqk], w=[pkn])
        p.op("tensor", lambda e, c=c: e.matmul(pkn[0:1, :], onescb[:, 0:1],
                                               sqk2[:, c * 512:(c + 1) * 512], start=False, stop=True),
             r=[onescb, sqk2], w=[pkn])
        p.op("scalar", lambda e, c=c: e.copy(out=kn_row[0:1, c * 512:(c + 1) * 512], in_=pkn[0:1, :]),
             r=[pkn], w=[kn_row])
    p.op("vector", lambda e: e.tensor_reduce(out=ab_sb[0:1, 7:8], in_=kn_row[0:1, :], axis=AX.X,
                                             op=ALU.max), r=[kn_row], w=[ab_sb])
    b.free(sqk2)
    b.free(kn_row)
    b.free(sqk)

    lrows = b.sbuf([NH * nC, CH], name="lrows2")
    frows = b.sbuf([NH * nC, CH], name="frows2")
    tmpr = b.sbuf([1, CH], name="tmpr2")
    sqq = b.sbuf([64, CH], BF16, name="sqq2")
    bndr = b.sbuf([1, CH], name="bndr2")
    bndb = b.sbuf([1, CH], BF16, name="bndb2")
    probsT = b.sbuf([P, (T // P) * CH], BF16, name="probsT2")
    pl = b.psum([1, CH], name="pl2")
    psc2b = [b.psum([P, CH], name=f"psc2{j}") for j in range(2)]
    pout = b.psum([P, 4 * Dh], name="pout2")
    colf = b.sbuf([P, 1], name="colf2")
    osb = b.sbuf([P, NH * Dh], BF16, name="osb2")
    o_scr = b.dscratch("o_scr2", [T, NH * Dh], BF16)

    for h in range(NH):
        jt = h // 2
        hb = (h % 2) * 64
        for ci in range(nC):
            ns = (ci + 1) * (CH // P)
            p.op("scalar", lambda e, jt=jt, hb=hb, ci=ci: e.activation(
                out=sqq[:, :], in_=q_sb[0 + hb:64 + hb, jt * T + ci * CH:jt * T + (ci + 1) * CH]
                if False else q_sb[hb:hb + 64, jt * T + ci * CH:jt * T + (ci + 1) * CH],
                func=AF.Square), r=[q_sb], w=[sqq])
            p.op("tensor", lambda e: e.matmul(pmisc[0:1, 0:CH], onescb[0:64, 0:1], sqq[:, :],
                                              start=True, stop=True), r=[onescb, sqq], w=[pmisc])
            p.op("vector", lambda e: e.tensor_scalar(out=bndr[0:1, :], in0=pmisc[0:1, 0:CH],
                                                     scalar1=ab_sb[0:1, 7:8], scalar2=None,
                                                     op0=ALU.mult), r=[pmisc, ab_sb], w=[bndr])
            p.op("scalar", lambda e: e.activation(out=bndr[0:1, :], in_=bndr[0:1, :], func=AF.Sqrt),
                 r=[bndr], w=[bndr])
            p.op("scalar", lambda e: e.activation(out=bndb[0:1, :], in_=bndr[0:1, :], func=AF.Identity,
                                                  scale=-1.0), r=[bndr], w=[bndb])
            for sc in range(ns):
                psc = psc2b[sc % 2]
                p.op("tensor", lambda e, jt=jt, hb=hb, ci=ci, sc=sc, psc=psc: e.matmul(
                    psc[:, :], k_sb[hb:hb + 64, jt * T + sc * P:jt * T + (sc + 1) * P],
                    q_sb[hb:hb + 64, jt * T + ci * CH:jt * T + (ci + 1) * CH],
                    start=True, stop=False), r=[k_sb, q_sb], w=[psc])
                p.op("tensor", lambda e, psc=psc: e.matmul(psc[:, :], ones1[0:1, 0:P], bndb[0:1, :],
                                                  start=False, stop=True), r=[ones1, bndb], w=[psc])
                p.op("scalar", lambda e, sc=sc, psc=psc: e.activation(out=probsT[:, sc * CH:(sc + 1) * CH],
                                                             in_=psc[:, :], func=AF.Exp, scale=0.125),
                     r=[psc], w=[probsT])
            for sc in range(ns):
                p.op("tensor", lambda e, sc=sc, ns=ns: e.matmul(pl[0:1, :], onescb[:, 0:1],
                                                                probsT[:, sc * CH:(sc + 1) * CH],
                                                                start=(sc == 0), stop=(sc == ns - 1)),
                     r=[onescb, probsT], w=[pl])
            p.op("scalar", lambda e: e.copy(out=tmpr[0:1, :], in_=pl[0:1, :]), r=[pl], w=[tmpr])
            p.dma("gpsimd", lrows[h * nC + ci:h * nC + ci + 1, :], tmpr[0:1, :], r=[tmpr], w=[lrows])
            for mt in range(CH // P):
                for sc in range(ns):
                    p.op("tensor", lambda e, mt=mt, sc=sc, ns=ns: e.matmul(
                        pout[:, 0:NH * Dh], probsT[:, sc * CH + mt * P:sc * CH + (mt + 1) * P],
                        v_sb[:, sc * NH * Dh:(sc + 1) * NH * Dh],
                        start=(sc == 0), stop=(sc == ns - 1)), r=[probsT, v_sb], w=[pout])
                p.op("scalar", lambda e, mt=mt, h=h: e.copy(
                    out=osb[:, 0:Dh], in_=pout[:, h * Dh:(h + 1) * Dh]), r=[pout], w=[osb])
                p.dma("sync", o_scr[ci * CH + mt * P:ci * CH + (mt + 1) * P, h * Dh:(h + 1) * Dh],
                      osb[:, 0:Dh], r=[osb], w=[o_scr])

    en = b.sbuf([NH * nC, CH], name="en2")
    t1 = b.sbuf([NH * nC, CH], name="t1_2")
    t2 = b.sbuf([NH * nC, CH], name="t2_2")
    t3 = b.sbuf([NH * nC, CH], name="t3_2")
    vv = b.sbuf([NH * nC, CH], name="vv2")
    ww = b.sbuf([NH * nC, CH], name="ww2")
    NR = NH * nC
    p.op("vector", lambda e: e.reciprocal(out=en[:, :], in_=lrows[:, :]), r=[lrows], w=[en])
    p.op("vector", lambda e: e.tensor_mul(out=en[:, :], in0=en[:, :], in1=lrows[:, :]),
         r=[en, lrows], w=[en])
    p.op("scalar", lambda e: e.activation(out=t1[:, :], in_=en[:, :], func=AF.Abs), r=[en], w=[t1])
    p.op("vector", lambda e: e.tensor_scalar_max(out=t2[:, :], in0=t1[:, :], scalar1=1e-6),
         r=[t1], w=[t2])
    p.op("vector", lambda e: e.reciprocal(out=t3[:, :], in_=t2[:, :]), r=[t2], w=[t3])
    p.op("vector", lambda e: e.tensor_mul(out=t2[:, :], in0=en[:, :], in1=t3[:, :]),
         r=[en, t3], w=[t2])
    fivec = b.cconst(-5.0)
    p.op("scalar", lambda e, fivec=fivec: e.activation(out=t1[:, :], in_=t1[:, :], func=AF.Sigmoid,
                                                       scale=10.0, bias=fivec[0:64, 0:1]), r=[t1, b._cpool], w=[t1])
    p.op("vector", lambda e: e.tensor_scalar(out=t1[:, :], in0=t1[:, :], scalar1=0.9, scalar2=0.1,
                                             op0=ALU.mult, op1=ALU.add), r=[t1], w=[t1])
    p.op("vector", lambda e: e.tensor_mul(out=t1[:, :], in0=t2[:, :], in1=t1[:, :]),
         r=[t1, t2], w=[t1])
    p.op("vector", lambda e: e.memset(vv[:, :], 0.0), w=[vv])
    p.op("vector", lambda e: e.memset(ww[:, :], 0.0), w=[ww])
    for step in range(2):
        p.op("scalar", lambda e: e.activation(out=t2[:, :], in_=vv[:, :], func=AF.Square),
             r=[vv], w=[t2])
        p.op("vector", lambda e: e.tensor_mul(out=t2[:, :], in0=t2[:, :], in1=vv[:, :]),
             r=[t2, vv], w=[t2])
        p.op("vector", lambda e: e.tensor_scalar(out=t2[:, :], in0=t2[:, :], scalar1=-1.0 / 3.0,
                                                 scalar2=None, op0=ALU.mult), r=[t2], w=[t2])
        p.op("vector", lambda e: e.tensor_add(out=t2[:, :], in0=t2[:, :], in1=vv[:, :]),
             r=[t2, vv], w=[t2])
        p.op("vector", lambda e: e.tensor_sub(out=t2[:, :], in0=t2[:, :], in1=ww[:, :]),
             r=[t2, ww], w=[t2])
        p.op("vector", lambda e: e.tensor_add(out=t2[:, :], in0=t2[:, :], in1=t1[:, :]),
             r=[t2, t1], w=[t2])
        p.op("vector", lambda e: e.scalar_tensor_tensor(out=t2[:, :], in0=t2[:, :],
                                                        scalar=ab64[:, 2:3], in1=vv[:, :],
                                                        op0=ALU.mult, op1=ALU.add),
             r=[t2, ab64, vv], w=[t2])
        p.op("vector", lambda e: e.tensor_scalar(out=frows[:, :],
                                                 in0=t2[:, :], scalar1=ab64[:, 0:1],
                                                 scalar2=ab64[:, 3:4], op0=ALU.add, op1=ALU.mult),
             r=[t2, ab64], w=[frows])
        p.op("vector", lambda e: e.tensor_add(out=ww[:, :], in0=ww[:, :], in1=frows[:, :]),
             r=[ww, frows], w=[ww])
        p.op("vector", lambda e: e.tensor_scalar(out=ww[:, :], in0=ww[:, :],
                                                 scalar1=ab64[:, 6:7], scalar2=None,
                                                 op0=ALU.mult), r=[ww, ab64], w=[ww])
        p.op("vector", lambda e: e.tensor_scalar_min(out=vv[:, :], in0=t2[:, :], scalar1=3.0),
             r=[t2], w=[vv])
        p.op("vector", lambda e: e.tensor_scalar_max(out=vv[:, :], in0=vv[:, :], scalar1=-3.0),
             r=[vv], w=[vv])
        p.op("vector", lambda e: e.tensor_scalar_min(out=ww[:, :], in0=ww[:, :], scalar1=3.0),
             r=[ww], w=[ww])
        p.op("vector", lambda e: e.tensor_scalar_max(out=ww[:, :], in0=ww[:, :], scalar1=-3.0),
             r=[ww], w=[ww])
    p.op("scalar", lambda e: e.activation(out=t1[:, :], in_=en[:, :], func=AF.Abs), r=[en], w=[t1])
    p.op("vector", lambda e: e.tensor_scalar_max(out=t1[:, :], in0=t1[:, :], scalar1=1e-6),
         r=[t1], w=[t1])
    p.op("vector", lambda e: e.tensor_mul(out=t2[:, :], in0=vv[:, :], in1=t1[:, :]),
         r=[vv, t1], w=[t2])
    p.op("scalar", lambda e: e.activation(out=t2[:, :], in_=t2[:, :], func=AF.Sigmoid),
         r=[t2], w=[t2])
    p.op("vector", lambda e: e.tensor_scalar(out=t2[:, :], in0=t2[:, :], scalar1=0.5, scalar2=0.5,
                                             op0=ALU.mult, op1=ALU.add), r=[t2], w=[t2])
    eps8 = b.cconst(1e-8)
    p.op("scalar", lambda e, eps8=eps8: e.activation(out=frows[:, :], in_=t2[:, :], func=AF.Identity,
                                                     bias=eps8[0:64, 0:1]), r=[t2, b._cpool], w=[frows])
    p.op("vector", lambda e: e.reciprocal(out=frows[:, :], in_=frows[:, :]), r=[frows], w=[frows])
    p.op("vector", lambda e: e.tensor_mul(out=frows[:, :], in0=t2[:, :], in1=frows[:, :]),
         r=[t2, frows], w=[frows])
    p.op("vector", lambda e: e.reciprocal(out=t3[:, :], in_=lrows[:, :]), r=[lrows], w=[t3])
    p.op("vector", lambda e: e.tensor_mul(out=frows[:, :], in0=frows[:, :], in1=t3[:, :]),
         r=[frows, t3], w=[frows])
    p.dma("sync", fs_scr[:, :], frows[:, :], r=[frows], w=[fs_scr])
    ot = b.sbuf([P, NH * Dh], BF16, name="ot2")
    fcol = b.sbuf([P, NH], name="fcol2")
    for ci in range(nC):
        for mt in range(CH // P):
            p.dma("sync", ot[:, :], o_scr[ci * CH + mt * P:ci * CH + (mt + 1) * P, :],
                  r=[o_scr], w=[ot])
            for h in range(NH):
                p.dma("gpsimd", fcol[:, h:h + 1],
                      bass.AP(fs_scr.tensor, (h * nC + ci) * CH + mt * P, [[1, P], [1, 1]]),
                      r=[fs_scr], w=[fcol])
            for h in range(NH):
                p.op("vector", lambda e, h=h: e.tensor_scalar(
                    out=ot[:, h * Dh:(h + 1) * Dh], in0=ot[:, h * Dh:(h + 1) * Dh],
                    scalar1=fcol[:, h:h + 1], scalar2=None, op0=ALU.mult), r=[ot, fcol], w=[ot])
            p.dma("sync", out[ci * CH + mt * P:ci * CH + (mt + 1) * P, :], ot[:, :],
                  r=[ot], w=[out])
    p.op("gpsimd", lambda e: e.engine_nop(), no_inc=True, r=[out])
    nc = b.build()
    _prog_cache[key] = nc
    return nc


def build_l1ab_out(R=2048):
    key = ("l1ab_out", R)
    if key in _prog_cache:
        return _prog_cache[key]
    D = 1024
    nT = R // P
    b = Builder()
    p = b.p
    x_in = b.din("x", [R, D])
    oin = b.din("oin", [R, D], BF16)
    b.eye_hbm = b.din("eye", [P, P], BF16)
    wT = b.din("wT", [D, D], BF16)
    out = b.dout("out", [R, D])
    eyeT = b.get_eye()
    ptr = b.get_ptr()
    w_sb = b.sbuf([P, 8 * D], BF16, name="woutsb")
    for kk in range(8):
        p.dma("sync", w_sb[:, kk * D:(kk + 1) * D], wT[kk * P:(kk + 1) * P, :], w=[w_sb])
    ot = b.sbuf([P, D], BF16, name="otd")
    of = b.sbuf([P, 8 * P], BF16, name="ofd")
    xt = b.sbuf([P, D], name="xtd")
    pm = b.psum([P, 512], name="pmd")
    rt = b.sbuf([P, D], name="rtd")
    for i in range(nT):
        p.dma("sync", ot[:], oin[i * P:(i + 1) * P, :], w=[ot])
        p.dma("gpsimd", xt[:], x_in[i * P:(i + 1) * P, :], w=[xt])
        for c in range(8):
            p.op("tensor", lambda e, c=c: e.transpose(ptr[:], ot[:, c * P:(c + 1) * P], eyeT[:]),
                 r=[ot, eyeT], w=[ptr])
            p.op("scalar", lambda e, c=c: e.copy(out=of[:, c * P:(c + 1) * P], in_=ptr[:]),
                 r=[ptr], w=[of])
        for j in range(2):
            for kk in range(8):
                p.op("tensor", lambda e, j=j, kk=kk: e.matmul(
                    pm[:, :], of[:, kk * P:(kk + 1) * P],
                    w_sb[:, kk * D + j * 512:kk * D + (j + 1) * 512],
                    start=(kk == 0), stop=(kk == 7)), r=[of, w_sb], w=[pm])
            p.op("vector", lambda e, j=j: e.tensor_add(out=rt[:, j * 512:(j + 1) * 512],
                                                       in0=pm[:, :], in1=xt[:, j * 512:(j + 1) * 512]),
                 r=[pm, xt], w=[rt])
        p.dma("sync", out[i * P:(i + 1) * P, :], rt[:], r=[rt], w=[out])
    p.op("gpsimd", lambda e: e.engine_nop(), no_inc=True, r=[out])
    nc = b.build()
    _prog_cache[key] = nc
    return nc


def run_l1ab(x, W_qkv, W_out, a_s, b_s, dt_s):
    B, T, D = x.shape
    rows = x.reshape(B * T, D)
    R = (B * T) // NCORES
    eye = np.eye(P, dtype=np.float32).astype(NP_BF16)
    nc1 = build_l1ab_proj(R)
    in_maps = [{"x": np.ascontiguousarray(rows[c * R:(c + 1) * R]), "eye": eye,
                "wT": _bf(W_qkv.T)} for c in range(NCORES)]
    res = run_bass_kernel_spmd(nc1, in_maps, core_ids=list(range(NCORES)))
    qkv = np.concatenate([np.asarray(res.results[c]["qkv"]) for c in range(NCORES)], 0)
    qkv = qkv.reshape(B, T, 3, 16, 64)
    nc2 = build_l1ab_attn(T, 512, 4)
    in_maps = []
    for c in range(NCORES):
        bb, hg = c // 4, c % 4
        hs = slice(hg * 4, hg * 4 + 4)
        q = np.ascontiguousarray(qkv[bb, :, 0, hs].transpose(1, 2, 0).reshape(256, T))
        k = np.ascontiguousarray(qkv[bb, :, 1, hs].transpose(1, 2, 0).reshape(256, T))
        v = np.ascontiguousarray(qkv[bb, :, 2, hs].transpose(1, 0, 2).reshape(T, 256))
        in_maps.append({"qfm": q, "kfm": k, "vtk": v,
                        "a_in": np.full((1, 1), a_s, np.float32),
                        "b_in": np.full((1, 1), b_s, np.float32),
                        "dt_in": np.full((1, 1), dt_s, np.float32)})
    res2 = run_bass_kernel_spmd(nc2, in_maps, core_ids=list(range(NCORES)))
    out_all = np.zeros((B, T, 16, 64), NP_BF16)
    for c in range(NCORES):
        bb, hg = c // 4, c % 4
        o = np.asarray(res2.results[c]["out"]).reshape(T, 4, 64)
        out_all[bb, :, hg * 4:hg * 4 + 4] = o
    oin = out_all.reshape(B * T, D)
    nc3 = build_l1ab_out(R)
    in_maps = [{"x": np.ascontiguousarray(rows[c * R:(c + 1) * R]),
                "oin": np.ascontiguousarray(oin[c * R:(c + 1) * R]), "eye": eye,
                "wT": _bf(W_out.T)} for c in range(NCORES)]
    res3 = run_bass_kernel_spmd(nc3, in_maps, core_ids=list(range(NCORES)))
    outr = np.concatenate([res3.results[c]["out"] for c in range(NCORES)], 0)
    return outr.reshape(B, T, D)


# version 06
# speedup vs baseline: 1.1626x; 1.1007x over previous
import numpy as np
import concourse.bass as bass
import concourse.mybir as mybir
from concourse.bass_utils import run_bass_kernel_spmd

F32 = mybir.dt.float32
BF16 = mybir.dt.bfloat16
AF = mybir.ActivationFunctionType
ALU = mybir.AluOpType
AX = mybir.AxisListType

NCORES = 8
P = 128

import ml_dtypes
NP_BF16 = ml_dtypes.bfloat16

ENGINES = ("sync", "vector", "scalar", "tensor", "gpsimd")


class Op:
    __slots__ = ("eng", "emit", "deps", "inc", "cum", "sem")

    def __init__(self, eng, emit, deps, inc, sem):
        self.eng, self.emit, self.deps, self.inc = eng, emit, deps, inc
        self.sem = sem
        self.cum = None


class Prog:

    def __init__(self, nc):
        self.nc = nc
        self.ops = {e: [] for e in ENGINES}
        self.cum = {e: 0 for e in ENGINES}
        self.last_writer = {}
        self.readers = {}
        self.fence_deps = []

    def op(self, eng, emit, r=(), w=(), deps=(), dma=False, no_inc=False):
        dep = set(d for d in deps if d is not None)
        dep.update(self.fence_deps)
        sem = "gpsimd_c" if (eng == "gpsimd" and not dma) else eng
        for buf in r:
            lw = self.last_writer.get(id(buf))
            if lw is not None:
                dep.add(lw)
        for buf in w:
            lw = self.last_writer.get(id(buf))
            if lw is not None:
                dep.add(lw)
            for rd in self.readers.get(id(buf), ()):
                dep.add(rd)
        o = Op(eng, emit, list(dep), 0 if no_inc else (16 if dma else 1), sem)
        self.cum[sem] = self.cum.get(sem, 0) + o.inc
        o.cum = self.cum[sem]
        self.ops[eng].append(o)
        for buf in r:
            self.readers.setdefault(id(buf), []).append(o)
        for buf in w:
            self.last_writer[id(buf)] = o
            self.readers[id(buf)] = []
        return o

    def dma(self, eng, out, in_, r=(), w=(), deps=(), **kw):
        return self.op(eng, lambda e: e.dma_start(out=out, in_=in_, **kw),
                       r=r, w=w, deps=deps, dma=True)

    def _batch_ends(self):
        be = {}
        for eng in ENGINES:
            ops = self.ops[eng]
            i = 0
            while i < len(ops):
                j = i
                while j + 1 < len(ops) and not ops[j + 1].deps:
                    j += 1
                last_by_sem = {}
                for k2 in range(i, j + 1):
                    last_by_sem[ops[k2].sem] = ops[k2].cum
                for k2 in range(i, j + 1):
                    be[ops[k2]] = last_by_sem[ops[k2].sem]
                i = j + 1
        return be

    def finalize(self, block, sems):
        batch_end = self._batch_ends()
        for eng in ENGINES:
            ops = self.ops[eng]
            if not ops:
                continue

            has_dma = any(o.inc == 16 for o in ops)

            def section(e, _ops=ops, _eng=eng, _has_dma=has_dma):
                seen = {}
                prev_dma_cum = 0
                for o in _ops:
                    if o.deps and _has_dma and prev_dma_cum > seen.get(_eng, 0):
                        e.wait_ge(sems[_eng], prev_dma_cum)
                        seen[_eng] = prev_dma_cum
                    need = {}
                    for d in o.deps:
                        if d.sem == "tensor" and _eng == "tensor":
                            continue
                        v = batch_end[d] if d.inc == 16 else d.cum
                        if v > seen.get(d.sem, 0) and v > need.get(d.sem, 0):
                            need[d.sem] = v
                    for sn in sorted(need):
                        e.wait_ge(sems[sn], need[sn])
                        seen[sn] = need[sn]
                    inst = o.emit(e)
                    if o.inc:
                        inst.then_inc(sems[o.sem], o.inc)
                    if o.inc == 16:
                        prev_dma_cum = o.cum

            getattr(block, eng)(section)


class Builder:
    def __init__(self):
        self.nc = bass.Bass("TRN2", target_bir_lowering=False, debug=False)
        self.p = Prog(self.nc)
        self._ctx = []
        self._name_i = 0

    def name(self, pfx):
        self._name_i += 1
        return f"{pfx}{self._name_i}"

    def get_eye(self):
        if not hasattr(self, "_eyeT"):
            self._eyeT = self.sbuf([P, P], BF16, name="eyeTg")
            self.p.dma("sync", self._eyeT[:], self.eye_hbm[:, :], w=[self._eyeT])
        return self._eyeT

    def get_ptr(self):
        if not hasattr(self, "_ptr"):
            self._ptr = self.psum([P, P], BF16, name="ptrg")
        return self._ptr

    def get_pmisc(self):
        if not hasattr(self, "_pmisc"):
            self._pmisc = self.psum([P, 512], name="pmiscg")
        return self._pmisc

    def get_onesr(self):
        if not hasattr(self, "_onesr"):
            self._onesr = self.sbuf([1, P], name="onesrg")
            self.p.op("vector", lambda e: e.memset(self._onesr[:], 1.0), w=[self._onesr])
        return self._onesr

    def cconst(self, value):
        if not hasattr(self, "_cpool"):
            self._cpool = self.sbuf([P, 16], F32, name="cpool")
            self._cvals = {}
        v = float(value)
        if v not in self._cvals:
            k = len(self._cvals)
            assert k < 16
            self._cvals[v] = k
            col = self._cpool[:, k:k + 1]
            self.p.op("vector", lambda e, col=col, v=v: e.memset(col, v), w=[self._cpool])
        k = self._cvals[v]
        return self._cpool[:, k:k + 1]

    def sbuf(self, shape, dtype=F32, name=None):
        cm = self.nc.sbuf_tensor(name or self.name("sb"), shape, dtype)
        t = cm.__enter__()
        self._ctx.append(cm)
        self._ctx_t = getattr(self, "_ctx_t", {})
        self._ctx_t[id(cm)] = t
        return t

    def free(self, t):
        for i, cm in enumerate(self._ctx):
            if getattr(self, "_ctx_t", {}).get(id(cm)) is t:
                cm.__exit__(None, None, None)
                del self._ctx[i]
                break
        lw = self.p.last_writer.get(id(t))
        rd = self.p.readers.get(id(t), [])
        out = ([lw] if lw is not None else []) + list(rd)
        self.p.fence_deps = list(set(self.p.fence_deps) | set(out))
        return out

    def psum(self, shape, dtype=F32, name=None):
        cm = self.nc.psum_tensor(name or self.name("ps"), shape, dtype)
        t = cm.__enter__()
        self._ctx.append(cm)
        self._ctx_t = getattr(self, "_ctx_t", {})
        self._ctx_t[id(cm)] = t
        return t

    def din(self, name, shape, dtype=F32):
        return self.nc.dram_tensor(name, shape, dtype, kind="ExternalInput").ap()

    def dout(self, name, shape, dtype=F32):
        return self.nc.dram_tensor(name, shape, dtype, kind="ExternalOutput").ap()

    def dscratch(self, name, shape, dtype=F32):
        return self.nc.dram_tensor(name, shape, dtype).ap()

    def build(self):
        sems = {}
        for e in list(ENGINES) + ["gpsimd_c"]:
            cm = self.nc.semaphore(f"sem_{e}")
            sems[e] = cm.__enter__()
            self._ctx.append(cm)
        blk_cm = self.nc.Block()
        block = blk_cm.__enter__()
        self.p.finalize(block, sems)
        blk_cm.__exit__(None, None, None)
        for cm in reversed(self._ctx):
            cm.__exit__(None, None, None)
        return self.nc



def emit_plain_ln(b, xt, out_t, sq, st, D, eps=1e-5, extra_r=()):
    p = b.p
    p.op("vector", lambda e: e.tensor_reduce(out=st[:, 0:1], in_=xt[:], axis=AX.X, op=ALU.add),
         r=[xt], w=[st])
    p.op("vector", lambda e: e.tensor_scalar_mul(out=st[:, 1:2], in0=st[:, 0:1], scalar1=1.0 / D),
         r=[st], w=[st])
    p.op("vector", lambda e: e.tensor_scalar(out=out_t[:], in0=xt[:], scalar1=st[:, 1:2],
                                             scalar2=None, op0=ALU.subtract),
         r=[xt, st], w=[out_t])
    p.op("scalar", lambda e: e.activation(out=sq[:, 0:D], in_=out_t[:], func=AF.Square,
                                          accum_out=st[:, 2:3]),
         r=[out_t], w=[sq, st])
    epsc = b.cconst(float(eps))
    p.op("scalar", lambda e: e.activation(out=st[:, 4:5], in_=st[:, 2:3], func=AF.Ln,
                                          bias=epsc, scale=1.0 / D),
         r=[st, b._cpool], w=[st])
    p.op("scalar", lambda e: e.activation(out=st[:, 5:6], in_=st[:, 4:5], func=AF.Exp, scale=-0.5),
         r=[st], w=[st])
    return p.op("vector", lambda e: e.tensor_scalar(out=out_t[:], in0=out_t[:], scalar1=st[:, 5:6],
                                                    scalar2=None, op0=ALU.mult),
                r=[out_t, st], w=[out_t])


def emit_bcast_row(b, dst, src_row, cols, ones_row, pmisc, r_extra=()):
    p = b.p
    for c0 in range(0, cols, 512):
        c1 = min(c0 + 512, cols)
        p.op("tensor", lambda e, c0=c0, c1=c1: e.matmul(
            pmisc[:, 0:c1 - c0], ones_row[0:1, 0:P], src_row[0:1, c0:c1], start=True, stop=True),
            r=[ones_row] + list(r_extra), w=[pmisc])
        p.op("scalar", lambda e, c0=c0, c1=c1, w_=list(r_extra): e.copy(out=dst[:, c0:c1], in_=pmisc[:, 0:c1 - c0]),
             r=[pmisc], w=[dst] + list(r_extra))



_prog_cache = {}


def build_l1ffn(R):
    key = ("l1ffn", R)
    if key in _prog_cache:
        return _prog_cache[key]
    D, H = 1024, 4096
    nT = R // P
    b = Builder()
    p = b.p
    x_in = b.din("x", [R, D])
    wfcT = b.din("wfcT", [D, H])
    wprT = b.din("wprT", [H, D])
    lnw = b.din("lnw", [1, H])
    lnb = b.din("lnb", [1, H])
    eye = b.din("eye", [P, P], BF16)
    out = b.dout("out", [R, D])
    hn_scr = b.dscratch("hn_scr", [R, H], BF16)

    eyeT = b.sbuf([P, P], BF16)
    onesr = b.sbuf([1, P], F32)
    p.dma("sync", eyeT[:], eye[:, :], w=[eyeT])
    p.op("vector", lambda e: e.memset(onesr[:], 1.0), w=[onesr])
    onesc = b.sbuf([P, 1], F32)
    p.op("vector", lambda e: e.memset(onesc[:], 1.0), w=[onesc])

    qw = b.sbuf([P, 32768], BF16, name="qw")
    wtmp = [b.sbuf([P, H], F32) for _ in range(2)]
    gtb = b.sbuf([P, H], F32)
    pmisc = b.psum([P, 512], F32)

    def quantize(whbm, nrow_tiles, W, dst_slices, tag):
        absr = b.sbuf([P, nrow_tiles + 2], F32, name=f"absr{tag}")
        scl = b.sbuf([P, 2], F32, name=f"scl{tag}")
        thrb = b.sbuf([P, 2], F32, name=f"thrb{tag}")
        for j in range(nrow_tiles):
            t = wtmp[j % 2]
            p.dma("sync", t[:, 0:W], whbm[j * P:(j + 1) * P, :], w=[t])
            p.op("vector", lambda e, j=j, t=t: e.tensor_reduce(
                out=absr[:, j:j + 1], in_=t[:, 0:W], axis=AX.X, op=ALU.add,
                apply_absolute_value=True), r=[t], w=[absr])
        p.op("vector", lambda e: e.tensor_reduce(out=absr[:, nrow_tiles:nrow_tiles + 1],
                                                 in_=absr[:, 0:nrow_tiles], axis=AX.X, op=ALU.add),
             r=[absr], w=[absr])
        p.op("tensor", lambda e: e.matmul(pmisc[0:1, 0:1], absr[:, nrow_tiles:nrow_tiles + 1],
                                          onesc[:, 0:1], start=True, stop=True),
             r=[absr, onesc], w=[pmisc])
        N = nrow_tiles * P * W
        halfeps = b.cconst(0.5e-6)
        p.op("scalar", lambda e: e.activation(out=scl[0:1, 0:1], in_=pmisc[0:1, 0:1],
                                              func=AF.Identity, scale=0.5 / N, bias=halfeps[0:1, 0:1]),
             r=[pmisc, b._cpool], w=[scl])
        p.op("scalar", lambda e: e.activation(out=scl[0:1, 1:2], in_=scl[0:1, 0:1],
                                              func=AF.Identity, scale=-1.0), r=[scl], w=[scl])
        emit_bcast_row(b, thrb, scl, 2, onesr, pmisc, r_extra=[scl])
        for j in range(nrow_tiles):
            t = wtmp[j % 2]
            p.dma("sync", t[:, 0:W], whbm[j * P:(j + 1) * P, :], w=[t], r=[thrb])
            p.op("vector", lambda e, t=t: e.tensor_scalar(
                out=gtb[:, 0:W], in0=t[:, 0:W], scalar1=thrb[:, 0:1], scalar2=None, op0=ALU.is_gt),
                r=[t, thrb], w=[gtb])
            p.op("vector", lambda e, t=t: e.tensor_scalar(
                out=t[:, 0:W], in0=t[:, 0:W], scalar1=thrb[:, 1:2], scalar2=None, op0=ALU.is_lt),
                r=[t, thrb], w=[t])
            p.op("vector", lambda e, j=j, t=t, d=dst_slices[j]: e.tensor_sub(
                out=d, in0=gtb[:, 0:W], in1=t[:, 0:W]), r=[gtb, t], w=[qw])

    q1_slices = [qw[:, j * H:(j + 1) * H] for j in range(8)]
    quantize(wfcT, 8, H, q1_slices, "fc")

    wbc = b.sbuf([P, H], F32)
    bbc = b.sbuf([P, H], F32)
    p.dma("sync", wbc[0:1, :], lnw[:, :], w=[wbc])
    p.dma("sync", bbc[0:1, :], lnb[:, :], w=[bbc])
    emit_bcast_row(b, wbc, wbc[0:1, 0:H], H, onesr, pmisc, r_extra=[wbc])
    emit_bcast_row(b, bbc, bbc[0:1, 0:H], H, onesr, pmisc, r_extra=[bbc])

    xt = b.sbuf([P, D], F32)
    st = b.sbuf([P, 8], F32)
    sq = gtb
    xn32 = b.sbuf([P, D], F32)
    xnb = b.sbuf([P, D], BF16)
    ptr = b.psum([P, P], BF16)
    xf = b.sbuf([P, 8 * P], BF16)
    ph = b.psum([P, 2048], F32)
    hg = b.sbuf([P, H], F32)
    hnb = b.sbuf([P, H], BF16)
    hfm = b.sbuf([P, 32 * P], BF16)

    for i in range(nT):
        p.dma("sync", xt[:], x_in[i * P:(i + 1) * P, :], w=[xt])
        emit_plain_ln(b, xt, xn32, sq, st, D)
        p.op("vector", lambda e: e.tensor_copy(out=xnb[:], in_=xn32[:]), r=[xn32], w=[xnb])
        for c in range(8):
            p.op("tensor", lambda e, c=c: e.transpose(ptr[:], xnb[:, c * P:(c + 1) * P], eyeT[:]),
                 r=[xnb, eyeT], w=[ptr])
            p.op("scalar", lambda e, c=c: e.copy(out=xf[:, c * P:(c + 1) * P], in_=ptr[:]),
                 r=[ptr], w=[xf])
        for half in range(2):
            for nb_ in range(4):
                for k in range(8):
                    p.op("tensor", lambda e, k=k, nb_=nb_, half=half: e.matmul(
                        ph[:, nb_ * 512:(nb_ + 1) * 512],
                        xf[:, k * P:(k + 1) * P],
                        qw[:, k * H + half * 2048 + nb_ * 512:k * H + half * 2048 + (nb_ + 1) * 512],
                        start=(k == 0), stop=(k == 7)),
                        r=[xf, qw], w=[ph])
            p.op("scalar", lambda e, half=half: e.activation(
                out=hg[:, half * 2048:(half + 1) * 2048], in_=ph[:, :], func=AF.Gelu),
                r=[ph], w=[hg])
        p.op("vector", lambda e: e.tensor_reduce(out=st[:, 0:1], in_=hg[:], axis=AX.X, op=ALU.add),
             r=[hg], w=[st])
        p.op("vector", lambda e: e.tensor_scalar(out=hg[:], in0=hg[:], scalar1=st[:, 0:1],
                                                 scalar2=None, op0=ALU.subtract), r=[hg, st], w=[hg],
             deps=[p.op("vector", lambda e: e.tensor_scalar_mul(out=st[:, 0:1], in0=st[:, 0:1],
                                                                scalar1=1.0 / H), r=[st], w=[st])])
        p.op("scalar", lambda e: e.activation(out=sq[:], in_=hg[:], func=AF.Square,
                                              accum_out=st[:, 2:3]), r=[hg], w=[sq, st])
        lnepsc = b.cconst(1e-5)
        p.op("scalar", lambda e, lnepsc=lnepsc: e.activation(out=st[:, 4:5], in_=st[:, 2:3], func=AF.Ln,
                                              bias=lnepsc, scale=1.0 / H), r=[st, b._cpool], w=[st])
        p.op("scalar", lambda e: e.activation(out=st[:, 5:6], in_=st[:, 4:5], func=AF.Exp,
                                              scale=-0.5), r=[st], w=[st])
        p.op("vector", lambda e: e.tensor_scalar(out=hg[:], in0=hg[:], scalar1=st[:, 5:6],
                                                 scalar2=None, op0=ALU.mult), r=[hg, st], w=[hg])
        p.op("vector", lambda e: e.tensor_mul(out=hg[:], in0=hg[:], in1=wbc[:]), r=[hg, wbc], w=[hg])
        p.op("vector", lambda e: e.tensor_add(out=hnb[:], in0=hg[:], in1=bbc[:]), r=[hg, bbc], w=[hnb])
        for c in range(32):
            p.op("tensor", lambda e, c=c: e.transpose(ptr[:], hnb[:, c * P:(c + 1) * P], eyeT[:]),
                 r=[hnb, eyeT], w=[ptr])
            p.op("scalar", lambda e, c=c: e.copy(out=hfm[:, c * P:(c + 1) * P], in_=ptr[:]),
                 r=[ptr], w=[hfm])
        p.dma("gpsimd", hn_scr[i * P:(i + 1) * P, :], hfm[:], r=[hfm], w=[hn_scr])

    b.free(hfm)
    b.free(hnb)
    q2_slices = [qw[:, j * D:(j + 1) * D] for j in range(32)]
    quantize(wprT, 32, D, q2_slices, "pr")

    hstrip = [b.sbuf([P, 32 * P], BF16) for _ in range(2)]
    pofb = [b.psum([P, 512], F32) for _ in range(2)]
    outt = b.sbuf([P, D], F32)
    xt2 = b.sbuf([P, D], F32)
    for i in range(nT):
        hs_ = hstrip[i % 2]
        ot_ = outt
        xx_ = xt2
        p.dma("sync", hs_[:], hn_scr[i * P:(i + 1) * P, :], r=[hn_scr], w=[hs_])
        p.dma("gpsimd", xx_[:], x_in[i * P:(i + 1) * P, :], w=[xx_])
        for half in range(2):
            pof = pofb[half]
            for k in range(32):
                p.op("tensor", lambda e, k=k, half=half, pof=pof, hs_=hs_: e.matmul(
                    pof[:, :], hs_[:, k * P:(k + 1) * P],
                    qw[:, k * D + half * 512:k * D + (half + 1) * 512],
                    start=(k == 0), stop=(k == 31)),
                    r=[hs_, qw], w=[pof])
            p.op("vector", lambda e, half=half, pof=pof, ot_=ot_, xx_=xx_: e.tensor_add(
                out=ot_[:, half * 512:(half + 1) * 512], in0=pof[:, :],
                in1=xx_[:, half * 512:(half + 1) * 512]), r=[pof, xx_], w=[ot_])
        p.dma("sync", out[i * P:(i + 1) * P, :], ot_[:], r=[ot_], w=[out])

    p.op("gpsimd", lambda e: e.engine_nop(), no_inc=True, r=[out])
    nc = b.build()
    _prog_cache[key] = nc
    return nc


def run_l1ffn(x, W_fc, ln_w, ln_b, W_proj):
    B, T, D = x.shape
    rows = x.reshape(B * T, D)
    R = (B * T) // NCORES
    nc = build_l1ffn(R)
    wfcT = np.ascontiguousarray(W_fc.T)
    wprT = np.ascontiguousarray(W_proj.T)
    eye = np.eye(P, dtype=np.float32).astype(NP_BF16)
    in_maps = []
    for c in range(NCORES):
        in_maps.append({
            "x": np.ascontiguousarray(rows[c * R:(c + 1) * R]),
            "wfcT": wfcT, "wprT": wprT,
            "lnw": ln_w.reshape(1, -1), "lnb": ln_b.reshape(1, -1),
            "eye": eye,
        })
    res = run_bass_kernel_spmd(nc, in_maps, core_ids=list(range(NCORES)))
    outr = np.concatenate([res.results[c]["out"] for c in range(NCORES)], 0)
    return outr.reshape(B, T, D)


def kernel(**inputs):
    g = inputs
    x = np.ascontiguousarray(g["L0_attn_a_hidden_states"], dtype=np.float32)
    x = run_l0_stack(x, g["L0_attn_a_norm_w"], g["L0_attn_a_norm_b"], g["L0_attn_a_in_proj_w"],
                     g["L0_attn_a_conv_w"], g["L0_attn_a_conv_b"], g["L0_attn_a_x_proj_w"],
                     g["L0_attn_a_dt_proj_w"], g["L0_attn_a_dt_proj_b"], g["L0_attn_a_A_log"],
                     g["L0_attn_a_D_param"], g["L0_attn_a_out_proj_w"], g["L0_attn_a_fnorm_w"],
                     g["L0_attn_a_fnorm_b"])
    ffn_args = (g["L0_ffn_gamma1"], g["L0_ffn_Wq"], g["L0_ffn_bq"], g["L0_ffn_Wdown"],
                g["L0_ffn_bdown"], g["L0_ffn_Wup"], g["L0_ffn_bup"], g["L0_ffn_Wo"],
                g["L0_ffn_bo"], g["L0_ffn_gamma2"], g["L0_ffn_Wg"], g["L0_ffn_bg"], g["L0_ffn_We"])
    x = run_l0ffn(x, *ffn_args)
    x = run_attnb_full(x, g["L0_attn_b_Wq"], g["L0_attn_b_Wk"], g["L0_attn_b_Wv"])
    x = run_l0ffn(x, *ffn_args)
    x = run_l1_stack(x, g["L1_attn_a_lin1_w"], g["L1_attn_a_lin1_b"], g["L1_attn_a_norm_w"],
                     g["L1_attn_a_in_w"], g["L1_attn_a_conv_w"], g["L1_attn_a_conv_b"],
                     g["L1_attn_a_xproj_w"], g["L1_attn_a_dt_w"], g["L1_attn_a_dt_b"],
                     g["L1_attn_a_A_log"], g["L1_attn_a_Dp"], g["L1_attn_a_out_w"],
                     g["L1_attn_a_lin2_w"], g["L1_attn_a_lin2_b"])
    l1f_args = (g["L1_ffn_W_fc"], g["L1_ffn_ln_w"], g["L1_ffn_ln_b"], g["L1_ffn_W_proj"])
    x = run_l1ffn(x, *l1f_args)
    x = run_l1ab(x, g["L1_attn_b_W_qkv"], g["L1_attn_b_W_out"],
                 float(np.asarray(g["L1_attn_b_a"])), float(np.asarray(g["L1_attn_b_b"])),
                 float(np.asarray(g["L1_attn_b_dt"])))
    x = run_l1ffn(x, *l1f_args)
    return np.ascontiguousarray(x, dtype=np.float32)


L0_RQ, L0_RH = 2048, 2176
L0_DI = 2048


def emit_affine_ln_tile(b, rt, ln32, sq, st, D, nwbc, nbbc, mask_buf=None):
    p = b.p
    emit_plain_ln(b, rt, ln32, sq, st, D)
    p.op("vector", lambda e: e.tensor_mul(out=ln32[:], in0=ln32[:], in1=nwbc[:]),
         r=[ln32, nwbc], w=[ln32])
    p.op("vector", lambda e: e.tensor_add(out=ln32[:], in0=ln32[:], in1=nbbc[:]),
         r=[ln32, nbbc], w=[ln32])
    if mask_buf is not None:
        p.op("vector", lambda e: e.tensor_scalar(out=ln32[:], in0=ln32[:], scalar1=mask_buf[:, 0:1],
                                                 scalar2=None, op0=ALU.mult),
             r=[ln32, mask_buf], w=[ln32])


def emit_l0_pre_body(b, provider, nT, inw_hbm, convw_hbm, convb_hbm, xpwT_hbm,
                     dtpwT_hbm, dtpb_hbm, nw_hbm, nb_hbm, hmask_hbm,
                     xc_out, zs_out, dt_out, bc_out, D=1024, DI=2048, DTR=64, first_tile_mask=True):
    p = b.p
    TC = nT * P
    OWN = TC - P
    onesr = b.get_onesr()
    pmisc = b.get_pmisc()
    nwbc = b.sbuf([P, D], name="nwbc")
    nbbc = b.sbuf([P, D], name="nbbc")
    p.dma("sync", nwbc[0:1, 0:D], nw_hbm[0:D, 0:1].rearrange("a b -> b a"), w=[nwbc])
    p.dma("sync", nbbc[0:1, 0:D], nb_hbm[0:D, 0:1].rearrange("a b -> b a"), w=[nbbc])
    emit_bcast_row(b, nwbc, nwbc[0:1, 0:D], D, onesr, pmisc, r_extra=[nwbc])
    emit_bcast_row(b, nbbc, nbbc[0:1, 0:D], D, onesr, pmisc, r_extra=[nbbc])
    hmask = b.sbuf([P, 1], name="hmask_sb")
    p.dma("sync", hmask[:], hmask_hbm[:, :], w=[hmask])

    eyeT = b.get_eye()

    hs_fm = b.sbuf([P, 8 * TC], BF16, name="hs_fm")
    ln32 = b.sbuf([P, D], name="ln32")
    sq = b.sbuf([P, D], name="sq0")
    st = b.sbuf([P, 8], name="st0")
    lnb16 = b.sbuf([P, D], BF16, name="lnb16")
    ptr = b.get_ptr()
    for i in range(nT):
        rt = provider(i)
        emit_affine_ln_tile(b, rt, ln32, sq, st, D, nwbc, nbbc,
                            mask_buf=(hmask if (i == 0 and first_tile_mask) else None))
        p.op("vector", lambda e: e.tensor_copy(out=lnb16[:], in_=ln32[:]), r=[ln32], w=[lnb16])
        for c in range(8):
            p.op("tensor", lambda e, c=c: e.transpose(ptr[:], lnb16[:, c * P:(c + 1) * P], eyeT[:]),
                 r=[lnb16, eyeT], w=[ptr])
            p.op("scalar", lambda e, c=c, i=i: e.copy(out=hs_fm[:, c * TC + i * P:c * TC + (i + 1) * P],
                                                      in_=ptr[:]), r=[ptr], w=[hs_fm])

    NSL = [(j * 512, min((j + 1) * 512, TC)) for j in range((TC + 511) // 512)]
    inw_t = [b.sbuf([P, 8 * P], BF16, name=f"inw{j}") for j in range(2)]
    pxdbl = b.psum([96, 2048], name="pxdbl")
    pxzb = [b.psum([P, 512], name=f"pxz{j}") for j in range(2)]
    xzx = b.sbuf([P, TC], name="xzx")
    acc = b.sbuf([P, OWN], name="acc")
    xc_m = b.sbuf([P, OWN], BF16, name="xc_m")
    zs_m = b.sbuf([P, OWN], BF16, name="zs_m")
    cw = b.sbuf([P, 4], name="cw")
    cb = b.sbuf([P, 1], name="cb")
    xpwT_t = b.sbuf([P, 96], BF16, name="xpwT_t")
    for m in range(32):
        wbuf = inw_t[m % 2]
        for k in range(8):
            p.dma("sync", wbuf[:, k * P:(k + 1) * P],
                  inw_hbm[k * P:(k + 1) * P, m * P:(m + 1) * P], w=[wbuf])
        if m < 16:
            p.dma("gpsimd", cw[:, :], convw_hbm[m * P:(m + 1) * P, :], w=[cw])
            p.dma("gpsimd", cb[:, :], convb_hbm[m * P:(m + 1) * P, :], w=[cb])
            p.dma("gpsimd", xpwT_t[:, :], xpwT_hbm[m * P:(m + 1) * P, :], w=[xpwT_t])
        for ni, (n0, n1) in enumerate(NSL):
            pxz = pxzb[ni % 2]
            for k in range(8):
                p.op("tensor", lambda e, k=k, n0=n0, n1=n1, wbuf=wbuf, pxz=pxz: e.matmul(
                    pxz[:, 0:n1 - n0], wbuf[:, k * P:(k + 1) * P],
                    hs_fm[:, k * TC + n0:k * TC + n1],
                    start=(k == 0), stop=(k == 7)), r=[wbuf, hs_fm], w=[pxz])
            if m < 16:
                p.op("scalar", lambda e, n0=n0, n1=n1, pxz=pxz: e.copy(out=xzx[:, n0:n1], in_=pxz[:, 0:n1 - n0]),
                     r=[pxz], w=[xzx])
            else:
                o0, o1 = max(n0, P), n1
                if o1 > o0:
                    p.op("scalar", lambda e, n0=n0, o0=o0, o1=o1, pxz=pxz: e.activation(
                        out=zs_m[:, o0 - P:o1 - P], in_=pxz[:, o0 - n0:o1 - n0], func=AF.Silu),
                        r=[pxz], w=[zs_m])
        if m < 16:
            p.op("vector", lambda e: e.tensor_scalar(out=acc[:], in0=xzx[:, P - 3:P - 3 + OWN],
                                                     scalar1=cw[:, 0:1], scalar2=None, op0=ALU.mult),
                 r=[xzx, cw], w=[acc])
            for k in range(1, 4):
                p.op("vector", lambda e, k=k: e.scalar_tensor_tensor(
                    out=acc[:], in0=xzx[:, P - 3 + k:P - 3 + k + OWN], scalar=cw[:, k:k + 1],
                    in1=acc[:], op0=ALU.mult, op1=ALU.add), r=[xzx, cw, acc], w=[acc])
            p.op("scalar", lambda e: e.activation(out=xc_m[:], in_=acc[:], func=AF.Silu,
                                                  bias=cb[:, 0:1]), r=[acc, cb], w=[xc_m])
            p.dma("sync", xc_out[m * P:(m + 1) * P, :], xc_m[:], r=[xc_m], w=[xc_out])
            for j in range(4):
                p.op("tensor", lambda e, j=j, m=m: e.matmul(
                    pxdbl[:, j * 512:(j + 1) * 512], xpwT_t[:, :], xc_m[:, j * 512:(j + 1) * 512],
                    start=(m == 0), stop=(m == 15)), r=[xpwT_t, xc_m], w=[pxdbl])
        else:
            p.dma("sync", zs_out[(m - 16) * P:(m - 15) * P, :], zs_m[:], r=[zs_m], w=[zs_out])
    b.free(pxzb[1])
    b.free(pxzb[0])
    xdbl = b.sbuf([96, 2048], name="xdbl")
    for j in range(4):
        p.op("scalar", lambda e, j=j: e.copy(out=xdbl[:, j * 512:(j + 1) * 512],
                                             in_=pxdbl[:, j * 512:(j + 1) * 512]),
             r=[pxdbl], w=[xdbl])
    p.dma("sync", bc_out[:, :], xdbl[DTR:96, :], r=[xdbl], w=[bc_out])
    dtin_b = b.sbuf([64, 2048], BF16, name="dtin_b")
    p.op("vector", lambda e: e.tensor_copy(out=dtin_b[:, :], in_=xdbl[0:DTR, :]), r=[xdbl], w=[dtin_b])
    dtpwT_t = b.sbuf([64, DI], BF16, name="dtpwT_t")
    p.dma("sync", dtpwT_t[:, :], dtpwT_hbm[:, :], w=[dtpwT_t])
    b.free(pxdbl)
    dtb = b.sbuf([P, 1], name="dtb")
    pdt = b.psum([P, 512], name="pdt")
    v32 = b.sbuf([P, 2048], name="v32")
    a1 = b.sbuf([P, 2048], name="a1_sp")
    for m2 in range(DI // P):
        p.dma("gpsimd", dtb[:, :], dtpb_hbm[m2 * P:(m2 + 1) * P, :], w=[dtb])
        for j in range(4):
            p.op("tensor", lambda e, m2=m2, j=j: e.matmul(
                pdt[:, :], dtpwT_t[:, m2 * P:(m2 + 1) * P], dtin_b[:, j * 512:(j + 1) * 512],
                start=True, stop=True), r=[dtpwT_t, dtin_b], w=[pdt])
            p.op("scalar", lambda e, j=j: e.activation(out=v32[:, j * 512:(j + 1) * 512],
                                                       in_=pdt[:, :], func=AF.Identity,
                                                       bias=dtb[:, 0:1]), r=[pdt, dtb], w=[v32])
        p.op("scalar", lambda e: e.activation(out=a1[:], in_=v32[:], func=AF.Abs), r=[v32], w=[a1])
        p.op("scalar", lambda e: e.activation(out=a1[:], in_=a1[:], func=AF.Exp, scale=-1.0),
             r=[a1], w=[a1])
        onec = b.cconst(1.0)
        p.op("scalar", lambda e, onec=onec: e.activation(out=a1[:], in_=a1[:], func=AF.Ln, bias=onec),
             r=[a1, b._cpool], w=[a1])
        p.op("vector", lambda e: e.tensor_relu(out=v32[:], in_=v32[:]), r=[v32], w=[v32])
        p.op("vector", lambda e: e.tensor_add(out=v32[:], in0=v32[:], in1=a1[:]), r=[v32, a1], w=[v32])
        p.dma("sync", dt_out[m2 * P:(m2 + 1) * P, :], v32[:], r=[v32], w=[dt_out])


def build_l0first():
    key = ("l0first",)
    if key in _prog_cache:
        return _prog_cache[key]
    b = Builder()
    p = b.p
    D = 1024
    x_in = b.din("x", [L0_RH, D])
    b.eye_hbm = b.din("eye", [P, P], BF16)
    inw = b.din("inwT", [D, 2 * L0_DI], BF16)
    convw = b.din("convw", [L0_DI, 4])
    convb = b.din("convb", [L0_DI, 1])
    xpwT = b.din("xpwT", [L0_DI, 96], BF16)
    dtpwT = b.din("dtpwT", [64, L0_DI], BF16)
    dtpb = b.din("dtpb", [L0_DI, 1])
    nw = b.din("nw", [D, 1])
    nb = b.din("nb", [D, 1])
    hmaskh = b.din("hmask", [P, 1])
    resid0 = b.dout("resid0", [L0_RQ, D])
    xc_out = b.dout("xc", [L0_DI, L0_RQ], BF16)
    zs_out = b.dout("zs", [L0_DI, L0_RQ], BF16)
    dt_out = b.dout("dt", [L0_DI, L0_RQ])
    bc_out = b.dout("bc", [32, L0_RQ])

    xt = b.sbuf([P, D], name="xt")
    rt = b.sbuf([P, D], name="rt")
    sq1 = b.sbuf([P, D], name="sq1")
    st1 = b.sbuf([P, 8], name="st1")

    def provider(i):
        p.dma("sync", xt[:], x_in[i * P:(i + 1) * P, :], w=[xt])
        emit_plain_ln(b, xt, rt, sq1, st1, D)
        if i > 0:
            p.dma("gpsimd", resid0[(i - 1) * P:i * P, :], rt[:], r=[rt], w=[resid0])
        return rt

    emit_l0_pre_body(b, provider, 17, inw, convw, convb, xpwT, dtpwT, dtpb,
                     nw, nb, hmaskh, xc_out, zs_out, dt_out, bc_out)
    p.op("gpsimd", lambda e: e.engine_nop(), no_inc=True, r=[resid0, xc_out, zs_out, dt_out, bc_out])
    nc = b.build()
    _prog_cache[key] = nc
    return nc


def build_l0scan(T=8192, NCH=512, DS=16):
    key = ("l0scan", T, NCH)
    if key in _prog_cache:
        return _prog_cache[key]
    b = Builder()
    p = b.p
    nD = NCH // P
    CH = 2048
    nC = T // CH
    dt_in = b.din("dt", [NCH, T])
    xc_in = b.din("xc", [NCH, T], BF16)
    zs_in = b.din("zs", [NCH, T], BF16)
    Brow = b.din("Brow", [DS, T])
    Crow = b.din("Crow", [DS, T])
    alog = b.din("alog", [NCH, DS])
    dp = b.din("dp", [NCH, 1])
    yg_out = b.dout("yg", [NCH, T], BF16)

    onesr = b.sbuf([1, P], name="onesr")
    p.op("vector", lambda e: e.memset(onesr[:], 1.0), w=[onesr])
    pmisc = b.psum([P, 512], name="pmisc")
    b1row = b.sbuf([1, 2048], name="b1row")
    c1row = b.sbuf([1, 2048], name="c1row")
    A_sb = [b.sbuf([P, DS], name=f"A{d}") for d in range(nD)]
    dp_sb = [b.sbuf([P, 1], name=f"dp{d}") for d in range(nD)]
    for d in range(nD):
        p.dma("sync", A_sb[d][:, :], alog[d * P:(d + 1) * P, :], w=[A_sb[d]])
        p.op("scalar", lambda e, d=d: e.activation(out=A_sb[d][:, :], in_=A_sb[d][:, :], func=AF.Exp),
             r=[A_sb[d]], w=[A_sb[d]])
        p.op("vector", lambda e, d=d: e.tensor_scalar_mul(out=A_sb[d][:, :], in0=A_sb[d][:, :],
                                                          scalar1=-1.0), r=[A_sb[d]], w=[A_sb[d]])
        p.dma("sync", dp_sb[d][:, :], dp[d * P:(d + 1) * P, :], w=[dp_sb[d]])

    Bbc = b.sbuf([P, CH], BF16, name="Bbc")
    Cbc = b.sbuf([P, CH], BF16, name="Cbc")
    dt_t = [b.sbuf([P, CH], name=f"dtt{d}") for d in range(nD)]
    xc_t = [b.sbuf([P, CH], BF16, name=f"xct{d}") for d in range(nD)]
    dtx = [b.sbuf([P, CH], BF16, name=f"dtx{d}") for d in range(nD)]
    yacc = [[b.sbuf([P, CH], BF16, name=f"yacc{d}_{j}") for j in range(2)] for d in range(nD)]
    dA = [b.sbuf([P, CH], BF16, name=f"dA{j}") for j in range(2)]
    dBu = [b.sbuf([P, CH], BF16, name=f"dBu{j}") for j in range(2)]
    hS = [b.sbuf([P, CH], BF16, name=f"hS{j}") for j in range(2)]
    prod = [b.sbuf([P, CH], BF16, name=f"prod{j}") for j in range(2)]
    Bbc2 = b.sbuf([P, CH], BF16, name="Bbc2")
    Cbc2 = b.sbuf([P, CH], BF16, name="Cbc2")
    carry = b.sbuf([P, nD * DS], BF16, name="carry")
    zs_t = b.sbuf([P, CH], BF16, name="zs_t")
    ygt = b.sbuf([P, CH], BF16, name="ygt")

    for c in range(nC):
        t0, t1 = c * CH, (c + 1) * CH
        for d in range(nD):
            p.dma("sync", dt_t[d][:, :], dt_in[d * P:(d + 1) * P, t0:t1], w=[dt_t[d]])
            p.dma("gpsimd", xc_t[d][:, :], xc_in[d * P:(d + 1) * P, t0:t1], w=[xc_t[d]])
            p.op("gpsimd", lambda e, d=d: e.tensor_mul(out=dtx[d][:, :], in0=dt_t[d][:, :],
                                                       in1=xc_t[d][:, :]), r=[dt_t[d], xc_t[d]], w=[dtx[d]])
            p.op("vector", lambda e, d=d: e.memset(yacc[d][0][:, :], 0.0), w=[yacc[d][0]])
            p.op("vector", lambda e, d=d: e.memset(yacc[d][1][:, :], 0.0), w=[yacc[d][1]])
        for n in range(DS):
            pb = n % 2
            Bb = Bbc if pb == 0 else Bbc2
            Cb = Cbc if pb == 0 else Cbc2
            p.dma("sync", b1row[0:1, :], Brow[n:n + 1, t0:t1], w=[b1row])
            p.dma("sync", c1row[0:1, :], Crow[n:n + 1, t0:t1], w=[c1row])
            for rsrc, dst in ((b1row, Bb), (c1row, Cb)):
                for j in range(CH // 512):
                    p.op("tensor", lambda e, j=j, rsrc=rsrc: e.matmul(
                        pmisc[:, :], onesr[0:1, 0:P], rsrc[0:1, j * 512:(j + 1) * 512],
                        start=True, stop=True), r=[onesr, rsrc], w=[pmisc])
                    p.op("scalar", lambda e, j=j, dst=dst: e.copy(out=dst[:, j * 512:(j + 1) * 512],
                                                                  in_=pmisc[:, :]), r=[pmisc], w=[dst])
            for d in range(nD):
                pp = (n * nD + d) % 2
                p.op("scalar", lambda e, d=d, n=n, pp=pp: e.activation(
                    out=dA[pp][:], in_=dt_t[d][:, :], func=AF.Exp, scale=A_sb[d][:, n:n + 1]),
                    r=[dt_t[d], A_sb[d]], w=[dA[pp]])
                p.op("gpsimd", lambda e, d=d, pp=pp, Bb=Bb: e.tensor_mul(
                    out=dBu[pp][:], in0=dtx[d][:, :], in1=Bb[:]), r=[dtx[d], Bb], w=[dBu[pp]])
                if c == 0:
                    p.op("vector", lambda e, pp=pp: e.tensor_tensor_scan(
                        out=hS[pp][:], data0=dA[pp][:], data1=dBu[pp][:], initial=0.0,
                        op0=ALU.mult, op1=ALU.add), r=[dA[pp], dBu[pp]], w=[hS[pp]])
                else:
                    p.op("vector", lambda e, d=d, n=n, pp=pp: e.tensor_tensor_scan(
                        out=hS[pp][:], data0=dA[pp][:], data1=dBu[pp][:],
                        initial=carry[:, d * DS + n:d * DS + n + 1],
                        op0=ALU.mult, op1=ALU.add), r=[dA[pp], dBu[pp], carry], w=[hS[pp]])
                p.op("vector", lambda e, d=d, n=n, pp=pp: e.tensor_copy(
                    out=carry[:, d * DS + n:d * DS + n + 1], in_=hS[pp][:, CH - 1:CH]),
                    r=[hS[pp]], w=[carry])
                p.op("vector", lambda e, pp=pp, Cb=Cb: e.tensor_mul(
                    out=prod[pp][:], in0=hS[pp][:], in1=Cb[:]), r=[hS[pp], Cb], w=[prod[pp]])
                yp = n % 2
                p.op("vector", lambda e, d=d, pp=pp, yp=yp: e.tensor_add(
                    out=yacc[d][yp][:, :], in0=yacc[d][1 - yp][:, :], in1=prod[pp][:]),
                    r=[prod[pp], yacc[d][1 - yp]], w=[yacc[d][yp]])
        for d in range(nD):
            p.dma("sync", zs_t[:, :], zs_in[d * P:(d + 1) * P, t0:t1], w=[zs_t])
            yfin = yacc[d][(DS - 1) % 2]
            p.op("vector", lambda e, d=d, yfin=yfin: e.scalar_tensor_tensor(
                out=yfin[:, :], in0=xc_t[d][:, :], scalar=dp_sb[d][:, 0:1], in1=yfin[:, :],
                op0=ALU.mult, op1=ALU.add), r=[xc_t[d], dp_sb[d], yfin], w=[yfin])
            p.op("vector", lambda e, d=d, yfin=yfin: e.tensor_mul(out=ygt[:], in0=yfin[:, :], in1=zs_t[:]),
                 r=[yfin, zs_t], w=[ygt])
            p.dma("sync", yg_out[d * P:(d + 1) * P, t0:t1], ygt[:], r=[ygt], w=[yg_out])
    p.op("gpsimd", lambda e: e.engine_nop(), no_inc=True, r=[yg_out])
    nc = b.build()
    _prog_cache[key] = nc
    return nc


def emit_l0_post(b, resid_in, yg_fm, owT_hbm, nT, D=1024, DI=2048):
    p = b.p
    TC = nT * P
    eyeT2 = b.get_eye()
    out_fm = b.sbuf([P, 8 * TC], BF16, name="out_fm")
    ygk = [b.sbuf([P, TC], BF16, name=f"ygk{k}") for k in range(DI // P)]
    owk = [b.sbuf([P, D], BF16, name=f"owk{k}") for k in range(DI // P)]
    for k in range(DI // P):
        p.dma("sync", ygk[k][:, :], yg_fm[k * P:(k + 1) * P, :], w=[ygk[k]])
        p.dma("gpsimd", owk[k][:, :], owT_hbm[k * P:(k + 1) * P, :], w=[owk[k]])
    pofb = [b.psum([P, 512], name=f"pof{j}") for j in range(2)]
    NSL = [(j * 512, min((j + 1) * 512, TC)) for j in range((TC + 511) // 512)]
    for m in range(8):
        for ni, (n0, n1) in enumerate(NSL):
            pof = pofb[ni % 2]
            for k in range(DI // P):
                p.op("tensor", lambda e, m=m, k=k, n0=n0, n1=n1, pof=pof: e.matmul(
                    pof[:, 0:n1 - n0], owk[k][:, m * P:(m + 1) * P], ygk[k][:, n0:n1],
                    start=(k == 0), stop=(k == DI // P - 1)), r=[owk[k], ygk[k]], w=[pof])
            p.op("scalar", lambda e, m=m, n0=n0, n1=n1, pof=pof: e.copy(
                out=out_fm[:, m * TC + n0:m * TC + n1], in_=pof[:, 0:n1 - n0]), r=[pof], w=[out_fm])
    deps = []
    deps += b.free(pofb[1])
    deps += b.free(pofb[0])
    for k in reversed(range(DI // P)):
        deps += b.free(owk[k])
    for k in reversed(range(DI // P)):
        deps += b.free(ygk[k])

    rti = b.sbuf([P, D], name="rti")
    rto = b.sbuf([P, D], name="rto")
    ptr2 = b.get_ptr()
    first = [True]

    def provider(i):
        dd = deps if first[0] else []
        first[0] = False
        p.dma("sync", rti[:], resid_in[i * P:(i + 1) * P, :], w=[rti], deps=dd)
        for m in range(8):
            p.op("tensor", lambda e, m=m, i=i: e.transpose(
                ptr2[:], out_fm[:, m * TC + i * P:m * TC + (i + 1) * P], eyeT2[:]),
                r=[out_fm, eyeT2], w=[ptr2])
            p.op("vector", lambda e, m=m: e.tensor_add(
                out=rto[:, m * P:(m + 1) * P], in0=rti[:, m * P:(m + 1) * P], in1=ptr2[:]),
                r=[rti, ptr2], w=[rto])
        return rto

    return provider


def build_l0mid():
    key = ("l0mid",)
    if key in _prog_cache:
        return _prog_cache[key]
    b = Builder()
    p = b.p
    D = 1024
    resid_in = b.din("resid", [L0_RH, D])
    b.eye_hbm = b.din("eye", [P, P], BF16)
    yg_fm = b.din("ygfm", [L0_DI, L0_RH], BF16)
    owT = b.din("owT", [L0_DI, D], BF16)
    inw = b.din("inwT", [D, 2 * L0_DI], BF16)
    convw = b.din("convw", [L0_DI, 4])
    convb = b.din("convb", [L0_DI, 1])
    xpwT = b.din("xpwT", [L0_DI, 96], BF16)
    dtpwT = b.din("dtpwT", [64, L0_DI], BF16)
    dtpb = b.din("dtpb", [L0_DI, 1])
    nw = b.din("nw", [D, 1])
    nb = b.din("nb", [D, 1])
    hmaskh = b.din("hmask", [P, 1])
    resid_out = b.dout("resid_o", [L0_RQ, D])
    xc_out = b.dout("xc", [L0_DI, L0_RQ], BF16)
    zs_out = b.dout("zs", [L0_DI, L0_RQ], BF16)
    dt_out = b.dout("dt", [L0_DI, L0_RQ])
    bc_out = b.dout("bc", [32, L0_RQ])

    provider0 = emit_l0_post(b, resid_in, yg_fm, owT, 17)

    def provider(i):
        rto = provider0(i)
        if i > 0:
            p.dma("gpsimd", resid_out[(i - 1) * P:i * P, :], rto[:], r=[rto], w=[resid_out])
        return rto

    emit_l0_pre_body(b, provider, 17, inw, convw, convb, xpwT, dtpwT, dtpb,
                     nw, nb, hmaskh, xc_out, zs_out, dt_out, bc_out)
    p.op("gpsimd", lambda e: e.engine_nop(), no_inc=True, r=[resid_out, xc_out, zs_out, dt_out, bc_out])
    nc = b.build()
    _prog_cache[key] = nc
    return nc


def build_l0last():
    key = ("l0last",)
    if key in _prog_cache:
        return _prog_cache[key]
    b = Builder()
    p = b.p
    D = 1024
    resid_in = b.din("resid", [L0_RQ, D])
    b.eye_hbm = b.din("eye", [P, P], BF16)
    yg_fm = b.din("ygfm", [L0_DI, L0_RQ], BF16)
    owT = b.din("owT", [L0_DI, D], BF16)
    fw = b.din("fw", [D, 1])
    fb = b.din("fb", [D, 1])
    xouter = b.din("xouter", [L0_RQ, D])
    out = b.dout("out", [L0_RQ, D])

    provider0 = emit_l0_post(b, resid_in, yg_fm, owT, 16)

    onesr = b.get_onesr()
    pmisc = b.get_pmisc()
    fwbc = b.sbuf([P, D], name="fwbc")
    fbbc = b.sbuf([P, D], name="fbbc")
    p.dma("sync", fwbc[0:1, 0:D], fw[0:D, 0:1].rearrange("a b -> b a"), w=[fwbc])
    p.dma("sync", fbbc[0:1, 0:D], fb[0:D, 0:1].rearrange("a b -> b a"), w=[fbbc])
    emit_bcast_row(b, fwbc, fwbc[0:1, 0:D], D, onesr, pmisc, r_extra=[fwbc])
    emit_bcast_row(b, fbbc, fbbc[0:1, 0:D], D, onesr, pmisc, r_extra=[fbbc])
    ln32 = b.sbuf([P, D], name="ln32L")
    sq = b.sbuf([P, D], name="sqL")
    st = b.sbuf([P, 8], name="stL")
    xot = b.sbuf([P, D], name="xot")
    for i in range(16):
        rto = provider0(i)
        emit_affine_ln_tile(b, rto, ln32, sq, st, D, fwbc, fbbc)
        p.dma("gpsimd", xot[:], xouter[i * P:(i + 1) * P, :], w=[xot])
        p.op("vector", lambda e: e.tensor_add(out=ln32[:], in0=ln32[:], in1=xot[:]),
             r=[ln32, xot], w=[ln32])
        p.dma("sync", out[i * P:(i + 1) * P, :], ln32[:], r=[ln32], w=[out])
    p.op("gpsimd", lambda e: e.engine_nop(), no_inc=True, r=[out])
    nc = b.build()
    _prog_cache[key] = nc
    return nc


def _bf(a):
    return np.ascontiguousarray(a).astype(NP_BF16)


def _shard_rows_halo(arr_b, beta, q, RQ=2048, HB=P):
    T = arr_b.shape[1]
    lo = q * RQ - HB
    if lo >= 0:
        return np.ascontiguousarray(arr_b[beta, lo:(q + 1) * RQ])
    blk = np.zeros((HB + RQ, arr_b.shape[2]), arr_b.dtype)
    blk[HB:] = arr_b[beta, 0:(q + 1) * RQ]
    return blk


def _shard_cols_halo(arr_2d, q, RQ=2048, HB=P):
    lo = q * RQ - HB
    if lo >= 0:
        return np.ascontiguousarray(arr_2d[:, lo:(q + 1) * RQ])
    blk = np.zeros((arr_2d.shape[0], HB + RQ), arr_2d.dtype)
    blk[:, HB:] = arr_2d[:, 0:(q + 1) * RQ]
    return blk


def run_l0_stack(x, norm_w, norm_b, in_proj_w, conv_w, conv_b, x_proj_w,
                 dt_proj_w, dt_proj_b, A_log, D_param, out_proj_w, fnorm_w, fnorm_b):
    B, T, D = x.shape
    RQ = T // 4
    eye = np.eye(P, dtype=np.float32).astype(NP_BF16)

    def pre_weights(i):
        return {
            "inwT": _bf(in_proj_w[i].T), "convw": np.ascontiguousarray(conv_w[i]),
            "convb": conv_b[i].reshape(-1, 1), "xpwT": _bf(x_proj_w[i].T),
            "dtpwT": _bf(dt_proj_w[i].T), "dtpb": dt_proj_b[i].reshape(-1, 1),
            "nw": norm_w[i].reshape(-1, 1), "nb": norm_b[i].reshape(-1, 1),
        }

    ones_m = np.ones((P, 1), np.float32)
    zeros_m = np.zeros((P, 1), np.float32)

    def launch_pre(prog, extra_per_core, wts):
        in_maps = []
        for c in range(NCORES):
            beta, q = c // 4, c % 4
            im = {"eye": eye, "hmask": (zeros_m if q == 0 else ones_m)}
            im.update(wts)
            im.update(extra_per_core(c, beta, q))
            in_maps.append(im)
        return run_bass_kernel_spmd(prog, in_maps, core_ids=list(range(NCORES)))

    res = launch_pre(build_l0first(), lambda c, beta, q: {"x": _shard_rows_halo(x, beta, q, RQ)},
                     pre_weights(0))
    R = np.stack([np.concatenate([res.results[b_ * 4 + q]["resid0"] for q in range(4)], 0)
                  for b_ in range(B)], 0)

    def gather_chan(res, key):
        return [np.concatenate([np.asarray(res.results[b_ * 4 + q][key]) for q in range(4)], 1)
                for b_ in range(B)]

    for i in range(4):
        XC = gather_chan(res, "xc")
        ZS = gather_chan(res, "zs")
        DT = gather_chan(res, "dt")
        BCm = gather_chan(res, "bc")
        scan_maps = []
        for c in range(NCORES):
            beta, g = c // 4, c % 4
            scan_maps.append({
                "dt": np.ascontiguousarray(DT[beta][g * 512:(g + 1) * 512]),
                "xc": np.ascontiguousarray(XC[beta][g * 512:(g + 1) * 512]),
                "zs": np.ascontiguousarray(ZS[beta][g * 512:(g + 1) * 512]),
                "Brow": np.ascontiguousarray(BCm[beta][0:16]),
                "Crow": np.ascontiguousarray(BCm[beta][16:32]),
                "alog": np.ascontiguousarray(A_log[i][g * 512:(g + 1) * 512]),
                "dp": D_param[i][g * 512:(g + 1) * 512].reshape(-1, 1),
            })
        sres = run_bass_kernel_spmd(build_l0scan(T, 512), scan_maps, core_ids=list(range(NCORES)))
        YG = [np.concatenate([np.asarray(sres.results[b_ * 4 + g]["yg"]) for g in range(4)], 0)
              for b_ in range(B)]
        if i < 3:
            res = launch_pre(
                build_l0mid(),
                lambda c, beta, q: {
                    "resid": _shard_rows_halo(R, beta, q, RQ),
                    "ygfm": _shard_cols_halo(YG[beta], q, RQ).astype(NP_BF16, copy=False),
                    "owT": _bf(out_proj_w[i].T),
                }, pre_weights(i + 1))
            R = np.stack([np.concatenate([res.results[b_ * 4 + q]["resid_o"] for q in range(4)], 0)
                          for b_ in range(B)], 0)
        else:
            in_maps = []
            for c in range(NCORES):
                beta, q = c // 4, c % 4
                in_maps.append({
                    "eye": eye,
                    "resid": np.ascontiguousarray(R[beta, q * RQ:(q + 1) * RQ]),
                    "ygfm": np.ascontiguousarray(YG[beta][:, q * RQ:(q + 1) * RQ]).astype(NP_BF16, copy=False),
                    "owT": _bf(out_proj_w[3].T),
                    "fw": fnorm_w.reshape(-1, 1), "fb": fnorm_b.reshape(-1, 1),
                    "xouter": np.ascontiguousarray(x[beta, q * RQ:(q + 1) * RQ]),
                })
            lres = run_bass_kernel_spmd(build_l0last(), in_maps, core_ids=list(range(NCORES)))
            out = np.stack([np.concatenate([lres.results[b_ * 4 + q]["out"] for q in range(4)], 0)
                            for b_ in range(B)], 0)
            return out


L1_D, L1_DI, L1_DTR = 512, 1024, 32


def emit_rms_tile(b, rt, ln32, sq, st, D, nwbc, eps=1e-5, mask_buf=None):
    p = b.p
    epsc = b.cconst(float(eps))
    p.op("scalar", lambda e: e.activation(out=sq[:, 0:D], in_=rt[:], func=AF.Square,
                                          accum_out=st[:, 2:3]), r=[rt], w=[sq, st])
    p.op("scalar", lambda e: e.activation(out=st[:, 4:5], in_=st[:, 2:3], func=AF.Ln,
                                          bias=epsc, scale=1.0 / D), r=[st, b._cpool], w=[st])
    p.op("scalar", lambda e: e.activation(out=st[:, 5:6], in_=st[:, 4:5], func=AF.Exp, scale=-0.5),
         r=[st], w=[st])
    p.op("vector", lambda e: e.tensor_scalar(out=ln32[:], in0=rt[:], scalar1=st[:, 5:6],
                                             scalar2=None, op0=ALU.mult), r=[rt, st], w=[ln32])
    p.op("vector", lambda e: e.tensor_mul(out=ln32[:], in0=ln32[:], in1=nwbc[:]),
         r=[ln32, nwbc], w=[ln32])
    if mask_buf is not None:
        p.op("vector", lambda e: e.tensor_scalar(out=ln32[:], in0=ln32[:], scalar1=mask_buf[:, 0:1],
                                                 scalar2=None, op0=ALU.mult), r=[ln32, mask_buf], w=[ln32])


def emit_l1_pre_body(b, provider, nT, inw_hbm, convw_hbm, convb_hbm, xpw_hbm,
                     dtw_hbm, dtb_hbm, nw_hbm, hmask_hbm,
                     xc_out, zs_out, dt_out, bc_out):
    p = b.p
    D, DI, DTR = L1_D, L1_DI, L1_DTR
    TC = nT * P
    OWN = TC - P
    onesr = b.get_onesr()
    pmisc = b.get_pmisc()
    nwbc = b.sbuf([P, D], name="nwbc1")
    p.dma("sync", nwbc[0:1, 0:D], nw_hbm[0:D, 0:1].rearrange("a b -> b a"), w=[nwbc])
    emit_bcast_row(b, nwbc, nwbc[0:1, 0:D], D, onesr, pmisc, r_extra=[nwbc])
    hmask = b.sbuf([P, 1], name="hmask1")
    p.dma("sync", hmask[:], hmask_hbm[:, :], w=[hmask])
    eyeT = b.get_eye()

    hs_fm = b.sbuf([P, 4 * TC], BF16, name="hs_fm1")
    ln32 = b.sbuf([P, D], name="ln321")
    sq = b.sbuf([P, D], name="sq01")
    st = b.sbuf([P, 8], name="st01")
    lnb16 = b.sbuf([P, D], BF16, name="lnb161")
    ptr = b.get_ptr()
    for i in range(nT):
        rt = provider(i)
        emit_rms_tile(b, rt, ln32, sq, st, D, nwbc, mask_buf=(hmask if i == 0 else None))
        p.op("vector", lambda e: e.tensor_copy(out=lnb16[:], in_=ln32[:]), r=[ln32], w=[lnb16])
        for c in range(4):
            p.op("tensor", lambda e, c=c: e.transpose(ptr[:], lnb16[:, c * P:(c + 1) * P], eyeT[:]),
                 r=[lnb16, eyeT], w=[ptr])
            p.op("scalar", lambda e, c=c, i=i: e.copy(out=hs_fm[:, c * TC + i * P:c * TC + (i + 1) * P],
                                                      in_=ptr[:]), r=[ptr], w=[hs_fm])

    NSL = [(j * 512, min((j + 1) * 512, TC)) for j in range((TC + 511) // 512)]
    inw_t = [b.sbuf([P, 4 * P], BF16, name=f"inw1_{j}") for j in range(2)]
    pxdbl = b.psum([64, 2048], name="pxdbl1")
    pxzb = [b.psum([P, 512], name=f"pxz1{j}") for j in range(2)]
    xzx = b.sbuf([P, TC], name="xzx1")
    acc = b.sbuf([P, OWN], name="acc1")
    xc_m = b.sbuf([P, OWN], BF16, name="xc_m1")
    zs_m = b.sbuf([P, OWN], BF16, name="zs_m1")
    cw = b.sbuf([P, 4], name="cw1")
    cb = b.sbuf([P, 1], name="cb1")
    xpw_t = b.sbuf([P, 64], BF16, name="xpw_t1")
    for m in range(16):
        wbuf = inw_t[m % 2]
        for k in range(4):
            p.dma("sync", wbuf[:, k * P:(k + 1) * P],
                  inw_hbm[k * P:(k + 1) * P, m * P:(m + 1) * P], w=[wbuf])
        if m < 8:
            p.dma("gpsimd", cw[:, :], convw_hbm[m * P:(m + 1) * P, :], w=[cw])
            p.dma("gpsimd", cb[:, :], convb_hbm[m * P:(m + 1) * P, :], w=[cb])
            p.dma("gpsimd", xpw_t[:, :], xpw_hbm[m * P:(m + 1) * P, :], w=[xpw_t])
        for ni, (n0, n1) in enumerate(NSL):
            pxz = pxzb[ni % 2]
            for k in range(4):
                p.op("tensor", lambda e, k=k, n0=n0, n1=n1, wbuf=wbuf, pxz=pxz: e.matmul(
                    pxz[:, 0:n1 - n0], wbuf[:, k * P:(k + 1) * P],
                    hs_fm[:, k * TC + n0:k * TC + n1],
                    start=(k == 0), stop=(k == 3)), r=[wbuf, hs_fm], w=[pxz])
            if m < 8:
                p.op("scalar", lambda e, n0=n0, n1=n1, pxz=pxz: e.copy(out=xzx[:, n0:n1], in_=pxz[:, 0:n1 - n0]),
                     r=[pxz], w=[xzx])
            else:
                o0, o1 = max(n0, P), n1
                if o1 > o0:
                    p.op("scalar", lambda e, n0=n0, o0=o0, o1=o1, pxz=pxz: e.activation(
                        out=zs_m[:, o0 - P:o1 - P], in_=pxz[:, o0 - n0:o1 - n0], func=AF.Silu),
                        r=[pxz], w=[zs_m])
        if m < 8:
            p.op("vector", lambda e: e.tensor_scalar(out=acc[:], in0=xzx[:, P - 3:P - 3 + OWN],
                                                     scalar1=cw[:, 0:1], scalar2=None, op0=ALU.mult),
                 r=[xzx, cw], w=[acc])
            for k in range(1, 4):
                p.op("vector", lambda e, k=k: e.scalar_tensor_tensor(
                    out=acc[:], in0=xzx[:, P - 3 + k:P - 3 + k + OWN], scalar=cw[:, k:k + 1],
                    in1=acc[:], op0=ALU.mult, op1=ALU.add), r=[xzx, cw, acc], w=[acc])
            p.op("scalar", lambda e: e.activation(out=xc_m[:], in_=acc[:], func=AF.Silu,
                                                  bias=cb[:, 0:1]), r=[acc, cb], w=[xc_m])
            p.dma("sync", xc_out[m * P:(m + 1) * P, :], xc_m[:], r=[xc_m], w=[xc_out])
            for j in range(4):
                p.op("tensor", lambda e, j=j, m=m: e.matmul(
                    pxdbl[:, j * 512:(j + 1) * 512], xpw_t[:, :], xc_m[:, j * 512:(j + 1) * 512],
                    start=(m == 0), stop=(m == 7)), r=[xpw_t, xc_m], w=[pxdbl])
        else:
            p.dma("sync", zs_out[(m - 8) * P:(m - 7) * P, :], zs_m[:], r=[zs_m], w=[zs_out])
    b.free(pxzb[1])
    b.free(pxzb[0])
    xdbl = b.sbuf([64, 2048], name="xdbl1")
    for j in range(4):
        p.op("scalar", lambda e, j=j: e.copy(out=xdbl[:, j * 512:(j + 1) * 512],
                                             in_=pxdbl[:, j * 512:(j + 1) * 512]),
             r=[pxdbl], w=[xdbl])
    p.dma("sync", bc_out[:, :], xdbl[DTR:64, :], r=[xdbl], w=[bc_out])
    dtin_b = b.sbuf([DTR, 2048], BF16, name="dtin_b1")
    p.op("vector", lambda e: e.tensor_copy(out=dtin_b[:, :], in_=xdbl[0:DTR, :]), r=[xdbl], w=[dtin_b])
    dtw_t = b.sbuf([DTR, DI], BF16, name="dtw_t1")
    p.dma("sync", dtw_t[:, :], dtw_hbm[:, :], w=[dtw_t])
    b.free(pxdbl)
    dtb = b.sbuf([P, 1], name="dtb1")
    pdt = b.psum([P, 512], name="pdt1")
    v32 = b.sbuf([P, 2048], name="v321")
    a1 = b.sbuf([P, 2048], name="a1_sp1")
    for m2 in range(DI // P):
        p.dma("gpsimd", dtb[:, :], dtb_hbm[m2 * P:(m2 + 1) * P, :], w=[dtb])
        for j in range(4):
            p.op("tensor", lambda e, m2=m2, j=j: e.matmul(
                pdt[:, :], dtw_t[:, m2 * P:(m2 + 1) * P], dtin_b[:, j * 512:(j + 1) * 512],
                start=True, stop=True), r=[dtw_t, dtin_b], w=[pdt])
            p.op("scalar", lambda e, j=j: e.activation(out=v32[:, j * 512:(j + 1) * 512],
                                                       in_=pdt[:, :], func=AF.Identity,
                                                       bias=dtb[:, 0:1]), r=[pdt, dtb], w=[v32])
        p.op("scalar", lambda e: e.activation(out=a1[:], in_=v32[:], func=AF.Abs), r=[v32], w=[a1])
        p.op("scalar", lambda e: e.activation(out=a1[:], in_=a1[:], func=AF.Exp, scale=-1.0),
             r=[a1], w=[a1])
        onec = b.cconst(1.0)
        p.op("scalar", lambda e, onec=onec: e.activation(out=a1[:], in_=a1[:], func=AF.Ln, bias=onec),
             r=[a1, b._cpool], w=[a1])
        p.op("vector", lambda e: e.tensor_relu(out=v32[:], in_=v32[:]), r=[v32], w=[v32])
        p.op("vector", lambda e: e.tensor_add(out=v32[:], in0=v32[:], in1=a1[:]), r=[v32, a1], w=[v32])
        p.dma("sync", dt_out[m2 * P:(m2 + 1) * P, :], v32[:], r=[v32], w=[dt_out])


def build_l1first():
    key = ("l1first",)
    if key in _prog_cache:
        return _prog_cache[key]
    b = Builder()
    p = b.p
    D, DM = 1024, L1_D
    x_in = b.din("x", [L0_RH, D])
    b.eye_hbm = b.din("eye", [P, P], BF16)
    lin1T = b.din("lin1T", [D, DM], BF16)
    lin1b = b.din("lin1b", [DM, 1])
    inw = b.din("inw", [DM, 2 * L1_DI], BF16)
    convw = b.din("convw", [L1_DI, 4])
    convb = b.din("convb", [L1_DI, 1])
    xpw = b.din("xpw", [L1_DI, 64], BF16)
    dtw = b.din("dtw", [L1_DTR, L1_DI], BF16)
    dtbh = b.din("dtb", [L1_DI, 1])
    nw = b.din("nw", [DM, 1])
    hmaskh = b.din("hmask", [P, 1])
    resid0 = b.dout("resid0", [L0_RQ, DM])
    xc_out = b.dout("xc", [L1_DI, L0_RQ], BF16)
    zs_out = b.dout("zs", [L1_DI, L0_RQ], BF16)
    dt_out = b.dout("dt", [L1_DI, L0_RQ])
    bc_out = b.dout("bc", [32, L0_RQ])

    onesr = b.get_onesr()
    pmisc = b.get_pmisc()
    l1bbc = b.sbuf([P, DM], name="l1bbc")
    p.dma("sync", l1bbc[0:1, 0:DM], lin1b[0:DM, 0:1].rearrange("a b -> b a"), w=[l1bbc])
    emit_bcast_row(b, l1bbc, l1bbc[0:1, 0:DM], DM, onesr, pmisc, r_extra=[l1bbc])
    l1w = b.sbuf([P, 8 * DM], BF16, name="l1w")
    for k in range(8):
        p.dma("sync", l1w[:, k * DM:(k + 1) * DM], lin1T[k * P:(k + 1) * P, :], w=[l1w])
    eyeT = b.get_eye()
    ptr = b.get_ptr()
    xt = b.sbuf([P, D], name="xt1")
    rt = b.sbuf([P, D], name="rt1")
    sq1 = b.sbuf([P, D], name="sq11")
    st1 = b.sbuf([P, 8], name="st11")
    xf = b.sbuf([P, 8 * P], BF16, name="xf1")
    ph = b.get_pmisc()
    ht = b.sbuf([P, DM], name="ht1")

    def provider(i):
        p.dma("sync", xt[:], x_in[i * P:(i + 1) * P, :], w=[xt])
        emit_plain_ln(b, xt, rt, sq1, st1, D)
        p.op("vector", lambda e: e.tensor_copy(out=sq1[:, 0:D], in_=rt[:]), r=[rt], w=[sq1])
        p.op("vector", lambda e: e.tensor_copy(out=xf[:, 0:P], in_=rt[:, 0:P]), r=[rt], w=[xf]) if False else None
        rb16 = sq1
        for c in range(8):
            p.op("vector", lambda e, c=c: e.tensor_copy(out=xf[:, c * P:(c + 1) * P],
                                                        in_=rt[:, c * P:(c + 1) * P]), r=[rt], w=[xf]) if False else None
        p.op("vector", lambda e: e.tensor_copy(out=rtb[:], in_=rt[:]), r=[rt], w=[rtb])
        for c in range(8):
            p.op("tensor", lambda e, c=c: e.transpose(ptr[:], rtb[:, c * P:(c + 1) * P], eyeT[:]),
                 r=[rtb, eyeT], w=[ptr])
            p.op("scalar", lambda e, c=c: e.copy(out=xf[:, c * P:(c + 1) * P], in_=ptr[:]),
                 r=[ptr], w=[xf])
        for k in range(8):
            p.op("tensor", lambda e, k=k: e.matmul(ph[:, :], xf[:, k * P:(k + 1) * P],
                                                   l1w[:, k * DM:(k + 1) * DM],
                                                   start=(k == 0), stop=(k == 7)), r=[xf, l1w], w=[ph])
        p.op("vector", lambda e: e.tensor_add(out=ht[:], in0=ph[:, :], in1=l1bbc[:]),
             r=[ph, l1bbc], w=[ht])
        if i > 0:
            p.dma("gpsimd", resid0[(i - 1) * P:i * P, :], ht[:], r=[ht], w=[resid0])
        return ht

    rtb = b.sbuf([P, D], BF16, name="rtb1")
    emit_l1_pre_body(b, provider, 17, inw, convw, convb, xpw, dtw, dtbh,
                     nw, hmaskh, xc_out, zs_out, dt_out, bc_out)
    p.op("gpsimd", lambda e: e.engine_nop(), no_inc=True, r=[resid0, xc_out, zs_out, dt_out, bc_out])
    nc = b.build()
    _prog_cache[key] = nc
    return nc


def emit_l1_post(b, resid_in, yg_fm, ow_hbm, nT):
    p = b.p
    D, DI = L1_D, L1_DI
    TC = nT * P
    eyeT2 = b.get_eye()
    out_fm = b.sbuf([P, 4 * TC], BF16, name="out_fm1")
    ygk = [b.sbuf([P, TC], BF16, name=f"yg1k{k}") for k in range(DI // P)]
    owk = [b.sbuf([P, D], BF16, name=f"ow1k{k}") for k in range(DI // P)]
    for k in range(DI // P):
        p.dma("sync", ygk[k][:, :], yg_fm[k * P:(k + 1) * P, :], w=[ygk[k]])
        p.dma("gpsimd", owk[k][:, :], ow_hbm[k * P:(k + 1) * P, :], w=[owk[k]])
    pofb = [b.psum([P, 512], name=f"pof1{j}") for j in range(2)]
    NSL = [(j * 512, min((j + 1) * 512, TC)) for j in range((TC + 511) // 512)]
    for m in range(4):
        for ni, (n0, n1) in enumerate(NSL):
            pof = pofb[ni % 2]
            for k in range(DI // P):
                p.op("tensor", lambda e, m=m, k=k, n0=n0, n1=n1, pof=pof: e.matmul(
                    pof[:, 0:n1 - n0], owk[k][:, m * P:(m + 1) * P], ygk[k][:, n0:n1],
                    start=(k == 0), stop=(k == DI // P - 1)), r=[owk[k], ygk[k]], w=[pof])
            p.op("scalar", lambda e, m=m, n0=n0, n1=n1, pof=pof: e.copy(
                out=out_fm[:, m * TC + n0:m * TC + n1], in_=pof[:, 0:n1 - n0]), r=[pof], w=[out_fm])
    deps = []
    deps += b.free(pofb[1])
    deps += b.free(pofb[0])
    for k in reversed(range(DI // P)):
        deps += b.free(owk[k])
    for k in reversed(range(DI // P)):
        deps += b.free(ygk[k])

    rti = b.sbuf([P, D], name="rti1")
    rto = b.sbuf([P, D], name="rto1")
    ptr2 = b.get_ptr()
    first = [True]

    def provider(i):
        dd = deps if first[0] else []
        first[0] = False
        p.dma("sync", rti[:], resid_in[i * P:(i + 1) * P, :], w=[rti], deps=dd)
        for m in range(4):
            p.op("tensor", lambda e, m=m, i=i: e.transpose(
                ptr2[:], out_fm[:, m * TC + i * P:m * TC + (i + 1) * P], eyeT2[:]),
                r=[out_fm, eyeT2], w=[ptr2])
            p.op("vector", lambda e, m=m: e.tensor_add(
                out=rto[:, m * P:(m + 1) * P], in0=rti[:, m * P:(m + 1) * P], in1=ptr2[:]),
                r=[rti, ptr2], w=[rto])
        return rto

    return provider


def build_l1mid():
    key = ("l1mid",)
    if key in _prog_cache:
        return _prog_cache[key]
    b = Builder()
    p = b.p
    DM = L1_D
    resid_in = b.din("resid", [L0_RH, DM])
    b.eye_hbm = b.din("eye", [P, P], BF16)
    yg_fm = b.din("ygfm", [L1_DI, L0_RH], BF16)
    ow = b.din("ow", [L1_DI, DM], BF16)
    inw = b.din("inw", [DM, 2 * L1_DI], BF16)
    convw = b.din("convw", [L1_DI, 4])
    convb = b.din("convb", [L1_DI, 1])
    xpw = b.din("xpw", [L1_DI, 64], BF16)
    dtw = b.din("dtw", [L1_DTR, L1_DI], BF16)
    dtbh = b.din("dtb", [L1_DI, 1])
    nw = b.din("nw", [DM, 1])
    hmaskh = b.din("hmask", [P, 1])
    resid_out = b.dout("resid_o", [L0_RQ, DM])
    xc_out = b.dout("xc", [L1_DI, L0_RQ], BF16)
    zs_out = b.dout("zs", [L1_DI, L0_RQ], BF16)
    dt_out = b.dout("dt", [L1_DI, L0_RQ])
    bc_out = b.dout("bc", [32, L0_RQ])

    provider0 = emit_l1_post(b, resid_in, yg_fm, ow, 17)

    def provider(i):
        rto = provider0(i)
        if i > 0:
            p.dma("gpsimd", resid_out[(i - 1) * P:i * P, :], rto[:], r=[rto], w=[resid_out])
        return rto

    emit_l1_pre_body(b, provider, 17, inw, convw, convb, xpw, dtw, dtbh,
                     nw, hmaskh, xc_out, zs_out, dt_out, bc_out)
    p.op("gpsimd", lambda e: e.engine_nop(), no_inc=True, r=[resid_out, xc_out, zs_out, dt_out, bc_out])
    nc = b.build()
    _prog_cache[key] = nc
    return nc


def build_l1last():
    key = ("l1last",)
    if key in _prog_cache:
        return _prog_cache[key]
    b = Builder()
    p = b.p
    DM, D = L1_D, 1024
    resid_in = b.din("resid", [L0_RQ, DM])
    b.eye_hbm = b.din("eye", [P, P], BF16)
    yg_fm = b.din("ygfm", [L1_DI, L0_RQ], BF16)
    ow = b.din("ow", [L1_DI, DM], BF16)
    lin2 = b.din("lin2", [DM, D], BF16)
    lin2b = b.din("lin2b", [D, 1])
    xouter = b.din("xouter", [L0_RQ, D])
    out = b.dout("out", [L0_RQ, D])

    provider0 = emit_l1_post(b, resid_in, yg_fm, ow, 16)

    onesr = b.get_onesr()
    pmisc = b.get_pmisc()
    l2bbc = b.sbuf([P, D], name="l2bbc")
    p.dma("sync", l2bbc[0:1, 0:D], lin2b[0:D, 0:1].rearrange("a b -> b a"), w=[l2bbc])
    emit_bcast_row(b, l2bbc, l2bbc[0:1, 0:D], D, onesr, pmisc, r_extra=[l2bbc])
    l2w = b.sbuf([P, 4 * D], BF16, name="l2w")
    for k in range(4):
        p.dma("sync", l2w[:, k * D:(k + 1) * D], lin2[k * P:(k + 1) * P, :], w=[l2w])
    eyeT = b.get_eye()
    ptr = b.get_ptr()
    hb16 = b.sbuf([P, DM], BF16, name="hb16")
    hfm = b.sbuf([P, 4 * P], BF16, name="hfm1")
    pl2 = b.psum([P, 512], name="pl2")
    ot32 = b.sbuf([P, D], name="ot32")
    mx = b.sbuf([P, 32], name="mx1")
    ex = b.sbuf([P, D], name="ex1")
    sm = b.sbuf([P, 32], name="sm1")
    xot = b.sbuf([P, D], name="xot1")
    import concourse.bass as bass_mod
    for i in range(16):
        rto = provider0(i)
        p.op("vector", lambda e: e.tensor_copy(out=hb16[:], in_=rto[:]), r=[rto], w=[hb16])
        for c in range(4):
            p.op("tensor", lambda e, c=c: e.transpose(ptr[:], hb16[:, c * P:(c + 1) * P], eyeT[:]),
                 r=[hb16, eyeT], w=[ptr])
            p.op("scalar", lambda e, c=c: e.copy(out=hfm[:, c * P:(c + 1) * P], in_=ptr[:]),
                 r=[ptr], w=[hfm])
        for j in range(2):
            for k in range(4):
                p.op("tensor", lambda e, j=j, k=k: e.matmul(
                    pl2[:, :], hfm[:, k * P:(k + 1) * P], l2w[:, k * D + j * 512:k * D + (j + 1) * 512],
                    start=(k == 0), stop=(k == 3)), r=[hfm, l2w], w=[pl2])
            p.op("vector", lambda e, j=j: e.tensor_add(out=ot32[:, j * 512:(j + 1) * 512],
                                                       in0=pl2[:, :], in1=l2bbc[:, j * 512:(j + 1) * 512]),
                 r=[pl2, l2bbc], w=[ot32])
        p.op("vector", lambda e: e.tensor_reduce(
            out=mx[:], in_=ot32.ap().rearrange("p (a c) -> p a c", c=32), axis=AX.X, op=ALU.max),
            r=[ot32], w=[mx])
        def sub_emit(e):
            in1 = bass_mod.AP(mx.ap().tensor, 0, [[32, P], [1, 32], [0, 32]])
            return e.tensor_tensor(out=ot32.ap().rearrange("p (a c) -> p a c", c=32),
                                   in0=ot32.ap().rearrange("p (a c) -> p a c", c=32),
                                   in1=in1, op=ALU.subtract)
        p.op("vector", sub_emit, r=[ot32, mx], w=[ot32])
        p.op("scalar", lambda e: e.activation(out=ex[:], in_=ot32[:], func=AF.Exp), r=[ot32], w=[ex])
        p.op("vector", lambda e: e.tensor_reduce(
            out=sm[:], in_=ex.ap().rearrange("p (a c) -> p a c", c=32), axis=AX.X, op=ALU.add),
            r=[ex], w=[sm])
        p.op("vector", lambda e: e.reciprocal(out=sm[:], in_=sm[:]), r=[sm], w=[sm])
        def mul_emit(e):
            in1 = bass_mod.AP(sm.ap().tensor, 0, [[32, P], [1, 32], [0, 32]])
            return e.tensor_tensor(out=ex.ap().rearrange("p (a c) -> p a c", c=32),
                                   in0=ex.ap().rearrange("p (a c) -> p a c", c=32),
                                   in1=in1, op=ALU.mult)
        p.op("vector", mul_emit, r=[ex, sm], w=[ex])
        p.dma("gpsimd", xot[:], xouter[i * P:(i + 1) * P, :], w=[xot])
        p.op("vector", lambda e: e.tensor_add(out=ex[:], in0=ex[:], in1=xot[:]), r=[ex, xot], w=[ex])
        p.dma("sync", out[i * P:(i + 1) * P, :], ex[:], r=[ex], w=[out])
    p.op("gpsimd", lambda e: e.engine_nop(), no_inc=True, r=[out])
    nc = b.build()
    _prog_cache[key] = nc
    return nc


def run_l1_stack(x, lin1_w, lin1_b, norm_w, in_w, conv_w, conv_b, xproj_w,
                 dt_w, dt_b, A_log, Dp, out_w, lin2_w, lin2_b):
    B, T, D = x.shape
    RQ = T // 4
    eye = np.eye(P, dtype=np.float32).astype(NP_BF16)
    ones_m = np.ones((P, 1), np.float32)
    zeros_m = np.zeros((P, 1), np.float32)

    def pre_weights(i):
        return {"inw": _bf(in_w[i]), "convw": np.ascontiguousarray(conv_w[i]),
                "convb": conv_b[i].reshape(-1, 1), "xpw": _bf(xproj_w[i]),
                "dtw": _bf(dt_w[i]), "dtb": dt_b[i].reshape(-1, 1),
                "nw": norm_w[i].reshape(-1, 1)}

    in_maps = []
    for c in range(NCORES):
        beta, q = c // 4, c % 4
        im = {"eye": eye, "hmask": (zeros_m if q == 0 else ones_m),
              "x": _shard_rows_halo(x, beta, q, RQ),
              "lin1T": _bf(lin1_w), "lin1b": lin1_b.reshape(-1, 1)}
        im.update(pre_weights(0))
        in_maps.append(im)
    res = run_bass_kernel_spmd(build_l1first(), in_maps, core_ids=list(range(NCORES)))
    R = np.stack([np.concatenate([res.results[b_ * 4 + q]["resid0"] for q in range(4)], 0)
                  for b_ in range(B)], 0)

    def gather_chan(res, key):
        return [np.concatenate([np.asarray(res.results[b_ * 4 + q][key]) for q in range(4)], 1)
                for b_ in range(B)]

    for i in range(4):
        XC = gather_chan(res, "xc")
        ZS = gather_chan(res, "zs")
        DT = gather_chan(res, "dt")
        BCm = gather_chan(res, "bc")
        NCH = L1_DI // 4
        scan_maps = []
        for c in range(NCORES):
            beta, g = c // 4, c % 4
            scan_maps.append({
                "dt": np.ascontiguousarray(DT[beta][g * NCH:(g + 1) * NCH]),
                "xc": np.ascontiguousarray(XC[beta][g * NCH:(g + 1) * NCH]),
                "zs": np.ascontiguousarray(ZS[beta][g * NCH:(g + 1) * NCH]),
                "Brow": np.ascontiguousarray(BCm[beta][0:16]),
                "Crow": np.ascontiguousarray(BCm[beta][16:32]),
                "alog": np.ascontiguousarray(A_log[i][g * NCH:(g + 1) * NCH]),
                "dp": Dp[i][g * NCH:(g + 1) * NCH].reshape(-1, 1),
            })
        sres = run_bass_kernel_spmd(build_l0scan(T, NCH), scan_maps, core_ids=list(range(NCORES)))
        YG = [np.concatenate([np.asarray(sres.results[b_ * 4 + g]["yg"]) for g in range(4)], 0)
              for b_ in range(B)]
        if i < 3:
            in_maps = []
            for c in range(NCORES):
                beta, q = c // 4, c % 4
                im = {"eye": eye, "hmask": (zeros_m if q == 0 else ones_m),
                      "resid": _shard_rows_halo(R, beta, q, RQ),
                      "ygfm": _shard_cols_halo(YG[beta], q, RQ).astype(NP_BF16, copy=False),
                      "ow": _bf(out_w[i])}
                im.update(pre_weights(i + 1))
                in_maps.append(im)
            res = run_bass_kernel_spmd(build_l1mid(), in_maps, core_ids=list(range(NCORES)))
            R = np.stack([np.concatenate([res.results[b_ * 4 + q]["resid_o"] for q in range(4)], 0)
                          for b_ in range(B)], 0)
        else:
            in_maps = []
            for c in range(NCORES):
                beta, q = c // 4, c % 4
                in_maps.append({
                    "eye": eye,
                    "resid": np.ascontiguousarray(R[beta, q * RQ:(q + 1) * RQ]),
                    "ygfm": np.ascontiguousarray(YG[beta][:, q * RQ:(q + 1) * RQ]).astype(NP_BF16, copy=False),
                    "ow": _bf(out_w[3]),
                    "lin2": _bf(lin2_w), "lin2b": lin2_b.reshape(-1, 1),
                    "xouter": np.ascontiguousarray(x[beta, q * RQ:(q + 1) * RQ]),
                })
            lres = run_bass_kernel_spmd(build_l1last(), in_maps, core_ids=list(range(NCORES)))
            return np.stack([np.concatenate([lres.results[b_ * 4 + q]["out"] for q in range(4)], 0)
                             for b_ in range(B)], 0)


def build_l0ffn(R=2048):
    key = ("l0ffn", R)
    if key in _prog_cache:
        return _prog_cache[key]
    D, H_, HD, E = 1024, 16, 64, 8
    nT = R // P
    b = Builder()
    p = b.p
    x_in = b.din("x", [R, D])
    b.eye_hbm = b.din("eye", [P, P], BF16)
    wq = b.din("wq", [D, D], BF16)
    bq = b.din("bq", [D, 1])
    wdn = b.din("wdn", [D, 64], BF16)
    bdn = b.din("bdn", [64, 1])
    wup = b.din("wup", [64, 2 * D], BF16)
    bup = b.din("bup", [2 * D, 1])
    wo = b.din("wo", [D, D], BF16)
    bo = b.din("bo", [D, 1])
    g1 = b.din("g1", [D, 1])
    g2 = b.din("g2", [D, 1])
    wg = b.din("wg", [D, E], BF16)
    bg = b.din("bg", [E, 1])
    we = b.din("we", [E * D, D], BF16)
    bdm = b.din("bdm", [P, P])
    out = b.dout("out", [R, D])
    x2_scr = b.dscratch("x2_scr", [R, D])
    q_scr = b.dscratch("q_scr", [16 * P, 64], BF16)
    kv_scr = b.dscratch("kv_scr", [16 * P, P], BF16)
    o_scr = b.dscratch("o_scr", [16 * P, 64], BF16)

    onesr = b.get_onesr()
    pmisc = b.get_pmisc()
    eyeT = b.get_eye()
    ptr = b.get_ptr()

    def bcast(src, cols, name):
        t = b.sbuf([P, cols], name=name)
        p.dma("sync", t[0:1, 0:cols], src[0:cols, 0:1].rearrange("a b -> b a"), w=[t])
        emit_bcast_row(b, t, t[0:1, 0:cols], cols, onesr, pmisc, r_extra=[t])
        return t

    bq_bc = bcast(bq, D, "bq_bc")
    bdn_bc = bcast(bdn, 64, "bdn_bc")
    bup_bc = bcast(bup, 2 * D, "bup_bc")
    bo_bc = bcast(bo, D, "bo_bc")
    g1_bc = bcast(g1, D, "g1_bc")
    g2_bc = bcast(g2, D, "g2_bc")
    bg_bc = bcast(bg, E, "bg_bc")
    bdm_sb = b.sbuf([P, P], name="bdm_sb")
    p.dma("sync", bdm_sb[:], bdm[:, :], w=[bdm_sb])

    wo_sb = b.sbuf([P, 8 * D], BF16, name="wo_sb")
    for k in range(8):
        p.dma("sync", wo_sb[:, k * D:(k + 1) * D], wo[k * P:(k + 1) * P, :], w=[wo_sb])
    wg_sb = b.sbuf([P, 8 * E], BF16, name="wg_sb")
    for k in range(8):
        p.dma("sync", wg_sb[:, k * E:(k + 1) * E], wg[k * P:(k + 1) * P, :], w=[wg_sb])

    h2fm = b.sbuf([P, 8 * R], BF16, name="h2fm")
    gate = b.sbuf([P, 16 * E], name="gate")
    comb = [b.sbuf([P, D], name=f"comb{i}") for i in range(nT // 2)]

    xt = b.sbuf([P, D], name="xtf")
    xn = b.sbuf([P, D], name="xnf")
    sq = b.sbuf([P, D], name="sqf")
    st = b.sbuf([P, 8], name="stf")
    hrm = b.sbuf([P, D], name="hrm")
    hb = b.sbuf([P, D], BF16, name="hbf")
    hfm = b.sbuf([P, 8 * P], BF16, name="hfmf")
    pq = b.psum([P, 512], name="pq")
    qb16 = b.sbuf([P, D], BF16, name="qb16")
    rsb = b.sbuf([P, 64], BF16, name="rsb")
    rfm = b.sbuf([64, P], BF16, name="rfm")
    kvb16 = b.sbuf([P, 2 * D], BF16, name="kvb16")
    kn = b.sbuf([P, 16], name="kn")
    qn = b.sbuf([P, 16], name="qn")
    bnd = b.sbuf([P, 16], name="bnd")
    bnd_row = b.sbuf([1, 2048], name="bnd_row")
    bnd_scr = b.dscratch("bnd_scr", [P, 16])
    qth = b.sbuf([P, 64], BF16, name="qth")
    kvtg = b.sbuf([P, P], BF16, name="kvtg")
    qth2 = b.sbuf([P, 64], BF16, name="qth2")
    kvtg2 = b.sbuf([P, P], BF16, name="kvtg2")
    lhs_a = b.sbuf([65, P], BF16, name="lhs_a")
    rhs_a = b.sbuf([65, P], BF16, name="rhs_a")
    lhs_b = b.sbuf([65, P], BF16, name="lhs_b")
    rhs_b = b.sbuf([65, P], BF16, name="rhs_b")
    patt = b.psum([P, 1024], name="patt")
    pq2 = b.psum([P, 512], name="pq2")
    ptr2f = b.psum([P, P], BF16, name="ptr2f")
    probs = b.sbuf([P, 2048], BF16, name="probs")
    rs = b.sbuf([P, 16], name="rs")
    probsT = b.sbuf([P, P], BF16, name="probsT")
    probsT2 = b.sbuf([P, P], BF16, name="probsT2x")
    pout = b.psum([P, 64], name="pout")
    oth = b.sbuf([P, 64], BF16, name="oth")
    ott = b.sbuf([P, D], BF16, name="ott")
    otfm = b.sbuf([P, 8 * P], BF16, name="otfm")
    x2t = b.sbuf([P, D], name="x2t")
    h2t = b.sbuf([P, D], name="h2t")
    h2b = b.sbuf([P, D], BF16, name="h2b")
    gt32 = b.sbuf([P, E], name="gt32")

    p.op("vector", lambda e: e.memset(rhs_a[64:65, :], 1.0), w=[rhs_a])
    p.op("vector", lambda e: e.memset(rhs_b[64:65, :], 1.0), w=[rhs_b])
    b.cconst(1e-5)
    b.cconst(1e-6)

    wq_sb = b.sbuf([P, 8 * D], BF16, name="wq_sb")
    for k in range(8):
        p.dma("sync", wq_sb[:, k * D:(k + 1) * D], wq[k * P:(k + 1) * P, :], w=[wq_sb])
    wdn_sb = b.sbuf([P, 8 * 64], BF16, name="wdn_sb")
    for k in range(8):
        p.dma("sync", wdn_sb[:, k * 64:(k + 1) * 64], wdn[k * P:(k + 1) * P, :], w=[wdn_sb])
    wup_sb = b.sbuf([64, 2 * D], BF16, name="wup_sb")
    p.dma("sync", wup_sb[:, :], wup[:, :], w=[wup_sb])

    for i in range(nT):
        p.dma("sync", xt[:], x_in[i * P:(i + 1) * P, :], w=[xt])
        emit_plain_ln(b, xt, xn, sq, st, D)
        emit_rms_tile(b, xn, hrm, sq, st, D, g1_bc, eps=1e-6)
        p.op("vector", lambda e: e.tensor_copy(out=hb[:], in_=hrm[:]), r=[hrm], w=[hb])
        for c in range(8):
            p.op("tensor", lambda e, c=c: e.transpose(ptr[:], hb[:, c * P:(c + 1) * P], eyeT[:]),
                 r=[hb, eyeT], w=[ptr])
            p.op("scalar", lambda e, c=c: e.copy(out=hfm[:, c * P:(c + 1) * P], in_=ptr[:]),
                 r=[ptr], w=[hfm])
        for j in range(2):
            for k in range(8):
                p.op("tensor", lambda e, j=j, k=k: e.matmul(
                    pq[:, :], hfm[:, k * P:(k + 1) * P], wq_sb[:, k * D + j * 512:k * D + (j + 1) * 512],
                    start=(k == 0), stop=(k == 7)), r=[hfm, wq_sb], w=[pq])
            p.op("vector", lambda e, j=j: e.tensor_add(out=qb16[:, j * 512:(j + 1) * 512],
                                                       in0=pq[:, :], in1=bq_bc[:, j * 512:(j + 1) * 512]),
                 r=[pq, bq_bc], w=[qb16])
        for k in range(8):
            p.op("tensor", lambda e, k=k: e.matmul(
                pq[:, 0:64], hfm[:, k * P:(k + 1) * P], wdn_sb[:, k * 64:(k + 1) * 64],
                start=(k == 0), stop=(k == 7)), r=[hfm, wdn_sb], w=[pq])
        p.op("vector", lambda e: e.tensor_add(out=sq[:, 0:64], in0=pq[:, 0:64], in1=bdn_bc[:, 0:64]),
             r=[pq, bdn_bc], w=[sq])
        p.op("vector", lambda e: e.tensor_copy(out=rsb[:], in_=sq[:, 0:64]), r=[sq], w=[rsb])
        p.op("tensor", lambda e: e.transpose(ptr[0:64, 0:P], rsb.ap(), eyeT[:]), r=[rsb, eyeT], w=[ptr])
        p.op("scalar", lambda e: e.copy(out=rfm[:, :], in_=ptr[0:64, 0:P]), r=[ptr], w=[rfm])
        for j in range(4):
            p.op("tensor", lambda e, j=j: e.matmul(
                pq[:, :], rfm[:, :], wup_sb[:, j * 512:(j + 1) * 512],
                start=True, stop=True), r=[rfm, wup_sb], w=[pq])
            p.op("vector", lambda e, j=j: e.tensor_add(out=kvb16[:, j * 512:(j + 1) * 512],
                                                       in0=pq[:, :], in1=bup_bc[:, j * 512:(j + 1) * 512]),
                 r=[pq, bup_bc], w=[kvb16])
        for hh in range(2):
            p.op("scalar", lambda e, hh=hh: e.activation(out=sq[:, 0:D], in_=kvb16[:, hh * D:(hh + 1) * D],
                                                         func=AF.Square), r=[kvb16], w=[sq])
            p.op("vector", lambda e, hh=hh: e.tensor_reduce(
                out=kn[:, hh * 8:(hh + 1) * 8], in_=bass.AP(sq.ap().tensor, 0, [[D, P], [P, 8], [1, 64]]),
                axis=AX.X, op=ALU.add), r=[sq], w=[kn])
        p.op("vector", lambda e: e.tensor_reduce(out=st[:, 6:7], in_=kn[:], axis=AX.X, op=ALU.max),
             r=[kn], w=[st])
        p.op("scalar", lambda e: e.activation(out=sq[:, 0:D], in_=qb16[:], func=AF.Square),
             r=[qb16], w=[sq])
        p.op("vector", lambda e: e.tensor_reduce(
            out=qn[:], in_=bass.AP(sq.ap().tensor, 0, [[D, P], [64, 16], [1, 64]]),
            axis=AX.X, op=ALU.add), r=[sq], w=[qn])
        p.op("vector", lambda e: e.tensor_scalar(out=bnd[:], in0=qn[:], scalar1=st[:, 6:7],
                                                 scalar2=None, op0=ALU.mult), r=[qn, st], w=[bnd])
        p.op("scalar", lambda e: e.activation(out=bnd[:], in_=bnd[:], func=AF.Sqrt, scale=1.0 / 64.0),
             r=[bnd], w=[bnd])
        p.dma("gpsimd", q_scr.rearrange("(a b) d -> a (b d)", b=16), qb16[:], r=[qb16], w=[q_scr])
        p.dma("gpsimd", kv_scr.rearrange("(a b) d -> a (b d)", b=16), kvb16[:], r=[kvb16], w=[kv_scr])
        p.dma("sync", bnd_scr[:, :], bnd[:], r=[bnd], w=[bnd_scr])
        p.dma("sync", bnd_row[0:1, :], bnd_scr.rearrange("(o a) b -> o (a b)", o=1),
              r=[bnd_scr], w=[bnd_row])
        lhsb = [lhs_a, lhs_b]
        rhsb = [rhs_a, rhs_b]
        ptrs = [ptr, ptr2f]
        qthb = [qth, qth2]
        kvtgb = [kvtg, kvtg2]
        for wave in range(2):
            for mci in range(8):
                mc = wave * 8 + mci
                pb_ = mci % 2
                p.dma("gpsimd", qthb[pb_][:, :], q_scr[mc * P:(mc + 1) * P, :], r=[q_scr], w=[qthb[pb_]])
                p.dma("gpsimd", kvtgb[pb_][:, :], kv_scr[mc * P:(mc + 1) * P, :], r=[kv_scr], w=[kvtgb[pb_]])
                p.op("tensor", lambda e, pb_=pb_: e.transpose(ptrs[pb_][0:64, 0:P], qthb[pb_].ap(), eyeT[:]),
                     r=[qthb[pb_], eyeT], w=[ptrs[pb_]])
                p.op("scalar", lambda e, pb_=pb_: e.activation(out=lhsb[pb_][0:64, :], in_=ptrs[pb_][0:64, 0:P],
                                                      func=AF.Identity, scale=0.125), r=[ptrs[pb_]], w=[lhsb[pb_]])
                p.op("scalar", lambda e, mc=mc, pb_=pb_: e.activation(out=lhsb[pb_][64:65, :],
                                                             in_=bnd_row[0:1, mc * P:(mc + 1) * P],
                                                             func=AF.Identity, scale=-1.0),
                     r=[bnd_row], w=[lhsb[pb_]])
                p.op("tensor", lambda e, pb_=pb_: e.transpose(ptrs[pb_][0:64, 0:P], kvtgb[pb_][:, 0:64], eyeT[:]),
                     r=[kvtgb[pb_], eyeT], w=[ptrs[pb_]])
                p.op("scalar", lambda e, pb_=pb_: e.copy(out=rhsb[pb_][0:64, :], in_=ptrs[pb_][0:64, 0:P]),
                     r=[ptrs[pb_]], w=[rhsb[pb_]])
                p.op("tensor", lambda e, mci=mci, pb_=pb_: e.matmul(patt[:, mci * P:(mci + 1) * P],
                                                         lhsb[pb_][:, :], rhsb[pb_][:, :],
                                                         start=True, stop=True),
                     r=[lhsb[pb_], rhsb[pb_]], w=[patt])
            p.op("scalar", lambda e, wave=wave: e.activation(out=probs[:, wave * 1024:(wave + 1) * 1024],
                                                             in_=patt[:, :], func=AF.Exp),
                 r=[patt], w=[probs])
        def mask_emit(e):
            in1 = bass.AP(bdm_sb.ap().tensor, 0, [[P, P], [0, 16], [1, P]])
            return e.tensor_tensor(out=probs.ap().rearrange("p (a c) -> p a c", c=P),
                                   in0=probs.ap().rearrange("p (a c) -> p a c", c=P),
                                   in1=in1, op=ALU.mult)
        p.op("vector", mask_emit, r=[probs, bdm_sb], w=[probs])
        p.op("vector", lambda e: e.tensor_reduce(
            out=rs[:], in_=probs.ap().rearrange("p (a c) -> p a c", c=P), axis=AX.X, op=ALU.add),
            r=[probs], w=[rs])
        p.op("vector", lambda e: e.reciprocal(out=rs[:], in_=rs[:]), r=[rs], w=[rs])
        def norm_emit(e):
            in1 = bass.AP(rs.ap().tensor, 0, [[16, P], [1, 16], [0, P]])
            return e.tensor_tensor(out=probs.ap().rearrange("p (a c) -> p a c", c=P),
                                   in0=probs.ap().rearrange("p (a c) -> p a c", c=P),
                                   in1=in1, op=ALU.mult)
        p.op("vector", norm_emit, r=[probs, rs], w=[probs])
        for mc in range(16):
            pb_ = mc % 2
            ptx = [ptr, ptr2f][pb_]
            kvx = [kvtg, kvtg2][pb_]
            pTx = [probsT, probsT2][pb_]
            p.op("tensor", lambda e, mc=mc, ptx=ptx: e.transpose(ptx[:], probs[:, mc * P:(mc + 1) * P], eyeT[:]),
                 r=[probs, eyeT], w=[ptx])
            p.op("scalar", lambda e, ptx=ptx, pTx=pTx: e.copy(out=pTx[:], in_=ptx[:]), r=[ptx], w=[pTx])
            p.dma("gpsimd", kvx[:, :], kv_scr[mc * P:(mc + 1) * P, :], r=[kv_scr], w=[kvx])
            p.op("tensor", lambda e, mc=mc, kvx=kvx, pTx=pTx: e.matmul(pout[:, :], pTx[:, :], kvx[:, 64:P],
                                                     start=True, stop=True),
                 r=[pTx, kvx], w=[pout])
            p.op("scalar", lambda e: e.copy(out=oth[:], in_=pout[:, :]), r=[pout], w=[oth])
            p.dma("sync", o_scr[mc * P:(mc + 1) * P, :], oth[:, :], r=[oth], w=[o_scr])
        p.dma("sync", ott[:, :], o_scr.rearrange("(a b) d -> a (b d)", b=16), r=[o_scr], w=[ott])
        for c in range(8):
            p.op("tensor", lambda e, c=c: e.transpose(ptr[:], ott[:, c * P:(c + 1) * P], eyeT[:]),
                 r=[ott, eyeT], w=[ptr])
            p.op("scalar", lambda e, c=c: e.copy(out=otfm[:, c * P:(c + 1) * P], in_=ptr[:]),
                 r=[ptr], w=[otfm])
        for j in range(2):
            for k in range(8):
                p.op("tensor", lambda e, j=j, k=k: e.matmul(
                    pq[:, :], otfm[:, k * P:(k + 1) * P], wo_sb[:, k * D + j * 512:k * D + (j + 1) * 512],
                    start=(k == 0), stop=(k == 7)), r=[otfm, wo_sb], w=[pq])
            p.op("vector", lambda e, j=j: e.tensor_add(out=x2t[:, j * 512:(j + 1) * 512],
                                                       in0=pq[:, :], in1=bo_bc[:, j * 512:(j + 1) * 512]),
                 r=[pq, bo_bc], w=[x2t])
        p.op("vector", lambda e: e.tensor_add(out=x2t[:], in0=x2t[:], in1=xn[:]), r=[x2t, xn], w=[x2t])
        p.dma("sync", x2_scr[i * P:(i + 1) * P, :], x2t[:], r=[x2t], w=[x2_scr])
        emit_rms_tile(b, x2t, h2t, sq, st, D, g2_bc, eps=1e-6)
        p.op("vector", lambda e: e.tensor_copy(out=h2b[:], in_=h2t[:]), r=[h2t], w=[h2b])
        for c in range(8):
            p.op("tensor", lambda e, c=c: e.transpose(ptr[:], h2b[:, c * P:(c + 1) * P], eyeT[:]),
                 r=[h2b, eyeT], w=[ptr])
            p.op("scalar", lambda e, c=c, i=i: e.copy(out=h2fm[:, c * R + i * P:c * R + (i + 1) * P],
                                                      in_=ptr[:]), r=[ptr], w=[h2fm])
        for k in range(8):
            p.op("tensor", lambda e, k=k, i=i: e.matmul(
                pout[:, 0:E], h2fm[:, k * R + i * P:k * R + (i + 1) * P], wg_sb[:, k * E:(k + 1) * E],
                start=(k == 0), stop=(k == 7)), r=[h2fm, wg_sb], w=[pout])
        p.op("vector", lambda e: e.tensor_add(out=gt32[:], in0=pout[:, 0:E], in1=bg_bc[:, 0:E]),
             r=[pout, bg_bc], w=[gt32])
        p.op("vector", lambda e: e.tensor_reduce(out=st[:, 6:7], in_=gt32[:], axis=AX.X, op=ALU.max),
             r=[gt32], w=[st])
        p.op("vector", lambda e: e.tensor_scalar(out=gt32[:], in0=gt32[:], scalar1=st[:, 6:7],
                                                 scalar2=None, op0=ALU.subtract), r=[gt32, st], w=[gt32])
        p.op("scalar", lambda e: e.activation(out=gt32[:], in_=gt32[:], func=AF.Exp,
                                              accum_out=st[:, 7:8]), r=[gt32], w=[gt32, st])
        p.op("vector", lambda e: e.reciprocal(out=st[:, 7:8], in_=st[:, 7:8]), r=[st], w=[st])
        p.op("vector", lambda e, i=i: e.tensor_scalar(out=gate[:, i * E:(i + 1) * E], in0=gt32[:],
                                                      scalar1=st[:, 7:8], scalar2=None, op0=ALU.mult),
             r=[gt32, st], w=[gate])

    b.free(wup_sb)
    b.free(wdn_sb)
    b.free(wq_sb)
    we_sb = [b.sbuf([P, 8 * D], BF16, name=f"we_sb{j}") for j in range(2)]
    xo = xt
    x2r = xn
    for half in range(2):
        hi0 = half * (nT // 2)
        for ci in range(nT // 2):
            p.op("vector", lambda e, ci=ci: e.memset(comb[ci][:, :], 0.0), w=[comb[ci]])
        for ei in range(E):
            wb = we_sb[ei % 2]
            for k in range(8):
                p.dma("sync", wb[:, k * D:(k + 1) * D], we[ei * D + k * P:ei * D + (k + 1) * P, :], w=[wb])
            for ci in range(nT // 2):
                i = hi0 + ci
                for j in range(2):
                    pqx = pq if j == 0 else pq2
                    for k in range(8):
                        p.op("tensor", lambda e, i=i, j=j, k=k, wb=wb, pqx=pqx: e.matmul(
                            pqx[:, :], h2fm[:, k * R + i * P:k * R + (i + 1) * P],
                            wb[:, k * D + j * 512:k * D + (j + 1) * 512],
                            start=(k == 0), stop=(k == 7)), r=[h2fm, wb], w=[pqx])
                    p.op("vector", lambda e, i=i, j=j, ei=ei, ci=ci, pqx=pqx: e.scalar_tensor_tensor(
                        out=comb[ci][:, j * 512:(j + 1) * 512], in0=pqx[:, :],
                        scalar=gate[:, i * E + ei:i * E + ei + 1],
                        in1=comb[ci][:, j * 512:(j + 1) * 512], op0=ALU.mult, op1=ALU.add),
                        r=[pqx, gate, comb[ci]], w=[comb[ci]])
        for ci in range(nT // 2):
            i = hi0 + ci
            p.dma("sync", xo[:], x_in[i * P:(i + 1) * P, :], w=[xo])
            p.dma("gpsimd", x2r[:], x2_scr[i * P:(i + 1) * P, :], r=[x2_scr], w=[x2r])
            p.op("vector", lambda e, ci=ci: e.tensor_add(out=x2r[:], in0=x2r[:], in1=comb[ci][:, :]),
                 r=[x2r, comb[ci]], w=[x2r])
            p.op("vector", lambda e: e.tensor_add(out=x2r[:], in0=x2r[:], in1=xo[:]), r=[x2r, xo], w=[x2r])
            p.dma("sync", out[i * P:(i + 1) * P, :], x2r[:], r=[x2r], w=[out])
    p.op("gpsimd", lambda e: e.engine_nop(), no_inc=True, r=[out])
    nc = b.build()
    _prog_cache[key] = nc
    return nc


def run_l0ffn(x, gamma1, Wq, bq_, Wdown, bdown, Wup, bup_, Wo, bo_, gamma2, Wg, bg_, We):
    B, T, D = x.shape
    rows = x.reshape(B * T, D)
    R = (B * T) // NCORES
    nc = build_l0ffn(R)
    eye = np.eye(P, dtype=np.float32).astype(NP_BF16)
    bdm = np.zeros((P, P), np.float32)
    for t in range(8):
        bdm[t * 16:(t + 1) * 16, t * 16:(t + 1) * 16] = 1.0
    wts = {"eye": eye, "wq": _bf(Wq), "bq": bq_.reshape(-1, 1), "wdn": _bf(Wdown),
           "bdn": bdown.reshape(-1, 1), "wup": _bf(Wup), "bup": bup_.reshape(-1, 1),
           "wo": _bf(Wo), "bo": bo_.reshape(-1, 1), "g1": gamma1.reshape(-1, 1),
           "g2": gamma2.reshape(-1, 1), "wg": _bf(Wg), "bg": bg_.reshape(-1, 1),
           "we": _bf(We.reshape(8 * 1024, 1024)), "bdm": bdm}
    in_maps = []
    for c in range(NCORES):
        im = dict(wts)
        im["x"] = np.ascontiguousarray(rows[c * R:(c + 1) * R])
        in_maps.append(im)
    res = run_bass_kernel_spmd(nc, in_maps, core_ids=list(range(NCORES)))
    outr = np.concatenate([res.results[c]["out"] for c in range(NCORES)], 0)
    return outr.reshape(B, T, D)


def build_attnb_proj(R=2048):
    key = ("attnb_proj", R)
    if key in _prog_cache:
        return _prog_cache[key]
    D = 1024
    nT = R // P
    b = Builder()
    p = b.p
    x_in = b.din("x", [R, D])
    b.eye_hbm = b.din("eye", [P, P], BF16)
    wqT = b.din("wqT", [D, D], BF16)
    wkT = b.din("wkT", [D, D], BF16)
    wvT = b.din("wvT", [D, D], BF16)
    qs = b.din("qscale", [P, 1])
    q_out = b.dout("q", [R, D], BF16)
    k_out = b.dout("k", [R, D], BF16)
    v_out = b.dout("v", [R, D], BF16)

    eyeT = b.get_eye()
    ptr = b.get_ptr()
    qsc = b.sbuf([P, 1], name="qsc")
    p.dma("sync", qsc[:], qs[:, :], w=[qsc])
    w_sb = {}
    for nm, src in (("q", wqT), ("k", wkT), ("v", wvT)):
        t = b.sbuf([P, 8 * D], BF16, name=f"w{nm}sb")
        for kk in range(8):
            p.dma("sync", t[:, kk * D:(kk + 1) * D], src[kk * P:(kk + 1) * P, :], w=[t])
        w_sb[nm] = t
    xt = b.sbuf([P, D], name="xta")
    xn = b.sbuf([P, D], name="xna")
    sq = b.sbuf([P, D], name="sqa")
    st = b.sbuf([P, 8], name="sta")
    xnb = b.sbuf([P, D], BF16, name="xnba")
    xf = b.sbuf([P, 8 * P], BF16, name="xfa")
    pm = b.psum([P, 512], name="pma")
    ob = b.sbuf([P, D], BF16, name="oba")
    for i in range(nT):
        p.dma("sync", xt[:], x_in[i * P:(i + 1) * P, :], w=[xt])
        emit_plain_ln(b, xt, xn, sq, st, D)
        p.op("vector", lambda e: e.tensor_copy(out=xnb[:], in_=xn[:]), r=[xn], w=[xnb])
        for c in range(8):
            p.op("tensor", lambda e, c=c: e.transpose(ptr[:], xnb[:, c * P:(c + 1) * P], eyeT[:]),
                 r=[xnb, eyeT], w=[ptr])
            p.op("scalar", lambda e, c=c: e.copy(out=xf[:, c * P:(c + 1) * P], in_=ptr[:]),
                 r=[ptr], w=[xf])
        for nm, outd in (("q", q_out), ("k", k_out), ("v", v_out)):
            for j in range(2):
                for kk in range(8):
                    p.op("tensor", lambda e, nm=nm, j=j, kk=kk: e.matmul(
                        pm[:, :], xf[:, kk * P:(kk + 1) * P],
                        w_sb[nm][:, kk * D + j * 512:kk * D + (j + 1) * 512],
                        start=(kk == 0), stop=(kk == 7)), r=[xf, w_sb[nm]], w=[pm])
                if nm == "q":
                    p.op("vector", lambda e, j=j: e.tensor_scalar(
                        out=ob[:, j * 512:(j + 1) * 512], in0=pm[:, :], scalar1=qsc[:, 0:1],
                        scalar2=None, op0=ALU.mult), r=[pm, qsc], w=[ob])
                else:
                    p.op("scalar", lambda e, j=j: e.copy(out=ob[:, j * 512:(j + 1) * 512],
                                                         in_=pm[:, :]), r=[pm], w=[ob])
            p.dma("sync", outd[i * P:(i + 1) * P, :], ob[:], r=[ob], w=[outd])
    p.op("gpsimd", lambda e: e.engine_nop(), no_inc=True, r=[q_out, k_out, v_out])
    nc = b.build()
    _prog_cache[key] = nc
    return nc


def build_attnb_attn(T=8192, SLOT=512, NSLOT=2):
    key = ("attnb_attn", T, SLOT, NSLOT)
    if key in _prog_cache:
        return _prog_cache[key]
    D = 1024
    GS = 1024
    nG = T // GS
    b = Builder()
    p = b.p
    qf = [b.din(f"qf{s}", [D, 2 * SLOT], BF16) for s in range(NSLOT)]
    kT0 = b.din("kT0", [D, T], BF16)
    kT1 = b.din("kT1", [D, T], BF16)
    v0 = b.din("v0", [T, D], BF16)
    v1 = b.din("v1", [T, D], BF16)
    rowp = b.din("rowp", [1, NSLOT * SLOT])
    pidx = b.din("pidx", [P, 1])
    xr = b.din("xr", [2 * NSLOT * SLOT, D])
    out = b.dout("out", [2 * NSLOT * SLOT, D])

    onesr = b.get_onesr()
    onesc = b.sbuf([P, 1], BF16, name="onescb")
    p.op("vector", lambda e: e.memset(onesc[:], 1.0), w=[onesc])
    pmisc = b.get_pmisc()
    rp_sb = b.sbuf([1, NSLOT * SLOT], name="rp_sb")
    p.dma("sync", rp_sb[:, :], rowp[:, :], w=[rp_sb])
    rp_bc = [b.sbuf([P, SLOT], name=f"rpbc{s}") for s in range(NSLOT)]
    for s in range(NSLOT):
        emit_bcast_row(b, rp_bc[s], rp_sb[0:1, s * SLOT:(s + 1) * SLOT], SLOT, onesr, pmisc,
                       r_extra=[rp_sb])
    kg = [b.sbuf([P, 8 * GS], BF16, name=f"kg{bb}") for bb in range(2)]
    vg = [b.sbuf([P, 8 * D], BF16, name=f"vg{bb}") for bb in range(2)]
    qsb = [b.sbuf([P, 8 * 2 * SLOT], BF16, name=f"qsb{s}") for s in range(NSLOT)]
    for s in range(NSLOT):
        for kk in range(8):
            p.dma("sync", qsb[s][:, kk * 2 * SLOT:(kk + 1) * 2 * SLOT],
                  qf[s][kk * P:(kk + 1) * P, :], w=[qsb[s]])
    acc = [[b.sbuf([P, (SLOT // P) * D], name=f"acc{s}_{bb}") for bb in range(2)]
           for s in range(NSLOT)]
    for s in range(NSLOT):
        for bb in range(2):
            p.op("vector", lambda e, s=s, bb=bb: e.memset(acc[s][bb][:, :], 0.0), w=[acc[s][bb]])
    pSv = b.psum([1, 2 * D], name="pSv")
    pD = b.psum([P, SLOT], name="pD")
    pV = b.psum([P, D], name="pV")
    scol = b.sbuf([P, 1], name="scol")
    siota = b.sbuf([P, 1], name="siota")
    p.dma("sync", siota[:], pidx[:, :], w=[siota])
    a0 = b.sbuf([P, SLOT], BF16, name="a0b")
    a1 = b.sbuf([P, SLOT], BF16, name="a1b")
    msk = b.sbuf([P, SLOT], name="mskb")
    ae = [b.sbuf([P, 8 * SLOT], BF16, name=f"ae{bb}") for bb in range(2)]
    Svbc = [b.sbuf([P, D], name=f"Svbc{bb}") for bb in range(2)]
    xrt = b.sbuf([P, D], name="xrtb")
    outt = b.sbuf([P, D], name="outtb")

    for g in range(nG):
        for bb, (kT, vv) in enumerate(((kT0, v0), (kT1, v1))):
            for kk in range(8):
                p.dma("sync", kg[bb][:, kk * GS:(kk + 1) * GS],
                      kT[kk * P:(kk + 1) * P, g * GS:(g + 1) * GS], w=[kg[bb]])
            for j in range(8):
                p.dma("gpsimd", vg[bb][:, j * D:(j + 1) * D],
                      vv[g * GS + j * P:g * GS + (j + 1) * P, :], w=[vg[bb]])
            for j in range(8):
                for n in range(2):
                    p.op("tensor", lambda e, bb=bb, j=j, n=n, g=g: e.matmul(
                        pSv[0:1, bb * D + n * 512:bb * D + (n + 1) * 512], onesc[:, 0:1],
                        vg[bb][:, j * D + n * 512:j * D + (n + 1) * 512],
                        start=(g == 0 and j == 0), stop=(g == nG - 1 and j == 7)),
                        r=[onesc, vg[bb]], w=[pSv])
        for s in range(NSLOT):
            for sc in range(8):
                for bb in range(2):
                    for kk in range(8):
                        p.op("tensor", lambda e, s=s, sc=sc, bb=bb, kk=kk: e.matmul(
                            pD[:, :], kg[bb][:, kk * GS + sc * P:kk * GS + (sc + 1) * P],
                            qsb[s][:, kk * 2 * SLOT + bb * SLOT:kk * 2 * SLOT + (bb + 1) * SLOT],
                            start=(bb == 0 and kk == 0), stop=(bb == 1 and kk == 7)),
                            r=[kg[bb], qsb[s]], w=[pD])
                p.op("scalar", lambda e: e.activation(out=a0[:], in_=pD[:, :], func=AF.Sigmoid),
                     r=[pD], w=[a0])
                p.op("scalar", lambda e: e.activation(out=a1[:], in_=pD[:, :], func=AF.Sigmoid,
                                                      scale=-1.0), r=[pD], w=[a1])
                p.op("vector", lambda e, g=g, sc=sc: e.tensor_scalar(
                    out=scol[:], in0=siota[:], scalar1=float(g * GS + sc * P), scalar2=None,
                    op0=ALU.add), r=[siota], w=[scol])
                p.op("vector", lambda e, s=s: e.tensor_scalar(
                    out=msk[:], in0=rp_bc[s][:, :], scalar1=scol[:, 0:1], scalar2=None,
                    op0=ALU.is_ge), r=[rp_bc[s], scol], w=[msk])
                for bb, at in ((0, a0), (1, a1)):
                    p.op("vector", lambda e, at=at, bb=bb, sc=sc: e.scalar_tensor_tensor(
                        out=ae[bb][:, sc * SLOT:(sc + 1) * SLOT], in0=at[:], scalar=1e-9,
                        in1=msk[:], op0=ALU.subtract, op1=ALU.mult),
                        r=[at, msk], w=[ae[bb]])
            for bb in range(2):
                for mt in range(SLOT // P):
                    for n in range(2):
                        for sc in range(8):
                            p.op("tensor", lambda e, bb=bb, mt=mt, n=n, sc=sc, s=s: e.matmul(
                                pV[:, n * 512:(n + 1) * 512],
                                ae[bb][:, sc * SLOT + mt * P:sc * SLOT + (mt + 1) * P],
                                vg[bb][:, sc * D + n * 512:sc * D + (n + 1) * 512],
                                start=(sc == 0), stop=(sc == 7)), r=[ae[bb], vg[bb]], w=[pV])
                    p.op("vector", lambda e, s=s, bb=bb, mt=mt: e.tensor_add(
                        out=acc[s][bb][:, mt * D:(mt + 1) * D], in0=acc[s][bb][:, mt * D:(mt + 1) * D],
                        in1=pV[:, :]), r=[pV, acc[s][bb]], w=[acc[s][bb]])
    for bb in range(2):
        p.op("scalar", lambda e, bb=bb: e.copy(out=Svbc[0][0:1, 0:D] if bb == 0 else Svbc[1][0:1, 0:D],
                                               in_=pSv[0:1, bb * D:(bb + 1) * D]), r=[pSv], w=[Svbc[bb]])
        emit_bcast_row(b, Svbc[bb], Svbc[bb][0:1, 0:D], D, onesr, pmisc, r_extra=[Svbc[bb]])
    for bb in range(2):
        for s in range(NSLOT):
            for mt in range(SLOT // P):
                r0 = bb * NSLOT * SLOT + s * SLOT + mt * P
                p.dma("sync", xrt[:], xr[r0:r0 + P, :], w=[xrt])
                p.op("vector", lambda e, s=s, bb=bb, mt=mt: e.scalar_tensor_tensor(
                    out=outt[:], in0=Svbc[bb][:, :], scalar=1e-9, in1=acc[s][bb][:, mt * D:(mt + 1) * D],
                    op0=ALU.mult, op1=ALU.add), r=[Svbc[bb], acc[s][bb]], w=[outt])
                p.op("vector", lambda e: e.tensor_add(out=outt[:], in0=outt[:], in1=xrt[:]),
                     r=[outt, xrt], w=[outt])
                p.dma("sync", out[r0:r0 + P, :], outt[:], r=[outt], w=[out])
    p.op("gpsimd", lambda e: e.engine_nop(), no_inc=True, r=[out])
    nc = b.build()
    _prog_cache[key] = nc
    return nc


def run_attnb(x):
    raise NotImplementedError


def run_attnb_full(x, Wq, Wk, Wv):
    B, T, D = x.shape
    rows = x.reshape(B * T, D)
    R = (B * T) // NCORES
    eye = np.eye(P, dtype=np.float32).astype(NP_BF16)
    nc1 = build_attnb_proj(R)
    in_maps = []
    for c in range(NCORES):
        sgn = 1.0 if c < 4 else -1.0
        in_maps.append({"x": np.ascontiguousarray(rows[c * R:(c + 1) * R]), "eye": eye,
                        "wqT": _bf(Wq.T), "wkT": _bf(Wk.T), "wvT": _bf(Wv.T),
                        "qscale": np.full((P, 1), sgn / 32.0, np.float32)})
    res = run_bass_kernel_spmd(nc1, in_maps, core_ids=list(range(NCORES)))
    q = np.concatenate([np.asarray(res.results[c]["q"]) for c in range(NCORES)], 0).reshape(B, T, D)
    k = np.concatenate([np.asarray(res.results[c]["k"]) for c in range(NCORES)], 0).reshape(B, T, D)
    v = np.concatenate([np.asarray(res.results[c]["v"]) for c in range(NCORES)], 0).reshape(B, T, D)
    kT0 = np.ascontiguousarray(k[0].reshape(D, T))
    kT1 = np.ascontiguousarray(k[1].reshape(D, T))
    nc2 = build_attnb_attn(T, 512, 2)
    in_maps = []
    slot_of = [(c, 15 - c) for c in range(NCORES)]
    for c in range(NCORES):
        s0, s1 = slot_of[c]
        qf = []
        for s in (s0, s1):
            q0 = np.ascontiguousarray(q[0, s * 512:(s + 1) * 512].T)
            q1 = np.ascontiguousarray(q[1, s * 512:(s + 1) * 512].T)
            qf.append(np.concatenate([q0, q1], 1))
        rowp = np.concatenate([np.arange(s * 512, (s + 1) * 512, dtype=np.float32)
                               for s in (s0, s1)]).reshape(1, -1)
        xrows = np.concatenate([x[bb, s * 512:(s + 1) * 512] for bb in range(2) for s in (s0, s1)], 0)
        in_maps.append({"qf0": qf[0].astype(NP_BF16, copy=False), "qf1": qf[1].astype(NP_BF16, copy=False),
                        "kT0": kT0, "kT1": kT1, "v0": np.ascontiguousarray(v[0]),
                        "v1": np.ascontiguousarray(v[1]), "rowp": rowp,
                        "pidx": np.arange(P, dtype=np.float32).reshape(P, 1),
                        "xr": np.ascontiguousarray(xrows)})
    res2 = run_bass_kernel_spmd(nc2, in_maps, core_ids=list(range(NCORES)))
    xout = np.empty((B, T, D), np.float32)
    for c in range(NCORES):
        s0, s1 = slot_of[c]
        o = np.asarray(res2.results[c]["out"])
        for bi, bb in enumerate(range(2)):
            for si, s in enumerate((s0, s1)):
                xout[bb, s * 512:(s + 1) * 512] = o[(bi * 2 + si) * 512:(bi * 2 + si + 1) * 512]
    return xout


def build_l1ab_proj(R=2048):
    key = ("l1ab_proj", R)
    if key in _prog_cache:
        return _prog_cache[key]
    D, DO = 1024, 3072
    nT = R // P
    b = Builder()
    p = b.p
    x_in = b.din("x", [R, D])
    b.eye_hbm = b.din("eye", [P, P], BF16)
    wT = b.din("wT", [D, DO], BF16)
    qkv_out = b.dout("qkv", [R, DO], BF16)
    eyeT = b.get_eye()
    ptr = b.get_ptr()
    w_sb = b.sbuf([P, 8 * DO], BF16, name="wqkvsb")
    for kk in range(8):
        p.dma("sync", w_sb[:, kk * DO:(kk + 1) * DO], wT[kk * P:(kk + 1) * P, :], w=[w_sb])
    xt = b.sbuf([P, D], name="xtc")
    xn = b.sbuf([P, D], name="xnc")
    sq = b.sbuf([P, D], name="sqc")
    st = b.sbuf([P, 8], name="stc")
    xnb = b.sbuf([P, D], BF16, name="xnbc")
    xf = b.sbuf([P, 8 * P], BF16, name="xfc")
    pm = b.psum([P, 512], name="pmc")
    ob = b.sbuf([P, DO], BF16, name="obc")
    for i in range(nT):
        p.dma("sync", xt[:], x_in[i * P:(i + 1) * P, :], w=[xt])
        emit_plain_ln(b, xt, xn, sq, st, D)
        p.op("vector", lambda e: e.tensor_copy(out=xnb[:], in_=xn[:]), r=[xn], w=[xnb])
        for c in range(8):
            p.op("tensor", lambda e, c=c: e.transpose(ptr[:], xnb[:, c * P:(c + 1) * P], eyeT[:]),
                 r=[xnb, eyeT], w=[ptr])
            p.op("scalar", lambda e, c=c: e.copy(out=xf[:, c * P:(c + 1) * P], in_=ptr[:]),
                 r=[ptr], w=[xf])
        for j in range(6):
            for kk in range(8):
                p.op("tensor", lambda e, j=j, kk=kk: e.matmul(
                    pm[:, :], xf[:, kk * P:(kk + 1) * P],
                    w_sb[:, kk * DO + j * 512:kk * DO + (j + 1) * 512],
                    start=(kk == 0), stop=(kk == 7)), r=[xf, w_sb], w=[pm])
            p.op("scalar", lambda e, j=j: e.copy(out=ob[:, j * 512:(j + 1) * 512], in_=pm[:, :]),
                 r=[pm], w=[ob])
        p.dma("sync", qkv_out[i * P:(i + 1) * P, :], ob[:], r=[ob], w=[qkv_out])
    p.op("gpsimd", lambda e: e.engine_nop(), no_inc=True, r=[qkv_out])
    nc = b.build()
    _prog_cache[key] = nc
    return nc


def build_l1ab_attn(T=8192, CH=512, NH=4):
    key = ("l1ab_attn", T, CH, NH)
    if key in _prog_cache:
        return _prog_cache[key]
    Dh = 64
    nC = T // CH
    b = Builder()
    p = b.p
    qfm = b.din("qfm", [NH * Dh, T], BF16)
    kfm = b.din("kfm", [NH * Dh, T], BF16)
    vtk = b.din("vtk", [T, NH * Dh], BF16)
    a_in = b.din("a_in", [1, 1])
    b_in = b.din("b_in", [1, 1])
    dt_in = b.din("dt_in", [1, 1])
    out = b.dout("out", [T, NH * Dh], BF16)
    fs_scr = b.dscratch("fs_scr", [NH * nC, CH])

    onesr = b.get_onesr()
    pmisc = b.get_pmisc()
    ones1 = b.sbuf([1, P], BF16, name="ones1b")
    p.op("vector", lambda e: e.memset(ones1[:], 1.0), w=[ones1])
    onescb = b.sbuf([P, 1], BF16, name="onescb2")
    p.op("vector", lambda e: e.memset(onescb[:], 1.0), w=[onescb])

    q_sb = b.sbuf([P, 2 * T], BF16, name="q_sb2")
    k_sb = b.sbuf([P, 2 * T], BF16, name="k_sb2")
    v_sb = b.sbuf([P, (T // P) * NH * Dh], BF16, name="v_sb2")
    for j in range(2):
        p.dma("sync", q_sb[:, j * T:(j + 1) * T], qfm[j * P:(j + 1) * P, :], w=[q_sb])
        p.dma("sync", k_sb[:, j * T:(j + 1) * T], kfm[j * P:(j + 1) * P, :], w=[k_sb])
    for j in range(T // P):
        p.dma("sync", v_sb[:, j * NH * Dh:(j + 1) * NH * Dh], vtk[j * P:(j + 1) * P, :], w=[v_sb])
    ab_sb = b.sbuf([1, 8], name="ab_sb")
    p.dma("sync", ab_sb[0:1, 0:1], a_in[:, :], w=[ab_sb])
    p.dma("sync", ab_sb[0:1, 1:2], b_in[:, :], w=[ab_sb])
    p.dma("sync", ab_sb[0:1, 2:3], dt_in[:, :], w=[ab_sb])
    p.op("scalar", lambda e: e.activation(out=ab_sb[0:1, 3:4], in_=ab_sb[0:1, 2:3],
                                          func=AF.Identity, scale=1.0 / 12.5), r=[ab_sb], w=[ab_sb])
    p.op("vector", lambda e: e.tensor_tensor(out=ab_sb[0:1, 4:5], in0=ab_sb[0:1, 3:4],
                                             in1=ab_sb[0:1, 1:2], op=ALU.mult), r=[ab_sb], w=[ab_sb])
    onec2 = b.cconst(1.0)
    p.op("scalar", lambda e, onec2=onec2: e.activation(out=ab_sb[0:1, 5:6], in_=ab_sb[0:1, 4:5],
                                                       func=AF.Identity, bias=onec2[0:1, 0:1]),
         r=[ab_sb, b._cpool], w=[ab_sb])
    p.op("vector", lambda e: e.reciprocal(out=ab_sb[0:1, 6:7], in_=ab_sb[0:1, 5:6]), r=[ab_sb], w=[ab_sb])
    ab64 = b.sbuf([64, 8], name="ab64")
    p.op("tensor", lambda e: e.matmul(pmisc[0:64, 0:7], onesr[0:1, 0:64], ab_sb[0:1, 0:7],
                                      start=True, stop=True), r=[onesr, ab_sb], w=[pmisc])
    p.op("scalar", lambda e: e.copy(out=ab64[:, 0:7], in_=pmisc[0:64, 0:7]), r=[pmisc], w=[ab64])

    sqk = b.sbuf([P, T], BF16, name="sqk2")
    kn_row = b.sbuf([1, T], name="kn_row2")
    pkn = b.psum([1, 512], name="pkn2")
    sqk2 = b.sbuf([P, T], BF16, name="sqk3")
    p.op("scalar", lambda e: e.activation(out=sqk[:], in_=k_sb[:, 0:T], func=AF.Square),
         r=[k_sb], w=[sqk])
    p.op("scalar", lambda e: e.activation(out=sqk2[:], in_=k_sb[:, T:2 * T], func=AF.Square),
         r=[k_sb], w=[sqk2])
    for c in range(T // 512):
        p.op("tensor", lambda e, c=c: e.matmul(pkn[0:1, :], onescb[:, 0:1],
                                               sqk[:, c * 512:(c + 1) * 512], start=True, stop=False),
             r=[onescb, sqk], w=[pkn])
        p.op("tensor", lambda e, c=c: e.matmul(pkn[0:1, :], onescb[:, 0:1],
                                               sqk2[:, c * 512:(c + 1) * 512], start=False, stop=True),
             r=[onescb, sqk2], w=[pkn])
        p.op("scalar", lambda e, c=c: e.copy(out=kn_row[0:1, c * 512:(c + 1) * 512], in_=pkn[0:1, :]),
             r=[pkn], w=[kn_row])
    p.op("vector", lambda e: e.tensor_reduce(out=ab_sb[0:1, 7:8], in_=kn_row[0:1, :], axis=AX.X,
                                             op=ALU.max), r=[kn_row], w=[ab_sb])
    b.free(sqk2)
    b.free(kn_row)
    b.free(sqk)

    lrows = b.sbuf([NH * nC, CH], name="lrows2")
    frows = b.sbuf([NH * nC, CH], name="frows2")
    tmpr = b.sbuf([1, CH], name="tmpr2")
    sqq = b.sbuf([64, CH], BF16, name="sqq2")
    bndr = b.sbuf([1, CH], name="bndr2")
    bndb = b.sbuf([1, CH], BF16, name="bndb2")
    probsT = b.sbuf([P, (T // P) * CH], BF16, name="probsT2")
    pl = b.psum([1, CH], name="pl2")
    psc2b = [b.psum([P, CH], name=f"psc2{j}") for j in range(2)]
    pout = b.psum([P, 4 * Dh], name="pout2")
    colf = b.sbuf([P, 1], name="colf2")
    osb = b.sbuf([P, NH * Dh], BF16, name="osb2")
    o_scr = b.dscratch("o_scr2", [T, NH * Dh], BF16)

    for h in range(NH):
        jt = h // 2
        hb = (h % 2) * 64
        for ci in range(nC):
            ns = (ci + 1) * (CH // P)
            p.op("scalar", lambda e, jt=jt, hb=hb, ci=ci: e.activation(
                out=sqq[:, :], in_=q_sb[0 + hb:64 + hb, jt * T + ci * CH:jt * T + (ci + 1) * CH]
                if False else q_sb[hb:hb + 64, jt * T + ci * CH:jt * T + (ci + 1) * CH],
                func=AF.Square), r=[q_sb], w=[sqq])
            p.op("tensor", lambda e: e.matmul(pmisc[0:1, 0:CH], onescb[0:64, 0:1], sqq[:, :],
                                              start=True, stop=True), r=[onescb, sqq], w=[pmisc])
            p.op("vector", lambda e: e.tensor_scalar(out=bndr[0:1, :], in0=pmisc[0:1, 0:CH],
                                                     scalar1=ab_sb[0:1, 7:8], scalar2=None,
                                                     op0=ALU.mult), r=[pmisc, ab_sb], w=[bndr])
            p.op("scalar", lambda e: e.activation(out=bndr[0:1, :], in_=bndr[0:1, :], func=AF.Sqrt),
                 r=[bndr], w=[bndr])
            p.op("scalar", lambda e: e.activation(out=bndb[0:1, :], in_=bndr[0:1, :], func=AF.Identity,
                                                  scale=-1.0), r=[bndr], w=[bndb])
            for sc in range(ns):
                psc = psc2b[sc % 2]
                p.op("tensor", lambda e, jt=jt, hb=hb, ci=ci, sc=sc, psc=psc: e.matmul(
                    psc[:, :], k_sb[hb:hb + 64, jt * T + sc * P:jt * T + (sc + 1) * P],
                    q_sb[hb:hb + 64, jt * T + ci * CH:jt * T + (ci + 1) * CH],
                    start=True, stop=False), r=[k_sb, q_sb], w=[psc])
                p.op("tensor", lambda e, psc=psc: e.matmul(psc[:, :], ones1[0:1, 0:P], bndb[0:1, :],
                                                  start=False, stop=True), r=[ones1, bndb], w=[psc])
                p.op("scalar", lambda e, sc=sc, psc=psc: e.activation(out=probsT[:, sc * CH:(sc + 1) * CH],
                                                             in_=psc[:, :], func=AF.Exp, scale=0.125),
                     r=[psc], w=[probsT])
            for sc in range(ns):
                p.op("tensor", lambda e, sc=sc, ns=ns: e.matmul(pl[0:1, :], onescb[:, 0:1],
                                                                probsT[:, sc * CH:(sc + 1) * CH],
                                                                start=(sc == 0), stop=(sc == ns - 1)),
                     r=[onescb, probsT], w=[pl])
            p.op("scalar", lambda e: e.copy(out=tmpr[0:1, :], in_=pl[0:1, :]), r=[pl], w=[tmpr])
            p.dma("gpsimd", lrows[h * nC + ci:h * nC + ci + 1, :], tmpr[0:1, :], r=[tmpr], w=[lrows])
            for mt in range(CH // P):
                for sc in range(ns):
                    p.op("tensor", lambda e, mt=mt, sc=sc, ns=ns: e.matmul(
                        pout[:, 0:NH * Dh], probsT[:, sc * CH + mt * P:sc * CH + (mt + 1) * P],
                        v_sb[:, sc * NH * Dh:(sc + 1) * NH * Dh],
                        start=(sc == 0), stop=(sc == ns - 1)), r=[probsT, v_sb], w=[pout])
                p.op("scalar", lambda e, mt=mt, h=h: e.copy(
                    out=osb[:, 0:Dh], in_=pout[:, h * Dh:(h + 1) * Dh]), r=[pout], w=[osb])
                p.dma("sync", o_scr[ci * CH + mt * P:ci * CH + (mt + 1) * P, h * Dh:(h + 1) * Dh],
                      osb[:, 0:Dh], r=[osb], w=[o_scr])

    en = b.sbuf([NH * nC, CH], name="en2")
    t1 = b.sbuf([NH * nC, CH], name="t1_2")
    t2 = b.sbuf([NH * nC, CH], name="t2_2")
    t3 = b.sbuf([NH * nC, CH], name="t3_2")
    vv = b.sbuf([NH * nC, CH], name="vv2")
    ww = b.sbuf([NH * nC, CH], name="ww2")
    NR = NH * nC
    p.op("vector", lambda e: e.reciprocal(out=en[:, :], in_=lrows[:, :]), r=[lrows], w=[en])
    p.op("vector", lambda e: e.tensor_mul(out=en[:, :], in0=en[:, :], in1=lrows[:, :]),
         r=[en, lrows], w=[en])
    p.op("scalar", lambda e: e.activation(out=t1[:, :], in_=en[:, :], func=AF.Abs), r=[en], w=[t1])
    p.op("vector", lambda e: e.tensor_scalar_max(out=t2[:, :], in0=t1[:, :], scalar1=1e-6),
         r=[t1], w=[t2])
    p.op("vector", lambda e: e.reciprocal(out=t3[:, :], in_=t2[:, :]), r=[t2], w=[t3])
    p.op("vector", lambda e: e.tensor_mul(out=t2[:, :], in0=en[:, :], in1=t3[:, :]),
         r=[en, t3], w=[t2])
    fivec = b.cconst(-5.0)
    p.op("scalar", lambda e, fivec=fivec: e.activation(out=t1[:, :], in_=t1[:, :], func=AF.Sigmoid,
                                                       scale=10.0, bias=fivec[0:64, 0:1]), r=[t1, b._cpool], w=[t1])
    p.op("vector", lambda e: e.tensor_scalar(out=t1[:, :], in0=t1[:, :], scalar1=0.9, scalar2=0.1,
                                             op0=ALU.mult, op1=ALU.add), r=[t1], w=[t1])
    p.op("vector", lambda e: e.tensor_mul(out=t1[:, :], in0=t2[:, :], in1=t1[:, :]),
         r=[t1, t2], w=[t1])
    p.op("vector", lambda e: e.memset(vv[:, :], 0.0), w=[vv])
    p.op("vector", lambda e: e.memset(ww[:, :], 0.0), w=[ww])
    for step in range(2):
        p.op("scalar", lambda e: e.activation(out=t2[:, :], in_=vv[:, :], func=AF.Square),
             r=[vv], w=[t2])
        p.op("vector", lambda e: e.tensor_mul(out=t2[:, :], in0=t2[:, :], in1=vv[:, :]),
             r=[t2, vv], w=[t2])
        p.op("vector", lambda e: e.tensor_scalar(out=t2[:, :], in0=t2[:, :], scalar1=-1.0 / 3.0,
                                                 scalar2=None, op0=ALU.mult), r=[t2], w=[t2])
        p.op("vector", lambda e: e.tensor_add(out=t2[:, :], in0=t2[:, :], in1=vv[:, :]),
             r=[t2, vv], w=[t2])
        p.op("vector", lambda e: e.tensor_sub(out=t2[:, :], in0=t2[:, :], in1=ww[:, :]),
             r=[t2, ww], w=[t2])
        p.op("vector", lambda e: e.tensor_add(out=t2[:, :], in0=t2[:, :], in1=t1[:, :]),
             r=[t2, t1], w=[t2])
        p.op("vector", lambda e: e.scalar_tensor_tensor(out=t2[:, :], in0=t2[:, :],
                                                        scalar=ab64[:, 2:3], in1=vv[:, :],
                                                        op0=ALU.mult, op1=ALU.add),
             r=[t2, ab64, vv], w=[t2])
        p.op("vector", lambda e: e.tensor_scalar(out=frows[:, :],
                                                 in0=t2[:, :], scalar1=ab64[:, 0:1],
                                                 scalar2=ab64[:, 3:4], op0=ALU.add, op1=ALU.mult),
             r=[t2, ab64], w=[frows])
        p.op("vector", lambda e: e.tensor_add(out=ww[:, :], in0=ww[:, :], in1=frows[:, :]),
             r=[ww, frows], w=[ww])
        p.op("vector", lambda e: e.tensor_scalar(out=ww[:, :], in0=ww[:, :],
                                                 scalar1=ab64[:, 6:7], scalar2=None,
                                                 op0=ALU.mult), r=[ww, ab64], w=[ww])
        p.op("vector", lambda e: e.tensor_scalar_min(out=vv[:, :], in0=t2[:, :], scalar1=3.0),
             r=[t2], w=[vv])
        p.op("vector", lambda e: e.tensor_scalar_max(out=vv[:, :], in0=vv[:, :], scalar1=-3.0),
             r=[vv], w=[vv])
        p.op("vector", lambda e: e.tensor_scalar_min(out=ww[:, :], in0=ww[:, :], scalar1=3.0),
             r=[ww], w=[ww])
        p.op("vector", lambda e: e.tensor_scalar_max(out=ww[:, :], in0=ww[:, :], scalar1=-3.0),
             r=[ww], w=[ww])
    p.op("scalar", lambda e: e.activation(out=t1[:, :], in_=en[:, :], func=AF.Abs), r=[en], w=[t1])
    p.op("vector", lambda e: e.tensor_scalar_max(out=t1[:, :], in0=t1[:, :], scalar1=1e-6),
         r=[t1], w=[t1])
    p.op("vector", lambda e: e.tensor_mul(out=t2[:, :], in0=vv[:, :], in1=t1[:, :]),
         r=[vv, t1], w=[t2])
    p.op("scalar", lambda e: e.activation(out=t2[:, :], in_=t2[:, :], func=AF.Sigmoid),
         r=[t2], w=[t2])
    p.op("vector", lambda e: e.tensor_scalar(out=t2[:, :], in0=t2[:, :], scalar1=0.5, scalar2=0.5,
                                             op0=ALU.mult, op1=ALU.add), r=[t2], w=[t2])
    eps8 = b.cconst(1e-8)
    p.op("scalar", lambda e, eps8=eps8: e.activation(out=frows[:, :], in_=t2[:, :], func=AF.Identity,
                                                     bias=eps8[0:64, 0:1]), r=[t2, b._cpool], w=[frows])
    p.op("vector", lambda e: e.reciprocal(out=frows[:, :], in_=frows[:, :]), r=[frows], w=[frows])
    p.op("vector", lambda e: e.tensor_mul(out=frows[:, :], in0=t2[:, :], in1=frows[:, :]),
         r=[t2, frows], w=[frows])
    p.op("vector", lambda e: e.reciprocal(out=t3[:, :], in_=lrows[:, :]), r=[lrows], w=[t3])
    p.op("vector", lambda e: e.tensor_mul(out=frows[:, :], in0=frows[:, :], in1=t3[:, :]),
         r=[frows, t3], w=[frows])
    p.dma("sync", fs_scr[:, :], frows[:, :], r=[frows], w=[fs_scr])
    ot = b.sbuf([P, NH * Dh], BF16, name="ot2")
    fcol = b.sbuf([P, NH], name="fcol2")
    for ci in range(nC):
        for mt in range(CH // P):
            p.dma("sync", ot[:, :], o_scr[ci * CH + mt * P:ci * CH + (mt + 1) * P, :],
                  r=[o_scr], w=[ot])
            for h in range(NH):
                p.dma("gpsimd", fcol[:, h:h + 1],
                      bass.AP(fs_scr.tensor, (h * nC + ci) * CH + mt * P, [[1, P], [1, 1]]),
                      r=[fs_scr], w=[fcol])
            for h in range(NH):
                p.op("vector", lambda e, h=h: e.tensor_scalar(
                    out=ot[:, h * Dh:(h + 1) * Dh], in0=ot[:, h * Dh:(h + 1) * Dh],
                    scalar1=fcol[:, h:h + 1], scalar2=None, op0=ALU.mult), r=[ot, fcol], w=[ot])
            p.dma("sync", out[ci * CH + mt * P:ci * CH + (mt + 1) * P, :], ot[:, :],
                  r=[ot], w=[out])
    p.op("gpsimd", lambda e: e.engine_nop(), no_inc=True, r=[out])
    nc = b.build()
    _prog_cache[key] = nc
    return nc


def build_l1ab_out(R=2048):
    key = ("l1ab_out", R)
    if key in _prog_cache:
        return _prog_cache[key]
    D = 1024
    nT = R // P
    b = Builder()
    p = b.p
    x_in = b.din("x", [R, D])
    oin = b.din("oin", [R, D], BF16)
    b.eye_hbm = b.din("eye", [P, P], BF16)
    wT = b.din("wT", [D, D], BF16)
    out = b.dout("out", [R, D])
    eyeT = b.get_eye()
    ptr = b.get_ptr()
    w_sb = b.sbuf([P, 8 * D], BF16, name="woutsb")
    for kk in range(8):
        p.dma("sync", w_sb[:, kk * D:(kk + 1) * D], wT[kk * P:(kk + 1) * P, :], w=[w_sb])
    ot = b.sbuf([P, D], BF16, name="otd")
    of = b.sbuf([P, 8 * P], BF16, name="ofd")
    xt = b.sbuf([P, D], name="xtd")
    pm = b.psum([P, 512], name="pmd")
    rt = b.sbuf([P, D], name="rtd")
    for i in range(nT):
        p.dma("sync", ot[:], oin[i * P:(i + 1) * P, :], w=[ot])
        p.dma("gpsimd", xt[:], x_in[i * P:(i + 1) * P, :], w=[xt])
        for c in range(8):
            p.op("tensor", lambda e, c=c: e.transpose(ptr[:], ot[:, c * P:(c + 1) * P], eyeT[:]),
                 r=[ot, eyeT], w=[ptr])
            p.op("scalar", lambda e, c=c: e.copy(out=of[:, c * P:(c + 1) * P], in_=ptr[:]),
                 r=[ptr], w=[of])
        for j in range(2):
            for kk in range(8):
                p.op("tensor", lambda e, j=j, kk=kk: e.matmul(
                    pm[:, :], of[:, kk * P:(kk + 1) * P],
                    w_sb[:, kk * D + j * 512:kk * D + (j + 1) * 512],
                    start=(kk == 0), stop=(kk == 7)), r=[of, w_sb], w=[pm])
            p.op("vector", lambda e, j=j: e.tensor_add(out=rt[:, j * 512:(j + 1) * 512],
                                                       in0=pm[:, :], in1=xt[:, j * 512:(j + 1) * 512]),
                 r=[pm, xt], w=[rt])
        p.dma("sync", out[i * P:(i + 1) * P, :], rt[:], r=[rt], w=[out])
    p.op("gpsimd", lambda e: e.engine_nop(), no_inc=True, r=[out])
    nc = b.build()
    _prog_cache[key] = nc
    return nc


def run_l1ab(x, W_qkv, W_out, a_s, b_s, dt_s):
    B, T, D = x.shape
    rows = x.reshape(B * T, D)
    R = (B * T) // NCORES
    eye = np.eye(P, dtype=np.float32).astype(NP_BF16)
    nc1 = build_l1ab_proj(R)
    in_maps = [{"x": np.ascontiguousarray(rows[c * R:(c + 1) * R]), "eye": eye,
                "wT": _bf(W_qkv.T)} for c in range(NCORES)]
    res = run_bass_kernel_spmd(nc1, in_maps, core_ids=list(range(NCORES)))
    qkv = np.concatenate([np.asarray(res.results[c]["qkv"]) for c in range(NCORES)], 0)
    qkv = qkv.reshape(B, T, 3, 16, 64)
    nc2 = build_l1ab_attn(T, 512, 4)
    in_maps = []
    for c in range(NCORES):
        bb, hg = c // 4, c % 4
        hs = slice(hg * 4, hg * 4 + 4)
        q = np.ascontiguousarray(qkv[bb, :, 0, hs].transpose(1, 2, 0).reshape(256, T))
        k = np.ascontiguousarray(qkv[bb, :, 1, hs].transpose(1, 2, 0).reshape(256, T))
        v = np.ascontiguousarray(qkv[bb, :, 2, hs].transpose(1, 0, 2).reshape(T, 256))
        in_maps.append({"qfm": q, "kfm": k, "vtk": v,
                        "a_in": np.full((1, 1), a_s, np.float32),
                        "b_in": np.full((1, 1), b_s, np.float32),
                        "dt_in": np.full((1, 1), dt_s, np.float32)})
    res2 = run_bass_kernel_spmd(nc2, in_maps, core_ids=list(range(NCORES)))
    out_all = np.zeros((B, T, 16, 64), NP_BF16)
    for c in range(NCORES):
        bb, hg = c // 4, c % 4
        o = np.asarray(res2.results[c]["out"]).reshape(T, 4, 64)
        out_all[bb, :, hg * 4:hg * 4 + 4] = o
    oin = out_all.reshape(B * T, D)
    nc3 = build_l1ab_out(R)
    in_maps = [{"x": np.ascontiguousarray(rows[c * R:(c + 1) * R]),
                "oin": np.ascontiguousarray(oin[c * R:(c + 1) * R]), "eye": eye,
                "wT": _bf(W_out.T)} for c in range(NCORES)]
    res3 = run_bass_kernel_spmd(nc3, in_maps, core_ids=list(range(NCORES)))
    outr = np.concatenate([res3.results[c]["out"] for c in range(NCORES)], 0)
    return outr.reshape(B, T, D)
